# version 02
# baseline (speedup 1.0000x reference)
import numpy as np
import ml_dtypes
from contextlib import ExitStack
import concourse.bass as bass
import concourse.mybir as mybir
from concourse.bass_utils import run_bass_kernel_spmd

F32 = mybir.dt.float32
BF16 = mybir.dt.bfloat16
ALU = mybir.AluOpType
ACT = mybir.ActivationFunctionType
AX = mybir.AxisListType.X

S = 2048
D = 1024
NT = 16
DC = 8
NH = 16
E = 8
CAP = 512
HR = 5464
HP = 5504
NHC = 43
NW01 = 10
WO_PAIRS = 21


def build_nc(upto="full", nhrun=NH):
    nc = bass.Bass("TRN2", target_bir_lowering=False)

    def din(name, shape, dt=F32):
        return nc.dram_tensor(name, shape, dt, kind="ExternalInput").ap()

    X = din("x", [S, D])
    WQ = din("wq", [D, D])
    WK = din("wk", [D, D])
    WV = din("wv", [D, D])
    WO = din("wo", [D, D])
    RW = din("rw", [D, E])
    W0 = din("w0", [E, D, HR])
    W1 = din("w1", [E, D, HR])
    WOE = din("woe", [E, HR, D])
    IOTA = din("iota", [128, 512])
    IDB = din("idb", [128, 128], BF16)
    IDF = din("idf", [128, 128])
    TRI = din("tri", [128, 128])
    ONESF = din("onesf", [128, 128])
    MASKT = din("maskt", [128, 128])
    ONECOL = din("onecol", [128, 1])
    ONECOLB = din("onecolb", [128, 1], BF16)
    ONEROW = din("onerow", [1, 128])
    E01 = din("e01", [1, 128], BF16)
    E10 = din("e10", [1, 128], BF16)

    Y = nc.dram_tensor("y", [S, D], F32, kind="ExternalOutput").ap()
    NDBG = {"ln1": DC * S, "attn": (NH // 2) * S, "gate": 10 * NT * E,
            "gather": E * DC * 512, "vprime": NT * NH * 65}.get(upto, 1)
    DBG = nc.dram_tensor("dbg", [128, NDBG], F32, kind="ExternalOutput").ap()
    X1S = nc.dram_tensor("x1s", [S, D], F32, kind="Internal").ap()
    XL2S = nc.dram_tensor("xl2s", [S, D], BF16, kind="Internal").ap()
    AVS = nc.dram_tensor("avs", [NH // 2, 128, S], BF16, kind="Internal").ap()
    EOS = nc.dram_tensor("eos", [E, CAP, D], BF16, kind="Internal").ap()

    Xt = X.rearrange("(i p) m -> i p m", p=128)
    X1t = X1S.rearrange("(i p) m -> i p m", p=128)
    XL2t = XL2S.rearrange("(i p) m -> i p m", p=128)
    EOSt = EOS.rearrange("e (k p) m -> e p k m", p=128)

    stack = ExitStack()
    scopes = []
    _ES = ExitStack

    def ExitStackR():
        es = _ES()
        scopes.append(es)
        return es

    def finish():
        for es in reversed(scopes):
            es.close()
        stack.close()

    def sb(es, name, shape, dt):
        return es.enter_context(nc.sbuf_tensor(name, shape, dt))

    def ps(es, name, shape, dt=F32):
        return es.enter_context(nc.psum_tensor(name, shape, dt))

    class Ctr:
        def __init__(self, name):
            self.h = stack.enter_context(nc.semaphore(name))
            self.n = 0

        def bump(self, k=1):
            self.n += k
            return self.n

    dC = Ctr("dC")
    dW = Ctr("dW")
    dEo = Ctr("dEo")
    dY = Ctr("dY")
    eT = Ctr("eT")
    eV = Ctr("eV")
    eS = Ctr("eS")
    eG = Ctr("eG")
    dXi = [Ctr("dXi0"), Ctr("dXi1")]
    dX2 = [Ctr("dX20"), Ctr("dX21")]
    dAv = [Ctr("dAv0"), Ctr("dAv1")]
    dSpX = [Ctr("dSpX0"), Ctr("dSpX1")]
    dSpL = [Ctr("dSpL0"), Ctr("dSpL1")]
    dBr = Ctr("dBr")
    dRc = Ctr("dRc")
    dAt = Ctr("dAt")
    dR0 = Ctr("dR0")
    dR1 = Ctr("dR1")
    dAl = [Ctr("dAl0"), Ctr("dAl1")]
    dAw = [Ctr("dAw0"), Ctr("dAw1")]
    dW0 = [Ctr("dW00"), Ctr("dW01"), Ctr("dW02")]
    dW1 = [Ctr("dW10"), Ctr("dW11"), Ctr("dW12")]
    dWt = [Ctr("dWt0"), Ctr("dWt1")]
    dWo = [Ctr("dWo0"), Ctr("dWo1")]
    dWot = Ctr("dWot")
    dEr = [Ctr("dEr0"), Ctr("dEr1")]
    dGr = [Ctr("dGr0"), Ctr("dGr1")]
    dXr = [Ctr("dXr0"), Ctr("dXr1")]

    gl = stack
    iota_sb = sb(gl, "iota_sb", [128, 512], F32)
    idb_sb = sb(gl, "idb_sb", [128, 128], BF16)
    idf_sb = sb(gl, "idf_sb", [128, 128], F32)
    tri_sb = sb(gl, "tri_sb", [128, 128], F32)
    onesf_sb = sb(gl, "onesf_sb", [128, 128], F32)
    maskt_sb = sb(gl, "maskt_sb", [128, 128], F32)
    onecol_sb = sb(gl, "onecol_sb", [128, 1], F32)
    onecolb_sb = sb(gl, "onecolb_sb", [128, 1], BF16)
    onerow_sb = sb(gl, "onerow_sb", [1, 128], F32)
    e01_sb = sb(gl, "e01_sb", [1, 128], BF16)
    e10_sb = sb(gl, "e10_sb", [1, 128], BF16)
    rw_sb = sb(gl, "rw_sb", [128, DC * E], F32)
    logits = sb(gl, "logits", [128, NT * E], F32)
    m1a = sb(gl, "m1a", [128, NT * E], F32)
    m2a = sb(gl, "m2a", [128, NT * E], F32)
    m1k = sb(gl, "m1k", [128, NT * E], F32)
    m2k = sb(gl, "m2k", [128, NT * E], F32)
    q1a = sb(gl, "q1a", [128, NT * E], F32)
    q2a = sb(gl, "q2a", [128, NT * E], F32)
    p1x = sb(gl, "p1x", [128, NT * E], F32)
    p2x = sb(gl, "p2x", [128, NT * E], F32)
    g1kc = sb(gl, "g1kc", [128, NT], F32)
    g2kc = sb(gl, "g2kc", [128, NT], F32)
    county = sb(gl, "county", [1, E], F32)
    gsc = sb(gl, "gsc", [128, 48], F32)

    block = stack.enter_context(nc.Block())

    def on(eng):
        return getattr(block, eng)

    @on("sync")
    def _(sy):
        for dst, src in [(iota_sb, IOTA), (idb_sb, IDB), (idf_sb, IDF),
                         (tri_sb, TRI), (onesf_sb, ONESF), (maskt_sb, MASKT),
                         (onecol_sb, ONECOL), (onecolb_sb, ONECOLB),
                         (onerow_sb, ONEROW), (e01_sb, E01), (e10_sb, E10)]:
            sy.dma_start(out=dst[:], in_=src).then_inc(dC.h, 16)
        sy.dma_start(out=rw_sb[:],
                     in_=RW.rearrange("(dc p) e -> p dc e", p=128)).then_inc(dC.h, 16)
    CONSTS = dC.bump(16 * 12)

    es_at = ExitStackR()
    es_qk = ExitStackR()
    es_ws = ExitStackR()
    es_wv = ExitStackR()
    es_ln = ExitStackR()
    avtmp = sb(es_at, "avtmp", [64, S], BF16)
    recrow = sb(es_at, "recrow", [65, S], BF16)
    recall = sb(es_at, "recall", [16, S], BF16)
    wqT = sb(es_qk, "wqT", [128, DC * D], BF16)
    wkT = sb(es_qk, "wkT", [128, DC * D], BF16)
    xfm = sb(es_qk, "xfm", [128, DC * S], BF16)
    vall = sb(es_qk, "vall", [128, NT * NH * 65], BF16)
    qbuf = [sb(es_qk, f"qb{i}", [65, S], BF16) for i in range(2)]
    kbuf = [sb(es_qk, f"kb{i}", [65, S], BF16) for i in range(2)]
    ptb = [sb(es_qk, f"ptb{i}", [128, S], BF16) for i in range(2)]
    sqq = sb(es_qk, "sqq", [64, S], BF16)
    nrow = sb(es_qk, "nrow", [1, S + 16], F32)
    brow = sb(es_qk, "brow", [1, S], BF16)
    wstage = sb(es_ws, "wstage", [128, D], BF16)
    wvT = sb(es_wv, "wvT", [128, DC * D], BF16)
    xin = [sb(es_ln, f"xin{i}", [128, D], F32) for i in range(2)]
    xlt = sb(es_ln, "xlt", [128, D], BF16)
    sqd = sb(es_ln, "sqd", [128, D], BF16)
    mcol = sb(es_ln, "mcol", [128, 16], F32)

    es_ptr = ExitStackR()
    ptr = ps(es_ptr, "ptr", [128, 1024], BF16)

    def load_wT(Wsrc, dstT, stage_free_t, ptr_free_v, ptrps, stg):
        Wr = Wsrc.rearrange("(oc p) d -> oc p d", p=128)
        ndma = [dW.bump(16) for _ in range(DC)]
        t_vals = [eT.bump() for _ in range(DC)]
        v_vals = [eV.bump() for _ in range(DC)]

        @on("gpsimd")
        def _(gp):
            for oc in range(DC):
                if oc == 0 and stage_free_t:
                    gp.wait_ge(eT.h, stage_free_t)
                if oc >= 1:
                    gp.wait_ge(eT.h, t_vals[oc - 1])
                gp.dma_start(out=stg[:], in_=Wr[oc]).then_inc(dW.h, 16)

        @on("tensor")
        def _(t):
            t.wait_ge(dC.h, CONSTS)
            for oc in range(DC):
                t.wait_ge(dW.h, ndma[oc])
                if oc == 0 and ptr_free_v:
                    t.wait_ge(eV.h, ptr_free_v)
                if oc >= 1:
                    t.wait_ge(eV.h, v_vals[oc - 1])
                for dc in range(DC):
                    ins = t.transpose(ptrps[:, dc * 128:dc * 128 + 128],
                                      stg[:, dc * 128:dc * 128 + 128],
                                      idb_sb[:])
                ins.then_inc(eT.h, 1)

        @on("vector")
        def _(v):
            for oc in range(DC):
                v.wait_ge(eT.h, t_vals[oc])
                dst = dstT[:].rearrange("p (dc x) -> p dc x", dc=DC)[:, :, oc * 128:oc * 128 + 128]
                src = ptrps[:].rearrange("p (dc x) -> p dc x", dc=DC)
                v.tensor_copy(dst, src).then_inc(eV.h, 1)
        return t_vals[-1], v_vals[-1]

    wv_t, wv_v = load_wT(WV, wvT, 0, 0, ptr, wstage)

    ln_nx = [dXi[i % 2].bump(16) for i in range(NT)]
    ln_mean, ln_var, ln_norm, ln_ev = [], [], [], []
    ln_sq, ln_rstd, ln_tp = [], [], []
    for i in range(NT):
        ln_mean.append(eV.bump())
        ln_var.append(eV.bump())
        ln_norm.append(eV.bump())
        ln_ev.append(eV.bump())
        ln_sq.append(eS.bump())
        ln_rstd.append(eS.bump())
        ln_tp.append(eT.bump())

    @on("sync")
    def _(sy):
        for i in range(NT):
            if i >= 2:
                sy.wait_ge(eS.h, ln_sq[i - 2])
                sy.wait_ge(eV.h, ln_norm[i - 2])
            sy.dma_start(out=xin[i % 2][:], in_=Xt[i]).then_inc(dXi[i % 2].h, 16)

    @on("vector")
    def _(v):
        for i in range(NT):
            b = (i % 2) * 8
            v.wait_ge(dXi[i % 2].h, ln_nx[i])
            v.reduce_sum(mcol[:, b:b + 1], xin[i % 2][:], axis=AX)
            v.drain()
            v.tensor_scalar(mcol[:, b:b + 1], mcol[:, b:b + 1], 1.0 / D, 0.0,
                            ALU.mult, ALU.add).then_inc(eV.h, 1)
            v.wait_ge(eS.h, ln_sq[i])
            v.wait_ge(eV.h, ln_mean[i])
            v.tensor_scalar(mcol[:, b + 2:b + 3], mcol[:, b + 1:b + 2], 1.0 / D, 0.0,
                            ALU.mult, ALU.add)
            v.tensor_scalar(mcol[:, b + 3:b + 4], mcol[:, b:b + 1], mcol[:, b:b + 1],
                            None, ALU.mult)
            v.drain()
            v.tensor_scalar(mcol[:, b + 2:b + 3], mcol[:, b + 2:b + 3],
                            mcol[:, b + 3:b + 4], None, ALU.subtract)
            v.drain()
            v.tensor_scalar(mcol[:, b + 2:b + 3], mcol[:, b + 2:b + 3], 1e-5, 1.0,
                            ALU.add, ALU.mult).then_inc(eV.h, 1)
            v.wait_ge(eS.h, ln_rstd[i])
            if i >= 1:
                v.wait_ge(eT.h, ln_tp[i - 1])
            v.tensor_scalar(xlt[:], xin[i % 2][:], mcol[:, b:b + 1], mcol[:, b + 4:b + 5],
                            ALU.subtract, ALU.mult).then_inc(eV.h, 1)
            v.wait_ge(eT.h, ln_tp[i])
            dst = xfm[:].rearrange("p (dc x) -> p dc x", dc=DC)[:, :, i * 128:i * 128 + 128]
            src = ptr[:].rearrange("p (dc x) -> p dc x", dc=DC)
            v.tensor_copy(dst, src).then_inc(eV.h, 1)

    @on("scalar")
    def _(s):
        for i in range(NT):
            b = (i % 2) * 8
            s.wait_ge(dXi[i % 2].h, ln_nx[i])
            s.activation(sqd[:], xin[i % 2][:], ACT.Square,
                         accum_out=mcol[:, b + 1:b + 2]).then_inc(eS.h, 1)
            s.wait_ge(eV.h, ln_var[i])
            s.activation(mcol[:, b + 3:b + 4], mcol[:, b + 2:b + 3], ACT.Ln)
            s.drain()
            s.activation(mcol[:, b + 4:b + 5], mcol[:, b + 3:b + 4], ACT.Exp,
                         scale=-0.5).then_inc(eS.h, 1)

    @on("tensor")
    def _(t):
        for i in range(NT):
            t.wait_ge(eV.h, ln_norm[i])
            t.wait_ge(eV.h, wv_v if i == 0 else ln_ev[i - 1])
            for dc in range(DC):
                ins = t.transpose(ptr[:, dc * 128:dc * 128 + 128],
                                  xlt[:, dc * 128:dc * 128 + 128], idb_sb[:])
            ins.then_inc(eT.h, 1)

    XFM_V = ln_ev[-1]

    if upto == "ln1":
        FV = dY.bump(16)

        @on("gpsimd")
        def _(gp):
            gp.wait_ge(eV.h, XFM_V)
            gp.dma_start(out=DBG, in_=xfm[:]).then_inc(dY.h, 16)

        @on("sync")
        def _(sy):
            sy.wait_ge(dY.h, FV)
        return nc, finish

    es_vp = ExitStackR()
    pvp = [ps(es_vp, f"pvp{i}", [128, 512]) for i in range(2)]
    VONES = eV.bump()
    vp_t, vp_v = [], []
    for _ in range(NT * 2):
        vp_t.append(eT.bump())
        vp_v.append(eV.bump())

    @on("vector")
    def _(v):
        v.memset(vall[:], 1.0).then_inc(eV.h, 1)

    @on("tensor")
    def _(t):
        t.wait_ge(eV.h, max(wv_v, XFM_V))
        k = 0
        for i in range(NT):
            for h2 in range(2):
                if k >= 2:
                    t.wait_ge(eV.h, vp_v[k - 2])
                for dc in range(DC):
                    ins = t.matmul(pvp[k % 2][:],
                                   xfm[:, dc * S + i * 128: dc * S + i * 128 + 128],
                                   wvT[:, dc * D + h2 * 512: dc * D + h2 * 512 + 512],
                                   start=(dc == 0), stop=(dc == DC - 1))
                ins.then_inc(eT.h, 1)
                k += 1

    @on("vector")
    def _(v):
        v.wait_ge(eV.h, VONES)
        k = 0
        for i in range(NT):
            for h2 in range(2):
                v.wait_ge(eT.h, vp_t[k])
                dst = vall[:, i * NH * 65: (i + 1) * NH * 65]
                dst = dst.rearrange("p (h c) -> p h c", h=NH)[:, h2 * 8:(h2 + 1) * 8, 0:64]
                src = pvp[k % 2][:].rearrange("p (h c) -> p h c", c=64)
                v.tensor_copy(dst, src).then_inc(eV.h, 1)
                k += 1
    es_vp.close()
    es_wv_close_marker = vp_v[-1]

    if upto == "vprime":
        FV = dY.bump(16)

        @on("gpsimd")
        def _(gp):
            gp.wait_ge(eV.h, vp_v[-1])
            gp.dma_start(out=DBG, in_=vall[:]).then_inc(dY.h, 16)

        @on("sync")
        def _(sy):
            sy.wait_ge(dY.h, FV)
        return nc, finish

    wq_t, wq_v = load_wT(WQ, wqT, wv_t, ln_ev[-1], ptr, wstage)
    wk_t, wk_v = load_wT(WK, wkT, wq_t, wq_v, ptr, wstage)
    es_ln.close()
    es_wv.close()
    es_ws.close()
    es_ptr.close()

    es_hd = ExitStackR()
    pqk = [ps(es_hd, f"pqk{i}", [64, 512]) for i in range(2)]
    psc = [ps(es_hd, f"psc{i}", [128, 512]) for i in range(2)]
    pav = ps(es_hd, "pav", [65, S])

    KONES = eG.bump()

    @on("gpsimd")
    def _(gp):
        gp.memset(kbuf[0][64:65, :], 1.0)
        gp.memset(kbuf[1][64:65, :], 1.0).then_inc(eG.h, 1)

    qk_free = [0, 0]
    pav_free_s = 0
    pav_free_v2 = 0
    avtmp_free = 0
    psc_free = [0, 0]
    pqk_free = [("S", 0), ("S", 0)]
    ptb_free = [0, 0]
    rec_free = 0

    for h in range(nhrun):
        hp = h % 2
        pr = h // 2
        qkt = [eT.bump() for _ in range(8)]
        qev = [eV.bump() for _ in range(4)]
        kev = [eV.bump() for _ in range(4)]

        @on("tensor")
        def _(t_, h=h, hp=hp, qw=qk_free[hp], pqw=list(pqk_free),
              qev=qev, kev=kev):
            if h == 0:
                t_.wait_ge(eV.h, wk_v)
            if qw:
                t_.wait_ge(eT.h, qw)
            k = 0
            for wT in (wqT, wkT):
                for c in range(4):
                    if k >= 2:
                        t_.wait_ge(eV.h, (qev + kev)[k - 2])
                    else:
                        kind, val = pqw[k % 2]
                        if val:
                            t_.wait_ge((eS if kind == "S" else eV).h, val)
                    for dc in range(DC):
                        ins = t_.matmul(pqk[k % 2][:],
                                        wT[:, dc * D + h * 64: dc * D + h * 64 + 64],
                                        xfm[:, dc * S + c * 512: dc * S + c * 512 + 512],
                                        start=(dc == 0), stop=(dc == DC - 1))
                    ins.then_inc(eT.h, 1)
                    k += 1

        @on("vector")
        def _(v, hp=hp, qkt=qkt):
            for c in range(4):
                v.wait_ge(eT.h, qkt[c])
                v.tensor_scalar(qbuf[hp][0:64, c * 512:(c + 1) * 512], pqk[c % 2][:],
                                0.125, 0.0, ALU.mult, ALU.add).then_inc(eV.h, 1)
            for c in range(4):
                v.wait_ge(eT.h, qkt[4 + c])
                v.tensor_copy(kbuf[hp][0:64, c * 512:(c + 1) * 512],
                              pqk[c % 2][:]).then_inc(eV.h, 1)

        sq_k = eV.bump()
        nmk_t = [eT.bump() for _ in range(4)]
        nmk_s = [eS.bump() for _ in range(4)]
        mk_done = eV.bump()
        sq_q = eV.bump()
        nmq_t = [eT.bump() for _ in range(4)]
        nmq_s = [eS.bump() for _ in range(4)]
        bexp_done = eS.bump()
        brow_dma = dBr.bump(16)

        @on("vector")
        def _(v, hp=hp, kev=kev, qev=qev, nmk_t=nmk_t, nmk_s=nmk_s):
            v.wait_ge(eV.h, kev[-1])
            v.tensor_tensor(sqq[:], kbuf[hp][0:64, :], kbuf[hp][0:64, :],
                            ALU.mult).then_inc(eV.h, 1)
            v.wait_ge(eS.h, nmk_s[-1])
            v.reduce_max(nrow[0:1, S:S + 1], nrow[0:1, 0:S],
                         axis=AX).then_inc(eV.h, 1)
            v.wait_ge(eT.h, nmk_t[-1])
            v.tensor_tensor(sqq[:], qbuf[hp][0:64, :], qbuf[hp][0:64, :],
                            ALU.mult).then_inc(eV.h, 1)

        @on("scalar")
        def _(s, nmk_t=nmk_t, nmq_t=nmq_t, mk_done=mk_done, nmq_s=nmq_s):
            for c in range(4):
                s.wait_ge(eT.h, nmk_t[c])
                s.copy(nrow[0:1, c * 512:(c + 1) * 512],
                       pqk[c % 2][0:1, :]).then_inc(eS.h, 1)
            for c in range(4):
                s.wait_ge(eT.h, nmq_t[c])
                s.copy(brow[0:1, c * 512:(c + 1) * 512],
                       pqk[c % 2][0:1, :]).then_inc(eS.h, 1)
            s.wait_ge(eV.h, mk_done)
            s.wait_ge(eS.h, nmq_s[-1])
            s.activation(nrow[0:1, S + 1:S + 2], nrow[0:1, S:S + 1], ACT.Ln)
            s.drain()
            s.activation(nrow[0:1, S + 2:S + 3], nrow[0:1, S + 1:S + 2],
                         ACT.Identity, scale=0.5)
            s.drain()
            s.activation(nrow[0:1, 0:S], brow[0:1, 0:S], ACT.Ln)
            s.drain()
            s.activation(nrow[0:1, 0:S], nrow[0:1, 0:S], ACT.Exp, scale=0.5,
                         bias=nrow[0:1, S + 2:S + 3])
            s.drain()
            s.mul(brow[:], nrow[0:1, 0:S], -1.05).then_inc(eS.h, 1)

        @on("tensor")
        def _(t_, sq_k=sq_k, sq_q=sq_q, nmk_s=nmk_s, nmq_s=nmq_s, kev=kev):
            t_.wait_ge(eV.h, sq_k)
            for c in range(4):
                if c >= 2:
                    t_.wait_ge(eS.h, nmk_s[c - 2])
                t_.matmul(pqk[c % 2][0:1, :], onecolb_sb[0:64, :],
                          sqq[:, c * 512:(c + 1) * 512],
                          start=True, stop=True).then_inc(eT.h, 1)
            t_.wait_ge(eV.h, sq_q)
            for c in range(4):
                t_.wait_ge(eS.h, (nmk_s + nmq_s)[c + 2])
                t_.matmul(pqk[c % 2][0:1, :], onecolb_sb[0:64, :],
                          sqq[:, c * 512:(c + 1) * 512],
                          start=True, stop=True).then_inc(eT.h, 1)

        @on("sync")
        def _(sy, hp=hp, bexp_done=bexp_done):
            sy.wait_ge(eS.h, bexp_done)
            sy.dma_start(out=qbuf[hp][64:65, :], in_=brow[:]).then_inc(dBr.h, 16)

        pqk_free = [("S", nmq_s[2]), ("S", nmq_s[3])]

        chunks = []
        for j in range(NT):
            for c in range(j // 4, 4):
                chunks.append((j, c))
        par = {jc: k % 2 for k, jc in enumerate(chunks)}
        sc_t, av_t, mask_v, exp_s = {}, {}, {}, {}
        for k, (j, c) in enumerate(chunks):
            sc_t[(j, c)] = eT.bump()
            if c == j // 4:
                mask_v[(j, c)] = eV.bump()
            av_t[(j, c)] = eT.bump()
        for (j, c) in chunks:
            exp_s[(j, c)] = eS.bump()

        @on("tensor")
        def _(t_, h=h, hp=hp, brow_dma=brow_dma, chunks=chunks, par=par,
              exp_s=exp_s, psc_w=list(psc_free), pav_s=pav_free_s,
              pav_v2=(pav_free_v2 if h > 0 else 0),
              av_free=avtmp_free, rec_f=rec_free):
            t_.wait_ge(dBr.h, brow_dma)
            if h == 0:
                t_.wait_ge(eG.h, KONES)
                t_.wait_ge(eV.h, vp_v[-1])
            if pav_s:
                t_.wait_ge(eS.h, pav_s)
            if pav_v2:
                t_.wait_ge(eV.h, pav_v2)
            if av_free:
                t_.wait_ge(dAt.h, av_free)
            if rec_f:
                t_.wait_ge(dRc.h, rec_f)
            for k, (j, c) in enumerate(chunks):
                off = j * 128 - c * 512 if c == j // 4 else 0
                w = 512 - off
                q0 = c * 512 + off
                if k >= 2:
                    t_.wait_ge(eS.h, exp_s[chunks[k - 2]])
                elif psc_w[k % 2]:
                    t_.wait_ge(eS.h, psc_w[k % 2])
                t_.matmul(psc[k % 2][:, 0:w], kbuf[hp][:, j * 128:(j + 1) * 128],
                          qbuf[hp][:, q0:q0 + w],
                          start=True, stop=True).then_inc(eT.h, 1)
                t_.wait_ge(eS.h, exp_s[(j, c)])
                t_.matmul(pav[0:65, q0:q0 + w],
                          vall[:, j * NH * 65 + h * 65: j * NH * 65 + (h + 1) * 65],
                          ptb[j % 2][:, q0:q0 + w],
                          start=(j == 0), stop=(j == min(15, 4 * c + 3))
                          ).then_inc(eT.h, 1)

        @on("vector")
        def _(v, sc_t=sc_t, par=par):
            for j in range(NT):
                c = j // 4
                v.wait_ge(eT.h, sc_t[(j, c)])
                p = psc[par[(j, c)]]
                v.tensor_tensor(p[:, 0:128], p[:, 0:128], maskt_sb[:],
                                ALU.add).then_inc(eV.h, 1)

        @on("scalar")
        def _(s, chunks=chunks, par=par, sc_t=sc_t, mask_v=mask_v,
              av_t=av_t, ptb_w=list(ptb_free)):
            for j in range(NT):
                if j >= 2:
                    s.wait_ge(eT.h, av_t[(j - 2, 3)])
                elif ptb_w[j % 2]:
                    s.wait_ge(eT.h, ptb_w[j % 2])
                for c in range(j // 4, 4):
                    off = j * 128 - c * 512 if c == j // 4 else 0
                    w = 512 - off
                    q0 = c * 512 + off
                    s.wait_ge(eT.h, sc_t[(j, c)])
                    if c == j // 4:
                        s.wait_ge(eV.h, mask_v[(j, c)])
                    s.activation(ptb[j % 2][:, q0:q0 + w], psc[par[(j, c)]][:, 0:w],
                                 ACT.Exp).then_inc(eS.h, 1)

        psc_free = [0, 0]
        for (j, c) in chunks:
            psc_free[par[(j, c)]] = exp_s[(j, c)]
        ptb_free = [av_t[(14, 3)], av_t[(15, 3)]]
        qk_free[hp] = av_t[(15, 3)]

        sig_evac = eS.bump()
        av_evac = eV.bump()

        @on("scalar")
        def _(s, h=h, av_t=av_t, rec_f=rec_free):
            s.wait_ge(eT.h, av_t[(15, 3)])
            if rec_f:
                s.wait_ge(dRc.h, rec_f)
            s.copy(recrow[64:65, :], pav[64:65, :]).then_inc(eS.h, 1)

        @on("vector")
        def _(v, av_t=av_t, av_free=avtmp_free):
            v.wait_ge(eT.h, av_t[(15, 3)])
            if av_free:
                v.wait_ge(dAt.h, av_free)
            v.tensor_copy(avtmp[:], pav[0:64, :]).then_inc(eV.h, 1)

        @on("sync")
        def _(sy, h=h, sig_evac=sig_evac):
            sy.wait_ge(eS.h, sig_evac)
            sy.dma_start(out=recall[h:h + 1, :], in_=recrow[64:65, :]).then_inc(dRc.h, 16)
        rec_free = dRc.bump(16)

        @on("sync")
        def _(sy, hp=hp, pr=pr, av_evac=av_evac):
            sy.wait_ge(eV.h, av_evac)
            sy.dma_start(out=AVS[pr, hp * 64:hp * 64 + 64, :],
                         in_=avtmp[:]).then_inc(dAt.h, 16)
        avtmp_free = dAt.bump(16)
        pav_free_s = sig_evac
        pav_free_v2 = av_evac

    SIG_S = pav_free_s
    HEADS_LAST_T = max(qk_free)
    es_hd.close()
    es_qk.close()

    es_wo = ExitStackR()
    woT = sb(es_wo, "woT", [128, DC * D], BF16)
    es_ln2 = ExitStackR()
    xin2 = [sb(es_ln2, f"xin2{i}", [128, D], F32) for i in range(2)]
    x1b = [sb(es_ln2, f"x1b{i}", [128, D], F32) for i in range(2)]
    xl2f = sb(es_ln2, "xl2f", [128, D], F32)
    xfm32 = sb(es_ln2, "xfm32", [128, D], F32)
    sqd2 = sb(es_ln2, "sqd2", [128, D], BF16)
    mcol2 = sb(es_ln2, "mcol2", [128, 16], F32)
    xl2b = [sb(es_ln2, f"xl2b{i}", [128, D], BF16) for i in range(2)]
    avo = [sb(es_ln2, f"avo{i}", [128, D], BF16) for i in range(2)]
    es_ws2 = ExitStackR()
    wstage2 = sb(es_ws2, "wstage2", [128, D], BF16)
    es_rec = ExitStackR()
    rec0 = sb(es_rec, "rec0", [1, S], BF16)
    rec1 = sb(es_rec, "rec1", [1, S], BF16)
    avsc = [sb(es_rec, f"avsc{i}", [128, S], BF16) for i in range(2)]
    sigr = sb(es_rec, "sigr", [128, 512], F32)

    es_sg = ExitStackR()
    psb = [ps(es_sg, f"psb{i}", [128, 512]) for i in range(2)]

    sg_r0, sg_r1, sg_ld, sg_wb = [], [], [], []
    sg_t = []
    sg_v = []
    for pr in range(nhrun // 2):
        sg_r0.append(dR0.bump(16))
        sg_r1.append(dR1.bump(16))
        sg_ld.append(dAl[pr % 2].bump(16))
        for c in range(4):
            sg_t.append(eT.bump())
            sg_v.append(eV.bump())
        sg_wb.append(dAw[pr % 2].bump(16))

    @on("sync")
    def _(sy):
        sy.wait_ge(dBr.h, dBr.n)
        sy.wait_ge(dRc.h, dRc.n)
        sy.wait_ge(dAt.h, dAt.n)
        for pr in range(nhrun // 2):
            if pr >= 1:
                sy.wait_ge(eT.h, sg_t[pr * 4 - 1])
            sy.dma_start(out=rec0[:], in_=recall[2 * pr:2 * pr + 1, :]).then_inc(dR0.h, 16)
            sy.dma_start(out=rec1[:], in_=recall[2 * pr + 1:2 * pr + 2, :]).then_inc(dR1.h, 16)
            if pr >= 2:
                sy.wait_ge(dAw[pr % 2].h, sg_wb[pr - 2])
            sy.dma_start(out=avsc[pr % 2][:], in_=AVS[pr]).then_inc(dAl[pr % 2].h, 16)
            sy.wait_ge(eV.h, sg_v[pr * 4 + 3])
            sy.dma_start(out=AVS[pr], in_=avsc[pr % 2][:]).then_inc(dAw[pr % 2].h, 16)

    @on("tensor")
    def _(t_):
        k = 0
        for pr in range(nhrun // 2):
            t_.wait_ge(dR0.h, sg_r0[pr])
            t_.wait_ge(dR1.h, sg_r1[pr])
            for c in range(4):
                if k >= 2:
                    t_.wait_ge(eV.h, sg_v[k - 2])
                t_.matmul(psb[k % 2][:], e01_sb[:], rec0[0:1, c * 512:(c + 1) * 512],
                          start=True, stop=False)
                t_.matmul(psb[k % 2][:], e10_sb[:], rec1[0:1, c * 512:(c + 1) * 512],
                          start=False, stop=True).then_inc(eT.h, 1)
                k += 1

    @on("vector")
    def _(v):
        v.wait_ge(eS.h, SIG_S)
        k = 0
        for pr in range(nhrun // 2):
            v.wait_ge(dAl[pr % 2].h, sg_ld[pr])
            for c in range(4):
                v.wait_ge(eT.h, sg_t[k])
                sl = avsc[pr % 2][:, c * 512:(c + 1) * 512]
                v.drain()
                v.reciprocal(sigr[:], psb[k % 2][:])
                v.drain()
                v.tensor_tensor(sl, sl, sigr[:], ALU.mult).then_inc(eV.h, 1)
                k += 1
    SGV = eV.n
    es_sg.close()

    if upto == "attn":
        FV = dY.bump(16)

        @on("gpsimd")
        def _(gp):
            gp.wait_ge(dAw[0].h, dAw[0].n)
            gp.wait_ge(dAw[1].h, dAw[1].n)
            gp.dma_start(out=DBG[:].rearrange("p (r q) -> p r q", r=NH // 2),
                         in_=AVS.rearrange("r p q -> p r q")).then_inc(dY.h, 16)

        @on("sync")
        def _(sy):
            sy.wait_ge(dY.h, FV)
        return nc, finish

    es_rec.close()
    es_ptr2 = ExitStackR()
    ptr2 = ps(es_ptr2, "ptr2", [128, 1024], BF16)
    wo_t, wo_v = load_wT(WO, woT, HEADS_LAST_T, sg_v[-1], ptr2, wstage2)
    es_ptr2.close()
    es_ws2.close()

    es_op = ExitStackR()
    pop = [ps(es_op, f"pop{i}", [128, 512]) for i in range(2)]
    prt = ps(es_op, "prt", [128, 1024])
    plg = ps(es_op, "plg", [128, 512])

    op_nx, op_t, op_add, op_var, op_x2, op_tp = [], [], [], [], [], []
    op_ev32, op_lg, op_lgv, op_sq, op_rstd = [], [], [], [], []
    gt_m, gt_s, gt_v, op_spill, op_l2sp = [], [], [], [], []
    op_av = []
    for i in range(NT):
        op_nx.append(dX2[i % 2].bump(16))
        op_av.append(dAv[i % 2].bump(16))
        op_t.append([eT.bump(), eT.bump()])
        op_add.append(eV.bump())
        op_var.append(eV.bump())
        op_x2.append(eV.bump())
        op_tp.append(eT.bump())
        op_ev32.append(eV.bump())
        op_lg.append(eT.bump())
        op_lgv.append(eV.bump())
        op_sq.append(eS.bump())
        op_rstd.append(eS.bump())
        op_spill.append(dSpX[i % 2].bump(16))
        op_l2sp.append(dSpL[i % 2].bump(16))
    for i in range(NT):
        gt_m.append(eV.bump())
        gt_v.append(eV.bump())
        gt_s.append(eS.bump())

    @on("sync")
    def _(sy):
        sy.wait_ge(dAw[0].h, dAw[0].n)
        sy.wait_ge(dAw[1].h, dAw[1].n)
        for i in range(NT):
            if i >= 2:
                sy.wait_ge(eV.h, op_add[i - 2])
                sy.wait_ge(eT.h, op_t[i - 2][1])
            sy.dma_start(out=avo[i % 2][:],
                         in_=AVS.rearrange("r p q -> p r q")[:, :, i * 128:(i + 1) * 128]
                         ).then_inc(dAv[i % 2].h, 16)
            sy.dma_start(out=xin2[i % 2][:], in_=Xt[i]).then_inc(dX2[i % 2].h, 16)
            sy.wait_ge(eV.h, op_x2[i])
            sy.dma_start(out=X1t[i], in_=x1b[i % 2][:]).then_inc(dSpX[i % 2].h, 16)
            sy.dma_start(out=XL2t[i], in_=xl2b[i % 2][:]).then_inc(dSpL[i % 2].h, 16)

    @on("tensor")
    def _(t_):
        t_.wait_ge(eV.h, wo_v)
        for i in range(NT):
            t_.wait_ge(dAv[i % 2].h, op_av[i])
            for dh in range(2):
                if i >= 1:
                    t_.wait_ge(eV.h, op_add[i - 1])
                for tc in range(DC):
                    ins = t_.matmul(pop[dh][:],
                                    avo[i % 2][:, tc * 128:(tc + 1) * 128],
                                    woT[:, tc * D + dh * 512: tc * D + dh * 512 + 512],
                                    start=(tc == 0), stop=(tc == DC - 1))
                ins.then_inc(eT.h, 1)
            t_.wait_ge(eV.h, op_x2[i])
            if i >= 1:
                t_.wait_ge(eV.h, op_ev32[i - 1])
            for dc in range(DC):
                ins = t_.transpose(prt[:, (dc % 8) * 128:(dc % 8) * 128 + 128],
                                   xl2f[:, dc * 128:dc * 128 + 128], idf_sb[:])
            ins.then_inc(eT.h, 1)
            t_.wait_ge(eV.h, op_ev32[i])
            if i >= 1:
                t_.wait_ge(eV.h, op_lgv[i - 1])
            for dc in range(DC):
                ins = t_.matmul(plg[:, 0:E], xfm32[:, dc * 128:dc * 128 + 128],
                                rw_sb[:, dc * E:(dc + 1) * E],
                                start=(dc == 0), stop=(dc == DC - 1))
            ins.then_inc(eT.h, 1)

    @on("vector")
    def _(v):
        for i in range(NT):
            b = (i % 2) * 8
            v.wait_ge(dX2[i % 2].h, op_nx[i])
            if i >= 2:
                v.wait_ge(eS.h, op_sq[i - 2])
                v.wait_ge(dSpX[i % 2].h, op_spill[i - 2])
            for dh in range(2):
                v.wait_ge(eT.h, op_t[i][dh])
                v.tensor_tensor(x1b[i % 2][:, dh * 512:(dh + 1) * 512],
                                xin2[i % 2][:, dh * 512:(dh + 1) * 512],
                                pop[dh][:], ALU.add)
            v.drain()
            v.reduce_sum(mcol2[:, b:b + 1], x1b[i % 2][:], axis=AX)
            v.drain()
            v.tensor_scalar(mcol2[:, b:b + 1], mcol2[:, b:b + 1], 1.0 / D, 0.0,
                            ALU.mult, ALU.add).then_inc(eV.h, 1)
            v.wait_ge(eS.h, op_sq[i])
            v.wait_ge(eV.h, op_add[i])
            v.tensor_scalar(mcol2[:, b + 2:b + 3], mcol2[:, b + 1:b + 2], 1.0 / D, 0.0,
                            ALU.mult, ALU.add)
            v.tensor_scalar(mcol2[:, b + 3:b + 4], mcol2[:, b:b + 1], mcol2[:, b:b + 1],
                            None, ALU.mult)
            v.drain()
            v.tensor_scalar(mcol2[:, b + 2:b + 3], mcol2[:, b + 2:b + 3],
                            mcol2[:, b + 3:b + 4], None, ALU.subtract)
            v.drain()
            v.tensor_scalar(mcol2[:, b + 2:b + 3], mcol2[:, b + 2:b + 3], 1e-5, 1.0,
                            ALU.add, ALU.mult).then_inc(eV.h, 1)
            v.wait_ge(eS.h, op_rstd[i])
            v.tensor_scalar(xl2f[:], x1b[i % 2][:], mcol2[:, b:b + 1], mcol2[:, b + 4:b + 5],
                            ALU.subtract, ALU.mult)
            if i >= 2:
                v.wait_ge(dSpL[i % 2].h, op_l2sp[i - 2])
            v.drain()
            v.tensor_copy(xl2b[i % 2][:], xl2f[:]).then_inc(eV.h, 1)
            v.wait_ge(eT.h, op_tp[i])
            v.tensor_copy(xfm32[:], prt[:]).then_inc(eV.h, 1)
            v.wait_ge(eT.h, op_lg[i])
            v.tensor_copy(logits[:, i * E:(i + 1) * E], plg[:, 0:E]).then_inc(eV.h, 1)


    @on("scalar")
    def _(s):
        for i in range(NT):
            b = (i % 2) * 8
            s.wait_ge(eV.h, op_add[i])
            s.activation(sqd2[:], x1b[i % 2][:], ACT.Square,
                         accum_out=mcol2[:, b + 1:b + 2]).then_inc(eS.h, 1)
            s.wait_ge(eV.h, op_var[i])
            s.activation(mcol2[:, b + 3:b + 4], mcol2[:, b + 2:b + 3], ACT.Ln)
            s.drain()
            s.activation(mcol2[:, b + 4:b + 5], mcol2[:, b + 3:b + 4], ACT.Exp,
                         scale=-0.5).then_inc(eS.h, 1)


    @on("vector")
    def _(v):
        for i in range(NT):
                L = logits[:, i * E:(i + 1) * E]
                if i >= 1:
                    v.wait_ge(eS.h, gt_s[i - 1])
                v.wait_ge(eV.h, op_lgv[i])
                v.drain()
                v.reduce_max(gsc[:, 0:1], L, axis=AX)
                v.drain()
                v.tensor_scalar(gsc[:, 1:2], gsc[:, 0:1], -1.0, 0.0,
                                ALU.mult, ALU.add).then_inc(eV.h, 1)
                v.wait_ge(eS.h, gt_s[i])
                v.reciprocal(gsc[:, 4:5], gsc[:, 3:4])
                v.tensor_scalar(gsc[:, 8:16], L, gsc[:, 0:1], None, ALU.is_equal)
                v.drain()
                v.tensor_tensor_scan(gsc[:, 16:24], gsc[:, 8:16], gsc[:, 8:16],
                                     0.0, ALU.add, ALU.bypass)
                v.drain()
                v.tensor_sub(gsc[:, 16:24], gsc[:, 16:24], gsc[:, 8:16])
                v.drain()
                v.tensor_scalar(gsc[:, 16:24], gsc[:, 16:24], 0.0, None, ALU.is_equal)
                v.drain()
                v.tensor_tensor(m1a[:, i * E:(i + 1) * E], gsc[:, 8:16], gsc[:, 16:24], ALU.mult)
                v.drain()
                v.tensor_tensor(gsc[:, 8:16], gsc[:, 24:32], m1a[:, i * E:(i + 1) * E], ALU.mult)
                v.drain()
                v.reduce_sum(gsc[:, 5:6], gsc[:, 8:16], axis=AX)
                v.drain()
                v.tensor_scalar(g1kc[:, i:i + 1], gsc[:, 5:6], gsc[:, 4:5], None, ALU.mult)
                v.scalar_tensor_tensor(gsc[:, 8:16], m1a[:, i * E:(i + 1) * E], -1e9, L,
                                       ALU.mult, ALU.add)
                v.drain()
                v.reduce_max(gsc[:, 6:7], gsc[:, 8:16], axis=AX)
                v.drain()
                v.tensor_scalar(gsc[:, 8:16], gsc[:, 8:16], gsc[:, 6:7], None, ALU.is_equal)
                v.drain()
                v.tensor_tensor_scan(gsc[:, 16:24], gsc[:, 8:16], gsc[:, 8:16],
                                     0.0, ALU.add, ALU.bypass)
                v.drain()
                v.tensor_sub(gsc[:, 16:24], gsc[:, 16:24], gsc[:, 8:16])
                v.drain()
                v.tensor_scalar(gsc[:, 16:24], gsc[:, 16:24], 0.0, None, ALU.is_equal)
                v.drain()
                v.tensor_tensor(m2a[:, i * E:(i + 1) * E], gsc[:, 8:16], gsc[:, 16:24], ALU.mult)
                v.drain()
                v.tensor_tensor(gsc[:, 8:16], gsc[:, 24:32], m2a[:, i * E:(i + 1) * E], ALU.mult)
                v.drain()
                v.reduce_sum(gsc[:, 5:6], gsc[:, 8:16], axis=AX)
                v.drain()
                v.tensor_scalar(g2kc[:, i:i + 1], gsc[:, 5:6], gsc[:, 4:5], None,
                                ALU.mult).then_inc(eV.h, 1)

    @on("scalar")
    def _(s):
        for i in range(NT):
            s.wait_ge(eV.h, gt_m[i])
            s.activation(gsc[:, 24:32], logits[:, i * E:(i + 1) * E], ACT.Exp,
                         bias=gsc[:, 1:2], accum_out=gsc[:, 3:4]).then_inc(eS.h, 1)

    es_op.close()
    es_ln2.close()
    es_wo.close()
    es_at.close()

    es_cs = ExitStackR()
    pcs = ps(es_cs, "pcs", [128, 512])
    pcnt = ps(es_cs, "pcnt", [1, 512])

    cs1_t = [eT.bump() for _ in range(NT)]
    cs1_v = [eV.bump() for _ in range(NT)]
    cnt_t = eT.bump()
    cnt_s = eS.bump()
    cs2_t = [eT.bump() for _ in range(NT)]
    cs2_v = [eV.bump() for _ in range(NT)]
    GATE_V = eV.bump()

    @on("tensor")
    def _(t_):
        t_.wait_ge(eV.h, gt_v[-1])
        for i in range(NT):
            if i >= 1:
                t_.wait_ge(eV.h, cs1_v[i - 1])
            for j in range(i + 1):
                ins = t_.matmul(pcs[:, 0:E],
                                (tri_sb if j == i else onesf_sb)[:],
                                m1a[:, j * E:(j + 1) * E],
                                start=(j == 0), stop=(j == i))
            ins.then_inc(eT.h, 1)
        t_.wait_ge(eV.h, cs1_v[-1])
        for i in range(NT):
            ins = t_.matmul(pcnt[:, 0:E], onecol_sb[:], m1k[:, i * E:(i + 1) * E],
                            start=(i == 0), stop=(i == NT - 1))
        ins.then_inc(eT.h, 1)
        t_.wait_ge(eS.h, cnt_s)
        for i in range(NT):
            if i >= 1:
                t_.wait_ge(eV.h, cs2_v[i - 1])
            for j in range(i + 1):
                t_.matmul(pcs[:, 0:E],
                          (tri_sb if j == i else onesf_sb)[:],
                          m2a[:, j * E:(j + 1) * E],
                          start=(j == 0), stop=False)
            ins = t_.matmul(pcs[:, 0:E], onerow_sb[:], county[:],
                            start=False, stop=True)
            ins.then_inc(eT.h, 1)

    @on("scalar")
    def _(s):
        s.wait_ge(eT.h, cnt_t)
        s.copy(county[:], pcnt[0:1, 0:E]).then_inc(eS.h, 1)

    @on("vector")
    def _(v):
        v.wait_ge(eV.h, gt_v[-1])
        for i in range(NT):
            v.wait_ge(eT.h, cs1_t[i])
            sl = slice(i * E, (i + 1) * E)
            v.tensor_copy(p1x[:, sl], pcs[:, 0:E])
            v.drain()
            v.tensor_scalar(gsc[:, 8:16], p1x[:, sl], float(CAP), None, ALU.is_lt)
            v.drain()
            v.tensor_tensor(m1k[:, sl], m1a[:, sl], gsc[:, 8:16], ALU.mult)
            v.drain()
            v.scalar_tensor_tensor(gsc[:, 8:16], p1x[:, sl], 1.0, m1k[:, sl],
                                   ALU.add, ALU.mult)
            v.drain()
            v.tensor_scalar(q1a[:, sl], gsc[:, 8:16], 1.0, 1.0,
                            ALU.subtract, ALU.mult).then_inc(eV.h, 1)
        for i in range(NT):
            v.wait_ge(eT.h, cs2_t[i])
            sl = slice(i * E, (i + 1) * E)
            v.tensor_copy(p2x[:, sl], pcs[:, 0:E])
            v.drain()
            v.tensor_scalar(gsc[:, 8:16], p2x[:, sl], float(CAP), None, ALU.is_lt)
            v.drain()
            v.tensor_tensor(m2k[:, sl], m2a[:, sl], gsc[:, 8:16], ALU.mult)
            v.drain()
            v.scalar_tensor_tensor(gsc[:, 8:16], p2x[:, sl], 1.0, m2k[:, sl],
                                   ALU.add, ALU.mult)
            v.drain()
            v.tensor_scalar(q2a[:, sl], gsc[:, 8:16], 1.0, 1.0,
                            ALU.subtract, ALU.mult).then_inc(eV.h, 1)
        for i in range(NT):
            sl = slice(i * E, (i + 1) * E)
            v.reduce_sum(gsc[:, 0:1], m1k[:, sl], axis=AX)
            v.drain()
            v.tensor_scalar(g1kc[:, i:i + 1], g1kc[:, i:i + 1], gsc[:, 0:1],
                            None, ALU.mult)
            v.reduce_sum(gsc[:, 1:2], m2k[:, sl], axis=AX)
            v.drain()
            ins = v.tensor_scalar(g2kc[:, i:i + 1], g2kc[:, i:i + 1], gsc[:, 1:2],
                                  None, ALU.mult)
        ins.then_inc(eV.h, 1)
    es_cs.close()

    if upto == "gate":
        FV = dY.bump(16 * 11)

        @on("gpsimd")
        def _(gp):
            gp.wait_ge(eV.h, GATE_V)
            for k, arr in enumerate([logits, m1a, m2a, m1k, m2k, q1a, q2a, p1x, p2x]):
                gp.dma_start(out=DBG[:, k * NT * E:(k + 1) * NT * E],
                             in_=arr[:]).then_inc(dY.h, 16)
            gp.dma_start(out=DBG[:, 9 * NT * E: 9 * NT * E + NT],
                         in_=g1kc[:]).then_inc(dY.h, 16)
            gp.dma_start(out=DBG[:, 9 * NT * E + NT: 9 * NT * E + 2 * NT],
                         in_=g2kc[:]).then_inc(dY.h, 16)

        @on("sync")
        def _(sy):
            sy.wait_ge(dY.h, FV)
        return nc, finish

    es_ga = ExitStackR()
    pg = [ps(es_ga, f"pg{mc}", [128, 512]) for mc in range(DC)]
    es_ffn = ExitStackR()
    xe = sb(es_ffn, "xe", [128, E * DC * 512], BF16)
    dtile = [sb(es_ga, f"dtile{i}", [128, 512], BF16) for i in range(2)]
    dtmp = sb(es_ga, "dtmp", [128, 512], BF16)
    xl2g = [sb(es_ga, f"xl2g{i}", [128, D], BF16) for i in range(2)]

    ga_bv, ga_t, ga_ev, ga_rl = {}, {}, {}, {}
    for e in range(E):
        for i in range(NT):
            ga_bv[(e, i)] = eV.bump()
            ga_t[(e, i)] = eT.bump()
            ga_rl[(e, i)] = dGr[(e * NT + i) % 2].bump(16)
        for mc in range(DC):
            ga_ev[(e, mc)] = eV.bump()

    @on("sync")
    def _(sy):
        for q in range(2):
            sy.wait_ge(dSpX[q].h, dSpX[q].n)
            sy.wait_ge(dSpL[q].h, dSpL[q].n)
        for e in range(E):
            for i in range(NT):
                k = e * NT + i
                if k >= 2:
                    sy.wait_ge(eT.h, ga_t[((k - 2) // NT, (k - 2) % NT)])
                sy.dma_start(out=xl2g[k % 2][:], in_=XL2t[i]).then_inc(dGr[k % 2].h, 16)

    @on("vector")
    def _(v):
        v.wait_ge(eV.h, GATE_V)
        for e in range(E):
            for i in range(NT):
                k = e * NT + i
                if k >= 2:
                    v.wait_ge(eT.h, ga_t[((k - 2) // NT, (k - 2) % NT)])
                if k >= 1:
                    v.wait_ge(eV.h, ga_bv[((k - 1) // NT, (k - 1) % NT)])
                v.tensor_scalar(dtile[k % 2][:], iota_sb[:],
                                q1a[:, i * E + e:i * E + e + 1], None, ALU.is_equal)
                v.tensor_scalar(dtmp[:], iota_sb[:],
                                q2a[:, i * E + e:i * E + e + 1], None, ALU.is_equal)
                v.drain()
                v.tensor_tensor(dtile[k % 2][:], dtile[k % 2][:], dtmp[:],
                                ALU.add).then_inc(eV.h, 1)
            for mc in range(DC):
                v.wait_ge(eT.h, ga_t[(e, NT - 1)])
                v.tensor_copy(xe[:, e * DC * 512 + mc * 512:(e * DC + mc + 1) * 512],
                              pg[mc][:]).then_inc(eV.h, 1)

    @on("tensor")
    def _(t_):
        for e in range(E):
            for i in range(NT):
                t_.wait_ge(eV.h, ga_bv[(e, i)])
                if i == 0 and e >= 1:
                    for mc in range(DC):
                        t_.wait_ge(eV.h, ga_ev[(e - 1, mc)])
                t_.wait_ge(dGr[(e * NT + i) % 2].h, ga_rl[(e, i)])
                for mc in range(DC):
                    ins = t_.matmul(pg[mc][:],
                                    xl2g[(e * NT + i) % 2][:, mc * 128:(mc + 1) * 128],
                                    dtile[(e * NT + i) % 2][:],
                                    start=(i == 0), stop=(i == NT - 1))
                ins.then_inc(eT.h, 1)
    GATHER_V = eV.n
    es_ga.close()

    if upto == "gather":
        FV = dY.bump(16)

        @on("gpsimd")
        def _(gp):
            gp.wait_ge(eV.h, GATHER_V)
            gp.dma_start(out=DBG, in_=xe[:]).then_inc(dY.h, 16)

        @on("sync")
        def _(sy):
            sy.wait_ge(dY.h, FV)
        return nc, finish

    h_fm = sb(es_ffn, "h_fm", [128, NHC * 512], BF16)
    w0b = [sb(es_ffn, f"w0b{i}", [128, DC * 512], BF16) for i in range(3)]
    w1b = [sb(es_ffn, f"w1b{i}", [128, DC * 512], BF16) for i in range(3)]
    w0tl = sb(es_ffn, "w0tl", [128, DC * 384], BF16)
    w1tl = sb(es_ffn, "w1tl", [128, DC * 384], BF16)
    wob = [sb(es_ffn, f"wob{i}", [128, 2 * D], BF16) for i in range(2)]
    wotl = sb(es_ffn, "wotl", [128, D], BF16)
    gtmp = [sb(es_ffn, f"gtmp{i}", [128, 512], F32) for i in range(2)]
    eob = sb(es_ffn, "eob", [128, 4 * D], BF16)
    es_px = ExitStackR()
    pall = [ps(es_px, f"pall{i}", [128, 512]) for i in range(8)]

    ZTAILS = eV.bump()

    @on("vector")
    def _(v):
        v.memset(w0tl[:], 0.0)
        v.memset(w1tl[:], 0.0)
        v.memset(wotl[:], 0.0).then_inc(eV.h, 1)

    w01_dma, wo_dma = {}, {}
    hmm_t, gel_s, hfm_v, wo_t = {}, {}, {}, {}
    eo_s, eo_d = {}, {}
    for e in range(E):
        for lb in range(NW01 + 1):
            if lb < NW01:
                k = e * NW01 + lb
                w01_dma[(e, lb)] = (dW0[k % 3].bump(16), dW1[k % 3].bump(16))
            else:
                w01_dma[(e, lb)] = (dWt[0].bump(16), dWt[1].bump(16))
        for pb in range(WO_PAIRS + 1):
            if pb < WO_PAIRS:
                k = e * WO_PAIRS + pb
                wo_dma[(e, pb)] = dWo[k % 2].bump(16)
            else:
                wo_dma[(e, pb)] = dWot.bump(16)
        for hc in range(NHC):
            hmm_t[(e, hc)] = eT.bump(2)
            gel_s[(e, hc)] = eS.bump()
            hfm_v[(e, hc)] = eV.bump()
        for hc in range(NHC):
            wo_t[(e, hc)] = eT.bump()
        eo_s[e] = eS.bump(8)
        eo_d[e] = dEo.bump(16)

    W0r = W0.rearrange("e (mc p) h -> e p mc h", p=128)
    W1r = W1.rearrange("e (mc p) h -> e p mc h", p=128)
    WOr = WOE[:, 0:5376, :].rearrange("e (hc p) m -> e p hc m", p=128)

    @on("gpsimd")
    def _(gp):
        for q in range(2):
            gp.wait_ge(dSpX[q].h, dSpX[q].n)
            gp.wait_ge(dSpL[q].h, dSpL[q].n)
        gp.wait_ge(eV.h, ZTAILS)
        for e in range(E):
            for lb in range(NW01 + 1):
                if lb < NW01:
                    pty = (e * NW01 + lb) % 3
                    kk = e * NW01 + lb - 3
                    if kk >= 0:
                        pe, plb = kk // NW01, kk % NW01
                        gp.wait_ge(eT.h, hmm_t[(pe, plb * 4 + 3)])
                    gp.dma_start(out=w0b[pty][:],
                                 in_=W0r[e, :, :, lb * 512:(lb + 1) * 512]
                                 ).then_inc(dW0[pty].h, 16)
                    gp.dma_start(out=w1b[pty][:],
                                 in_=W1r[e, :, :, lb * 512:(lb + 1) * 512]
                                 ).then_inc(dW1[pty].h, 16)
                else:
                    if e >= 1:
                        gp.wait_ge(eT.h, hmm_t[(e - 1, NHC - 1)])
                    tl0 = w0tl[:].rearrange("p (mc h) -> p mc h", mc=DC)[:, :, 0:344]
                    tl1 = w1tl[:].rearrange("p (mc h) -> p mc h", mc=DC)[:, :, 0:344]
                    gp.dma_start(out=tl0, in_=W0r[e, :, :, 5120:HR]).then_inc(dWt[0].h, 16)
                    gp.dma_start(out=tl1, in_=W1r[e, :, :, 5120:HR]).then_inc(dWt[1].h, 16)
            for pb in range(WO_PAIRS + 1):
                if pb < WO_PAIRS:
                    kk = e * WO_PAIRS + pb - 2
                    pty = (e * WO_PAIRS + pb) % 2
                    if kk >= 0:
                        pe, ppb = kk // WO_PAIRS, kk % WO_PAIRS
                        gp.wait_ge(eT.h, wo_t[(pe, ppb * 2 + 1)])
                    gp.dma_start(out=wob[pty][:],
                                 in_=WOr[e, :, 2 * pb:2 * pb + 2, :]).then_inc(dWo[pty].h, 16)
                else:
                    if e >= 1:
                        gp.wait_ge(eT.h, wo_t[(e - 1, NHC - 1)])
                    gp.dma_start(out=wotl[0:88, :],
                                 in_=WOE[e, 5376:HR, :]).then_inc(dWot.h, 16)

    @on("tensor")
    def _(t_):
        for e in range(E):
            for hc in range(NHC):
                if hc < 40:
                    lb = hc // 4
                    pty = (e * NW01 + lb) % 3
                    s0 = w0b[pty][:].rearrange("p (mc x) -> p mc x", mc=DC)
                    s1 = w1b[pty][:].rearrange("p (mc x) -> p mc x", mc=DC)
                    col = (hc % 4) * 128
                    v0, v1 = w01_dma[(e, lb)]
                    t_.wait_ge(dW0[pty].h, v0)
                    t_.wait_ge(dW1[pty].h, v1)
                else:
                    s0 = w0tl[:].rearrange("p (mc x) -> p mc x", mc=DC)
                    s1 = w1tl[:].rearrange("p (mc x) -> p mc x", mc=DC)
                    col = (hc - 40) * 128
                    v0, v1 = w01_dma[(e, NW01)]
                    t_.wait_ge(dWt[0].h, v0)
                    t_.wait_ge(dWt[1].h, v1)
                kg = e * NHC + hc
                if kg >= 2:
                    pe, phc = (kg - 2) // NHC, (kg - 2) % NHC
                    t_.wait_ge(eS.h, gel_s[(pe, phc)])
                    t_.wait_ge(eV.h, hfm_v[(pe, phc)])
                if hc < 2 and e >= 1:
                    t_.wait_ge(eS.h, eo_s[e - 1])
                if hc == 0 and e == 0:
                    for mc in range(DC):
                        t_.wait_ge(eV.h, ga_ev[(E - 1, mc)])
                for mc in range(DC):
                    ins = t_.matmul(pall[hc % 2][:], s0[:, mc, col:col + 128],
                                    xe[:, e * DC * 512 + mc * 512:(e * DC + mc + 1) * 512],
                                    start=(mc == 0), stop=(mc == DC - 1))
                ins.then_inc(eT.h, 1)
                for mc in range(DC):
                    ins = t_.matmul(pall[2 + hc % 2][:], s1[:, mc, col:col + 128],
                                    xe[:, e * DC * 512 + mc * 512:(e * DC + mc + 1) * 512],
                                    start=(mc == 0), stop=(mc == DC - 1))
                ins.then_inc(eT.h, 1)
            for hc in range(NHC):
                t_.wait_ge(eV.h, hfm_v[(e, hc)])
                if hc == 0:
                    t_.wait_ge(eS.h, gel_s[(e, NHC - 1)])
                    t_.wait_ge(eV.h, hfm_v[(e, NHC - 1)])
                    if e >= 1:
                        t_.wait_ge(eS.h, eo_s[e - 1])
                if hc < 42:
                    pb = hc // 2
                    pty = (e * WO_PAIRS + pb) % 2
                    t_.wait_ge(dWo[pty].h, wo_dma[(e, pb)])
                    wsrc = wob[pty]
                    woff = (hc % 2) * D
                else:
                    t_.wait_ge(dWot.h, wo_dma[(e, WO_PAIRS)])
                    wsrc = wotl
                    woff = 0
                for st in range(4):
                    for mh in range(2):
                        ins = t_.matmul(pall[st * 2 + mh][:],
                                        h_fm[:, hc * 512 + st * 128: hc * 512 + st * 128 + 128],
                                        wsrc[:, woff + mh * 512: woff + (mh + 1) * 512],
                                        start=(hc == 0), stop=(hc == NHC - 1))
                ins.then_inc(eT.h, 1)

    @on("scalar")
    def _(s):
        for e in range(E):
            for hc in range(NHC):
                kg = e * NHC + hc
                s.wait_ge(eT.h, hmm_t[(e, hc)] - 1)
                if kg >= 2:
                    pe, phc = (kg - 2) // NHC, (kg - 2) % NHC
                    s.wait_ge(eV.h, hfm_v[(pe, phc)])
                s.activation(gtmp[hc % 2][:], pall[hc % 2][:],
                             ACT.Gelu_apprx_tanh).then_inc(eS.h, 1)
            s.wait_ge(eT.h, wo_t[(e, NHC - 1)])
            if e >= 1:
                s.wait_ge(dEo.h, eo_d[e - 1])
            for st in range(4):
                for mh in range(2):
                    s.copy(eob[:, st * D + mh * 512: st * D + mh * 512 + 512],
                           pall[st * 2 + mh][:]).then_inc(eS.h, 1)

    @on("vector")
    def _(v):
        for e in range(E):
            for hc in range(NHC):
                v.wait_ge(eS.h, gel_s[(e, hc)])
                v.wait_ge(eT.h, hmm_t[(e, hc)])
                if e >= 1:
                    v.wait_ge(eT.h, wo_t[(e - 1, hc)])
                v.tensor_tensor(h_fm[:, hc * 512:(hc + 1) * 512], gtmp[hc % 2][:],
                                pall[2 + hc % 2][:], ALU.mult).then_inc(eV.h, 1)

    @on("sync")
    def _(sy):
        for e in range(E):
            sy.wait_ge(eS.h, eo_s[e])
            sy.dma_start(out=EOSt[e],
                         in_=eob[:].rearrange("p (k m) -> p k m", k=4)).then_inc(dEo.h, 16)

    EXP_DONE_S = eS.n
    es_px.close()
    es_ffn.close()

    es_sc = ExitStackR()
    oacc = sb(es_sc, "oacc", [128, NT * D], F32)
    eor = [sb(es_sc, f"eor{i}", [128, 4 * D], BF16) for i in range(2)]
    ctm = [sb(es_sc, f"ctm{i}", [128, 512], BF16) for i in range(4)]
    ctmp2 = sb(es_sc, "ctmp2", [128, 512], BF16)
    cT = [sb(es_sc, f"cT{i}", [128, 512], BF16) for i in range(4)]
    x1r = [sb(es_sc, f"x1r{i}", [128, D], F32) for i in range(2)]
    ptc = [ps(es_sc, f"ptc{i}", [128, 1024], BF16) for i in range(4)]
    psc2 = [ps(es_sc, f"psc2{i}", [128, 512]) for i in range(2)]

    eo_rel, x1_rel = {}, {}
    sc_bv, sc_tp, sc_te, sc_mm, sc_av = {}, {}, {}, {}, {}
    for e in range(E):
        eo_rel[e] = dEr[e % 2].bump(16)
        for i in range(NT):
            if e == 0:
                x1_rel[i] = dXr[i % 2].bump(16)
            sc_bv[(e, i)] = eV.bump()
            sc_tp[(e, i)] = eT.bump()
            sc_te[(e, i)] = eV.bump()
            for mh in range(2):
                sc_mm[(e, i, mh)] = eT.bump()
            sc_av[(e, i)] = eV.bump()

    @on("sync")
    def _(sy):
        sy.wait_ge(eS.h, EXP_DONE_S)
        for e in range(E):
            sy.wait_ge(dEo.h, eo_d[e])
            if e >= 2:
                sy.wait_ge(eT.h, sc_mm[(e - 2, NT - 1, 1)])
            sy.dma_start(out=eor[e % 2][:].rearrange("p (k m) -> p k m", k=4),
                         in_=EOSt[e]).then_inc(dEr[e % 2].h, 16)
            if e == 0:
                for i in range(NT):
                    if i >= 2:
                        sy.wait_ge(eV.h, sc_av[(0, i - 2)])
                    sy.dma_start(out=x1r[i % 2][:], in_=X1t[i]).then_inc(dXr[i % 2].h, 16)

    @on("vector")
    def _(v):
        v.wait_ge(eS.h, EXP_DONE_S)
        v.wait_ge(eV.h, GATE_V)
        for e in range(E):
            for i in range(NT):
                k = e * NT + i
                if k >= 4:
                    v.wait_ge(eT.h, sc_tp[((k - 4) // NT, (k - 4) % NT)])
                if k >= 1:
                    v.wait_ge(eV.h, sc_bv[((k - 1) // NT, (k - 1) % NT)])
                v.tensor_scalar(ctm[k % 4][:], iota_sb[:],
                                q1a[:, i * E + e:i * E + e + 1],
                                g1kc[:, i:i + 1], ALU.is_equal, ALU.mult)
                v.tensor_scalar(ctmp2[:], iota_sb[:],
                                q2a[:, i * E + e:i * E + e + 1],
                                g2kc[:, i:i + 1], ALU.is_equal, ALU.mult)
                v.drain()
                v.tensor_tensor(ctm[k % 4][:], ctm[k % 4][:], ctmp2[:],
                                ALU.add).then_inc(eV.h, 1)
                v.wait_ge(eT.h, sc_tp[(e, i)])
                if k >= 4:
                    v.wait_ge(eT.h, sc_mm[((k - 4) // NT, (k - 4) % NT, 1)])
                v.tensor_copy(cT[k % 4][:], ptc[k % 4][:, 0:512]).then_inc(eV.h, 1)
                if e >= 1:
                    v.wait_ge(eV.h, sc_av[(e - 1, i)])
                for mh in range(2):
                    v.wait_ge(eT.h, sc_mm[(e, i, mh)])
                    dst = oacc[:, i * D + mh * 512: i * D + mh * 512 + 512]
                    if e == 0:
                        v.wait_ge(dXr[i % 2].h, x1_rel[i])
                        ins = v.tensor_tensor(dst, x1r[i % 2][:, mh * 512:(mh + 1) * 512],
                                              psc2[mh][:], ALU.add)
                    else:
                        ins = v.tensor_tensor(dst, dst, psc2[mh][:], ALU.add)
                ins.then_inc(eV.h, 1)

    @on("tensor")
    def _(t_):
        for e in range(E):
            for i in range(NT):
                k = e * NT + i
                t_.wait_ge(eV.h, sc_bv[(e, i)])
                if k >= 4:
                    t_.wait_ge(eV.h, sc_te[((k - 4) // NT, (k - 4) % NT)])
                for kk4 in range(4):
                    ins = t_.transpose(ptc[k % 4][:, kk4 * 128:kk4 * 128 + 128],
                                       ctm[k % 4][:, kk4 * 128:kk4 * 128 + 128],
                                       idb_sb[:])
                ins.then_inc(eT.h, 1)
                t_.wait_ge(eV.h, sc_te[(e, i)])
                t_.wait_ge(dEr[e % 2].h, eo_rel[e])
                if k >= 1:
                    t_.wait_ge(eV.h, sc_av[((k - 1) // NT, (k - 1) % NT)])
                for mh in range(2):
                    for kk4 in range(4):
                        ins = t_.matmul(psc2[mh][:],
                                        cT[k % 4][:, kk4 * 128:kk4 * 128 + 128],
                                        eor[e % 2][:, kk4 * D + mh * 512: kk4 * D + mh * 512 + 512],
                                        start=(kk4 == 0), stop=(kk4 == 3))
                    ins.then_inc(eT.h, 1)

    FIN_V = eV.n
    FIN = dY.bump(16)

    @on("sync")
    def _(sy):
        sy.wait_ge(eV.h, FIN_V)
        sy.dma_start(out=Y.rearrange("(i p) m -> p i m", p=128),
                     in_=oacc[:].rearrange("p (i m) -> p i m", i=NT)).then_inc(dY.h, 16)
        sy.wait_ge(dY.h, FIN)

    es_sc.close()
    return nc, finish


def _consts():
    c = {}
    c["iota"] = np.tile(np.arange(512, dtype=np.float32), (128, 1))
    c["idb"] = np.eye(128).astype(ml_dtypes.bfloat16)
    c["idf"] = np.eye(128, dtype=np.float32)
    c["tri"] = np.triu(np.ones((128, 128), dtype=np.float32), k=1)
    c["onesf"] = np.ones((128, 128), dtype=np.float32)
    q = np.arange(128)
    s_ = np.arange(128)
    c["maskt"] = np.where(q[None, :] >= s_[:, None], 0.0, -1e30).astype(np.float32)
    c["onecol"] = np.ones((128, 1), dtype=np.float32)
    c["onecolb"] = np.ones((128, 1)).astype(ml_dtypes.bfloat16)
    c["onerow"] = np.ones((1, 128), dtype=np.float32)
    e01 = np.zeros((1, 128), dtype=np.float32)
    e01[0, 0:64] = 1.0
    e10 = np.zeros((1, 128), dtype=np.float32)
    e10[0, 64:128] = 1.0
    c["e01"] = e01.astype(ml_dtypes.bfloat16)
    c["e10"] = e10.astype(ml_dtypes.bfloat16)
    return c


def kernel(L0_attn_X, L0_attn_q_proj, L0_attn_k_proj, L0_attn_v_proj,
           L0_attn_o_proj, L0_ffn_router_w, L0_ffn_w0_kernel,
           L0_ffn_w1_kernel, L0_ffn_wo_kernel, _upto="full", _nh=NH):
    X = np.ascontiguousarray(np.asarray(L0_attn_X, dtype=np.float32))
    shared = {
        "wq": np.ascontiguousarray(np.asarray(L0_attn_q_proj, dtype=np.float32)),
        "wk": np.ascontiguousarray(np.asarray(L0_attn_k_proj, dtype=np.float32)),
        "wv": np.ascontiguousarray(np.asarray(L0_attn_v_proj, dtype=np.float32)),
        "wo": np.ascontiguousarray(np.asarray(L0_attn_o_proj, dtype=np.float32)),
        "rw": np.ascontiguousarray(np.asarray(L0_ffn_router_w, dtype=np.float32)),
        "w0": np.ascontiguousarray(np.asarray(L0_ffn_w0_kernel, dtype=np.float32)),
        "w1": np.ascontiguousarray(np.asarray(L0_ffn_w1_kernel, dtype=np.float32)),
        "woe": np.ascontiguousarray(np.asarray(L0_ffn_wo_kernel, dtype=np.float32)),
    }
    shared.update(_consts())
    nc, fin = build_nc(upto=_upto, nhrun=_nh)
    fin()
    in_maps = [dict(shared, x=np.ascontiguousarray(X[c])) for c in range(8)]
    res = run_bass_kernel_spmd(nc, in_maps, core_ids=list(range(8)))
    out = np.stack([np.asarray(res.results[c]["y"]) for c in range(8)], axis=0)
    if _upto != "full":
        return out, [np.asarray(res.results[c]["dbg"]) for c in range(8)]
    return out


# version 03
# speedup vs baseline: 1.0208x; 1.0208x over previous
import numpy as np
import ml_dtypes
from contextlib import ExitStack
import concourse.bass as bass
import concourse.mybir as mybir
from concourse.bass_utils import run_bass_kernel_spmd

F32 = mybir.dt.float32
BF16 = mybir.dt.bfloat16
ALU = mybir.AluOpType
ACT = mybir.ActivationFunctionType
AX = mybir.AxisListType.X

S = 2048
D = 1024
NT = 16
DC = 8
NH = 16
E = 8
CAP = 512
HR = 5464
HP = 5504
NHC = 43
NW01 = 10
WO_PAIRS = 21


def build_nc(upto="full", nhrun=NH):
    nc = bass.Bass("TRN2", target_bir_lowering=False)

    def din(name, shape, dt=F32):
        return nc.dram_tensor(name, shape, dt, kind="ExternalInput").ap()

    X = din("x", [S, D])
    WQ = din("wq", [D, D])
    WK = din("wk", [D, D])
    WV = din("wv", [D, D])
    WO = din("wo", [D, D])
    RW = din("rw", [D, E])
    W0 = din("w0", [E, D, HR])
    W1 = din("w1", [E, D, HR])
    WOE = din("woe", [E, HR, D])
    IOTA = din("iota", [128, 512])
    IDB = din("idb", [128, 128], BF16)
    IDF = din("idf", [128, 128])
    TRI = din("tri", [128, 128])
    ONESF = din("onesf", [128, 128])
    MASKT = din("maskt", [128, 128])
    ONECOL = din("onecol", [128, 1])
    ONECOLB = din("onecolb", [128, 1], BF16)
    ONEROW = din("onerow", [1, 128])
    E01 = din("e01", [1, 128], BF16)
    E10 = din("e10", [1, 128], BF16)

    Y = nc.dram_tensor("y", [S, D], F32, kind="ExternalOutput").ap()
    NDBG = {"ln1": DC * S, "attn": (NH // 2) * S, "gate": 10 * NT * E,
            "gather": E * DC * 512, "vprime": NT * NH * 65}.get(upto, 1)
    DBG = nc.dram_tensor("dbg", [128, NDBG], F32, kind="ExternalOutput").ap()
    X1S = nc.dram_tensor("x1s", [S, D], F32, kind="Internal").ap()
    XL2S = nc.dram_tensor("xl2s", [S, D], BF16, kind="Internal").ap()
    AVS = nc.dram_tensor("avs", [NH // 2, 128, S], BF16, kind="Internal").ap()
    EOS = nc.dram_tensor("eos", [E, CAP, D], BF16, kind="Internal").ap()

    Xt = X.rearrange("(i p) m -> i p m", p=128)
    X1t = X1S.rearrange("(i p) m -> i p m", p=128)
    XL2t = XL2S.rearrange("(i p) m -> i p m", p=128)
    EOSt = EOS.rearrange("e (k p) m -> e p k m", p=128)

    stack = ExitStack()
    scopes = []
    _ES = ExitStack

    def ExitStackR():
        es = _ES()
        scopes.append(es)
        return es

    def finish():
        for es in reversed(scopes):
            es.close()
        stack.close()

    def sb(es, name, shape, dt):
        return es.enter_context(nc.sbuf_tensor(name, shape, dt))

    def ps(es, name, shape, dt=F32):
        return es.enter_context(nc.psum_tensor(name, shape, dt))

    class Ctr:
        def __init__(self, name):
            self.h = stack.enter_context(nc.semaphore(name))
            self.n = 0

        def bump(self, k=1):
            self.n += k
            return self.n

    dC = Ctr("dC")
    dW = Ctr("dW")
    dEo = Ctr("dEo")
    dY = Ctr("dY")
    eT = Ctr("eT")
    eV = Ctr("eV")
    eS = Ctr("eS")
    eG = Ctr("eG")
    dXi = [Ctr("dXi0"), Ctr("dXi1")]
    dX2 = [Ctr("dX20"), Ctr("dX21")]
    dAv = [Ctr("dAv0"), Ctr("dAv1")]
    dSpX = [Ctr("dSpX0"), Ctr("dSpX1")]
    dSpL = [Ctr("dSpL0"), Ctr("dSpL1")]
    dBr = Ctr("dBr")
    dRc = Ctr("dRc")
    dAt = Ctr("dAt")
    dR0 = Ctr("dR0")
    dR1 = Ctr("dR1")
    dAl = [Ctr("dAl0"), Ctr("dAl1")]
    dAw = [Ctr("dAw0"), Ctr("dAw1")]
    dW0 = [Ctr("dW00"), Ctr("dW01"), Ctr("dW02")]
    dW1 = [Ctr("dW10"), Ctr("dW11"), Ctr("dW12")]
    dWt = [Ctr("dWt0"), Ctr("dWt1")]
    dWo = [Ctr("dWo0"), Ctr("dWo1")]
    dWot = Ctr("dWot")
    dEr = [Ctr("dEr0"), Ctr("dEr1")]
    dGr = [Ctr("dGr0"), Ctr("dGr1")]
    dXr = [Ctr("dXr0"), Ctr("dXr1")]

    gl = stack
    iota_sb = sb(gl, "iota_sb", [128, 512], F32)
    idb_sb = sb(gl, "idb_sb", [128, 128], BF16)
    idf_sb = sb(gl, "idf_sb", [128, 128], F32)
    tri_sb = sb(gl, "tri_sb", [128, 128], F32)
    onesf_sb = sb(gl, "onesf_sb", [128, 128], F32)
    maskt_sb = sb(gl, "maskt_sb", [128, 128], F32)
    onecol_sb = sb(gl, "onecol_sb", [128, 1], F32)
    onecolb_sb = sb(gl, "onecolb_sb", [128, 1], BF16)
    onerow_sb = sb(gl, "onerow_sb", [1, 128], F32)
    e01_sb = sb(gl, "e01_sb", [1, 128], BF16)
    e10_sb = sb(gl, "e10_sb", [1, 128], BF16)
    rw_sb = sb(gl, "rw_sb", [128, DC * E], F32)
    logits = sb(gl, "logits", [128, NT * E], F32)
    m1a = sb(gl, "m1a", [128, NT * E], F32)
    m2a = sb(gl, "m2a", [128, NT * E], F32)
    m1k = sb(gl, "m1k", [128, NT * E], F32)
    m2k = sb(gl, "m2k", [128, NT * E], F32)
    q1a = sb(gl, "q1a", [128, NT * E], F32)
    q2a = sb(gl, "q2a", [128, NT * E], F32)
    p1x = sb(gl, "p1x", [128, NT * E], F32)
    p2x = sb(gl, "p2x", [128, NT * E], F32)
    g1kc = sb(gl, "g1kc", [128, NT], F32)
    g2kc = sb(gl, "g2kc", [128, NT], F32)
    county = sb(gl, "county", [1, E], F32)
    gsc = sb(gl, "gsc", [128, 48], F32)

    block = stack.enter_context(nc.Block())

    def on(eng):
        return getattr(block, eng)

    @on("sync")
    def _(sy):
        for dst, src in [(iota_sb, IOTA), (idb_sb, IDB), (idf_sb, IDF),
                         (tri_sb, TRI), (onesf_sb, ONESF), (maskt_sb, MASKT),
                         (onecol_sb, ONECOL), (onecolb_sb, ONECOLB),
                         (onerow_sb, ONEROW), (e01_sb, E01), (e10_sb, E10)]:
            sy.dma_start(out=dst[:], in_=src).then_inc(dC.h, 16)
        sy.dma_start(out=rw_sb[:],
                     in_=RW.rearrange("(dc p) e -> p dc e", p=128)).then_inc(dC.h, 16)
    CONSTS = dC.bump(16 * 12)

    es_at = ExitStackR()
    es_qk = ExitStackR()
    es_ws = ExitStackR()
    es_wv = ExitStackR()
    es_ln = ExitStackR()
    avtmp = sb(es_at, "avtmp", [64, S], BF16)
    recrow = sb(es_at, "recrow", [65, S], BF16)
    recall = sb(es_at, "recall", [16, S], BF16)
    wqT = sb(es_qk, "wqT", [128, DC * D], BF16)
    wkT = sb(es_qk, "wkT", [128, DC * D], BF16)
    xfm = sb(es_qk, "xfm", [128, DC * S], BF16)
    vall = sb(es_qk, "vall", [128, NT * NH * 65], BF16)
    qbuf = [sb(es_qk, f"qb{i}", [65, S], BF16) for i in range(2)]
    kbuf = [sb(es_qk, f"kb{i}", [65, S], BF16) for i in range(2)]
    ptb = [sb(es_qk, f"ptb{i}", [128, S], BF16) for i in range(2)]
    sqq = sb(es_qk, "sqq", [64, S], BF16)
    nrow = sb(es_qk, "nrow", [1, S + 16], F32)
    brow = sb(es_qk, "brow", [1, S], BF16)
    wstage = sb(es_ws, "wstage", [128, D], BF16)
    wvT = sb(es_wv, "wvT", [128, DC * D], BF16)
    xin = [sb(es_ln, f"xin{i}", [128, D], F32) for i in range(2)]
    xlt = sb(es_ln, "xlt", [128, D], BF16)
    sqd = sb(es_ln, "sqd", [128, D], BF16)
    mcol = sb(es_ln, "mcol", [128, 16], F32)

    es_ptr = ExitStackR()
    ptr = ps(es_ptr, "ptr", [128, 1024], BF16)

    def load_wT(Wsrc, dstT, stage_free_t, ptr_free_v, ptrps, stg):
        Wr = Wsrc.rearrange("(oc p) d -> oc p d", p=128)
        ndma = [dW.bump(16) for _ in range(DC)]
        t_vals = [eT.bump() for _ in range(DC)]
        v_vals = [eV.bump() for _ in range(DC)]

        @on("gpsimd")
        def _(gp):
            for oc in range(DC):
                if oc == 0 and stage_free_t:
                    gp.wait_ge(eT.h, stage_free_t)
                if oc >= 1:
                    gp.wait_ge(eT.h, t_vals[oc - 1])
                gp.dma_start(out=stg[:], in_=Wr[oc]).then_inc(dW.h, 16)

        @on("tensor")
        def _(t):
            t.wait_ge(dC.h, CONSTS)
            for oc in range(DC):
                t.wait_ge(dW.h, ndma[oc])
                if oc == 0 and ptr_free_v:
                    t.wait_ge(eV.h, ptr_free_v)
                if oc >= 1:
                    t.wait_ge(eV.h, v_vals[oc - 1])
                for dc in range(DC):
                    ins = t.transpose(ptrps[:, dc * 128:dc * 128 + 128],
                                      stg[:, dc * 128:dc * 128 + 128],
                                      idb_sb[:])
                ins.then_inc(eT.h, 1)

        @on("vector")
        def _(v):
            for oc in range(DC):
                v.wait_ge(eT.h, t_vals[oc])
                dst = dstT[:].rearrange("p (dc x) -> p dc x", dc=DC)[:, :, oc * 128:oc * 128 + 128]
                src = ptrps[:].rearrange("p (dc x) -> p dc x", dc=DC)
                v.tensor_copy(dst, src).then_inc(eV.h, 1)
        return t_vals[-1], v_vals[-1]

    wv_t, wv_v = load_wT(WV, wvT, 0, 0, ptr, wstage)

    ln_nx = [dXi[i % 2].bump(16) for i in range(NT)]
    ln_mean, ln_var, ln_norm, ln_ev = [], [], [], []
    ln_sq, ln_rstd, ln_tp = [], [], []
    for i in range(NT):
        ln_mean.append(eV.bump())
        ln_var.append(eV.bump())
        ln_norm.append(eV.bump())
        ln_ev.append(eV.bump())
        ln_sq.append(eS.bump())
        ln_rstd.append(eS.bump())
        ln_tp.append(eT.bump())

    @on("sync")
    def _(sy):
        for i in range(NT):
            if i >= 2:
                sy.wait_ge(eS.h, ln_sq[i - 2])
                sy.wait_ge(eV.h, ln_norm[i - 2])
            sy.dma_start(out=xin[i % 2][:], in_=Xt[i]).then_inc(dXi[i % 2].h, 16)

    @on("vector")
    def _(v):
        for i in range(NT):
            b = (i % 2) * 8
            v.wait_ge(dXi[i % 2].h, ln_nx[i])
            v.reduce_sum(mcol[:, b:b + 1], xin[i % 2][:], axis=AX)
            v.drain()
            v.tensor_scalar(mcol[:, b:b + 1], mcol[:, b:b + 1], 1.0 / D, 0.0,
                            ALU.mult, ALU.add).then_inc(eV.h, 1)
            v.wait_ge(eS.h, ln_sq[i])
            v.wait_ge(eV.h, ln_mean[i])
            v.tensor_scalar(mcol[:, b + 2:b + 3], mcol[:, b + 1:b + 2], 1.0 / D, 0.0,
                            ALU.mult, ALU.add)
            v.tensor_scalar(mcol[:, b + 3:b + 4], mcol[:, b:b + 1], mcol[:, b:b + 1],
                            None, ALU.mult)
            v.drain()
            v.tensor_scalar(mcol[:, b + 2:b + 3], mcol[:, b + 2:b + 3],
                            mcol[:, b + 3:b + 4], None, ALU.subtract)
            v.drain()
            v.tensor_scalar(mcol[:, b + 2:b + 3], mcol[:, b + 2:b + 3], 1e-5, 1.0,
                            ALU.add, ALU.mult).then_inc(eV.h, 1)
            v.wait_ge(eS.h, ln_rstd[i])
            if i >= 1:
                v.wait_ge(eT.h, ln_tp[i - 1])
            v.tensor_scalar(xlt[:], xin[i % 2][:], mcol[:, b:b + 1], mcol[:, b + 4:b + 5],
                            ALU.subtract, ALU.mult).then_inc(eV.h, 1)
            v.wait_ge(eT.h, ln_tp[i])
            dst = xfm[:].rearrange("p (dc x) -> p dc x", dc=DC)[:, :, i * 128:i * 128 + 128]
            src = ptr[:].rearrange("p (dc x) -> p dc x", dc=DC)
            v.tensor_copy(dst, src).then_inc(eV.h, 1)

    @on("scalar")
    def _(s):
        for i in range(NT):
            b = (i % 2) * 8
            s.wait_ge(dXi[i % 2].h, ln_nx[i])
            s.activation(sqd[:], xin[i % 2][:], ACT.Square,
                         accum_out=mcol[:, b + 1:b + 2]).then_inc(eS.h, 1)
            s.wait_ge(eV.h, ln_var[i])
            s.activation(mcol[:, b + 3:b + 4], mcol[:, b + 2:b + 3], ACT.Ln)
            s.drain()
            s.activation(mcol[:, b + 4:b + 5], mcol[:, b + 3:b + 4], ACT.Exp,
                         scale=-0.5).then_inc(eS.h, 1)

    @on("tensor")
    def _(t):
        for i in range(NT):
            t.wait_ge(eV.h, ln_norm[i])
            t.wait_ge(eV.h, wv_v if i == 0 else ln_ev[i - 1])
            for dc in range(DC):
                ins = t.transpose(ptr[:, dc * 128:dc * 128 + 128],
                                  xlt[:, dc * 128:dc * 128 + 128], idb_sb[:])
            ins.then_inc(eT.h, 1)

    XFM_V = ln_ev[-1]

    if upto == "ln1":
        FV = dY.bump(16)

        @on("gpsimd")
        def _(gp):
            gp.wait_ge(eV.h, XFM_V)
            gp.dma_start(out=DBG, in_=xfm[:]).then_inc(dY.h, 16)

        @on("sync")
        def _(sy):
            sy.wait_ge(dY.h, FV)
        return nc, finish

    es_vp = ExitStackR()
    pvp = [ps(es_vp, f"pvp{i}", [128, 512]) for i in range(2)]
    VONES = eV.bump()
    vp_t, vp_v = [], []
    for _ in range(NT * 2):
        vp_t.append(eT.bump())
        vp_v.append(eV.bump())

    @on("vector")
    def _(v):
        v.memset(vall[:], 1.0).then_inc(eV.h, 1)

    @on("tensor")
    def _(t):
        t.wait_ge(eV.h, max(wv_v, XFM_V))
        k = 0
        for i in range(NT):
            for h2 in range(2):
                if k >= 2:
                    t.wait_ge(eV.h, vp_v[k - 2])
                for dc in range(DC):
                    ins = t.matmul(pvp[k % 2][:],
                                   xfm[:, dc * S + i * 128: dc * S + i * 128 + 128],
                                   wvT[:, dc * D + h2 * 512: dc * D + h2 * 512 + 512],
                                   start=(dc == 0), stop=(dc == DC - 1))
                ins.then_inc(eT.h, 1)
                k += 1

    @on("vector")
    def _(v):
        v.wait_ge(eV.h, VONES)
        k = 0
        for i in range(NT):
            for h2 in range(2):
                v.wait_ge(eT.h, vp_t[k])
                dst = vall[:, i * NH * 65: (i + 1) * NH * 65]
                dst = dst.rearrange("p (h c) -> p h c", h=NH)[:, h2 * 8:(h2 + 1) * 8, 0:64]
                src = pvp[k % 2][:].rearrange("p (h c) -> p h c", c=64)
                v.tensor_copy(dst, src).then_inc(eV.h, 1)
                k += 1
    es_vp.close()
    es_wv_close_marker = vp_v[-1]

    if upto == "vprime":
        FV = dY.bump(16)

        @on("gpsimd")
        def _(gp):
            gp.wait_ge(eV.h, vp_v[-1])
            gp.dma_start(out=DBG, in_=vall[:]).then_inc(dY.h, 16)

        @on("sync")
        def _(sy):
            sy.wait_ge(dY.h, FV)
        return nc, finish

    wq_t, wq_v = load_wT(WQ, wqT, wv_t, ln_ev[-1], ptr, wstage)
    wk_t, wk_v = load_wT(WK, wkT, wq_t, wq_v, ptr, wstage)
    es_ln.close()
    es_wv.close()
    es_ws.close()
    es_ptr.close()

    es_hd = ExitStackR()
    pqk = [ps(es_hd, f"pqk{i}", [64, 512]) for i in range(2)]
    psc = [ps(es_hd, f"psc{i}", [128, 512]) for i in range(2)]
    pav = ps(es_hd, "pav", [65, S])

    KONES = eG.bump()

    @on("gpsimd")
    def _(gp):
        gp.memset(kbuf[0][64:65, :], 1.0)
        gp.memset(kbuf[1][64:65, :], 1.0).then_inc(eG.h, 1)

    qk_free = [0, 0]
    pav_free_s = 0
    pav_free_v2 = 0
    avtmp_free = 0
    psc_free = [0, 0]
    pqk_free = [("S", 0), ("S", 0)]
    ptb_free = [0, 0]
    rec_free = 0
    hst = {}

    def emit_qk_bound(h):
        nonlocal_state = None
        hp = h % 2
        st = {}
        st["qkt"] = qkt = [eT.bump() for _ in range(8)]
        st["qev"] = qev = [eV.bump() for _ in range(4)]
        st["kev"] = kev = [eV.bump() for _ in range(4)]

        @on("tensor")
        def _(t_, h=h, hp=hp, qw=qk_free[hp], pqw=list(pqk_free),
              qev=qev, kev=kev):
            if h == 0:
                t_.wait_ge(eV.h, wk_v)
            if qw:
                t_.wait_ge(eT.h, qw)
            k = 0
            for wT in (wqT, wkT):
                for c in range(4):
                    if k >= 2:
                        t_.wait_ge(eV.h, (qev + kev)[k - 2])
                    else:
                        kind, val = pqw[k % 2]
                        if val:
                            t_.wait_ge((eS if kind == "S" else eV).h, val)
                    for dc in range(DC):
                        ins = t_.matmul(pqk[k % 2][:],
                                        wT[:, dc * D + h * 64: dc * D + h * 64 + 64],
                                        xfm[:, dc * S + c * 512: dc * S + c * 512 + 512],
                                        start=(dc == 0), stop=(dc == DC - 1))
                    ins.then_inc(eT.h, 1)
                    k += 1

        @on("vector")
        def _(v, hp=hp, qkt=qkt):
            for c in range(4):
                v.wait_ge(eT.h, qkt[c])
                v.tensor_scalar(qbuf[hp][0:64, c * 512:(c + 1) * 512], pqk[c % 2][:],
                                0.125, 0.0, ALU.mult, ALU.add).then_inc(eV.h, 1)
            for c in range(4):
                v.wait_ge(eT.h, qkt[4 + c])
                v.tensor_copy(kbuf[hp][0:64, c * 512:(c + 1) * 512],
                              pqk[c % 2][:]).then_inc(eV.h, 1)

        st["sq_k"] = sq_k = eV.bump()
        st["nmk_t"] = nmk_t = [eT.bump() for _ in range(4)]
        st["nmk_s"] = nmk_s = [eS.bump() for _ in range(4)]
        st["mk_done"] = mk_done = eV.bump()
        st["sq_q"] = sq_q = eV.bump()
        st["nmq_t"] = nmq_t = [eT.bump() for _ in range(4)]
        st["nmq_s"] = nmq_s = [eS.bump() for _ in range(4)]
        st["bexp_done"] = bexp_done = eS.bump()
        st["brow_dma"] = dBr.bump(16)
        prev_nmq_t = hst[h - 1]["nmq_t"][-1] if h >= 1 else 0
        prev_brow = hst[h - 1]["brow_dma"] if h >= 1 else 0

        @on("vector")
        def _(v, hp=hp, kev=kev, nmk_t=nmk_t, nmk_s=nmk_s, prev_nmq_t=prev_nmq_t):
            v.wait_ge(eV.h, kev[-1])
            if prev_nmq_t:
                v.wait_ge(eT.h, prev_nmq_t)
            v.tensor_tensor(sqq[:], kbuf[hp][0:64, :], kbuf[hp][0:64, :],
                            ALU.mult).then_inc(eV.h, 1)
            v.wait_ge(eS.h, nmk_s[-1])
            v.reduce_max(nrow[0:1, S:S + 1], nrow[0:1, 0:S],
                         axis=AX).then_inc(eV.h, 1)
            v.wait_ge(eT.h, nmk_t[-1])
            v.tensor_tensor(sqq[:], qbuf[hp][0:64, :], qbuf[hp][0:64, :],
                            ALU.mult).then_inc(eV.h, 1)

        prev_bexp = hst[h - 1]["bexp_done"] if h >= 1 else 0

        @on("scalar")
        def _(s, nmk_t=nmk_t, nmq_t=nmq_t, mk_done=mk_done, nmq_s=nmq_s,
              prev_brow=prev_brow, prev_bexp=prev_bexp):
            if prev_bexp:
                s.wait_ge(eS.h, prev_bexp)
            for c in range(4):
                s.wait_ge(eT.h, nmk_t[c])
                s.copy(nrow[0:1, c * 512:(c + 1) * 512],
                       pqk[c % 2][0:1, :]).then_inc(eS.h, 1)
            if prev_brow:
                s.wait_ge(dBr.h, prev_brow)
            for c in range(4):
                s.wait_ge(eT.h, nmq_t[c])
                s.copy(brow[0:1, c * 512:(c + 1) * 512],
                       pqk[c % 2][0:1, :]).then_inc(eS.h, 1)
            s.wait_ge(eV.h, mk_done)
            s.wait_ge(eS.h, nmq_s[-1])
            s.activation(nrow[0:1, S + 1:S + 2], nrow[0:1, S:S + 1], ACT.Ln)
            s.drain()
            s.activation(nrow[0:1, S + 2:S + 3], nrow[0:1, S + 1:S + 2],
                         ACT.Identity, scale=0.5)
            s.drain()
            s.activation(nrow[0:1, 0:S], brow[0:1, 0:S], ACT.Ln)
            s.drain()
            s.activation(nrow[0:1, 0:S], nrow[0:1, 0:S], ACT.Exp, scale=0.5,
                         bias=nrow[0:1, S + 2:S + 3])
            s.drain()
            s.mul(brow[:], nrow[0:1, 0:S], -1.05).then_inc(eS.h, 1)

        @on("tensor")
        def _(t_, sq_k=sq_k, sq_q=sq_q, nmk_s=nmk_s, nmq_s=nmq_s):
            t_.wait_ge(eV.h, sq_k)
            for c in range(4):
                if c >= 2:
                    t_.wait_ge(eS.h, nmk_s[c - 2])
                t_.matmul(pqk[c % 2][0:1, :], onecolb_sb[0:64, :],
                          sqq[:, c * 512:(c + 1) * 512],
                          start=True, stop=True).then_inc(eT.h, 1)
            t_.wait_ge(eV.h, sq_q)
            for c in range(4):
                t_.wait_ge(eS.h, (nmk_s + nmq_s)[c + 2])
                t_.matmul(pqk[c % 2][0:1, :], onecolb_sb[0:64, :],
                          sqq[:, c * 512:(c + 1) * 512],
                          start=True, stop=True).then_inc(eT.h, 1)

        @on("sync")
        def _(sy, hp=hp, bexp_done=bexp_done):
            sy.wait_ge(eS.h, bexp_done)
            sy.dma_start(out=qbuf[hp][64:65, :], in_=brow[:]).then_inc(dBr.h, 16)

        hst[h] = st
        return [("S", nmq_s[2]), ("S", nmq_s[3])]

    def emit_pass2_sigma(h):
        hp = h % 2
        pr = h // 2
        st = hst[h]
        chunks = []
        for j in range(NT):
            for c in range(j // 4, 4):
                chunks.append((j, c))
        par = {jc: k % 2 for k, jc in enumerate(chunks)}
        sc_t, av_t, mask_v, exp_s = {}, {}, {}, {}
        for k, (j, c) in enumerate(chunks):
            sc_t[(j, c)] = eT.bump()
            if c == j // 4:
                mask_v[(j, c)] = eV.bump()
            av_t[(j, c)] = eT.bump()
        for (j, c) in chunks:
            exp_s[(j, c)] = eS.bump()

        @on("tensor")
        def _(t_, h=h, hp=hp, brow_dma=st["brow_dma"], chunks=chunks,
              exp_s=exp_s, psc_w=list(psc_free), pav_s=pav_free_s,
              pav_v2=pav_free_v2, av_free=avtmp_free, rec_f=rec_free):
            t_.wait_ge(dBr.h, brow_dma)
            if h == 0:
                t_.wait_ge(eG.h, KONES)
                t_.wait_ge(eV.h, vp_v[-1])
            if pav_s:
                t_.wait_ge(eS.h, pav_s)
            if pav_v2:
                t_.wait_ge(eV.h, pav_v2)
            if av_free:
                t_.wait_ge(dAt.h, av_free)
            if rec_f:
                t_.wait_ge(dRc.h, rec_f)
            for k, (j, c) in enumerate(chunks):
                off = j * 128 - c * 512 if c == j // 4 else 0
                w = 512 - off
                q0 = c * 512 + off
                if k >= 2:
                    t_.wait_ge(eS.h, exp_s[chunks[k - 2]])
                elif psc_w[k % 2]:
                    t_.wait_ge(eS.h, psc_w[k % 2])
                t_.matmul(psc[k % 2][:, 0:w], kbuf[hp][:, j * 128:(j + 1) * 128],
                          qbuf[hp][:, q0:q0 + w],
                          start=True, stop=True).then_inc(eT.h, 1)
                t_.wait_ge(eS.h, exp_s[(j, c)])
                t_.matmul(pav[0:65, q0:q0 + w],
                          vall[:, j * NH * 65 + h * 65: j * NH * 65 + (h + 1) * 65],
                          ptb[j % 2][:, q0:q0 + w],
                          start=(j == 0), stop=(j == min(15, 4 * c + 3))
                          ).then_inc(eT.h, 1)

        @on("vector")
        def _(v, sc_t=sc_t, par=par):
            for j in range(NT):
                c = j // 4
                v.wait_ge(eT.h, sc_t[(j, c)])
                p = psc[par[(j, c)]]
                v.tensor_tensor(p[:, 0:128], p[:, 0:128], maskt_sb[:],
                                ALU.add).then_inc(eV.h, 1)

        @on("scalar")
        def _(s, chunks=chunks, par=par, sc_t=sc_t, mask_v=mask_v,
              av_t=av_t, ptb_w=list(ptb_free)):
            for j in range(NT):
                if j >= 2:
                    s.wait_ge(eT.h, av_t[(j - 2, 3)])
                elif ptb_w[j % 2]:
                    s.wait_ge(eT.h, ptb_w[j % 2])
                for c in range(j // 4, 4):
                    off = j * 128 - c * 512 if c == j // 4 else 0
                    w = 512 - off
                    q0 = c * 512 + off
                    s.wait_ge(eT.h, sc_t[(j, c)])
                    if c == j // 4:
                        s.wait_ge(eV.h, mask_v[(j, c)])
                    s.activation(ptb[j % 2][:, q0:q0 + w], psc[par[(j, c)]][:, 0:w],
                                 ACT.Exp).then_inc(eS.h, 1)

        sig_evac = eS.bump()
        av_evac = eV.bump()

        @on("scalar")
        def _(s, av_t=av_t, rec_f=rec_free):
            s.wait_ge(eT.h, av_t[(15, 3)])
            if rec_f:
                s.wait_ge(dRc.h, rec_f)
            s.copy(recrow[64:65, :], pav[64:65, :]).then_inc(eS.h, 1)

        @on("vector")
        def _(v, av_t=av_t, av_free=avtmp_free):
            v.wait_ge(eT.h, av_t[(15, 3)])
            if av_free:
                v.wait_ge(dAt.h, av_free)
            v.tensor_copy(avtmp[:], pav[0:64, :]).then_inc(eV.h, 1)

        @on("sync")
        def _(sy, h=h, sig_evac=sig_evac):
            sy.wait_ge(eS.h, sig_evac)
            sy.dma_start(out=recall[h:h + 1, :], in_=recrow[64:65, :]).then_inc(dRc.h, 16)

        @on("sync")
        def _(sy, hp=hp, pr=pr, av_evac=av_evac):
            sy.wait_ge(eV.h, av_evac)
            sy.dma_start(out=AVS[pr, hp * 64:hp * 64 + 64, :],
                         in_=avtmp[:]).then_inc(dAt.h, 16)

        pf = [0, 0]
        for (j, c) in chunks:
            pf[par[(j, c)]] = exp_s[(j, c)]
        return (pf, [av_t[(14, 3)], av_t[(15, 3)]], av_t[(15, 3)],
                sig_evac, av_evac, dRc.bump(16), dAt.bump(16))

    for h in range(nhrun + 1):
        if h < nhrun:
            new_pqk_free = emit_qk_bound(h)
        if h >= 1:
            (psc_free, ptb_free, qk_free[(h - 1) % 2], pav_free_s,
             pav_free_v2, rec_free, avtmp_free) = emit_pass2_sigma(h - 1)
        if h < nhrun:
            pqk_free = new_pqk_free

    SIG_S = pav_free_s
    HEADS_LAST_T = max(qk_free)
    es_hd.close()
    es_qk.close()

    es_wo = ExitStackR()
    woT = sb(es_wo, "woT", [128, DC * D], BF16)
    es_ln2 = ExitStackR()
    xin2 = [sb(es_ln2, f"xin2{i}", [128, D], F32) for i in range(2)]
    x1b = [sb(es_ln2, f"x1b{i}", [128, D], F32) for i in range(2)]
    xl2f = sb(es_ln2, "xl2f", [128, D], F32)
    xfm32 = sb(es_ln2, "xfm32", [128, D], F32)
    sqd2 = sb(es_ln2, "sqd2", [128, D], BF16)
    mcol2 = sb(es_ln2, "mcol2", [128, 16], F32)
    xl2b = [sb(es_ln2, f"xl2b{i}", [128, D], BF16) for i in range(2)]
    avo = [sb(es_ln2, f"avo{i}", [128, D], BF16) for i in range(2)]
    es_ws2 = ExitStackR()
    wstage2 = sb(es_ws2, "wstage2", [128, D], BF16)
    es_rec = ExitStackR()
    rec0 = sb(es_rec, "rec0", [1, S], BF16)
    rec1 = sb(es_rec, "rec1", [1, S], BF16)
    avsc = [sb(es_rec, f"avsc{i}", [128, S], BF16) for i in range(2)]
    sigr = sb(es_rec, "sigr", [128, 512], F32)

    es_sg = ExitStackR()
    psb = [ps(es_sg, f"psb{i}", [128, 512]) for i in range(2)]

    sg_r0, sg_r1, sg_ld, sg_wb = [], [], [], []
    sg_t = []
    sg_v = []
    for pr in range(nhrun // 2):
        sg_r0.append(dR0.bump(16))
        sg_r1.append(dR1.bump(16))
        sg_ld.append(dAl[pr % 2].bump(16))
        for c in range(4):
            sg_t.append(eT.bump())
            sg_v.append(eV.bump())
        sg_wb.append(dAw[pr % 2].bump(16))

    @on("sync")
    def _(sy):
        sy.wait_ge(dBr.h, dBr.n)
        sy.wait_ge(dRc.h, dRc.n)
        sy.wait_ge(dAt.h, dAt.n)
        for pr in range(nhrun // 2):
            if pr >= 1:
                sy.wait_ge(eT.h, sg_t[pr * 4 - 1])
            sy.dma_start(out=rec0[:], in_=recall[2 * pr:2 * pr + 1, :]).then_inc(dR0.h, 16)
            sy.dma_start(out=rec1[:], in_=recall[2 * pr + 1:2 * pr + 2, :]).then_inc(dR1.h, 16)
            if pr >= 2:
                sy.wait_ge(dAw[pr % 2].h, sg_wb[pr - 2])
            sy.dma_start(out=avsc[pr % 2][:], in_=AVS[pr]).then_inc(dAl[pr % 2].h, 16)
            sy.wait_ge(eV.h, sg_v[pr * 4 + 3])
            sy.dma_start(out=AVS[pr], in_=avsc[pr % 2][:]).then_inc(dAw[pr % 2].h, 16)

    @on("tensor")
    def _(t_):
        k = 0
        for pr in range(nhrun // 2):
            t_.wait_ge(dR0.h, sg_r0[pr])
            t_.wait_ge(dR1.h, sg_r1[pr])
            for c in range(4):
                if k >= 2:
                    t_.wait_ge(eV.h, sg_v[k - 2])
                t_.matmul(psb[k % 2][:], e01_sb[:], rec0[0:1, c * 512:(c + 1) * 512],
                          start=True, stop=False)
                t_.matmul(psb[k % 2][:], e10_sb[:], rec1[0:1, c * 512:(c + 1) * 512],
                          start=False, stop=True).then_inc(eT.h, 1)
                k += 1

    @on("vector")
    def _(v):
        v.wait_ge(eS.h, SIG_S)
        k = 0
        for pr in range(nhrun // 2):
            v.wait_ge(dAl[pr % 2].h, sg_ld[pr])
            for c in range(4):
                v.wait_ge(eT.h, sg_t[k])
                sl = avsc[pr % 2][:, c * 512:(c + 1) * 512]
                v.drain()
                v.reciprocal(sigr[:], psb[k % 2][:])
                v.drain()
                v.tensor_tensor(sl, sl, sigr[:], ALU.mult).then_inc(eV.h, 1)
                k += 1
    SGV = eV.n
    es_sg.close()

    if upto == "attn":
        FV = dY.bump(16)

        @on("gpsimd")
        def _(gp):
            gp.wait_ge(dAw[0].h, dAw[0].n)
            gp.wait_ge(dAw[1].h, dAw[1].n)
            gp.dma_start(out=DBG[:].rearrange("p (r q) -> p r q", r=NH // 2),
                         in_=AVS.rearrange("r p q -> p r q")).then_inc(dY.h, 16)

        @on("sync")
        def _(sy):
            sy.wait_ge(dY.h, FV)
        return nc, finish

    es_rec.close()
    es_ptr2 = ExitStackR()
    ptr2 = ps(es_ptr2, "ptr2", [128, 1024], BF16)
    wo_t, wo_v = load_wT(WO, woT, HEADS_LAST_T, sg_v[-1], ptr2, wstage2)
    es_ptr2.close()
    es_ws2.close()

    es_op = ExitStackR()
    pop = [ps(es_op, f"pop{i}", [128, 512]) for i in range(2)]
    prt = ps(es_op, "prt", [128, 1024])
    plg = ps(es_op, "plg", [128, 512])

    op_nx, op_t, op_add, op_var, op_x2, op_tp = [], [], [], [], [], []
    op_ev32, op_lg, op_lgv, op_sq, op_rstd = [], [], [], [], []
    gt_m, gt_s, gt_v, op_spill, op_l2sp = [], [], [], [], []
    op_av = []
    for i in range(NT):
        op_nx.append(dX2[i % 2].bump(16))
        op_av.append(dAv[i % 2].bump(16))
        op_t.append([eT.bump(), eT.bump()])
        op_add.append(eV.bump())
        op_var.append(eV.bump())
        op_x2.append(eV.bump())
        op_tp.append(eT.bump())
        op_ev32.append(eV.bump())
        op_lg.append(eT.bump())
        op_lgv.append(eV.bump())
        op_sq.append(eS.bump())
        op_rstd.append(eS.bump())
        op_spill.append(dSpX[i % 2].bump(16))
        op_l2sp.append(dSpL[i % 2].bump(16))
    for i in range(NT):
        gt_m.append(eV.bump())
        gt_v.append(eV.bump())
        gt_s.append(eS.bump())

    @on("sync")
    def _(sy):
        sy.wait_ge(dAw[0].h, dAw[0].n)
        sy.wait_ge(dAw[1].h, dAw[1].n)
        for i in range(NT):
            if i >= 2:
                sy.wait_ge(eV.h, op_add[i - 2])
                sy.wait_ge(eT.h, op_t[i - 2][1])
            sy.dma_start(out=avo[i % 2][:],
                         in_=AVS.rearrange("r p q -> p r q")[:, :, i * 128:(i + 1) * 128]
                         ).then_inc(dAv[i % 2].h, 16)
            sy.dma_start(out=xin2[i % 2][:], in_=Xt[i]).then_inc(dX2[i % 2].h, 16)
            sy.wait_ge(eV.h, op_x2[i])
            sy.dma_start(out=X1t[i], in_=x1b[i % 2][:]).then_inc(dSpX[i % 2].h, 16)
            sy.dma_start(out=XL2t[i], in_=xl2b[i % 2][:]).then_inc(dSpL[i % 2].h, 16)

    @on("tensor")
    def _(t_):
        t_.wait_ge(eV.h, wo_v)
        for i in range(NT):
            t_.wait_ge(dAv[i % 2].h, op_av[i])
            for dh in range(2):
                if i >= 1:
                    t_.wait_ge(eV.h, op_add[i - 1])
                for tc in range(DC):
                    ins = t_.matmul(pop[dh][:],
                                    avo[i % 2][:, tc * 128:(tc + 1) * 128],
                                    woT[:, tc * D + dh * 512: tc * D + dh * 512 + 512],
                                    start=(tc == 0), stop=(tc == DC - 1))
                ins.then_inc(eT.h, 1)
            t_.wait_ge(eV.h, op_x2[i])
            if i >= 1:
                t_.wait_ge(eV.h, op_ev32[i - 1])
            for dc in range(DC):
                ins = t_.transpose(prt[:, (dc % 8) * 128:(dc % 8) * 128 + 128],
                                   xl2f[:, dc * 128:dc * 128 + 128], idf_sb[:])
            ins.then_inc(eT.h, 1)
            t_.wait_ge(eV.h, op_ev32[i])
            if i >= 1:
                t_.wait_ge(eV.h, op_lgv[i - 1])
            for dc in range(DC):
                ins = t_.matmul(plg[:, 0:E], xfm32[:, dc * 128:dc * 128 + 128],
                                rw_sb[:, dc * E:(dc + 1) * E],
                                start=(dc == 0), stop=(dc == DC - 1))
            ins.then_inc(eT.h, 1)

    @on("vector")
    def _(v):
        for i in range(NT):
            b = (i % 2) * 8
            v.wait_ge(dX2[i % 2].h, op_nx[i])
            if i >= 2:
                v.wait_ge(eS.h, op_sq[i - 2])
                v.wait_ge(dSpX[i % 2].h, op_spill[i - 2])
            for dh in range(2):
                v.wait_ge(eT.h, op_t[i][dh])
                v.tensor_tensor(x1b[i % 2][:, dh * 512:(dh + 1) * 512],
                                xin2[i % 2][:, dh * 512:(dh + 1) * 512],
                                pop[dh][:], ALU.add)
            v.drain()
            v.reduce_sum(mcol2[:, b:b + 1], x1b[i % 2][:], axis=AX)
            v.drain()
            v.tensor_scalar(mcol2[:, b:b + 1], mcol2[:, b:b + 1], 1.0 / D, 0.0,
                            ALU.mult, ALU.add).then_inc(eV.h, 1)
            v.wait_ge(eS.h, op_sq[i])
            v.wait_ge(eV.h, op_add[i])
            v.tensor_scalar(mcol2[:, b + 2:b + 3], mcol2[:, b + 1:b + 2], 1.0 / D, 0.0,
                            ALU.mult, ALU.add)
            v.tensor_scalar(mcol2[:, b + 3:b + 4], mcol2[:, b:b + 1], mcol2[:, b:b + 1],
                            None, ALU.mult)
            v.drain()
            v.tensor_scalar(mcol2[:, b + 2:b + 3], mcol2[:, b + 2:b + 3],
                            mcol2[:, b + 3:b + 4], None, ALU.subtract)
            v.drain()
            v.tensor_scalar(mcol2[:, b + 2:b + 3], mcol2[:, b + 2:b + 3], 1e-5, 1.0,
                            ALU.add, ALU.mult).then_inc(eV.h, 1)
            v.wait_ge(eS.h, op_rstd[i])
            v.tensor_scalar(xl2f[:], x1b[i % 2][:], mcol2[:, b:b + 1], mcol2[:, b + 4:b + 5],
                            ALU.subtract, ALU.mult)
            if i >= 2:
                v.wait_ge(dSpL[i % 2].h, op_l2sp[i - 2])
            v.drain()
            v.tensor_copy(xl2b[i % 2][:], xl2f[:]).then_inc(eV.h, 1)
            v.wait_ge(eT.h, op_tp[i])
            v.tensor_copy(xfm32[:], prt[:]).then_inc(eV.h, 1)
            v.wait_ge(eT.h, op_lg[i])
            v.tensor_copy(logits[:, i * E:(i + 1) * E], plg[:, 0:E]).then_inc(eV.h, 1)


    @on("scalar")
    def _(s):
        for i in range(NT):
            b = (i % 2) * 8
            s.wait_ge(eV.h, op_add[i])
            s.activation(sqd2[:], x1b[i % 2][:], ACT.Square,
                         accum_out=mcol2[:, b + 1:b + 2]).then_inc(eS.h, 1)
            s.wait_ge(eV.h, op_var[i])
            s.activation(mcol2[:, b + 3:b + 4], mcol2[:, b + 2:b + 3], ACT.Ln)
            s.drain()
            s.activation(mcol2[:, b + 4:b + 5], mcol2[:, b + 3:b + 4], ACT.Exp,
                         scale=-0.5).then_inc(eS.h, 1)


    @on("vector")
    def _(v):
        for i in range(NT):
                L = logits[:, i * E:(i + 1) * E]
                if i >= 1:
                    v.wait_ge(eS.h, gt_s[i - 1])
                v.wait_ge(eV.h, op_lgv[i])
                v.drain()
                v.reduce_max(gsc[:, 0:1], L, axis=AX)
                v.drain()
                v.tensor_scalar(gsc[:, 1:2], gsc[:, 0:1], -1.0, 0.0,
                                ALU.mult, ALU.add).then_inc(eV.h, 1)
                v.wait_ge(eS.h, gt_s[i])
                v.reciprocal(gsc[:, 4:5], gsc[:, 3:4])
                v.tensor_scalar(gsc[:, 8:16], L, gsc[:, 0:1], None, ALU.is_equal)
                v.drain()
                v.tensor_tensor_scan(gsc[:, 16:24], gsc[:, 8:16], gsc[:, 8:16],
                                     0.0, ALU.add, ALU.bypass)
                v.drain()
                v.tensor_sub(gsc[:, 16:24], gsc[:, 16:24], gsc[:, 8:16])
                v.drain()
                v.tensor_scalar(gsc[:, 16:24], gsc[:, 16:24], 0.0, None, ALU.is_equal)
                v.drain()
                v.tensor_tensor(m1a[:, i * E:(i + 1) * E], gsc[:, 8:16], gsc[:, 16:24], ALU.mult)
                v.drain()
                v.tensor_tensor(gsc[:, 8:16], gsc[:, 24:32], m1a[:, i * E:(i + 1) * E], ALU.mult)
                v.drain()
                v.reduce_sum(gsc[:, 5:6], gsc[:, 8:16], axis=AX)
                v.drain()
                v.tensor_scalar(g1kc[:, i:i + 1], gsc[:, 5:6], gsc[:, 4:5], None, ALU.mult)
                v.scalar_tensor_tensor(gsc[:, 8:16], m1a[:, i * E:(i + 1) * E], -1e9, L,
                                       ALU.mult, ALU.add)
                v.drain()
                v.reduce_max(gsc[:, 6:7], gsc[:, 8:16], axis=AX)
                v.drain()
                v.tensor_scalar(gsc[:, 8:16], gsc[:, 8:16], gsc[:, 6:7], None, ALU.is_equal)
                v.drain()
                v.tensor_tensor_scan(gsc[:, 16:24], gsc[:, 8:16], gsc[:, 8:16],
                                     0.0, ALU.add, ALU.bypass)
                v.drain()
                v.tensor_sub(gsc[:, 16:24], gsc[:, 16:24], gsc[:, 8:16])
                v.drain()
                v.tensor_scalar(gsc[:, 16:24], gsc[:, 16:24], 0.0, None, ALU.is_equal)
                v.drain()
                v.tensor_tensor(m2a[:, i * E:(i + 1) * E], gsc[:, 8:16], gsc[:, 16:24], ALU.mult)
                v.drain()
                v.tensor_tensor(gsc[:, 8:16], gsc[:, 24:32], m2a[:, i * E:(i + 1) * E], ALU.mult)
                v.drain()
                v.reduce_sum(gsc[:, 5:6], gsc[:, 8:16], axis=AX)
                v.drain()
                v.tensor_scalar(g2kc[:, i:i + 1], gsc[:, 5:6], gsc[:, 4:5], None,
                                ALU.mult).then_inc(eV.h, 1)

    @on("scalar")
    def _(s):
        for i in range(NT):
            s.wait_ge(eV.h, gt_m[i])
            s.activation(gsc[:, 24:32], logits[:, i * E:(i + 1) * E], ACT.Exp,
                         bias=gsc[:, 1:2], accum_out=gsc[:, 3:4]).then_inc(eS.h, 1)

    es_op.close()
    es_ln2.close()
    es_wo.close()
    es_at.close()

    es_cs = ExitStackR()
    pcs = ps(es_cs, "pcs", [128, 512])
    pcnt = ps(es_cs, "pcnt", [1, 512])

    cs1_t = [eT.bump() for _ in range(NT)]
    cs1_v = [eV.bump() for _ in range(NT)]
    cnt_t = eT.bump()
    cnt_s = eS.bump()
    cs2_t = [eT.bump() for _ in range(NT)]
    cs2_v = [eV.bump() for _ in range(NT)]
    GATE_V = eV.bump()

    @on("tensor")
    def _(t_):
        t_.wait_ge(eV.h, gt_v[-1])
        for i in range(NT):
            if i >= 1:
                t_.wait_ge(eV.h, cs1_v[i - 1])
            for j in range(i + 1):
                ins = t_.matmul(pcs[:, 0:E],
                                (tri_sb if j == i else onesf_sb)[:],
                                m1a[:, j * E:(j + 1) * E],
                                start=(j == 0), stop=(j == i))
            ins.then_inc(eT.h, 1)
        t_.wait_ge(eV.h, cs1_v[-1])
        for i in range(NT):
            ins = t_.matmul(pcnt[:, 0:E], onecol_sb[:], m1k[:, i * E:(i + 1) * E],
                            start=(i == 0), stop=(i == NT - 1))
        ins.then_inc(eT.h, 1)
        t_.wait_ge(eS.h, cnt_s)
        for i in range(NT):
            if i >= 1:
                t_.wait_ge(eV.h, cs2_v[i - 1])
            for j in range(i + 1):
                t_.matmul(pcs[:, 0:E],
                          (tri_sb if j == i else onesf_sb)[:],
                          m2a[:, j * E:(j + 1) * E],
                          start=(j == 0), stop=False)
            ins = t_.matmul(pcs[:, 0:E], onerow_sb[:], county[:],
                            start=False, stop=True)
            ins.then_inc(eT.h, 1)

    @on("scalar")
    def _(s):
        s.wait_ge(eT.h, cnt_t)
        s.copy(county[:], pcnt[0:1, 0:E]).then_inc(eS.h, 1)

    @on("vector")
    def _(v):
        v.wait_ge(eV.h, gt_v[-1])
        for i in range(NT):
            v.wait_ge(eT.h, cs1_t[i])
            sl = slice(i * E, (i + 1) * E)
            v.tensor_copy(p1x[:, sl], pcs[:, 0:E])
            v.drain()
            v.tensor_scalar(gsc[:, 8:16], p1x[:, sl], float(CAP), None, ALU.is_lt)
            v.drain()
            v.tensor_tensor(m1k[:, sl], m1a[:, sl], gsc[:, 8:16], ALU.mult)
            v.drain()
            v.scalar_tensor_tensor(gsc[:, 8:16], p1x[:, sl], 1.0, m1k[:, sl],
                                   ALU.add, ALU.mult)
            v.drain()
            v.tensor_scalar(q1a[:, sl], gsc[:, 8:16], 1.0, 1.0,
                            ALU.subtract, ALU.mult).then_inc(eV.h, 1)
        for i in range(NT):
            v.wait_ge(eT.h, cs2_t[i])
            sl = slice(i * E, (i + 1) * E)
            v.tensor_copy(p2x[:, sl], pcs[:, 0:E])
            v.drain()
            v.tensor_scalar(gsc[:, 8:16], p2x[:, sl], float(CAP), None, ALU.is_lt)
            v.drain()
            v.tensor_tensor(m2k[:, sl], m2a[:, sl], gsc[:, 8:16], ALU.mult)
            v.drain()
            v.scalar_tensor_tensor(gsc[:, 8:16], p2x[:, sl], 1.0, m2k[:, sl],
                                   ALU.add, ALU.mult)
            v.drain()
            v.tensor_scalar(q2a[:, sl], gsc[:, 8:16], 1.0, 1.0,
                            ALU.subtract, ALU.mult).then_inc(eV.h, 1)
        for i in range(NT):
            sl = slice(i * E, (i + 1) * E)
            v.reduce_sum(gsc[:, 0:1], m1k[:, sl], axis=AX)
            v.drain()
            v.tensor_scalar(g1kc[:, i:i + 1], g1kc[:, i:i + 1], gsc[:, 0:1],
                            None, ALU.mult)
            v.reduce_sum(gsc[:, 1:2], m2k[:, sl], axis=AX)
            v.drain()
            ins = v.tensor_scalar(g2kc[:, i:i + 1], g2kc[:, i:i + 1], gsc[:, 1:2],
                                  None, ALU.mult)
        ins.then_inc(eV.h, 1)
    es_cs.close()

    if upto == "gate":
        FV = dY.bump(16 * 11)

        @on("gpsimd")
        def _(gp):
            gp.wait_ge(eV.h, GATE_V)
            for k, arr in enumerate([logits, m1a, m2a, m1k, m2k, q1a, q2a, p1x, p2x]):
                gp.dma_start(out=DBG[:, k * NT * E:(k + 1) * NT * E],
                             in_=arr[:]).then_inc(dY.h, 16)
            gp.dma_start(out=DBG[:, 9 * NT * E: 9 * NT * E + NT],
                         in_=g1kc[:]).then_inc(dY.h, 16)
            gp.dma_start(out=DBG[:, 9 * NT * E + NT: 9 * NT * E + 2 * NT],
                         in_=g2kc[:]).then_inc(dY.h, 16)

        @on("sync")
        def _(sy):
            sy.wait_ge(dY.h, FV)
        return nc, finish

    es_ga = ExitStackR()
    pg = [ps(es_ga, f"pg{mc}", [128, 512]) for mc in range(DC)]
    es_ffn = ExitStackR()
    xe = sb(es_ffn, "xe", [128, E * DC * 512], BF16)
    dtile = [sb(es_ga, f"dtile{i}", [128, 512], BF16) for i in range(2)]
    dtmp = sb(es_ga, "dtmp", [128, 512], BF16)
    xl2g = [sb(es_ga, f"xl2g{i}", [128, D], BF16) for i in range(2)]

    ga_bv, ga_t, ga_ev, ga_rl = {}, {}, {}, {}
    for e in range(E):
        for i in range(NT):
            ga_bv[(e, i)] = eV.bump()
            ga_t[(e, i)] = eT.bump()
            ga_rl[(e, i)] = dGr[(e * NT + i) % 2].bump(16)
        for mc in range(DC):
            ga_ev[(e, mc)] = eV.bump()

    @on("sync")
    def _(sy):
        for q in range(2):
            sy.wait_ge(dSpX[q].h, dSpX[q].n)
            sy.wait_ge(dSpL[q].h, dSpL[q].n)
        for e in range(E):
            for i in range(NT):
                k = e * NT + i
                if k >= 2:
                    sy.wait_ge(eT.h, ga_t[((k - 2) // NT, (k - 2) % NT)])
                sy.dma_start(out=xl2g[k % 2][:], in_=XL2t[i]).then_inc(dGr[k % 2].h, 16)

    @on("vector")
    def _(v):
        v.wait_ge(eV.h, GATE_V)
        for e in range(E):
            for i in range(NT):
                k = e * NT + i
                if k >= 2:
                    v.wait_ge(eT.h, ga_t[((k - 2) // NT, (k - 2) % NT)])
                if k >= 1:
                    v.wait_ge(eV.h, ga_bv[((k - 1) // NT, (k - 1) % NT)])
                v.tensor_scalar(dtile[k % 2][:], iota_sb[:],
                                q1a[:, i * E + e:i * E + e + 1], None, ALU.is_equal)
                v.tensor_scalar(dtmp[:], iota_sb[:],
                                q2a[:, i * E + e:i * E + e + 1], None, ALU.is_equal)
                v.drain()
                v.tensor_tensor(dtile[k % 2][:], dtile[k % 2][:], dtmp[:],
                                ALU.add).then_inc(eV.h, 1)
            for mc in range(DC):
                v.wait_ge(eT.h, ga_t[(e, NT - 1)])
                v.tensor_copy(xe[:, e * DC * 512 + mc * 512:(e * DC + mc + 1) * 512],
                              pg[mc][:]).then_inc(eV.h, 1)

    @on("tensor")
    def _(t_):
        for e in range(E):
            for i in range(NT):
                t_.wait_ge(eV.h, ga_bv[(e, i)])
                if i == 0 and e >= 1:
                    for mc in range(DC):
                        t_.wait_ge(eV.h, ga_ev[(e - 1, mc)])
                t_.wait_ge(dGr[(e * NT + i) % 2].h, ga_rl[(e, i)])
                for mc in range(DC):
                    ins = t_.matmul(pg[mc][:],
                                    xl2g[(e * NT + i) % 2][:, mc * 128:(mc + 1) * 128],
                                    dtile[(e * NT + i) % 2][:],
                                    start=(i == 0), stop=(i == NT - 1))
                ins.then_inc(eT.h, 1)
    GATHER_V = eV.n
    es_ga.close()

    if upto == "gather":
        FV = dY.bump(16)

        @on("gpsimd")
        def _(gp):
            gp.wait_ge(eV.h, GATHER_V)
            gp.dma_start(out=DBG, in_=xe[:]).then_inc(dY.h, 16)

        @on("sync")
        def _(sy):
            sy.wait_ge(dY.h, FV)
        return nc, finish

    h_fm = sb(es_ffn, "h_fm", [128, NHC * 512], BF16)
    w0b = [sb(es_ffn, f"w0b{i}", [128, DC * 512], BF16) for i in range(3)]
    w1b = [sb(es_ffn, f"w1b{i}", [128, DC * 512], BF16) for i in range(3)]
    w0tl = sb(es_ffn, "w0tl", [128, DC * 384], BF16)
    w1tl = sb(es_ffn, "w1tl", [128, DC * 384], BF16)
    wob = [sb(es_ffn, f"wob{i}", [128, 2 * D], BF16) for i in range(2)]
    wotl = sb(es_ffn, "wotl", [128, D], BF16)
    gtmp = [sb(es_ffn, f"gtmp{i}", [128, 512], F32) for i in range(2)]
    eob = sb(es_ffn, "eob", [128, 4 * D], BF16)
    es_px = ExitStackR()
    pall = [ps(es_px, f"pall{i}", [128, 512]) for i in range(8)]

    ZTAILS = eV.bump()

    @on("vector")
    def _(v):
        v.memset(w0tl[:], 0.0)
        v.memset(w1tl[:], 0.0)
        v.memset(wotl[:], 0.0).then_inc(eV.h, 1)

    w01_dma, wo_dma = {}, {}
    hmm_t, gel_s, hfm_v, wo_t = {}, {}, {}, {}
    eo_s, eo_d = {}, {}
    for e in range(E):
        for lb in range(NW01 + 1):
            if lb < NW01:
                k = e * NW01 + lb
                w01_dma[(e, lb)] = (dW0[k % 3].bump(16), dW1[k % 3].bump(16))
            else:
                w01_dma[(e, lb)] = (dWt[0].bump(16), dWt[1].bump(16))
        for pb in range(WO_PAIRS + 1):
            if pb < WO_PAIRS:
                k = e * WO_PAIRS + pb
                wo_dma[(e, pb)] = dWo[k % 2].bump(16)
            else:
                wo_dma[(e, pb)] = dWot.bump(16)
        for hc in range(NHC):
            hmm_t[(e, hc)] = eT.bump(2)
            gel_s[(e, hc)] = eS.bump()
            hfm_v[(e, hc)] = eV.bump()
        for hc in range(NHC):
            wo_t[(e, hc)] = eT.bump()
        eo_s[e] = eS.bump(8)
        eo_d[e] = dEo.bump(16)

    W0r = W0.rearrange("e (mc p) h -> e p mc h", p=128)
    W1r = W1.rearrange("e (mc p) h -> e p mc h", p=128)
    WOr = WOE[:, 0:5376, :].rearrange("e (hc p) m -> e p hc m", p=128)

    @on("gpsimd")
    def _(gp):
        for q in range(2):
            gp.wait_ge(dSpX[q].h, dSpX[q].n)
            gp.wait_ge(dSpL[q].h, dSpL[q].n)
        gp.wait_ge(eV.h, ZTAILS)
        for e in range(E):
            for lb in range(NW01 + 1):
                if lb < NW01:
                    pty = (e * NW01 + lb) % 3
                    kk = e * NW01 + lb - 3
                    if kk >= 0:
                        pe, plb = kk // NW01, kk % NW01
                        gp.wait_ge(eT.h, hmm_t[(pe, plb * 4 + 3)])
                    gp.dma_start(out=w0b[pty][:],
                                 in_=W0r[e, :, :, lb * 512:(lb + 1) * 512]
                                 ).then_inc(dW0[pty].h, 16)
                    gp.dma_start(out=w1b[pty][:],
                                 in_=W1r[e, :, :, lb * 512:(lb + 1) * 512]
                                 ).then_inc(dW1[pty].h, 16)
                else:
                    if e >= 1:
                        gp.wait_ge(eT.h, hmm_t[(e - 1, NHC - 1)])
                    tl0 = w0tl[:].rearrange("p (mc h) -> p mc h", mc=DC)[:, :, 0:344]
                    tl1 = w1tl[:].rearrange("p (mc h) -> p mc h", mc=DC)[:, :, 0:344]
                    gp.dma_start(out=tl0, in_=W0r[e, :, :, 5120:HR]).then_inc(dWt[0].h, 16)
                    gp.dma_start(out=tl1, in_=W1r[e, :, :, 5120:HR]).then_inc(dWt[1].h, 16)
            for pb in range(WO_PAIRS + 1):
                if pb < WO_PAIRS:
                    kk = e * WO_PAIRS + pb - 2
                    pty = (e * WO_PAIRS + pb) % 2
                    if kk >= 0:
                        pe, ppb = kk // WO_PAIRS, kk % WO_PAIRS
                        gp.wait_ge(eT.h, wo_t[(pe, ppb * 2 + 1)])
                    gp.dma_start(out=wob[pty][:],
                                 in_=WOr[e, :, 2 * pb:2 * pb + 2, :]).then_inc(dWo[pty].h, 16)
                else:
                    if e >= 1:
                        gp.wait_ge(eT.h, wo_t[(e - 1, NHC - 1)])
                    gp.dma_start(out=wotl[0:88, :],
                                 in_=WOE[e, 5376:HR, :]).then_inc(dWot.h, 16)

    @on("tensor")
    def _(t_):
        for e in range(E):
            for hc in range(NHC):
                if hc < 40:
                    lb = hc // 4
                    pty = (e * NW01 + lb) % 3
                    s0 = w0b[pty][:].rearrange("p (mc x) -> p mc x", mc=DC)
                    s1 = w1b[pty][:].rearrange("p (mc x) -> p mc x", mc=DC)
                    col = (hc % 4) * 128
                    v0, v1 = w01_dma[(e, lb)]
                    t_.wait_ge(dW0[pty].h, v0)
                    t_.wait_ge(dW1[pty].h, v1)
                else:
                    s0 = w0tl[:].rearrange("p (mc x) -> p mc x", mc=DC)
                    s1 = w1tl[:].rearrange("p (mc x) -> p mc x", mc=DC)
                    col = (hc - 40) * 128
                    v0, v1 = w01_dma[(e, NW01)]
                    t_.wait_ge(dWt[0].h, v0)
                    t_.wait_ge(dWt[1].h, v1)
                kg = e * NHC + hc
                if kg >= 2:
                    pe, phc = (kg - 2) // NHC, (kg - 2) % NHC
                    t_.wait_ge(eS.h, gel_s[(pe, phc)])
                    t_.wait_ge(eV.h, hfm_v[(pe, phc)])
                if hc < 2 and e >= 1:
                    t_.wait_ge(eS.h, eo_s[e - 1])
                if hc == 0 and e == 0:
                    for mc in range(DC):
                        t_.wait_ge(eV.h, ga_ev[(E - 1, mc)])
                for mc in range(DC):
                    ins = t_.matmul(pall[hc % 2][:], s0[:, mc, col:col + 128],
                                    xe[:, e * DC * 512 + mc * 512:(e * DC + mc + 1) * 512],
                                    start=(mc == 0), stop=(mc == DC - 1))
                ins.then_inc(eT.h, 1)
                for mc in range(DC):
                    ins = t_.matmul(pall[2 + hc % 2][:], s1[:, mc, col:col + 128],
                                    xe[:, e * DC * 512 + mc * 512:(e * DC + mc + 1) * 512],
                                    start=(mc == 0), stop=(mc == DC - 1))
                ins.then_inc(eT.h, 1)
            for hc in range(NHC):
                t_.wait_ge(eV.h, hfm_v[(e, hc)])
                if hc == 0:
                    t_.wait_ge(eS.h, gel_s[(e, NHC - 1)])
                    t_.wait_ge(eV.h, hfm_v[(e, NHC - 1)])
                    if e >= 1:
                        t_.wait_ge(eS.h, eo_s[e - 1])
                if hc < 42:
                    pb = hc // 2
                    pty = (e * WO_PAIRS + pb) % 2
                    t_.wait_ge(dWo[pty].h, wo_dma[(e, pb)])
                    wsrc = wob[pty]
                    woff = (hc % 2) * D
                else:
                    t_.wait_ge(dWot.h, wo_dma[(e, WO_PAIRS)])
                    wsrc = wotl
                    woff = 0
                for st in range(4):
                    for mh in range(2):
                        ins = t_.matmul(pall[st * 2 + mh][:],
                                        h_fm[:, hc * 512 + st * 128: hc * 512 + st * 128 + 128],
                                        wsrc[:, woff + mh * 512: woff + (mh + 1) * 512],
                                        start=(hc == 0), stop=(hc == NHC - 1))
                ins.then_inc(eT.h, 1)

    @on("scalar")
    def _(s):
        for e in range(E):
            for hc in range(NHC):
                kg = e * NHC + hc
                s.wait_ge(eT.h, hmm_t[(e, hc)] - 1)
                if kg >= 2:
                    pe, phc = (kg - 2) // NHC, (kg - 2) % NHC
                    s.wait_ge(eV.h, hfm_v[(pe, phc)])
                s.activation(gtmp[hc % 2][:], pall[hc % 2][:],
                             ACT.Gelu_apprx_tanh).then_inc(eS.h, 1)
            s.wait_ge(eT.h, wo_t[(e, NHC - 1)])
            if e >= 1:
                s.wait_ge(dEo.h, eo_d[e - 1])
            for st in range(4):
                for mh in range(2):
                    s.copy(eob[:, st * D + mh * 512: st * D + mh * 512 + 512],
                           pall[st * 2 + mh][:]).then_inc(eS.h, 1)

    @on("vector")
    def _(v):
        for e in range(E):
            for hc in range(NHC):
                v.wait_ge(eS.h, gel_s[(e, hc)])
                v.wait_ge(eT.h, hmm_t[(e, hc)])
                if e >= 1:
                    v.wait_ge(eT.h, wo_t[(e - 1, hc)])
                v.tensor_tensor(h_fm[:, hc * 512:(hc + 1) * 512], gtmp[hc % 2][:],
                                pall[2 + hc % 2][:], ALU.mult).then_inc(eV.h, 1)

    @on("sync")
    def _(sy):
        for e in range(E):
            sy.wait_ge(eS.h, eo_s[e])
            sy.dma_start(out=EOSt[e],
                         in_=eob[:].rearrange("p (k m) -> p k m", k=4)).then_inc(dEo.h, 16)

    EXP_DONE_S = eS.n
    es_px.close()
    es_ffn.close()

    es_sc = ExitStackR()
    oacc = sb(es_sc, "oacc", [128, NT * D], F32)
    eor = [sb(es_sc, f"eor{i}", [128, 4 * D], BF16) for i in range(2)]
    ctm = [sb(es_sc, f"ctm{i}", [128, 512], BF16) for i in range(4)]
    ctmp2 = sb(es_sc, "ctmp2", [128, 512], BF16)
    cT = [sb(es_sc, f"cT{i}", [128, 512], BF16) for i in range(4)]
    x1r = [sb(es_sc, f"x1r{i}", [128, D], F32) for i in range(2)]
    ptc = [ps(es_sc, f"ptc{i}", [128, 1024], BF16) for i in range(4)]
    psc2 = [ps(es_sc, f"psc2{i}", [128, 512]) for i in range(4)]

    eo_rel, x1_rel = {}, {}
    sc_bv, sc_tp, sc_te, sc_mm, sc_av = {}, {}, {}, {}, {}
    for e in range(E):
        eo_rel[e] = dEr[e % 2].bump(16)
        for i in range(NT):
            if e == 0:
                x1_rel[i] = dXr[i % 2].bump(16)
            sc_bv[(e, i)] = eV.bump()
            sc_tp[(e, i)] = eT.bump()
            sc_te[(e, i)] = eV.bump()
            for mh in range(2):
                sc_mm[(e, i, mh)] = eT.bump()
            sc_av[(e, i)] = eV.bump()

    @on("sync")
    def _(sy):
        sy.wait_ge(eS.h, EXP_DONE_S)
        for e in range(E):
            sy.wait_ge(dEo.h, eo_d[e])
            if e >= 2:
                sy.wait_ge(eT.h, sc_mm[(e - 2, NT - 1, 1)])
            sy.dma_start(out=eor[e % 2][:].rearrange("p (k m) -> p k m", k=4),
                         in_=EOSt[e]).then_inc(dEr[e % 2].h, 16)
            if e == 0:
                for i in range(NT):
                    if i >= 2:
                        sy.wait_ge(eV.h, sc_av[(0, i - 2)])
                    sy.dma_start(out=x1r[i % 2][:], in_=X1t[i]).then_inc(dXr[i % 2].h, 16)

    @on("vector")
    def _(v):
        v.wait_ge(eS.h, EXP_DONE_S)
        v.wait_ge(eV.h, GATE_V)
        for e in range(E):
            for i in range(NT):
                k = e * NT + i
                if k >= 4:
                    v.wait_ge(eT.h, sc_tp[((k - 4) // NT, (k - 4) % NT)])
                if k >= 1:
                    v.wait_ge(eV.h, sc_bv[((k - 1) // NT, (k - 1) % NT)])
                v.tensor_scalar(ctm[k % 4][:], iota_sb[:],
                                q1a[:, i * E + e:i * E + e + 1],
                                g1kc[:, i:i + 1], ALU.is_equal, ALU.mult)
                v.tensor_scalar(ctmp2[:], iota_sb[:],
                                q2a[:, i * E + e:i * E + e + 1],
                                g2kc[:, i:i + 1], ALU.is_equal, ALU.mult)
                v.drain()
                v.tensor_tensor(ctm[k % 4][:], ctm[k % 4][:], ctmp2[:],
                                ALU.add).then_inc(eV.h, 1)
                v.wait_ge(eT.h, sc_tp[(e, i)])
                if k >= 4:
                    v.wait_ge(eT.h, sc_mm[((k - 4) // NT, (k - 4) % NT, 1)])
                v.tensor_copy(cT[k % 4][:], ptc[k % 4][:, 0:512]).then_inc(eV.h, 1)
                if e >= 1:
                    v.wait_ge(eV.h, sc_av[(e - 1, i)])
                for mh in range(2):
                    v.wait_ge(eT.h, sc_mm[(e, i, mh)])
                    dst = oacc[:, i * D + mh * 512: i * D + mh * 512 + 512]
                    if e == 0:
                        v.wait_ge(dXr[i % 2].h, x1_rel[i])
                        ins = v.tensor_tensor(dst, x1r[i % 2][:, mh * 512:(mh + 1) * 512],
                                              psc2[mh * 2 + k % 2][:], ALU.add)
                    else:
                        ins = v.tensor_tensor(dst, dst, psc2[mh * 2 + k % 2][:], ALU.add)
                ins.then_inc(eV.h, 1)

    @on("tensor")
    def _(t_):
        for e in range(E):
            for i in range(NT):
                k = e * NT + i
                t_.wait_ge(eV.h, sc_bv[(e, i)])
                if k >= 4:
                    t_.wait_ge(eV.h, sc_te[((k - 4) // NT, (k - 4) % NT)])
                for kk4 in range(4):
                    ins = t_.transpose(ptc[k % 4][:, kk4 * 128:kk4 * 128 + 128],
                                       ctm[k % 4][:, kk4 * 128:kk4 * 128 + 128],
                                       idb_sb[:])
                ins.then_inc(eT.h, 1)
                t_.wait_ge(eV.h, sc_te[(e, i)])
                t_.wait_ge(dEr[e % 2].h, eo_rel[e])
                if k >= 2:
                    t_.wait_ge(eV.h, sc_av[((k - 2) // NT, (k - 2) % NT)])
                for mh in range(2):
                    for kk4 in range(4):
                        ins = t_.matmul(psc2[mh * 2 + k % 2][:],
                                        cT[k % 4][:, kk4 * 128:kk4 * 128 + 128],
                                        eor[e % 2][:, kk4 * D + mh * 512: kk4 * D + mh * 512 + 512],
                                        start=(kk4 == 0), stop=(kk4 == 3))
                    ins.then_inc(eT.h, 1)

    FIN_V = eV.n
    FIN = dY.bump(16)

    @on("sync")
    def _(sy):
        sy.wait_ge(eV.h, FIN_V)
        sy.dma_start(out=Y.rearrange("(i p) m -> p i m", p=128),
                     in_=oacc[:].rearrange("p (i m) -> p i m", i=NT)).then_inc(dY.h, 16)
        sy.wait_ge(dY.h, FIN)

    es_sc.close()
    return nc, finish


def _consts():
    c = {}
    c["iota"] = np.tile(np.arange(512, dtype=np.float32), (128, 1))
    c["idb"] = np.eye(128).astype(ml_dtypes.bfloat16)
    c["idf"] = np.eye(128, dtype=np.float32)
    c["tri"] = np.triu(np.ones((128, 128), dtype=np.float32), k=1)
    c["onesf"] = np.ones((128, 128), dtype=np.float32)
    q = np.arange(128)
    s_ = np.arange(128)
    c["maskt"] = np.where(q[None, :] >= s_[:, None], 0.0, -1e30).astype(np.float32)
    c["onecol"] = np.ones((128, 1), dtype=np.float32)
    c["onecolb"] = np.ones((128, 1)).astype(ml_dtypes.bfloat16)
    c["onerow"] = np.ones((1, 128), dtype=np.float32)
    e01 = np.zeros((1, 128), dtype=np.float32)
    e01[0, 0:64] = 1.0
    e10 = np.zeros((1, 128), dtype=np.float32)
    e10[0, 64:128] = 1.0
    c["e01"] = e01.astype(ml_dtypes.bfloat16)
    c["e10"] = e10.astype(ml_dtypes.bfloat16)
    return c


def kernel(L0_attn_X, L0_attn_q_proj, L0_attn_k_proj, L0_attn_v_proj,
           L0_attn_o_proj, L0_ffn_router_w, L0_ffn_w0_kernel,
           L0_ffn_w1_kernel, L0_ffn_wo_kernel, _upto="full", _nh=NH):
    X = np.ascontiguousarray(np.asarray(L0_attn_X, dtype=np.float32))
    shared = {
        "wq": np.ascontiguousarray(np.asarray(L0_attn_q_proj, dtype=np.float32)),
        "wk": np.ascontiguousarray(np.asarray(L0_attn_k_proj, dtype=np.float32)),
        "wv": np.ascontiguousarray(np.asarray(L0_attn_v_proj, dtype=np.float32)),
        "wo": np.ascontiguousarray(np.asarray(L0_attn_o_proj, dtype=np.float32)),
        "rw": np.ascontiguousarray(np.asarray(L0_ffn_router_w, dtype=np.float32)),
        "w0": np.ascontiguousarray(np.asarray(L0_ffn_w0_kernel, dtype=np.float32)),
        "w1": np.ascontiguousarray(np.asarray(L0_ffn_w1_kernel, dtype=np.float32)),
        "woe": np.ascontiguousarray(np.asarray(L0_ffn_wo_kernel, dtype=np.float32)),
    }
    shared.update(_consts())
    nc, fin = build_nc(upto=_upto, nhrun=_nh)
    fin()
    in_maps = [dict(shared, x=np.ascontiguousarray(X[c])) for c in range(8)]
    res = run_bass_kernel_spmd(nc, in_maps, core_ids=list(range(8)))
    out = np.stack([np.asarray(res.results[c]["y"]) for c in range(8)], axis=0)
    if _upto != "full":
        return out, [np.asarray(res.results[c]["dbg"]) for c in range(8)]
    return out


# version 04
# speedup vs baseline: 1.0810x; 1.0589x over previous
import numpy as np
import ml_dtypes
from contextlib import ExitStack
import concourse.bass as bass
import concourse.mybir as mybir
from concourse.bass_utils import run_bass_kernel_spmd

F32 = mybir.dt.float32
BF16 = mybir.dt.bfloat16
ALU = mybir.AluOpType
ACT = mybir.ActivationFunctionType
AX = mybir.AxisListType.X

S = 2048
D = 1024
NT = 16
DC = 8
NH = 16
E = 8
CAP = 512
HR = 5464
HP = 5504
NHC = 43
NW01 = 10
WO_PAIRS = 21


def build_nc(upto="full", nhrun=NH):
    nc = bass.Bass("TRN2", target_bir_lowering=False)

    def din(name, shape, dt=F32):
        return nc.dram_tensor(name, shape, dt, kind="ExternalInput").ap()

    X = din("x", [S, D])
    WQ = din("wq", [D, D])
    WK = din("wk", [D, D])
    WV = din("wv", [D, D])
    WO = din("wo", [D, D])
    RW = din("rw", [D, E])
    W0 = din("w0", [E, D, HR])
    W1 = din("w1", [E, D, HR])
    WOE = din("woe", [E, HR, D])
    IOTA = din("iota", [128, 512])
    IDB = din("idb", [128, 128], BF16)
    IDF = din("idf", [128, 128])
    TRI = din("tri", [128, 128])
    ONESF = din("onesf", [128, 128])
    MASKT = din("maskt", [128, 128])
    ONECOL = din("onecol", [128, 1])
    ONECOLB = din("onecolb", [128, 1], BF16)
    ONEROW = din("onerow", [1, 128])
    E01 = din("e01", [1, 128], BF16)
    E10 = din("e10", [1, 128], BF16)

    Y = nc.dram_tensor("y", [S, D], F32, kind="ExternalOutput").ap()
    NDBG = {"ln1": DC * S, "attn": (NH // 2) * S, "gate": 10 * NT * E,
            "gather": E * DC * 512, "vprime": NT * NH * 65}.get(upto, 1)
    DBG = nc.dram_tensor("dbg", [128, NDBG], F32, kind="ExternalOutput").ap()
    X1S = nc.dram_tensor("x1s", [S, D], F32, kind="Internal").ap()
    XL2S = nc.dram_tensor("xl2s", [S, D], BF16, kind="Internal").ap()
    AVS = nc.dram_tensor("avs", [NH // 2, 128, S], BF16, kind="Internal").ap()
    EOS = nc.dram_tensor("eos", [E, CAP, D], BF16, kind="Internal").ap()

    Xt = X.rearrange("(i p) m -> i p m", p=128)
    X1t = X1S.rearrange("(i p) m -> i p m", p=128)
    XL2t = XL2S.rearrange("(i p) m -> i p m", p=128)
    EOSt = EOS.rearrange("e (k p) m -> e p k m", p=128)

    stack = ExitStack()
    scopes = []
    _ES = ExitStack

    def ExitStackR():
        es = _ES()
        scopes.append(es)
        return es

    def finish():
        for es in reversed(scopes):
            es.close()
        stack.close()

    def sb(es, name, shape, dt):
        return es.enter_context(nc.sbuf_tensor(name, shape, dt))

    def ps(es, name, shape, dt=F32):
        return es.enter_context(nc.psum_tensor(name, shape, dt))

    class Ctr:
        def __init__(self, name):
            self.h = stack.enter_context(nc.semaphore(name))
            self.n = 0

        def bump(self, k=1):
            self.n += k
            return self.n

    dC = Ctr("dC")
    dW = Ctr("dW")
    dEo = Ctr("dEo")
    dY = Ctr("dY")
    eT = Ctr("eT")
    eV = Ctr("eV")
    eS = Ctr("eS")
    eG = Ctr("eG")
    dXi = [Ctr("dXi0"), Ctr("dXi1")]
    dX2 = [Ctr("dX20"), Ctr("dX21")]
    dAv = [Ctr("dAv0"), Ctr("dAv1")]
    dSpX = [Ctr("dSpX0"), Ctr("dSpX1")]
    dSpL = [Ctr("dSpL0"), Ctr("dSpL1")]
    dBr = Ctr("dBr")
    dRc = Ctr("dRc")
    dAt = Ctr("dAt")
    dR0 = Ctr("dR0")
    dR1 = Ctr("dR1")
    dAl = [Ctr("dAl0"), Ctr("dAl1")]
    dAw = [Ctr("dAw0"), Ctr("dAw1")]
    dW0 = [Ctr("dW00"), Ctr("dW01"), Ctr("dW02")]
    dW1 = [Ctr("dW10"), Ctr("dW11"), Ctr("dW12")]
    dWt = [Ctr("dWt0"), Ctr("dWt1")]
    dWo = [Ctr("dWo0"), Ctr("dWo1")]
    dWot = Ctr("dWot")
    dEr = [Ctr("dEr0"), Ctr("dEr1")]
    dGr = [Ctr("dGr0"), Ctr("dGr1")]
    dXr = [Ctr("dXr0"), Ctr("dXr1")]

    gl = stack
    iota_sb = sb(gl, "iota_sb", [128, 512], F32)
    idb_sb = sb(gl, "idb_sb", [128, 128], BF16)
    idf_sb = sb(gl, "idf_sb", [128, 128], F32)
    tri_sb = sb(gl, "tri_sb", [128, 128], F32)
    onesf_sb = sb(gl, "onesf_sb", [128, 128], F32)
    maskt_sb = sb(gl, "maskt_sb", [128, 128], F32)
    onecol_sb = sb(gl, "onecol_sb", [128, 1], F32)
    onecolb_sb = sb(gl, "onecolb_sb", [128, 1], BF16)
    onerow_sb = sb(gl, "onerow_sb", [1, 128], F32)
    e01_sb = sb(gl, "e01_sb", [1, 128], BF16)
    e10_sb = sb(gl, "e10_sb", [1, 128], BF16)
    rw_sb = sb(gl, "rw_sb", [128, DC * E], F32)
    logits = sb(gl, "logits", [128, NT * E], F32)
    m1a = sb(gl, "m1a", [128, NT * E], F32)
    m2a = sb(gl, "m2a", [128, NT * E], F32)
    m1k = sb(gl, "m1k", [128, NT * E], F32)
    m2k = sb(gl, "m2k", [128, NT * E], F32)
    q1a = sb(gl, "q1a", [128, NT * E], F32)
    q2a = sb(gl, "q2a", [128, NT * E], F32)
    p1x = sb(gl, "p1x", [128, NT * E], F32)
    p2x = sb(gl, "p2x", [128, NT * E], F32)
    g1kc = sb(gl, "g1kc", [128, NT], F32)
    g2kc = sb(gl, "g2kc", [128, NT], F32)
    county = sb(gl, "county", [1, E], F32)
    gsc = sb(gl, "gsc", [128, 48], F32)

    block = stack.enter_context(nc.Block())

    def on(eng):
        return getattr(block, eng)

    @on("sync")
    def _(sy):
        for dst, src in [(iota_sb, IOTA), (idb_sb, IDB), (idf_sb, IDF),
                         (tri_sb, TRI), (onesf_sb, ONESF), (maskt_sb, MASKT),
                         (onecol_sb, ONECOL), (onecolb_sb, ONECOLB),
                         (onerow_sb, ONEROW), (e01_sb, E01), (e10_sb, E10)]:
            sy.dma_start(out=dst[:], in_=src).then_inc(dC.h, 16)
        sy.dma_start(out=rw_sb[:],
                     in_=RW.rearrange("(dc p) e -> p dc e", p=128)).then_inc(dC.h, 16)
    CONSTS = dC.bump(16 * 12)

    es_at = ExitStackR()
    es_qk = ExitStackR()
    es_ws = ExitStackR()
    es_wv = ExitStackR()
    es_ln = ExitStackR()
    avtmp = sb(es_at, "avtmp", [64, S], BF16)
    recrow = sb(es_at, "recrow", [65, S], BF16)
    recall = sb(es_at, "recall", [16, S], BF16)
    wqT = sb(es_qk, "wqT", [128, DC * D], BF16)
    wkT = sb(es_qk, "wkT", [128, DC * D], BF16)
    xfm = sb(es_qk, "xfm", [128, DC * S], BF16)
    vall = sb(es_qk, "vall", [128, NT * NH * 65], BF16)
    qbuf = [sb(es_qk, f"qb{i}", [65, S], BF16) for i in range(2)]
    kbuf = [sb(es_qk, f"kb{i}", [65, S], BF16) for i in range(2)]
    ptb = [sb(es_qk, f"ptb{i}", [128, S], BF16) for i in range(2)]
    sqq = sb(es_qk, "sqq", [64, S], BF16)
    nrow = sb(es_qk, "nrow", [1, S + 16], F32)
    brow = sb(es_qk, "brow", [1, S], BF16)
    wstage = sb(es_ws, "wstage", [128, D], BF16)
    wvT = sb(es_wv, "wvT", [128, DC * D], BF16)
    xin = [sb(es_ln, f"xin{i}", [128, D], F32) for i in range(2)]
    xlt = sb(es_ln, "xlt", [128, D], BF16)
    sqd = sb(es_ln, "sqd", [128, D], BF16)
    mcol = sb(es_ln, "mcol", [128, 16], F32)

    es_ptr = ExitStackR()
    ptr = ps(es_ptr, "ptr", [128, 1024], BF16)

    def load_wT(Wsrc, dstT, stage_free_t, ptr_free_v, ptrps, stg):
        Wr = Wsrc.rearrange("(oc p) d -> oc p d", p=128)
        ndma = [dW.bump(16) for _ in range(DC)]
        t_vals = [eT.bump() for _ in range(DC)]
        v_vals = [eV.bump() for _ in range(DC)]

        @on("gpsimd")
        def _(gp):
            for oc in range(DC):
                if oc == 0 and stage_free_t:
                    gp.wait_ge(eT.h, stage_free_t)
                if oc >= 1:
                    gp.wait_ge(eT.h, t_vals[oc - 1])
                gp.dma_start(out=stg[:], in_=Wr[oc]).then_inc(dW.h, 16)

        @on("tensor")
        def _(t):
            t.wait_ge(dC.h, CONSTS)
            for oc in range(DC):
                t.wait_ge(dW.h, ndma[oc])
                if oc == 0 and ptr_free_v:
                    t.wait_ge(eV.h, ptr_free_v)
                if oc >= 1:
                    t.wait_ge(eV.h, v_vals[oc - 1])
                for dc in range(DC):
                    ins = t.transpose(ptrps[:, dc * 128:dc * 128 + 128],
                                      stg[:, dc * 128:dc * 128 + 128],
                                      idb_sb[:])
                ins.then_inc(eT.h, 1)

        @on("vector")
        def _(v):
            for oc in range(DC):
                v.wait_ge(eT.h, t_vals[oc])
                dst = dstT[:].rearrange("p (dc x) -> p dc x", dc=DC)[:, :, oc * 128:oc * 128 + 128]
                src = ptrps[:].rearrange("p (dc x) -> p dc x", dc=DC)
                v.tensor_copy(dst, src).then_inc(eV.h, 1)
        return t_vals[-1], v_vals[-1]

    wv_t, wv_v = load_wT(WV, wvT, 0, 0, ptr, wstage)

    ln_nx = [dXi[i % 2].bump(16) for i in range(NT)]
    ln_mean, ln_var, ln_norm, ln_ev = [], [], [], []
    ln_sq, ln_rstd, ln_tp = [], [], []
    for i in range(NT):
        ln_mean.append(eV.bump())
        ln_var.append(eV.bump())
        ln_norm.append(eV.bump())
        ln_ev.append(eV.bump())
        ln_sq.append(eS.bump())
        ln_rstd.append(eS.bump())
        ln_tp.append(eT.bump())

    @on("sync")
    def _(sy):
        for i in range(NT):
            if i >= 2:
                sy.wait_ge(eS.h, ln_sq[i - 2])
                sy.wait_ge(eV.h, ln_norm[i - 2])
            sy.dma_start(out=xin[i % 2][:], in_=Xt[i]).then_inc(dXi[i % 2].h, 16)

    @on("vector")
    def _(v):
        for i in range(NT):
            b = (i % 2) * 8
            v.wait_ge(dXi[i % 2].h, ln_nx[i])
            v.reduce_sum(mcol[:, b:b + 1], xin[i % 2][:], axis=AX)
            v.drain()
            v.tensor_scalar(mcol[:, b:b + 1], mcol[:, b:b + 1], 1.0 / D, 0.0,
                            ALU.mult, ALU.add).then_inc(eV.h, 1)
            v.wait_ge(eS.h, ln_sq[i])
            v.wait_ge(eV.h, ln_mean[i])
            v.tensor_scalar(mcol[:, b + 2:b + 3], mcol[:, b + 1:b + 2], 1.0 / D, 0.0,
                            ALU.mult, ALU.add)
            v.tensor_scalar(mcol[:, b + 3:b + 4], mcol[:, b:b + 1], mcol[:, b:b + 1],
                            None, ALU.mult)
            v.drain()
            v.tensor_scalar(mcol[:, b + 2:b + 3], mcol[:, b + 2:b + 3],
                            mcol[:, b + 3:b + 4], None, ALU.subtract)
            v.drain()
            v.tensor_scalar(mcol[:, b + 2:b + 3], mcol[:, b + 2:b + 3], 1e-5, 1.0,
                            ALU.add, ALU.mult).then_inc(eV.h, 1)
            v.wait_ge(eS.h, ln_rstd[i])
            if i >= 1:
                v.wait_ge(eT.h, ln_tp[i - 1])
            v.tensor_scalar(xlt[:], xin[i % 2][:], mcol[:, b:b + 1], mcol[:, b + 4:b + 5],
                            ALU.subtract, ALU.mult).then_inc(eV.h, 1)
            v.wait_ge(eT.h, ln_tp[i])
            dst = xfm[:].rearrange("p (dc x) -> p dc x", dc=DC)[:, :, i * 128:i * 128 + 128]
            src = ptr[:].rearrange("p (dc x) -> p dc x", dc=DC)
            v.tensor_copy(dst, src).then_inc(eV.h, 1)

    @on("scalar")
    def _(s):
        for i in range(NT):
            b = (i % 2) * 8
            s.wait_ge(dXi[i % 2].h, ln_nx[i])
            s.activation(sqd[:], xin[i % 2][:], ACT.Square,
                         accum_out=mcol[:, b + 1:b + 2]).then_inc(eS.h, 1)
            s.wait_ge(eV.h, ln_var[i])
            s.activation(mcol[:, b + 3:b + 4], mcol[:, b + 2:b + 3], ACT.Ln)
            s.drain()
            s.activation(mcol[:, b + 4:b + 5], mcol[:, b + 3:b + 4], ACT.Exp,
                         scale=-0.5).then_inc(eS.h, 1)

    @on("tensor")
    def _(t):
        for i in range(NT):
            t.wait_ge(eV.h, ln_norm[i])
            t.wait_ge(eV.h, wv_v if i == 0 else ln_ev[i - 1])
            for dc in range(DC):
                ins = t.transpose(ptr[:, dc * 128:dc * 128 + 128],
                                  xlt[:, dc * 128:dc * 128 + 128], idb_sb[:])
            ins.then_inc(eT.h, 1)

    XFM_V = ln_ev[-1]

    if upto == "ln1":
        FV = dY.bump(16)

        @on("gpsimd")
        def _(gp):
            gp.wait_ge(eV.h, XFM_V)
            gp.dma_start(out=DBG, in_=xfm[:]).then_inc(dY.h, 16)

        @on("sync")
        def _(sy):
            sy.wait_ge(dY.h, FV)
        return nc, finish

    es_vp = ExitStackR()
    pvp = [ps(es_vp, f"pvp{i}", [128, 512]) for i in range(2)]
    VONES = eV.bump()
    vp_t, vp_v = [], []
    for _ in range(NT * 2):
        vp_t.append(eT.bump())
        vp_v.append(eV.bump())

    @on("vector")
    def _(v):
        v.memset(vall[:], 1.0).then_inc(eV.h, 1)

    @on("tensor")
    def _(t):
        t.wait_ge(eV.h, max(wv_v, XFM_V))
        k = 0
        for i in range(NT):
            for h2 in range(2):
                if k >= 2:
                    t.wait_ge(eV.h, vp_v[k - 2])
                for dc in range(DC):
                    ins = t.matmul(pvp[k % 2][:],
                                   xfm[:, dc * S + i * 128: dc * S + i * 128 + 128],
                                   wvT[:, dc * D + h2 * 512: dc * D + h2 * 512 + 512],
                                   start=(dc == 0), stop=(dc == DC - 1))
                ins.then_inc(eT.h, 1)
                k += 1

    @on("vector")
    def _(v):
        v.wait_ge(eV.h, VONES)
        k = 0
        for i in range(NT):
            for h2 in range(2):
                v.wait_ge(eT.h, vp_t[k])
                dst = vall[:, i * NH * 65: (i + 1) * NH * 65]
                dst = dst.rearrange("p (h c) -> p h c", h=NH)[:, h2 * 8:(h2 + 1) * 8, 0:64]
                src = pvp[k % 2][:].rearrange("p (h c) -> p h c", c=64)
                v.tensor_copy(dst, src).then_inc(eV.h, 1)
                k += 1
    es_vp.close()
    es_wv_close_marker = vp_v[-1]

    if upto == "vprime":
        FV = dY.bump(16)

        @on("gpsimd")
        def _(gp):
            gp.wait_ge(eV.h, vp_v[-1])
            gp.dma_start(out=DBG, in_=vall[:]).then_inc(dY.h, 16)

        @on("sync")
        def _(sy):
            sy.wait_ge(dY.h, FV)
        return nc, finish

    wq_t, wq_v = load_wT(WQ, wqT, wv_t, ln_ev[-1], ptr, wstage)
    wk_t, wk_v = load_wT(WK, wkT, wq_t, wq_v, ptr, wstage)
    es_ln.close()
    es_wv.close()
    es_ws.close()
    es_ptr.close()

    es_hd = ExitStackR()
    pqk = [ps(es_hd, f"pqk{i}", [64, 512]) for i in range(2)]
    psc = [ps(es_hd, f"psc{i}", [128, 512]) for i in range(2)]
    pav = ps(es_hd, "pav", [65, S])

    KONES = eG.bump()

    @on("gpsimd")
    def _(gp):
        gp.memset(kbuf[0][64:65, :], 1.0)
        gp.memset(kbuf[1][64:65, :], 1.0).then_inc(eG.h, 1)

    qk_free = [0, 0]
    pav_free_s = 0
    pav_free_v2 = 0
    avtmp_free = 0
    psc_free = [0, 0]
    pqk_free = [("S", 0), ("S", 0)]
    ptb_free = [0, 0]
    rec_free = 0
    hst = {}

    def emit_qk_bound(h):
        nonlocal_state = None
        hp = h % 2
        st = {}
        st["qkt"] = qkt = [eT.bump() for _ in range(8)]
        st["qev"] = qev = [eV.bump() for _ in range(4)]
        st["kev"] = kev = [eV.bump() for _ in range(4)]

        @on("tensor")
        def _(t_, h=h, hp=hp, qw=qk_free[hp], pqw=list(pqk_free),
              qev=qev, kev=kev):
            if h == 0:
                t_.wait_ge(eV.h, wk_v)
            if qw:
                t_.wait_ge(eT.h, qw)
            k = 0
            for wT in (wqT, wkT):
                for c in range(4):
                    if k >= 2:
                        t_.wait_ge(eV.h, (qev + kev)[k - 2])
                    else:
                        kind, val = pqw[k % 2]
                        if val:
                            t_.wait_ge((eS if kind == "S" else eV).h, val)
                    for dc in range(DC):
                        ins = t_.matmul(pqk[k % 2][:],
                                        wT[:, dc * D + h * 64: dc * D + h * 64 + 64],
                                        xfm[:, dc * S + c * 512: dc * S + c * 512 + 512],
                                        start=(dc == 0), stop=(dc == DC - 1))
                    ins.then_inc(eT.h, 1)
                    k += 1

        @on("vector")
        def _(v, hp=hp, qkt=qkt):
            for c in range(4):
                v.wait_ge(eT.h, qkt[c])
                v.tensor_scalar(qbuf[hp][0:64, c * 512:(c + 1) * 512], pqk[c % 2][:],
                                0.125, 0.0, ALU.mult, ALU.add).then_inc(eV.h, 1)
            for c in range(4):
                v.wait_ge(eT.h, qkt[4 + c])
                v.tensor_copy(kbuf[hp][0:64, c * 512:(c + 1) * 512],
                              pqk[c % 2][:]).then_inc(eV.h, 1)

        st["sq_k"] = sq_k = eV.bump()
        st["nmk_t"] = nmk_t = [eT.bump() for _ in range(4)]
        st["nmk_s"] = nmk_s = [eS.bump() for _ in range(4)]
        st["mk_done"] = mk_done = eV.bump()
        st["sq_q"] = sq_q = eV.bump()
        st["nmq_t"] = nmq_t = [eT.bump() for _ in range(4)]
        st["nmq_s"] = nmq_s = [eS.bump() for _ in range(4)]
        st["bexp_done"] = bexp_done = eS.bump()
        st["brow_dma"] = dBr.bump(16)
        prev_nmq_t = hst[h - 1]["nmq_t"][-1] if h >= 1 else 0
        prev_brow = hst[h - 1]["brow_dma"] if h >= 1 else 0

        @on("vector")
        def _(v, hp=hp, kev=kev, nmk_t=nmk_t, nmk_s=nmk_s, prev_nmq_t=prev_nmq_t):
            v.wait_ge(eV.h, kev[-1])
            if prev_nmq_t:
                v.wait_ge(eT.h, prev_nmq_t)
            v.tensor_tensor(sqq[:], kbuf[hp][0:64, :], kbuf[hp][0:64, :],
                            ALU.mult).then_inc(eV.h, 1)
            v.wait_ge(eS.h, nmk_s[-1])
            v.reduce_max(nrow[0:1, S:S + 1], nrow[0:1, 0:S],
                         axis=AX).then_inc(eV.h, 1)
            v.wait_ge(eT.h, nmk_t[-1])
            v.tensor_tensor(sqq[:], qbuf[hp][0:64, :], qbuf[hp][0:64, :],
                            ALU.mult).then_inc(eV.h, 1)

        prev_bexp = hst[h - 1]["bexp_done"] if h >= 1 else 0

        @on("scalar")
        def _(s, nmk_t=nmk_t, nmq_t=nmq_t, mk_done=mk_done, nmq_s=nmq_s,
              prev_brow=prev_brow, prev_bexp=prev_bexp):
            if prev_bexp:
                s.wait_ge(eS.h, prev_bexp)
            for c in range(4):
                s.wait_ge(eT.h, nmk_t[c])
                s.copy(nrow[0:1, c * 512:(c + 1) * 512],
                       pqk[c % 2][0:1, :]).then_inc(eS.h, 1)
            if prev_brow:
                s.wait_ge(dBr.h, prev_brow)
            for c in range(4):
                s.wait_ge(eT.h, nmq_t[c])
                s.copy(brow[0:1, c * 512:(c + 1) * 512],
                       pqk[c % 2][0:1, :]).then_inc(eS.h, 1)
            s.wait_ge(eV.h, mk_done)
            s.wait_ge(eS.h, nmq_s[-1])
            s.activation(nrow[0:1, S + 1:S + 2], nrow[0:1, S:S + 1], ACT.Ln)
            s.drain()
            s.activation(nrow[0:1, S + 2:S + 3], nrow[0:1, S + 1:S + 2],
                         ACT.Identity, scale=0.5)
            s.drain()
            s.activation(nrow[0:1, 0:S], brow[0:1, 0:S], ACT.Ln)
            s.drain()
            s.activation(nrow[0:1, 0:S], nrow[0:1, 0:S], ACT.Exp, scale=0.5,
                         bias=nrow[0:1, S + 2:S + 3])
            s.drain()
            s.mul(brow[:], nrow[0:1, 0:S], -1.05).then_inc(eS.h, 1)

        @on("tensor")
        def _(t_, sq_k=sq_k, sq_q=sq_q, nmk_s=nmk_s, nmq_s=nmq_s):
            t_.wait_ge(eV.h, sq_k)
            for c in range(4):
                if c >= 2:
                    t_.wait_ge(eS.h, nmk_s[c - 2])
                t_.matmul(pqk[c % 2][0:1, :], onecolb_sb[0:64, :],
                          sqq[:, c * 512:(c + 1) * 512],
                          start=True, stop=True).then_inc(eT.h, 1)
            t_.wait_ge(eV.h, sq_q)
            for c in range(4):
                t_.wait_ge(eS.h, (nmk_s + nmq_s)[c + 2])
                t_.matmul(pqk[c % 2][0:1, :], onecolb_sb[0:64, :],
                          sqq[:, c * 512:(c + 1) * 512],
                          start=True, stop=True).then_inc(eT.h, 1)

        @on("sync")
        def _(sy, hp=hp, bexp_done=bexp_done):
            sy.wait_ge(eS.h, bexp_done)
            sy.dma_start(out=qbuf[hp][64:65, :], in_=brow[:]).then_inc(dBr.h, 16)

        hst[h] = st
        return [("S", nmq_s[2]), ("S", nmq_s[3])]

    def emit_pass2_sigma(h):
        hp = h % 2
        pr = h // 2
        st = hst[h]
        chunks = []
        for j in range(NT):
            for c in range(j // 4, 4):
                chunks.append((j, c))
        par = {jc: k % 2 for k, jc in enumerate(chunks)}
        sc_t, av_t, mask_v, exp_s = {}, {}, {}, {}
        for k, (j, c) in enumerate(chunks):
            sc_t[(j, c)] = eT.bump()
            if c == j // 4:
                mask_v[(j, c)] = eV.bump()
            av_t[(j, c)] = eT.bump()
        for (j, c) in chunks:
            exp_s[(j, c)] = eS.bump()

        @on("tensor")
        def _(t_, h=h, hp=hp, brow_dma=st["brow_dma"], chunks=chunks,
              exp_s=exp_s, psc_w=list(psc_free), pav_s=pav_free_s,
              pav_v2=pav_free_v2, av_free=avtmp_free, rec_f=rec_free):
            t_.wait_ge(dBr.h, brow_dma)
            if h == 0:
                t_.wait_ge(eG.h, KONES)
                t_.wait_ge(eV.h, vp_v[-1])
            if pav_s:
                t_.wait_ge(eS.h, pav_s)
            if pav_v2:
                t_.wait_ge(eV.h, pav_v2)
            if av_free:
                t_.wait_ge(dAt.h, av_free)
            if rec_f:
                t_.wait_ge(dRc.h, rec_f)
            for k, (j, c) in enumerate(chunks):
                off = j * 128 - c * 512 if c == j // 4 else 0
                w = 512 - off
                q0 = c * 512 + off
                if k >= 2:
                    t_.wait_ge(eS.h, exp_s[chunks[k - 2]])
                elif psc_w[k % 2]:
                    t_.wait_ge(eS.h, psc_w[k % 2])
                t_.matmul(psc[k % 2][:, 0:w], kbuf[hp][:, j * 128:(j + 1) * 128],
                          qbuf[hp][:, q0:q0 + w],
                          start=True, stop=True).then_inc(eT.h, 1)
                t_.wait_ge(eS.h, exp_s[(j, c)])
                t_.matmul(pav[0:65, q0:q0 + w],
                          vall[:, j * NH * 65 + h * 65: j * NH * 65 + (h + 1) * 65],
                          ptb[j % 2][:, q0:q0 + w],
                          start=(j == 0), stop=(j == min(15, 4 * c + 3))
                          ).then_inc(eT.h, 1)

        @on("vector")
        def _(v, sc_t=sc_t, par=par):
            for j in range(NT):
                c = j // 4
                v.wait_ge(eT.h, sc_t[(j, c)])
                p = psc[par[(j, c)]]
                v.tensor_tensor(p[:, 0:128], p[:, 0:128], maskt_sb[:],
                                ALU.add).then_inc(eV.h, 1)

        @on("scalar")
        def _(s, chunks=chunks, par=par, sc_t=sc_t, mask_v=mask_v,
              av_t=av_t, ptb_w=list(ptb_free)):
            for j in range(NT):
                if j >= 2:
                    s.wait_ge(eT.h, av_t[(j - 2, 3)])
                elif ptb_w[j % 2]:
                    s.wait_ge(eT.h, ptb_w[j % 2])
                for c in range(j // 4, 4):
                    off = j * 128 - c * 512 if c == j // 4 else 0
                    w = 512 - off
                    q0 = c * 512 + off
                    s.wait_ge(eT.h, sc_t[(j, c)])
                    if c == j // 4:
                        s.wait_ge(eV.h, mask_v[(j, c)])
                    s.activation(ptb[j % 2][:, q0:q0 + w], psc[par[(j, c)]][:, 0:w],
                                 ACT.Exp).then_inc(eS.h, 1)

        sig_evac = eS.bump()
        av_evac = eV.bump()

        @on("scalar")
        def _(s, av_t=av_t, rec_f=rec_free):
            s.wait_ge(eT.h, av_t[(15, 3)])
            if rec_f:
                s.wait_ge(dRc.h, rec_f)
            s.copy(recrow[64:65, :], pav[64:65, :]).then_inc(eS.h, 1)

        @on("vector")
        def _(v, av_t=av_t, av_free=avtmp_free):
            v.wait_ge(eT.h, av_t[(15, 3)])
            if av_free:
                v.wait_ge(dAt.h, av_free)
            v.tensor_copy(avtmp[:], pav[0:64, :]).then_inc(eV.h, 1)

        @on("sync")
        def _(sy, h=h, sig_evac=sig_evac):
            sy.wait_ge(eS.h, sig_evac)
            sy.dma_start(out=recall[h:h + 1, :], in_=recrow[64:65, :]).then_inc(dRc.h, 16)

        @on("sync")
        def _(sy, hp=hp, pr=pr, av_evac=av_evac):
            sy.wait_ge(eV.h, av_evac)
            sy.dma_start(out=AVS[pr, hp * 64:hp * 64 + 64, :],
                         in_=avtmp[:]).then_inc(dAt.h, 16)

        pf = [0, 0]
        for (j, c) in chunks:
            pf[par[(j, c)]] = exp_s[(j, c)]
        return (pf, [av_t[(14, 3)], av_t[(15, 3)]], av_t[(15, 3)],
                sig_evac, av_evac, dRc.bump(16), dAt.bump(16))

    for h in range(nhrun + 1):
        if h >= 1:
            (psc_free, ptb_free, qk_free[(h - 1) % 2], pav_free_s,
             pav_free_v2, rec_free, avtmp_free) = emit_pass2_sigma(h - 1)
        if h < nhrun:
            pqk_free = emit_qk_bound(h)

    SIG_S = pav_free_s
    HEADS_LAST_T = max(qk_free)
    es_hd.close()
    es_qk.close()

    es_wo = ExitStackR()
    woT = sb(es_wo, "woT", [128, DC * D], BF16)
    es_ln2 = ExitStackR()
    xin2 = [sb(es_ln2, f"xin2{i}", [128, D], F32) for i in range(2)]
    x1b = [sb(es_ln2, f"x1b{i}", [128, D], F32) for i in range(2)]
    xl2f = sb(es_ln2, "xl2f", [128, D], F32)
    xfm32 = sb(es_ln2, "xfm32", [128, D], F32)
    sqd2 = sb(es_ln2, "sqd2", [128, D], BF16)
    mcol2 = sb(es_ln2, "mcol2", [128, 16], F32)
    xl2b = [sb(es_ln2, f"xl2b{i}", [128, D], BF16) for i in range(2)]
    avo = [sb(es_ln2, f"avo{i}", [128, D], BF16) for i in range(2)]
    es_ws2 = ExitStackR()
    wstage2 = sb(es_ws2, "wstage2", [128, D], BF16)
    es_rec = ExitStackR()
    rec0 = sb(es_rec, "rec0", [1, S], BF16)
    rec1 = sb(es_rec, "rec1", [1, S], BF16)
    avsc = [sb(es_rec, f"avsc{i}", [128, S], BF16) for i in range(2)]
    sigr = sb(es_rec, "sigr", [128, 512], F32)

    es_sg = ExitStackR()
    psb = [ps(es_sg, f"psb{i}", [128, 512]) for i in range(2)]

    sg_r0, sg_r1, sg_ld, sg_wb = [], [], [], []
    sg_t = []
    sg_v = []
    for pr in range(nhrun // 2):
        sg_r0.append(dR0.bump(16))
        sg_r1.append(dR1.bump(16))
        sg_ld.append(dAl[pr % 2].bump(16))
        for c in range(4):
            sg_t.append(eT.bump())
            sg_v.append(eV.bump())
        sg_wb.append(dAw[pr % 2].bump(16))

    @on("sync")
    def _(sy):
        sy.wait_ge(dBr.h, dBr.n)
        sy.wait_ge(dRc.h, dRc.n)
        sy.wait_ge(dAt.h, dAt.n)
        for pr in range(nhrun // 2):
            if pr >= 1:
                sy.wait_ge(eT.h, sg_t[pr * 4 - 1])
            sy.dma_start(out=rec0[:], in_=recall[2 * pr:2 * pr + 1, :]).then_inc(dR0.h, 16)
            sy.dma_start(out=rec1[:], in_=recall[2 * pr + 1:2 * pr + 2, :]).then_inc(dR1.h, 16)
            if pr >= 2:
                sy.wait_ge(dAw[pr % 2].h, sg_wb[pr - 2])
            sy.dma_start(out=avsc[pr % 2][:], in_=AVS[pr]).then_inc(dAl[pr % 2].h, 16)
            sy.wait_ge(eV.h, sg_v[pr * 4 + 3])
            sy.dma_start(out=AVS[pr], in_=avsc[pr % 2][:]).then_inc(dAw[pr % 2].h, 16)

    @on("tensor")
    def _(t_):
        k = 0
        for pr in range(nhrun // 2):
            t_.wait_ge(dR0.h, sg_r0[pr])
            t_.wait_ge(dR1.h, sg_r1[pr])
            for c in range(4):
                if k >= 2:
                    t_.wait_ge(eV.h, sg_v[k - 2])
                t_.matmul(psb[k % 2][:], e01_sb[:], rec0[0:1, c * 512:(c + 1) * 512],
                          start=True, stop=False)
                t_.matmul(psb[k % 2][:], e10_sb[:], rec1[0:1, c * 512:(c + 1) * 512],
                          start=False, stop=True).then_inc(eT.h, 1)
                k += 1

    @on("vector")
    def _(v):
        v.wait_ge(eS.h, SIG_S)
        k = 0
        for pr in range(nhrun // 2):
            v.wait_ge(dAl[pr % 2].h, sg_ld[pr])
            for c in range(4):
                v.wait_ge(eT.h, sg_t[k])
                sl = avsc[pr % 2][:, c * 512:(c + 1) * 512]
                v.drain()
                v.reciprocal(sigr[:], psb[k % 2][:])
                v.drain()
                v.tensor_tensor(sl, sl, sigr[:], ALU.mult).then_inc(eV.h, 1)
                k += 1
    SGV = eV.n
    es_sg.close()

    if upto == "attn":
        FV = dY.bump(16)

        @on("gpsimd")
        def _(gp):
            gp.wait_ge(dAw[0].h, dAw[0].n)
            gp.wait_ge(dAw[1].h, dAw[1].n)
            gp.dma_start(out=DBG[:].rearrange("p (r q) -> p r q", r=NH // 2),
                         in_=AVS.rearrange("r p q -> p r q")).then_inc(dY.h, 16)

        @on("sync")
        def _(sy):
            sy.wait_ge(dY.h, FV)
        return nc, finish

    es_rec.close()
    es_ptr2 = ExitStackR()
    ptr2 = ps(es_ptr2, "ptr2", [128, 1024], BF16)
    wo_t, wo_v = load_wT(WO, woT, HEADS_LAST_T, sg_v[-1], ptr2, wstage2)
    es_ptr2.close()
    es_ws2.close()

    es_op = ExitStackR()
    pop = [ps(es_op, f"pop{i}", [128, 512]) for i in range(2)]
    prt = ps(es_op, "prt", [128, 1024])
    plg = ps(es_op, "plg", [128, 512])

    op_nx, op_t, op_add, op_var, op_x2, op_tp = [], [], [], [], [], []
    op_ev32, op_lg, op_lgv, op_sq, op_rstd = [], [], [], [], []
    gt_m, gt_s, gt_v, op_spill, op_l2sp = [], [], [], [], []
    op_av = []
    for i in range(NT):
        op_nx.append(dX2[i % 2].bump(16))
        op_av.append(dAv[i % 2].bump(16))
        op_t.append([eT.bump(), eT.bump()])
        op_add.append(eV.bump())
        op_var.append(eV.bump())
        op_x2.append(eV.bump())
        op_tp.append(eT.bump())
        op_ev32.append(eV.bump())
        op_lg.append(eT.bump())
        op_lgv.append(eV.bump())
        op_sq.append(eS.bump())
        op_rstd.append(eS.bump())
        op_spill.append(dSpX[i % 2].bump(16))
        op_l2sp.append(dSpL[i % 2].bump(16))
    for i in range(NT):
        gt_m.append(eV.bump())
        gt_v.append(eV.bump())
        gt_s.append(eS.bump())

    @on("sync")
    def _(sy):
        sy.wait_ge(dAw[0].h, dAw[0].n)
        sy.wait_ge(dAw[1].h, dAw[1].n)
        for i in range(NT):
            if i >= 2:
                sy.wait_ge(eV.h, op_add[i - 2])
                sy.wait_ge(eT.h, op_t[i - 2][1])
            sy.dma_start(out=avo[i % 2][:],
                         in_=AVS.rearrange("r p q -> p r q")[:, :, i * 128:(i + 1) * 128]
                         ).then_inc(dAv[i % 2].h, 16)
            sy.dma_start(out=xin2[i % 2][:], in_=Xt[i]).then_inc(dX2[i % 2].h, 16)
            sy.wait_ge(eV.h, op_x2[i])
            sy.dma_start(out=X1t[i], in_=x1b[i % 2][:]).then_inc(dSpX[i % 2].h, 16)
            sy.dma_start(out=XL2t[i], in_=xl2b[i % 2][:]).then_inc(dSpL[i % 2].h, 16)

    @on("tensor")
    def _(t_):
        t_.wait_ge(eV.h, wo_v)
        for i in range(NT):
            t_.wait_ge(dAv[i % 2].h, op_av[i])
            for dh in range(2):
                if i >= 1:
                    t_.wait_ge(eV.h, op_add[i - 1])
                for tc in range(DC):
                    ins = t_.matmul(pop[dh][:],
                                    avo[i % 2][:, tc * 128:(tc + 1) * 128],
                                    woT[:, tc * D + dh * 512: tc * D + dh * 512 + 512],
                                    start=(tc == 0), stop=(tc == DC - 1))
                ins.then_inc(eT.h, 1)
            t_.wait_ge(eV.h, op_x2[i])
            if i >= 1:
                t_.wait_ge(eV.h, op_ev32[i - 1])
            for dc in range(DC):
                ins = t_.transpose(prt[:, (dc % 8) * 128:(dc % 8) * 128 + 128],
                                   xl2f[:, dc * 128:dc * 128 + 128], idf_sb[:])
            ins.then_inc(eT.h, 1)
            t_.wait_ge(eV.h, op_ev32[i])
            if i >= 1:
                t_.wait_ge(eV.h, op_lgv[i - 1])
            for dc in range(DC):
                ins = t_.matmul(plg[:, 0:E], xfm32[:, dc * 128:dc * 128 + 128],
                                rw_sb[:, dc * E:(dc + 1) * E],
                                start=(dc == 0), stop=(dc == DC - 1))
            ins.then_inc(eT.h, 1)

    @on("vector")
    def _(v):
        for i in range(NT):
            b = (i % 2) * 8
            v.wait_ge(dX2[i % 2].h, op_nx[i])
            if i >= 2:
                v.wait_ge(eS.h, op_sq[i - 2])
                v.wait_ge(dSpX[i % 2].h, op_spill[i - 2])
            for dh in range(2):
                v.wait_ge(eT.h, op_t[i][dh])
                v.tensor_tensor(x1b[i % 2][:, dh * 512:(dh + 1) * 512],
                                xin2[i % 2][:, dh * 512:(dh + 1) * 512],
                                pop[dh][:], ALU.add)
            v.drain()
            v.reduce_sum(mcol2[:, b:b + 1], x1b[i % 2][:], axis=AX)
            v.drain()
            v.tensor_scalar(mcol2[:, b:b + 1], mcol2[:, b:b + 1], 1.0 / D, 0.0,
                            ALU.mult, ALU.add).then_inc(eV.h, 1)
            v.wait_ge(eS.h, op_sq[i])
            v.wait_ge(eV.h, op_add[i])
            v.tensor_scalar(mcol2[:, b + 2:b + 3], mcol2[:, b + 1:b + 2], 1.0 / D, 0.0,
                            ALU.mult, ALU.add)
            v.tensor_scalar(mcol2[:, b + 3:b + 4], mcol2[:, b:b + 1], mcol2[:, b:b + 1],
                            None, ALU.mult)
            v.drain()
            v.tensor_scalar(mcol2[:, b + 2:b + 3], mcol2[:, b + 2:b + 3],
                            mcol2[:, b + 3:b + 4], None, ALU.subtract)
            v.drain()
            v.tensor_scalar(mcol2[:, b + 2:b + 3], mcol2[:, b + 2:b + 3], 1e-5, 1.0,
                            ALU.add, ALU.mult).then_inc(eV.h, 1)
            v.wait_ge(eS.h, op_rstd[i])
            v.tensor_scalar(xl2f[:], x1b[i % 2][:], mcol2[:, b:b + 1], mcol2[:, b + 4:b + 5],
                            ALU.subtract, ALU.mult)
            if i >= 2:
                v.wait_ge(dSpL[i % 2].h, op_l2sp[i - 2])
            v.drain()
            v.tensor_copy(xl2b[i % 2][:], xl2f[:]).then_inc(eV.h, 1)
            v.wait_ge(eT.h, op_tp[i])
            v.tensor_copy(xfm32[:], prt[:]).then_inc(eV.h, 1)
            v.wait_ge(eT.h, op_lg[i])
            v.tensor_copy(logits[:, i * E:(i + 1) * E], plg[:, 0:E]).then_inc(eV.h, 1)


    @on("scalar")
    def _(s):
        for i in range(NT):
            b = (i % 2) * 8
            s.wait_ge(eV.h, op_add[i])
            s.activation(sqd2[:], x1b[i % 2][:], ACT.Square,
                         accum_out=mcol2[:, b + 1:b + 2]).then_inc(eS.h, 1)
            s.wait_ge(eV.h, op_var[i])
            s.activation(mcol2[:, b + 3:b + 4], mcol2[:, b + 2:b + 3], ACT.Ln)
            s.drain()
            s.activation(mcol2[:, b + 4:b + 5], mcol2[:, b + 3:b + 4], ACT.Exp,
                         scale=-0.5).then_inc(eS.h, 1)


    @on("vector")
    def _(v):
        for i in range(NT):
                L = logits[:, i * E:(i + 1) * E]
                if i >= 1:
                    v.wait_ge(eS.h, gt_s[i - 1])
                v.wait_ge(eV.h, op_lgv[i])
                v.drain()
                v.reduce_max(gsc[:, 0:1], L, axis=AX)
                v.drain()
                v.tensor_scalar(gsc[:, 1:2], gsc[:, 0:1], -1.0, 0.0,
                                ALU.mult, ALU.add).then_inc(eV.h, 1)
                v.wait_ge(eS.h, gt_s[i])
                v.reciprocal(gsc[:, 4:5], gsc[:, 3:4])
                v.tensor_scalar(gsc[:, 8:16], L, gsc[:, 0:1], None, ALU.is_equal)
                v.drain()
                v.tensor_tensor_scan(gsc[:, 16:24], gsc[:, 8:16], gsc[:, 8:16],
                                     0.0, ALU.add, ALU.bypass)
                v.drain()
                v.tensor_sub(gsc[:, 16:24], gsc[:, 16:24], gsc[:, 8:16])
                v.drain()
                v.tensor_scalar(gsc[:, 16:24], gsc[:, 16:24], 0.0, None, ALU.is_equal)
                v.drain()
                v.tensor_tensor(m1a[:, i * E:(i + 1) * E], gsc[:, 8:16], gsc[:, 16:24], ALU.mult)
                v.drain()
                v.tensor_tensor(gsc[:, 8:16], gsc[:, 24:32], m1a[:, i * E:(i + 1) * E], ALU.mult)
                v.drain()
                v.reduce_sum(gsc[:, 5:6], gsc[:, 8:16], axis=AX)
                v.drain()
                v.tensor_scalar(g1kc[:, i:i + 1], gsc[:, 5:6], gsc[:, 4:5], None, ALU.mult)
                v.scalar_tensor_tensor(gsc[:, 8:16], m1a[:, i * E:(i + 1) * E], -1e9, L,
                                       ALU.mult, ALU.add)
                v.drain()
                v.reduce_max(gsc[:, 6:7], gsc[:, 8:16], axis=AX)
                v.drain()
                v.tensor_scalar(gsc[:, 8:16], gsc[:, 8:16], gsc[:, 6:7], None, ALU.is_equal)
                v.drain()
                v.tensor_tensor_scan(gsc[:, 16:24], gsc[:, 8:16], gsc[:, 8:16],
                                     0.0, ALU.add, ALU.bypass)
                v.drain()
                v.tensor_sub(gsc[:, 16:24], gsc[:, 16:24], gsc[:, 8:16])
                v.drain()
                v.tensor_scalar(gsc[:, 16:24], gsc[:, 16:24], 0.0, None, ALU.is_equal)
                v.drain()
                v.tensor_tensor(m2a[:, i * E:(i + 1) * E], gsc[:, 8:16], gsc[:, 16:24], ALU.mult)
                v.drain()
                v.tensor_tensor(gsc[:, 8:16], gsc[:, 24:32], m2a[:, i * E:(i + 1) * E], ALU.mult)
                v.drain()
                v.reduce_sum(gsc[:, 5:6], gsc[:, 8:16], axis=AX)
                v.drain()
                v.tensor_scalar(g2kc[:, i:i + 1], gsc[:, 5:6], gsc[:, 4:5], None,
                                ALU.mult).then_inc(eV.h, 1)

    @on("scalar")
    def _(s):
        for i in range(NT):
            s.wait_ge(eV.h, gt_m[i])
            s.activation(gsc[:, 24:32], logits[:, i * E:(i + 1) * E], ACT.Exp,
                         bias=gsc[:, 1:2], accum_out=gsc[:, 3:4]).then_inc(eS.h, 1)

    es_op.close()
    es_ln2.close()
    es_wo.close()
    es_at.close()

    es_cs = ExitStackR()
    pcs = ps(es_cs, "pcs", [128, 512])
    pcnt = ps(es_cs, "pcnt", [1, 512])

    cs1_t = [eT.bump() for _ in range(NT)]
    cs1_v = [eV.bump() for _ in range(NT)]
    cnt_t = eT.bump()
    cnt_s = eS.bump()
    cs2_t = [eT.bump() for _ in range(NT)]
    cs2_v = [eV.bump() for _ in range(NT)]
    GATE_V = eV.bump()

    @on("tensor")
    def _(t_):
        t_.wait_ge(eV.h, gt_v[-1])
        for i in range(NT):
            if i >= 1:
                t_.wait_ge(eV.h, cs1_v[i - 1])
            for j in range(i + 1):
                ins = t_.matmul(pcs[:, 0:E],
                                (tri_sb if j == i else onesf_sb)[:],
                                m1a[:, j * E:(j + 1) * E],
                                start=(j == 0), stop=(j == i))
            ins.then_inc(eT.h, 1)
        t_.wait_ge(eV.h, cs1_v[-1])
        for i in range(NT):
            ins = t_.matmul(pcnt[:, 0:E], onecol_sb[:], m1k[:, i * E:(i + 1) * E],
                            start=(i == 0), stop=(i == NT - 1))
        ins.then_inc(eT.h, 1)
        t_.wait_ge(eS.h, cnt_s)
        for i in range(NT):
            if i >= 1:
                t_.wait_ge(eV.h, cs2_v[i - 1])
            for j in range(i + 1):
                t_.matmul(pcs[:, 0:E],
                          (tri_sb if j == i else onesf_sb)[:],
                          m2a[:, j * E:(j + 1) * E],
                          start=(j == 0), stop=False)
            ins = t_.matmul(pcs[:, 0:E], onerow_sb[:], county[:],
                            start=False, stop=True)
            ins.then_inc(eT.h, 1)

    @on("scalar")
    def _(s):
        s.wait_ge(eT.h, cnt_t)
        s.copy(county[:], pcnt[0:1, 0:E]).then_inc(eS.h, 1)

    @on("vector")
    def _(v):
        v.wait_ge(eV.h, gt_v[-1])
        for i in range(NT):
            v.wait_ge(eT.h, cs1_t[i])
            sl = slice(i * E, (i + 1) * E)
            v.tensor_copy(p1x[:, sl], pcs[:, 0:E])
            v.drain()
            v.tensor_scalar(gsc[:, 8:16], p1x[:, sl], float(CAP), None, ALU.is_lt)
            v.drain()
            v.tensor_tensor(m1k[:, sl], m1a[:, sl], gsc[:, 8:16], ALU.mult)
            v.drain()
            v.scalar_tensor_tensor(gsc[:, 8:16], p1x[:, sl], 1.0, m1k[:, sl],
                                   ALU.add, ALU.mult)
            v.drain()
            v.tensor_scalar(q1a[:, sl], gsc[:, 8:16], 1.0, 1.0,
                            ALU.subtract, ALU.mult).then_inc(eV.h, 1)
        for i in range(NT):
            v.wait_ge(eT.h, cs2_t[i])
            sl = slice(i * E, (i + 1) * E)
            v.tensor_copy(p2x[:, sl], pcs[:, 0:E])
            v.drain()
            v.tensor_scalar(gsc[:, 8:16], p2x[:, sl], float(CAP), None, ALU.is_lt)
            v.drain()
            v.tensor_tensor(m2k[:, sl], m2a[:, sl], gsc[:, 8:16], ALU.mult)
            v.drain()
            v.scalar_tensor_tensor(gsc[:, 8:16], p2x[:, sl], 1.0, m2k[:, sl],
                                   ALU.add, ALU.mult)
            v.drain()
            v.tensor_scalar(q2a[:, sl], gsc[:, 8:16], 1.0, 1.0,
                            ALU.subtract, ALU.mult).then_inc(eV.h, 1)
        for i in range(NT):
            sl = slice(i * E, (i + 1) * E)
            v.reduce_sum(gsc[:, 0:1], m1k[:, sl], axis=AX)
            v.drain()
            v.tensor_scalar(g1kc[:, i:i + 1], g1kc[:, i:i + 1], gsc[:, 0:1],
                            None, ALU.mult)
            v.reduce_sum(gsc[:, 1:2], m2k[:, sl], axis=AX)
            v.drain()
            ins = v.tensor_scalar(g2kc[:, i:i + 1], g2kc[:, i:i + 1], gsc[:, 1:2],
                                  None, ALU.mult)
        ins.then_inc(eV.h, 1)
    es_cs.close()

    QCA_V = eV.bump()

    @on("vector")
    def _(v):
        v.wait_ge(eV.h, GATE_V)
        v.tensor_tensor(p1x[:], q1a[:], q2a[:], ALU.max)
        v.tensor_scalar(m1k[:], q1a[:], -0.5, None, ALU.is_gt)
        v.tensor_scalar(m2k[:], q2a[:], -0.5, None, ALU.is_gt)
        v.drain()
        for i in range(NT):
            sl = slice(i * E, (i + 1) * E)
            v.tensor_scalar(m1k[:, sl], m1k[:, sl], g1kc[:, i:i + 1], None, ALU.mult)
            v.tensor_scalar(m2k[:, sl], m2k[:, sl], g2kc[:, i:i + 1], None, ALU.mult)
        v.drain()
        ins = v.tensor_tensor(p2x[:], m1k[:], m2k[:], ALU.add)
        ins.then_inc(eV.h, 1)

    if upto == "gate":
        FV = dY.bump(16 * 11)

        @on("gpsimd")
        def _(gp):
            gp.wait_ge(eV.h, GATE_V)
            for k, arr in enumerate([logits, m1a, m2a, m1k, m2k, q1a, q2a, p1x, p2x]):
                gp.dma_start(out=DBG[:, k * NT * E:(k + 1) * NT * E],
                             in_=arr[:]).then_inc(dY.h, 16)
            gp.dma_start(out=DBG[:, 9 * NT * E: 9 * NT * E + NT],
                         in_=g1kc[:]).then_inc(dY.h, 16)
            gp.dma_start(out=DBG[:, 9 * NT * E + NT: 9 * NT * E + 2 * NT],
                         in_=g2kc[:]).then_inc(dY.h, 16)

        @on("sync")
        def _(sy):
            sy.wait_ge(dY.h, FV)
        return nc, finish

    es_ga = ExitStackR()
    pg = [ps(es_ga, f"pg{mc}", [128, 512]) for mc in range(DC)]
    es_ffn = ExitStackR()
    xe = sb(es_ffn, "xe", [128, E * DC * 512], BF16)
    dtile = [sb(es_ga, f"dtile{i}", [128, 512], BF16) for i in range(2)]
    xl2g = [sb(es_ga, f"xl2g{i}", [128, D], BF16) for i in range(2)]

    ga_bv, ga_t, ga_ev, ga_rl = {}, {}, {}, {}
    for e in range(E):
        for i in range(NT):
            ga_bv[(e, i)] = eV.bump()
            ga_t[(e, i)] = eT.bump()
            ga_rl[(e, i)] = dGr[(e * NT + i) % 2].bump(16)
        for mc in range(DC):
            ga_ev[(e, mc)] = eV.bump()

    @on("sync")
    def _(sy):
        for q in range(2):
            sy.wait_ge(dSpX[q].h, dSpX[q].n)
            sy.wait_ge(dSpL[q].h, dSpL[q].n)
        for e in range(E):
            for i in range(NT):
                k = e * NT + i
                if k >= 2:
                    sy.wait_ge(eT.h, ga_t[((k - 2) // NT, (k - 2) % NT)])
                sy.dma_start(out=xl2g[k % 2][:], in_=XL2t[i]).then_inc(dGr[k % 2].h, 16)

    @on("vector")
    def _(v):
        v.wait_ge(eV.h, QCA_V)
        for e in range(E):
            for i in range(NT):
                k = e * NT + i
                if k >= 2:
                    v.wait_ge(eT.h, ga_t[((k - 2) // NT, (k - 2) % NT)])
                v.tensor_scalar(dtile[k % 2][:], iota_sb[:],
                                p1x[:, i * E + e:i * E + e + 1], None,
                                ALU.is_equal).then_inc(eV.h, 1)
            for mc in range(DC):
                v.wait_ge(eT.h, ga_t[(e, NT - 1)])
                v.tensor_copy(xe[:, e * DC * 512 + mc * 512:(e * DC + mc + 1) * 512],
                              pg[mc][:]).then_inc(eV.h, 1)

    @on("tensor")
    def _(t_):
        for e in range(E):
            for i in range(NT):
                t_.wait_ge(eV.h, ga_bv[(e, i)])
                if i == 0 and e >= 1:
                    for mc in range(DC):
                        t_.wait_ge(eV.h, ga_ev[(e - 1, mc)])
                t_.wait_ge(dGr[(e * NT + i) % 2].h, ga_rl[(e, i)])
                for mc in range(DC):
                    ins = t_.matmul(pg[mc][:],
                                    xl2g[(e * NT + i) % 2][:, mc * 128:(mc + 1) * 128],
                                    dtile[(e * NT + i) % 2][:],
                                    start=(i == 0), stop=(i == NT - 1))
                ins.then_inc(eT.h, 1)
    GATHER_V = eV.n
    es_ga.close()

    if upto == "gather":
        FV = dY.bump(16)

        @on("gpsimd")
        def _(gp):
            gp.wait_ge(eV.h, GATHER_V)
            gp.dma_start(out=DBG, in_=xe[:]).then_inc(dY.h, 16)

        @on("sync")
        def _(sy):
            sy.wait_ge(dY.h, FV)
        return nc, finish

    h_fm = sb(es_ffn, "h_fm", [128, NHC * 512], BF16)
    w0b = [sb(es_ffn, f"w0b{i}", [128, DC * 512], BF16) for i in range(3)]
    w1b = [sb(es_ffn, f"w1b{i}", [128, DC * 512], BF16) for i in range(3)]
    w0tl = sb(es_ffn, "w0tl", [128, DC * 384], BF16)
    w1tl = sb(es_ffn, "w1tl", [128, DC * 384], BF16)
    wob = [sb(es_ffn, f"wob{i}", [128, 2 * D], BF16) for i in range(2)]
    wotl = sb(es_ffn, "wotl", [128, D], BF16)
    gtmp = [sb(es_ffn, f"gtmp{i}", [128, 512], F32) for i in range(2)]
    eob = sb(es_ffn, "eob", [128, 4 * D], BF16)
    es_px = ExitStackR()
    pall = [ps(es_px, f"pall{i}", [128, 512]) for i in range(8)]

    ZTAILS = eV.bump()

    @on("vector")
    def _(v):
        v.memset(w0tl[:], 0.0)
        v.memset(w1tl[:], 0.0)
        v.memset(wotl[:], 0.0).then_inc(eV.h, 1)

    w01_dma, wo_dma = {}, {}
    hmm_t, gel_s, hfm_v, wo_t = {}, {}, {}, {}
    eo_s, eo_d = {}, {}
    for e in range(E):
        for lb in range(NW01 + 1):
            if lb < NW01:
                k = e * NW01 + lb
                w01_dma[(e, lb)] = (dW0[k % 3].bump(16), dW1[k % 3].bump(16))
            else:
                w01_dma[(e, lb)] = (dWt[0].bump(16), dWt[1].bump(16))
        for pb in range(WO_PAIRS + 1):
            if pb < WO_PAIRS:
                k = e * WO_PAIRS + pb
                wo_dma[(e, pb)] = dWo[k % 2].bump(16)
            else:
                wo_dma[(e, pb)] = dWot.bump(16)
        for hc in range(NHC):
            hmm_t[(e, hc)] = eT.bump(2)
            gel_s[(e, hc)] = eS.bump()
            hfm_v[(e, hc)] = eV.bump()
        for hc in range(NHC):
            wo_t[(e, hc)] = eT.bump()
        eo_s[e] = eS.bump(8)
        eo_d[e] = dEo.bump(16)

    W0r = W0.rearrange("e (mc p) h -> e p mc h", p=128)
    W1r = W1.rearrange("e (mc p) h -> e p mc h", p=128)
    WOr = WOE[:, 0:5376, :].rearrange("e (hc p) m -> e p hc m", p=128)

    @on("gpsimd")
    def _(gp):
        for q in range(2):
            gp.wait_ge(dSpX[q].h, dSpX[q].n)
            gp.wait_ge(dSpL[q].h, dSpL[q].n)
        gp.wait_ge(eV.h, ZTAILS)
        for e in range(E):
            for lb in range(NW01 + 1):
                if lb < NW01:
                    pty = (e * NW01 + lb) % 3
                    kk = e * NW01 + lb - 3
                    if kk >= 0:
                        pe, plb = kk // NW01, kk % NW01
                        gp.wait_ge(eT.h, hmm_t[(pe, plb * 4 + 3)])
                    gp.dma_start(out=w0b[pty][:],
                                 in_=W0r[e, :, :, lb * 512:(lb + 1) * 512]
                                 ).then_inc(dW0[pty].h, 16)
                    gp.dma_start(out=w1b[pty][:],
                                 in_=W1r[e, :, :, lb * 512:(lb + 1) * 512]
                                 ).then_inc(dW1[pty].h, 16)
                else:
                    if e >= 1:
                        gp.wait_ge(eT.h, hmm_t[(e - 1, NHC - 1)])
                    tl0 = w0tl[:].rearrange("p (mc h) -> p mc h", mc=DC)[:, :, 0:344]
                    tl1 = w1tl[:].rearrange("p (mc h) -> p mc h", mc=DC)[:, :, 0:344]
                    gp.dma_start(out=tl0, in_=W0r[e, :, :, 5120:HR]).then_inc(dWt[0].h, 16)
                    gp.dma_start(out=tl1, in_=W1r[e, :, :, 5120:HR]).then_inc(dWt[1].h, 16)
            for pb in range(WO_PAIRS + 1):
                if pb < WO_PAIRS:
                    kk = e * WO_PAIRS + pb - 2
                    pty = (e * WO_PAIRS + pb) % 2
                    if kk >= 0:
                        pe, ppb = kk // WO_PAIRS, kk % WO_PAIRS
                        gp.wait_ge(eT.h, wo_t[(pe, ppb * 2 + 1)])
                    gp.dma_start(out=wob[pty][:],
                                 in_=WOr[e, :, 2 * pb:2 * pb + 2, :]).then_inc(dWo[pty].h, 16)
                else:
                    if e >= 1:
                        gp.wait_ge(eT.h, wo_t[(e - 1, NHC - 1)])
                    gp.dma_start(out=wotl[0:88, :],
                                 in_=WOE[e, 5376:HR, :]).then_inc(dWot.h, 16)

    @on("tensor")
    def _(t_):
        for e in range(E):
            for hc in range(NHC):
                if hc < 40:
                    lb = hc // 4
                    pty = (e * NW01 + lb) % 3
                    s0 = w0b[pty][:].rearrange("p (mc x) -> p mc x", mc=DC)
                    s1 = w1b[pty][:].rearrange("p (mc x) -> p mc x", mc=DC)
                    col = (hc % 4) * 128
                    v0, v1 = w01_dma[(e, lb)]
                    t_.wait_ge(dW0[pty].h, v0)
                    t_.wait_ge(dW1[pty].h, v1)
                else:
                    s0 = w0tl[:].rearrange("p (mc x) -> p mc x", mc=DC)
                    s1 = w1tl[:].rearrange("p (mc x) -> p mc x", mc=DC)
                    col = (hc - 40) * 128
                    v0, v1 = w01_dma[(e, NW01)]
                    t_.wait_ge(dWt[0].h, v0)
                    t_.wait_ge(dWt[1].h, v1)
                kg = e * NHC + hc
                if kg >= 2:
                    pe, phc = (kg - 2) // NHC, (kg - 2) % NHC
                    t_.wait_ge(eS.h, gel_s[(pe, phc)])
                    t_.wait_ge(eV.h, hfm_v[(pe, phc)])
                if hc < 2 and e >= 1:
                    t_.wait_ge(eS.h, eo_s[e - 1])
                if hc == 0 and e == 0:
                    for mc in range(DC):
                        t_.wait_ge(eV.h, ga_ev[(E - 1, mc)])
                for mc in range(DC):
                    ins = t_.matmul(pall[hc % 2][:], s0[:, mc, col:col + 128],
                                    xe[:, e * DC * 512 + mc * 512:(e * DC + mc + 1) * 512],
                                    start=(mc == 0), stop=(mc == DC - 1))
                ins.then_inc(eT.h, 1)
                for mc in range(DC):
                    ins = t_.matmul(pall[2 + hc % 2][:], s1[:, mc, col:col + 128],
                                    xe[:, e * DC * 512 + mc * 512:(e * DC + mc + 1) * 512],
                                    start=(mc == 0), stop=(mc == DC - 1))
                ins.then_inc(eT.h, 1)
            for hc in range(NHC):
                t_.wait_ge(eV.h, hfm_v[(e, hc)])
                if hc == 0:
                    t_.wait_ge(eS.h, gel_s[(e, NHC - 1)])
                    t_.wait_ge(eV.h, hfm_v[(e, NHC - 1)])
                    if e >= 1:
                        t_.wait_ge(eS.h, eo_s[e - 1])
                if hc < 42:
                    pb = hc // 2
                    pty = (e * WO_PAIRS + pb) % 2
                    t_.wait_ge(dWo[pty].h, wo_dma[(e, pb)])
                    wsrc = wob[pty]
                    woff = (hc % 2) * D
                else:
                    t_.wait_ge(dWot.h, wo_dma[(e, WO_PAIRS)])
                    wsrc = wotl
                    woff = 0
                for st in range(4):
                    for mh in range(2):
                        ins = t_.matmul(pall[st * 2 + mh][:],
                                        h_fm[:, hc * 512 + st * 128: hc * 512 + st * 128 + 128],
                                        wsrc[:, woff + mh * 512: woff + (mh + 1) * 512],
                                        start=(hc == 0), stop=(hc == NHC - 1))
                ins.then_inc(eT.h, 1)

    @on("scalar")
    def _(s):
        for e in range(E):
            for hc in range(NHC):
                kg = e * NHC + hc
                s.wait_ge(eT.h, hmm_t[(e, hc)] - 1)
                if kg >= 2:
                    pe, phc = (kg - 2) // NHC, (kg - 2) % NHC
                    s.wait_ge(eV.h, hfm_v[(pe, phc)])
                s.activation(gtmp[hc % 2][:], pall[hc % 2][:],
                             ACT.Gelu_apprx_tanh).then_inc(eS.h, 1)
            s.wait_ge(eT.h, wo_t[(e, NHC - 1)])
            if e >= 1:
                s.wait_ge(dEo.h, eo_d[e - 1])
            for st in range(4):
                for mh in range(2):
                    s.copy(eob[:, st * D + mh * 512: st * D + mh * 512 + 512],
                           pall[st * 2 + mh][:]).then_inc(eS.h, 1)

    @on("vector")
    def _(v):
        for e in range(E):
            for hc in range(NHC):
                v.wait_ge(eS.h, gel_s[(e, hc)])
                v.wait_ge(eT.h, hmm_t[(e, hc)])
                if e >= 1:
                    v.wait_ge(eT.h, wo_t[(e - 1, hc)])
                v.tensor_tensor(h_fm[:, hc * 512:(hc + 1) * 512], gtmp[hc % 2][:],
                                pall[2 + hc % 2][:], ALU.mult).then_inc(eV.h, 1)

    @on("sync")
    def _(sy):
        for e in range(E):
            sy.wait_ge(eS.h, eo_s[e])
            sy.dma_start(out=EOSt[e],
                         in_=eob[:].rearrange("p (k m) -> p k m", k=4)).then_inc(dEo.h, 16)

    EXP_DONE_S = eS.n
    es_px.close()
    es_ffn.close()

    es_sc = ExitStackR()
    eors = sb(es_sc, "eors", [128, E * 4 * D], BF16)
    cTb = [sb(es_sc, f"cTb{i}", [128, E * 512], BF16) for i in range(2)]
    ctm = [sb(es_sc, f"ctm{i}", [128, 512], BF16) for i in range(2)]
    x1r = [sb(es_sc, f"x1r{i}", [128, D], F32) for i in range(2)]
    yout = [sb(es_sc, f"yout{i}", [128, D], F32) for i in range(2)]
    ptc = [ps(es_sc, f"ptc{i}", [128, 512], BF16) for i in range(2)]
    psc2 = [ps(es_sc, f"psc2{i}", [128, 512]) for i in range(4)]

    eo_rel = {e: dEr[0].bump(16) for e in range(E)}
    x1_rel, sc_bv, sc_tp, sc_te, sc_mm, sc_y, y_dma = {}, {}, {}, {}, {}, {}, {}
    for i in range(NT):
        x1_rel[i] = dXr[i % 2].bump(16)
        y_dma[i] = dY.bump(16)
    for ii in range(NT + 1):
        if ii < NT:
            for e in range(E):
                sc_bv[(ii, e)] = eV.bump()
                sc_te[(ii, e)] = eV.bump()
                sc_tp[(ii, e)] = eT.bump()
        if ii >= 1:
            sc_mm[ii - 1] = eT.bump()
            sc_y[ii - 1] = eV.bump()

    Yt = Y.rearrange("(i p) m -> i p m", p=128)

    @on("sync")
    def _(sy):
        sy.wait_ge(eS.h, EXP_DONE_S)
        for e in range(E):
            sy.wait_ge(dEo.h, eo_d[e])
            sy.dma_start(out=eors[:, e * 4 * D:(e + 1) * 4 * D]
                         .rearrange("p (k m) -> p k m", k=4),
                         in_=EOSt[e]).then_inc(dEr[0].h, 16)
        for i in range(NT):
            if i >= 2:
                sy.wait_ge(eV.h, sc_y[i - 2])
            sy.dma_start(out=x1r[i % 2][:], in_=X1t[i]).then_inc(dXr[i % 2].h, 16)
        sy.wait_ge(dY.h, dY.n)

    @on("gpsimd")
    def _(gp):
        for i in range(NT):
            gp.wait_ge(eV.h, sc_y[i])
            gp.dma_start(out=Yt[i], in_=yout[i % 2][:]).then_inc(dY.h, 16)

    def emit_build_chunk(i):
        @on("vector")
        def _(v, i=i):
            if i == 0:
                v.wait_ge(eS.h, EXP_DONE_S)
                v.wait_ge(eV.h, QCA_V)
            for e in range(E):
                k = i * E + e
                if k >= 2:
                    v.wait_ge(eT.h, sc_tp[((k - 2) // E, (k - 2) % E)])
                v.tensor_scalar(ctm[k % 2][:], iota_sb[:],
                                p1x[:, i * E + e:i * E + e + 1],
                                p2x[:, i * E + e:i * E + e + 1],
                                ALU.is_equal, ALU.mult).then_inc(eV.h, 1)
                v.wait_ge(eT.h, sc_tp[(i, e)])
                if i >= 2:
                    v.wait_ge(eT.h, sc_mm[i - 2])
                v.tensor_copy(cTb[i % 2][:, e * 512:(e + 1) * 512],
                              ptc[k % 2][:]).then_inc(eV.h, 1)

        @on("tensor")
        def _(t_, i=i):
            for e in range(E):
                k = i * E + e
                t_.wait_ge(eV.h, sc_bv[(i, e)])
                if k >= 2:
                    t_.wait_ge(eV.h, sc_te[((k - 2) // E, (k - 2) % E)])
                for kk4 in range(4):
                    ins = t_.transpose(ptc[k % 2][:, kk4 * 128:kk4 * 128 + 128],
                                       ctm[k % 2][:, kk4 * 128:kk4 * 128 + 128],
                                       idb_sb[:])
                ins.then_inc(eT.h, 1)

    def emit_mm_chunk(i):
        @on("tensor")
        def _(t_, i=i):
            if i >= 2:
                t_.wait_ge(eV.h, sc_y[i - 2])
            if i == 0:
                t_.wait_ge(dEr[0].h, eo_rel[E - 1])
            for e in range(E):
                t_.wait_ge(eV.h, sc_te[(i, e)])
                for kk4 in range(4):
                    for mh in range(2):
                        base = (e * 4 + kk4) * D + mh * 512
                        ins = t_.matmul(psc2[mh * 2 + i % 2][:],
                                        cTb[i % 2][:, e * 512 + kk4 * 128:
                                                   e * 512 + kk4 * 128 + 128],
                                        eors[:, base: base + 512],
                                        start=(e == 0 and kk4 == 0),
                                        stop=(e == E - 1 and kk4 == 3))
            ins.then_inc(eT.h, 1)

        @on("vector")
        def _(v, i=i):
            v.wait_ge(eT.h, sc_mm[i])
            v.wait_ge(dXr[i % 2].h, x1_rel[i])
            if i >= 2:
                v.wait_ge(dY.h, y_dma[i - 2])
            for mh in range(2):
                ins = v.tensor_tensor(yout[i % 2][:, mh * 512:(mh + 1) * 512],
                                      x1r[i % 2][:, mh * 512:(mh + 1) * 512],
                                      psc2[mh * 2 + i % 2][:], ALU.add)
            ins.then_inc(eV.h, 1)

    for i in range(NT + 1):
        if i < NT:
            emit_build_chunk(i)
        if i >= 1:
            emit_mm_chunk(i - 1)

    es_sc.close()
    return nc, finish


def _consts():
    c = {}
    c["iota"] = np.tile(np.arange(512, dtype=np.float32), (128, 1))
    c["idb"] = np.eye(128).astype(ml_dtypes.bfloat16)
    c["idf"] = np.eye(128, dtype=np.float32)
    c["tri"] = np.triu(np.ones((128, 128), dtype=np.float32), k=1)
    c["onesf"] = np.ones((128, 128), dtype=np.float32)
    q = np.arange(128)
    s_ = np.arange(128)
    c["maskt"] = np.where(q[None, :] >= s_[:, None], 0.0, -1e30).astype(np.float32)
    c["onecol"] = np.ones((128, 1), dtype=np.float32)
    c["onecolb"] = np.ones((128, 1)).astype(ml_dtypes.bfloat16)
    c["onerow"] = np.ones((1, 128), dtype=np.float32)
    e01 = np.zeros((1, 128), dtype=np.float32)
    e01[0, 0:64] = 1.0
    e10 = np.zeros((1, 128), dtype=np.float32)
    e10[0, 64:128] = 1.0
    c["e01"] = e01.astype(ml_dtypes.bfloat16)
    c["e10"] = e10.astype(ml_dtypes.bfloat16)
    return c


def kernel(L0_attn_X, L0_attn_q_proj, L0_attn_k_proj, L0_attn_v_proj,
           L0_attn_o_proj, L0_ffn_router_w, L0_ffn_w0_kernel,
           L0_ffn_w1_kernel, L0_ffn_wo_kernel, _upto="full", _nh=NH):
    X = np.ascontiguousarray(np.asarray(L0_attn_X, dtype=np.float32))
    shared = {
        "wq": np.ascontiguousarray(np.asarray(L0_attn_q_proj, dtype=np.float32)),
        "wk": np.ascontiguousarray(np.asarray(L0_attn_k_proj, dtype=np.float32)),
        "wv": np.ascontiguousarray(np.asarray(L0_attn_v_proj, dtype=np.float32)),
        "wo": np.ascontiguousarray(np.asarray(L0_attn_o_proj, dtype=np.float32)),
        "rw": np.ascontiguousarray(np.asarray(L0_ffn_router_w, dtype=np.float32)),
        "w0": np.ascontiguousarray(np.asarray(L0_ffn_w0_kernel, dtype=np.float32)),
        "w1": np.ascontiguousarray(np.asarray(L0_ffn_w1_kernel, dtype=np.float32)),
        "woe": np.ascontiguousarray(np.asarray(L0_ffn_wo_kernel, dtype=np.float32)),
    }
    shared.update(_consts())
    nc, fin = build_nc(upto=_upto, nhrun=_nh)
    fin()
    in_maps = [dict(shared, x=np.ascontiguousarray(X[c])) for c in range(8)]
    res = run_bass_kernel_spmd(nc, in_maps, core_ids=list(range(8)))
    out = np.stack([np.asarray(res.results[c]["y"]) for c in range(8)], axis=0)
    if _upto != "full":
        return out, [np.asarray(res.results[c]["dbg"]) for c in range(8)]
    return out


# version 05
# speedup vs baseline: 1.2167x; 1.1256x over previous
import numpy as np
import ml_dtypes
from contextlib import ExitStack
import concourse.bass as bass
import concourse.mybir as mybir
from concourse.bass_utils import run_bass_kernel_spmd

F32 = mybir.dt.float32
BF16 = mybir.dt.bfloat16
ALU = mybir.AluOpType
ACT = mybir.ActivationFunctionType
AX = mybir.AxisListType.X

S = 2048
D = 1024
NT = 16
DC = 8
NH = 16
E = 8
CAP = 512
HR = 5464
HP = 5504
NHC = 43
NW01 = 10
WO_PAIRS = 21


def build_nc(upto="full", nhrun=NH):
    nc = bass.Bass("TRN2", target_bir_lowering=False)

    def din(name, shape, dt=F32):
        return nc.dram_tensor(name, shape, dt, kind="ExternalInput").ap()

    X = din("x", [S, D])
    WQ = din("wq", [D, D])
    WK = din("wk", [D, D])
    WV = din("wv", [D, D])
    WO = din("wo", [D, D])
    RW = din("rw", [D, E])
    W0 = din("w0", [E, D, HR])
    W1 = din("w1", [E, D, HR])
    WOE = din("woe", [E, HR, D])
    IOTA = din("iota", [128, 512])
    IDB = din("idb", [128, 128], BF16)
    IDF = din("idf", [128, 128])
    TRI = din("tri", [128, 128])
    ONESF = din("onesf", [128, 128])
    MASKT = din("maskt", [128, 128])
    ONECOL = din("onecol", [128, 1])
    ONECOLB = din("onecolb", [128, 1], BF16)
    ONEROW = din("onerow", [1, 128])
    E01 = din("e01", [1, 128], BF16)
    E10 = din("e10", [1, 128], BF16)

    Y = nc.dram_tensor("y", [S, D], F32, kind="ExternalOutput").ap()
    NDBG = {"ln1": DC * S, "attn": (NH // 2) * S, "gate": 10 * NT * E,
            "gather": E * DC * 512, "vprime": NT * NH * 65}.get(upto, 1)
    DBG = nc.dram_tensor("dbg", [128, NDBG], F32, kind="ExternalOutput").ap()
    X1S = nc.dram_tensor("x1s", [S, D], F32, kind="Internal").ap()
    XL2S = nc.dram_tensor("xl2s", [S, D], BF16, kind="Internal").ap()
    AVS = nc.dram_tensor("avs", [NH // 2, 128, S], BF16, kind="Internal").ap()
    EOS = nc.dram_tensor("eos", [E, CAP, D], BF16, kind="Internal").ap()

    Xt = X.rearrange("(i p) m -> i p m", p=128)
    X1t = X1S.rearrange("(i p) m -> i p m", p=128)
    XL2t = XL2S.rearrange("(i p) m -> i p m", p=128)
    EOSt = EOS.rearrange("e (k p) m -> e p k m", p=128)

    stack = ExitStack()
    scopes = []
    _ES = ExitStack

    def ExitStackR():
        es = _ES()
        scopes.append(es)
        return es

    def finish():
        for es in reversed(scopes):
            es.close()
        stack.close()

    def sb(es, name, shape, dt):
        return es.enter_context(nc.sbuf_tensor(name, shape, dt))

    def ps(es, name, shape, dt=F32):
        return es.enter_context(nc.psum_tensor(name, shape, dt))

    class Ctr:
        def __init__(self, name):
            self.h = stack.enter_context(nc.semaphore(name))
            self.n = 0

        def bump(self, k=1):
            self.n += k
            return self.n

    dC = Ctr("dC")
    dW = Ctr("dW")
    dEo = Ctr("dEo")
    dY = Ctr("dY")
    eT = Ctr("eT")
    eV = Ctr("eV")
    eS = Ctr("eS")
    eG = Ctr("eG")
    dXi = [Ctr("dXi0"), Ctr("dXi1")]
    dX2 = [Ctr("dX20"), Ctr("dX21")]
    dAv = [Ctr("dAv0"), Ctr("dAv1")]
    dSpX = [Ctr("dSpX0"), Ctr("dSpX1")]
    dSpL = [Ctr("dSpL0"), Ctr("dSpL1")]
    dBr = Ctr("dBr")
    dRc = Ctr("dRc")
    dAt = Ctr("dAt")
    dR0 = Ctr("dR0")
    dR1 = Ctr("dR1")
    dAl = [Ctr("dAl0"), Ctr("dAl1")]
    dAw = [Ctr("dAw0"), Ctr("dAw1")]
    dW0 = [Ctr("dW00"), Ctr("dW01"), Ctr("dW02")]
    dW1 = [Ctr("dW10"), Ctr("dW11"), Ctr("dW12")]
    dWt = [Ctr("dWt0"), Ctr("dWt1")]
    dWo = [Ctr("dWo0"), Ctr("dWo1")]
    dWot = Ctr("dWot")
    dEr = [Ctr("dEr0"), Ctr("dEr1")]
    dGr = [Ctr("dGr0"), Ctr("dGr1")]
    dXr = [Ctr("dXr0"), Ctr("dXr1")]

    gl = stack
    iota_sb = sb(gl, "iota_sb", [128, 512], F32)
    idb_sb = sb(gl, "idb_sb", [128, 128], BF16)
    idf_sb = sb(gl, "idf_sb", [128, 128], F32)
    tri_sb = sb(gl, "tri_sb", [128, 128], F32)
    onesf_sb = sb(gl, "onesf_sb", [128, 128], F32)
    maskt_sb = sb(gl, "maskt_sb", [128, 128], F32)
    onecol_sb = sb(gl, "onecol_sb", [128, 1], F32)
    onecolb_sb = sb(gl, "onecolb_sb", [128, 1], BF16)
    onerow_sb = sb(gl, "onerow_sb", [1, 128], F32)
    e01_sb = sb(gl, "e01_sb", [1, 128], BF16)
    e10_sb = sb(gl, "e10_sb", [1, 128], BF16)
    rw_sb = sb(gl, "rw_sb", [128, DC * E], F32)
    logits = sb(gl, "logits", [128, NT * E], F32)
    m1a = sb(gl, "m1a", [128, NT * E], F32)
    m2a = sb(gl, "m2a", [128, NT * E], F32)
    m1k = sb(gl, "m1k", [128, NT * E], F32)
    m2k = sb(gl, "m2k", [128, NT * E], F32)
    q1a = sb(gl, "q1a", [128, NT * E], F32)
    q2a = sb(gl, "q2a", [128, NT * E], F32)
    p1x = sb(gl, "p1x", [128, NT * E], F32)
    p2x = sb(gl, "p2x", [128, NT * E], F32)
    g1kc = sb(gl, "g1kc", [128, NT], F32)
    g2kc = sb(gl, "g2kc", [128, NT], F32)
    county = sb(gl, "county", [1, E], F32)
    gsc = sb(gl, "gsc", [128, 48], F32)

    block = stack.enter_context(nc.Block())

    def on(eng):
        return getattr(block, eng)

    @on("sync")
    def _(sy):
        for dst, src in [(iota_sb, IOTA), (idb_sb, IDB), (idf_sb, IDF),
                         (tri_sb, TRI), (onesf_sb, ONESF), (maskt_sb, MASKT),
                         (onecol_sb, ONECOL), (onecolb_sb, ONECOLB),
                         (onerow_sb, ONEROW), (e01_sb, E01), (e10_sb, E10)]:
            sy.dma_start(out=dst[:], in_=src).then_inc(dC.h, 16)
        sy.dma_start(out=rw_sb[:],
                     in_=RW.rearrange("(dc p) e -> p dc e", p=128)).then_inc(dC.h, 16)
    CONSTS = dC.bump(16 * 12)

    es_at = ExitStackR()
    es_qk = ExitStackR()
    es_ws = ExitStackR()
    es_wv = ExitStackR()
    es_ln = ExitStackR()
    avtmp = sb(es_at, "avtmp", [64, S], BF16)
    recrow = sb(es_at, "recrow", [65, S], BF16)
    recall = sb(es_at, "recall", [16, S], BF16)
    wqT = sb(es_qk, "wqT", [128, DC * D], BF16)
    wkT = sb(es_qk, "wkT", [128, DC * D], BF16)
    xfm = sb(es_qk, "xfm", [128, DC * S], BF16)
    vall = sb(es_qk, "vall", [128, NT * NH * 65], BF16)
    qbuf = [sb(es_qk, f"qb{i}", [65, S], BF16) for i in range(2)]
    kbuf = [sb(es_qk, f"kb{i}", [65, S], BF16) for i in range(2)]
    ptb = [sb(es_qk, f"ptb{i}", [128, S], BF16) for i in range(2)]
    sqq = sb(es_qk, "sqq", [64, S], BF16)
    nrow = sb(es_qk, "nrow", [1, S + 16], F32)
    brow = sb(es_qk, "brow", [1, S], BF16)
    wstage = sb(es_ws, "wstage", [128, D], BF16)
    wvT = sb(es_wv, "wvT", [128, DC * D], BF16)
    xin = [sb(es_ln, f"xin{i}", [128, D], F32) for i in range(2)]
    xlt = sb(es_ln, "xlt", [128, D], BF16)
    sqd = sb(es_ln, "sqd", [128, D], BF16)
    mcol = sb(es_ln, "mcol", [128, 16], F32)

    es_ptr = ExitStackR()
    ptr = ps(es_ptr, "ptr", [128, 1024], BF16)

    def load_wT(Wsrc, dstT, stage_free_t, ptr_free_v, ptrps, stg):
        Wr = Wsrc.rearrange("(oc p) d -> oc p d", p=128)
        ndma = [dW.bump(16) for _ in range(DC)]
        t_vals = [eT.bump() for _ in range(DC)]
        v_vals = [eV.bump() for _ in range(DC)]

        @on("gpsimd")
        def _(gp):
            for oc in range(DC):
                if oc == 0 and stage_free_t:
                    gp.wait_ge(eT.h, stage_free_t)
                if oc >= 1:
                    gp.wait_ge(eT.h, t_vals[oc - 1])
                gp.dma_start(out=stg[:], in_=Wr[oc]).then_inc(dW.h, 16)

        @on("tensor")
        def _(t):
            t.wait_ge(dC.h, CONSTS)
            for oc in range(DC):
                t.wait_ge(dW.h, ndma[oc])
                if oc == 0 and ptr_free_v:
                    t.wait_ge(eV.h, ptr_free_v)
                if oc >= 1:
                    t.wait_ge(eV.h, v_vals[oc - 1])
                for dc in range(DC):
                    ins = t.transpose(ptrps[:, dc * 128:dc * 128 + 128],
                                      stg[:, dc * 128:dc * 128 + 128],
                                      idb_sb[:])
                ins.then_inc(eT.h, 1)

        @on("vector")
        def _(v):
            for oc in range(DC):
                v.wait_ge(eT.h, t_vals[oc])
                dst = dstT[:].rearrange("p (dc x) -> p dc x", dc=DC)[:, :, oc * 128:oc * 128 + 128]
                src = ptrps[:].rearrange("p (dc x) -> p dc x", dc=DC)
                v.tensor_copy(dst, src).then_inc(eV.h, 1)
        return t_vals[-1], v_vals[-1]

    wv_t, wv_v = load_wT(WV, wvT, 0, 0, ptr, wstage)

    ln_nx = [dXi[i % 2].bump(16) for i in range(NT)]
    ln_mean, ln_var, ln_norm, ln_ev = [], [], [], []
    ln_sq, ln_rstd, ln_tp = [], [], []
    for i in range(NT):
        ln_mean.append(eV.bump())
        ln_var.append(eV.bump())
        ln_norm.append(eV.bump())
        ln_ev.append(eV.bump())
        ln_sq.append(eS.bump())
        ln_rstd.append(eS.bump())
        ln_tp.append(eT.bump())

    @on("sync")
    def _(sy):
        for i in range(NT):
            if i >= 2:
                sy.wait_ge(eS.h, ln_sq[i - 2])
                sy.wait_ge(eV.h, ln_norm[i - 2])
            sy.dma_start(out=xin[i % 2][:], in_=Xt[i]).then_inc(dXi[i % 2].h, 16)

    @on("vector")
    def _(v):
        for i in range(NT):
            b = (i % 2) * 8
            v.wait_ge(dXi[i % 2].h, ln_nx[i])
            v.reduce_sum(mcol[:, b:b + 1], xin[i % 2][:], axis=AX)
            v.drain()
            v.tensor_scalar(mcol[:, b:b + 1], mcol[:, b:b + 1], 1.0 / D, 0.0,
                            ALU.mult, ALU.add).then_inc(eV.h, 1)
            v.wait_ge(eS.h, ln_sq[i])
            v.wait_ge(eV.h, ln_mean[i])
            v.tensor_scalar(mcol[:, b + 2:b + 3], mcol[:, b + 1:b + 2], 1.0 / D, 0.0,
                            ALU.mult, ALU.add)
            v.tensor_scalar(mcol[:, b + 3:b + 4], mcol[:, b:b + 1], mcol[:, b:b + 1],
                            None, ALU.mult)
            v.drain()
            v.tensor_scalar(mcol[:, b + 2:b + 3], mcol[:, b + 2:b + 3],
                            mcol[:, b + 3:b + 4], None, ALU.subtract)
            v.drain()
            v.tensor_scalar(mcol[:, b + 2:b + 3], mcol[:, b + 2:b + 3], 1e-5, 1.0,
                            ALU.add, ALU.mult).then_inc(eV.h, 1)
            v.wait_ge(eS.h, ln_rstd[i])
            if i >= 1:
                v.wait_ge(eT.h, ln_tp[i - 1])
            v.tensor_scalar(xlt[:], xin[i % 2][:], mcol[:, b:b + 1], mcol[:, b + 4:b + 5],
                            ALU.subtract, ALU.mult).then_inc(eV.h, 1)
            v.wait_ge(eT.h, ln_tp[i])
            dst = xfm[:].rearrange("p (dc x) -> p dc x", dc=DC)[:, :, i * 128:i * 128 + 128]
            src = ptr[:].rearrange("p (dc x) -> p dc x", dc=DC)
            v.tensor_copy(dst, src).then_inc(eV.h, 1)

    @on("scalar")
    def _(s):
        for i in range(NT):
            b = (i % 2) * 8
            s.wait_ge(dXi[i % 2].h, ln_nx[i])
            s.activation(sqd[:], xin[i % 2][:], ACT.Square,
                         accum_out=mcol[:, b + 1:b + 2]).then_inc(eS.h, 1)
            s.wait_ge(eV.h, ln_var[i])
            s.activation(mcol[:, b + 3:b + 4], mcol[:, b + 2:b + 3], ACT.Ln)
            s.drain()
            s.activation(mcol[:, b + 4:b + 5], mcol[:, b + 3:b + 4], ACT.Exp,
                         scale=-0.5).then_inc(eS.h, 1)

    @on("tensor")
    def _(t):
        for i in range(NT):
            t.wait_ge(eV.h, ln_norm[i])
            t.wait_ge(eV.h, wv_v if i == 0 else ln_ev[i - 1])
            for dc in range(DC):
                ins = t.transpose(ptr[:, dc * 128:dc * 128 + 128],
                                  xlt[:, dc * 128:dc * 128 + 128], idb_sb[:])
            ins.then_inc(eT.h, 1)

    XFM_V = ln_ev[-1]

    if upto == "ln1":
        FV = dY.bump(16)

        @on("gpsimd")
        def _(gp):
            gp.wait_ge(eV.h, XFM_V)
            gp.dma_start(out=DBG, in_=xfm[:]).then_inc(dY.h, 16)

        @on("sync")
        def _(sy):
            sy.wait_ge(dY.h, FV)
        return nc, finish

    es_vp = ExitStackR()
    pvp = [ps(es_vp, f"pvp{i}", [128, 512]) for i in range(2)]
    VONES = eV.bump()
    vp_t, vp_v = [], []
    for _ in range(NT * 2):
        vp_t.append(eT.bump())
        vp_v.append(eV.bump())

    @on("vector")
    def _(v):
        v.memset(vall[:], 1.0).then_inc(eV.h, 1)

    @on("tensor")
    def _(t):
        t.wait_ge(eV.h, max(wv_v, XFM_V))
        k = 0
        for i in range(NT):
            for h2 in range(2):
                if k >= 2:
                    t.wait_ge(eV.h, vp_v[k - 2])
                for dc in range(DC):
                    ins = t.matmul(pvp[k % 2][:],
                                   xfm[:, dc * S + i * 128: dc * S + i * 128 + 128],
                                   wvT[:, dc * D + h2 * 512: dc * D + h2 * 512 + 512],
                                   start=(dc == 0), stop=(dc == DC - 1))
                ins.then_inc(eT.h, 1)
                k += 1

    @on("vector")
    def _(v):
        v.wait_ge(eV.h, VONES)
        k = 0
        for i in range(NT):
            for h2 in range(2):
                v.wait_ge(eT.h, vp_t[k])
                dst = vall[:, i * NH * 65: (i + 1) * NH * 65]
                dst = dst.rearrange("p (h c) -> p h c", h=NH)[:, h2 * 8:(h2 + 1) * 8, 0:64]
                src = pvp[k % 2][:].rearrange("p (h c) -> p h c", c=64)
                v.tensor_copy(dst, src).then_inc(eV.h, 1)
                k += 1
    es_vp.close()
    es_wv_close_marker = vp_v[-1]

    if upto == "vprime":
        FV = dY.bump(16)

        @on("gpsimd")
        def _(gp):
            gp.wait_ge(eV.h, vp_v[-1])
            gp.dma_start(out=DBG, in_=vall[:]).then_inc(dY.h, 16)

        @on("sync")
        def _(sy):
            sy.wait_ge(dY.h, FV)
        return nc, finish

    wq_t, wq_v = load_wT(WQ, wqT, wv_t, ln_ev[-1], ptr, wstage)
    wk_t, wk_v = load_wT(WK, wkT, wq_t, wq_v, ptr, wstage)
    es_ln.close()
    es_wv.close()
    es_ws.close()
    es_ptr.close()

    es_hd = ExitStackR()
    pqk = [ps(es_hd, f"pqk{i}", [64, 512]) for i in range(2)]
    psc = [ps(es_hd, f"psc{i}", [128, 512]) for i in range(2)]
    pav = ps(es_hd, "pav", [65, S])

    KONES = eG.bump()

    @on("gpsimd")
    def _(gp):
        gp.memset(kbuf[0][64:65, :], 1.0)
        gp.memset(kbuf[1][64:65, :], 1.0).then_inc(eG.h, 1)

    qk_free = [0, 0]
    pav_free_s = 0
    pav_free_v2 = 0
    avtmp_free = 0
    psc_free = [0, 0]
    pqk_free = [("S", 0), ("S", 0)]
    ptb_free = [0, 0]
    rec_free = 0
    hst = {}

    def emit_qk_bound(h):
        nonlocal_state = None
        hp = h % 2
        st = {}
        st["qkt"] = qkt = [eT.bump() for _ in range(8)]
        st["qev"] = qev = [eV.bump() for _ in range(4)]
        st["kev"] = kev = [eV.bump() for _ in range(4)]

        @on("tensor")
        def _(t_, h=h, hp=hp, qw=qk_free[hp], pqw=list(pqk_free),
              qev=qev, kev=kev):
            if h == 0:
                t_.wait_ge(eV.h, wk_v)
            if qw:
                t_.wait_ge(eT.h, qw)
            k = 0
            for wT in (wqT, wkT):
                for c in range(4):
                    if k >= 2:
                        t_.wait_ge(eV.h, (qev + kev)[k - 2])
                    else:
                        kind, val = pqw[k % 2]
                        if val:
                            t_.wait_ge((eS if kind == "S" else eV).h, val)
                    for dc in range(DC):
                        ins = t_.matmul(pqk[k % 2][:],
                                        wT[:, dc * D + h * 64: dc * D + h * 64 + 64],
                                        xfm[:, dc * S + c * 512: dc * S + c * 512 + 512],
                                        start=(dc == 0), stop=(dc == DC - 1))
                    ins.then_inc(eT.h, 1)
                    k += 1

        @on("vector")
        def _(v, hp=hp, qkt=qkt):
            for c in range(4):
                v.wait_ge(eT.h, qkt[c])
                v.tensor_scalar(qbuf[hp][0:64, c * 512:(c + 1) * 512], pqk[c % 2][:],
                                0.125, 0.0, ALU.mult, ALU.add).then_inc(eV.h, 1)
            for c in range(4):
                v.wait_ge(eT.h, qkt[4 + c])
                v.tensor_copy(kbuf[hp][0:64, c * 512:(c + 1) * 512],
                              pqk[c % 2][:]).then_inc(eV.h, 1)

        st["sq_k"] = sq_k = eV.bump()
        st["nmk_t"] = nmk_t = [eT.bump() for _ in range(4)]
        st["nmk_s"] = nmk_s = [eS.bump() for _ in range(4)]
        st["mk_done"] = mk_done = eV.bump()
        st["sq_q"] = sq_q = eV.bump()
        st["nmq_t"] = nmq_t = [eT.bump() for _ in range(4)]
        st["nmq_s"] = nmq_s = [eS.bump() for _ in range(4)]
        st["bexp_done"] = bexp_done = eS.bump()
        st["brow_dma"] = dBr.bump(16)
        prev_nmq_t = hst[h - 1]["nmq_t"][-1] if h >= 1 else 0
        prev_brow = hst[h - 1]["brow_dma"] if h >= 1 else 0

        @on("vector")
        def _(v, hp=hp, kev=kev, nmk_t=nmk_t, nmk_s=nmk_s, prev_nmq_t=prev_nmq_t):
            v.wait_ge(eV.h, kev[-1])
            if prev_nmq_t:
                v.wait_ge(eT.h, prev_nmq_t)
            v.tensor_tensor(sqq[:], kbuf[hp][0:64, :], kbuf[hp][0:64, :],
                            ALU.mult).then_inc(eV.h, 1)
            v.wait_ge(eS.h, nmk_s[-1])
            v.reduce_max(nrow[0:1, S:S + 1], nrow[0:1, 0:S],
                         axis=AX).then_inc(eV.h, 1)
            v.wait_ge(eT.h, nmk_t[-1])
            v.tensor_tensor(sqq[:], qbuf[hp][0:64, :], qbuf[hp][0:64, :],
                            ALU.mult).then_inc(eV.h, 1)

        prev_bexp = hst[h - 1]["bexp_done"] if h >= 1 else 0

        @on("scalar")
        def _(s, nmk_t=nmk_t, nmq_t=nmq_t, mk_done=mk_done, nmq_s=nmq_s,
              prev_brow=prev_brow, prev_bexp=prev_bexp):
            if prev_bexp:
                s.wait_ge(eS.h, prev_bexp)
            for c in range(4):
                s.wait_ge(eT.h, nmk_t[c])
                s.copy(nrow[0:1, c * 512:(c + 1) * 512],
                       pqk[c % 2][0:1, :]).then_inc(eS.h, 1)
            if prev_brow:
                s.wait_ge(dBr.h, prev_brow)
            for c in range(4):
                s.wait_ge(eT.h, nmq_t[c])
                s.copy(brow[0:1, c * 512:(c + 1) * 512],
                       pqk[c % 2][0:1, :]).then_inc(eS.h, 1)
            s.wait_ge(eV.h, mk_done)
            s.wait_ge(eS.h, nmq_s[-1])
            s.activation(nrow[0:1, S + 1:S + 2], nrow[0:1, S:S + 1], ACT.Ln)
            s.drain()
            s.activation(nrow[0:1, S + 2:S + 3], nrow[0:1, S + 1:S + 2],
                         ACT.Identity, scale=0.5)
            s.drain()
            s.activation(nrow[0:1, 0:S], brow[0:1, 0:S], ACT.Ln)
            s.drain()
            s.activation(nrow[0:1, 0:S], nrow[0:1, 0:S], ACT.Exp, scale=0.5,
                         bias=nrow[0:1, S + 2:S + 3])
            s.drain()
            s.mul(brow[:], nrow[0:1, 0:S], -1.05).then_inc(eS.h, 1)

        @on("tensor")
        def _(t_, sq_k=sq_k, sq_q=sq_q, nmk_s=nmk_s, nmq_s=nmq_s):
            t_.wait_ge(eV.h, sq_k)
            for c in range(4):
                if c >= 2:
                    t_.wait_ge(eS.h, nmk_s[c - 2])
                t_.matmul(pqk[c % 2][0:1, :], onecolb_sb[0:64, :],
                          sqq[:, c * 512:(c + 1) * 512],
                          start=True, stop=True).then_inc(eT.h, 1)
            t_.wait_ge(eV.h, sq_q)
            for c in range(4):
                t_.wait_ge(eS.h, (nmk_s + nmq_s)[c + 2])
                t_.matmul(pqk[c % 2][0:1, :], onecolb_sb[0:64, :],
                          sqq[:, c * 512:(c + 1) * 512],
                          start=True, stop=True).then_inc(eT.h, 1)

        @on("sync")
        def _(sy, hp=hp, bexp_done=bexp_done):
            sy.wait_ge(eS.h, bexp_done)
            sy.dma_start(out=qbuf[hp][64:65, :], in_=brow[:]).then_inc(dBr.h, 16)

        hst[h] = st
        return [("S", nmq_s[2]), ("S", nmq_s[3])]

    def emit_pass2_sigma(h):
        hp = h % 2
        pr = h // 2
        st = hst[h]
        chunks = []
        for j in range(NT):
            for c in range(j // 4, 4):
                chunks.append((j, c))
        par = {jc: k % 2 for k, jc in enumerate(chunks)}
        ops = [("S", 0)]
        for k in range(1, len(chunks)):
            ops.append(("S", k))
            ops.append(("A", k - 1))
        ops.append(("A", len(chunks) - 1))
        sc_t, av_t, mask_v, exp_s = {}, {}, {}, {}
        for typ, k in ops:
            j, c = chunks[k]
            if typ == "S":
                sc_t[(j, c)] = eT.bump()
                if c == j // 4:
                    mask_v[(j, c)] = eV.bump()
            else:
                av_t[(j, c)] = eT.bump()
        for (j, c) in chunks:
            exp_s[(j, c)] = eS.bump()

        @on("tensor")
        def _(t_, h=h, hp=hp, brow_dma=st["brow_dma"], chunks=chunks,
              exp_s=exp_s, psc_w=list(psc_free), pav_s=pav_free_s,
              pav_v2=pav_free_v2, av_free=avtmp_free, rec_f=rec_free):
            t_.wait_ge(dBr.h, brow_dma)
            if h == 0:
                t_.wait_ge(eG.h, KONES)
                t_.wait_ge(eV.h, vp_v[-1])
            if pav_s:
                t_.wait_ge(eS.h, pav_s)
            if pav_v2:
                t_.wait_ge(eV.h, pav_v2)
            if av_free:
                t_.wait_ge(dAt.h, av_free)
            if rec_f:
                t_.wait_ge(dRc.h, rec_f)
            for typ, k in ops:
                j, c = chunks[k]
                off = j * 128 - c * 512 if c == j // 4 else 0
                w = 512 - off
                q0 = c * 512 + off
                if typ == "S":
                    if k >= 2:
                        t_.wait_ge(eS.h, exp_s[chunks[k - 2]])
                    elif psc_w[k % 2]:
                        t_.wait_ge(eS.h, psc_w[k % 2])
                    t_.matmul(psc[k % 2][:, 0:w], kbuf[hp][:, j * 128:(j + 1) * 128],
                              qbuf[hp][:, q0:q0 + w],
                              start=True, stop=True).then_inc(eT.h, 1)
                else:
                    t_.wait_ge(eS.h, exp_s[(j, c)])
                    t_.matmul(pav[0:65, q0:q0 + w],
                              vall[:, j * NH * 65 + h * 65: j * NH * 65 + (h + 1) * 65],
                              ptb[j % 2][:, q0:q0 + w],
                              start=(j == 0), stop=(j == min(15, 4 * c + 3))
                              ).then_inc(eT.h, 1)

        @on("vector")
        def _(v, sc_t=sc_t, par=par):
            for j in range(NT):
                c = j // 4
                v.wait_ge(eT.h, sc_t[(j, c)])
                p = psc[par[(j, c)]]
                v.tensor_tensor(p[:, 0:128], p[:, 0:128], maskt_sb[:],
                                ALU.add).then_inc(eV.h, 1)

        @on("scalar")
        def _(s, chunks=chunks, par=par, sc_t=sc_t, mask_v=mask_v,
              av_t=av_t, ptb_w=list(ptb_free)):
            for j in range(NT):
                if j >= 2:
                    s.wait_ge(eT.h, av_t[(j - 2, 3)])
                elif ptb_w[j % 2]:
                    s.wait_ge(eT.h, ptb_w[j % 2])
                for c in range(j // 4, 4):
                    off = j * 128 - c * 512 if c == j // 4 else 0
                    w = 512 - off
                    q0 = c * 512 + off
                    s.wait_ge(eT.h, sc_t[(j, c)])
                    if c == j // 4:
                        s.wait_ge(eV.h, mask_v[(j, c)])
                    s.activation(ptb[j % 2][:, q0:q0 + w], psc[par[(j, c)]][:, 0:w],
                                 ACT.Exp).then_inc(eS.h, 1)

        sig_evac = eS.bump()
        av_evac = eV.bump()

        @on("scalar")
        def _(s, av_t=av_t, rec_f=rec_free):
            s.wait_ge(eT.h, av_t[(15, 3)])
            if rec_f:
                s.wait_ge(dRc.h, rec_f)
            s.copy(recrow[64:65, :], pav[64:65, :]).then_inc(eS.h, 1)

        @on("vector")
        def _(v, av_t=av_t, av_free=avtmp_free):
            v.wait_ge(eT.h, av_t[(15, 3)])
            if av_free:
                v.wait_ge(dAt.h, av_free)
            v.tensor_copy(avtmp[:], pav[0:64, :]).then_inc(eV.h, 1)

        @on("sync")
        def _(sy, h=h, sig_evac=sig_evac):
            sy.wait_ge(eS.h, sig_evac)
            sy.dma_start(out=recall[h:h + 1, :], in_=recrow[64:65, :]).then_inc(dRc.h, 16)

        @on("sync")
        def _(sy, hp=hp, pr=pr, av_evac=av_evac):
            sy.wait_ge(eV.h, av_evac)
            sy.dma_start(out=AVS[pr, hp * 64:hp * 64 + 64, :],
                         in_=avtmp[:]).then_inc(dAt.h, 16)

        pf = [0, 0]
        for (j, c) in chunks:
            pf[par[(j, c)]] = exp_s[(j, c)]
        return (pf, [av_t[(14, 3)], av_t[(15, 3)]], av_t[(15, 3)],
                sig_evac, av_evac, dRc.bump(16), dAt.bump(16))

    for h in range(nhrun + 1):
        if h >= 1:
            (psc_free, ptb_free, qk_free[(h - 1) % 2], pav_free_s,
             pav_free_v2, rec_free, avtmp_free) = emit_pass2_sigma(h - 1)
        if h < nhrun:
            pqk_free = emit_qk_bound(h)

    SIG_S = pav_free_s
    HEADS_LAST_T = max(qk_free)
    es_hd.close()
    es_qk.close()

    es_wo = ExitStackR()
    woT = sb(es_wo, "woT", [128, DC * D], BF16)
    es_ln2 = ExitStackR()
    xin2 = [sb(es_ln2, f"xin2{i}", [128, D], F32) for i in range(2)]
    x1b = [sb(es_ln2, f"x1b{i}", [128, D], F32) for i in range(2)]
    xl2f = sb(es_ln2, "xl2f", [128, D], F32)
    xfm32 = sb(es_ln2, "xfm32", [128, D], F32)
    sqd2 = sb(es_ln2, "sqd2", [128, D], BF16)
    mcol2 = sb(es_ln2, "mcol2", [128, 16], F32)
    xl2b = [sb(es_ln2, f"xl2b{i}", [128, D], BF16) for i in range(2)]
    avo = [sb(es_ln2, f"avo{i}", [128, D], BF16) for i in range(2)]
    es_ws2 = ExitStackR()
    wstage2 = sb(es_ws2, "wstage2", [128, D], BF16)
    es_rec = ExitStackR()
    rec0 = sb(es_rec, "rec0", [1, S], BF16)
    rec1 = sb(es_rec, "rec1", [1, S], BF16)
    avsc = [sb(es_rec, f"avsc{i}", [128, S], BF16) for i in range(2)]
    rrecb = sb(es_rec, "rrecb", [16, S], F32)

    es_sg = ExitStackR()
    psb = [ps(es_sg, f"psb{i}", [128, 512]) for i in range(2)]

    RREC_V = eV.bump()
    sg_r0, sg_r1, sg_ld, sg_wb = [], [], [], []
    sg_t = []
    sg_v = []
    for pr in range(nhrun // 2):
        sg_r0.append(dR0.bump(16))
        sg_r1.append(dR1.bump(16))
        sg_ld.append(dAl[pr % 2].bump(16))
        for c in range(4):
            sg_t.append(eT.bump())
            sg_v.append(eV.bump())
        sg_wb.append(dAw[pr % 2].bump(16))

    @on("vector")
    def _(v):
        v.wait_ge(eS.h, SIG_S)
        v.wait_ge(dRc.h, dRc.n)
        v.reciprocal(rrecb[:], recall[:]).then_inc(eV.h, 1)

    @on("gpsimd")
    def _(gp):
        gp.wait_ge(eV.h, RREC_V)
        for pr in range(nhrun // 2):
            if pr >= 1:
                gp.wait_ge(eT.h, sg_t[pr * 4 - 1])
            gp.dma_start(out=rec0[:], in_=rrecb[2 * pr:2 * pr + 1, :]).then_inc(dR0.h, 16)
            gp.dma_start(out=rec1[:], in_=rrecb[2 * pr + 1:2 * pr + 2, :]).then_inc(dR1.h, 16)

    @on("sync")
    def _(sy):
        sy.wait_ge(dBr.h, dBr.n)
        sy.wait_ge(dAt.h, dAt.n)
        for pr in range(nhrun // 2):
            if pr >= 2:
                sy.wait_ge(dAw[pr % 2].h, sg_wb[pr - 2])
            sy.dma_start(out=avsc[pr % 2][:], in_=AVS[pr]).then_inc(dAl[pr % 2].h, 16)
            sy.wait_ge(eV.h, sg_v[pr * 4 + 3])
            sy.dma_start(out=AVS[pr], in_=avsc[pr % 2][:]).then_inc(dAw[pr % 2].h, 16)

    @on("tensor")
    def _(t_):
        k = 0
        for pr in range(nhrun // 2):
            t_.wait_ge(dR0.h, sg_r0[pr])
            t_.wait_ge(dR1.h, sg_r1[pr])
            for c in range(4):
                if k >= 2:
                    t_.wait_ge(eV.h, sg_v[k - 2])
                t_.matmul(psb[k % 2][:], e01_sb[:], rec0[0:1, c * 512:(c + 1) * 512],
                          start=True, stop=False)
                t_.matmul(psb[k % 2][:], e10_sb[:], rec1[0:1, c * 512:(c + 1) * 512],
                          start=False, stop=True).then_inc(eT.h, 1)
                k += 1

    @on("vector")
    def _(v):
        k = 0
        for pr in range(nhrun // 2):
            v.wait_ge(dAl[pr % 2].h, sg_ld[pr])
            for c in range(4):
                v.wait_ge(eT.h, sg_t[k])
                sl = avsc[pr % 2][:, c * 512:(c + 1) * 512]
                v.tensor_tensor(sl, sl, psb[k % 2][:], ALU.mult).then_inc(eV.h, 1)
                k += 1
    SGV = eV.n
    es_sg.close()

    if upto == "attn":
        FV = dY.bump(16)

        @on("gpsimd")
        def _(gp):
            gp.wait_ge(dAw[0].h, dAw[0].n)
            gp.wait_ge(dAw[1].h, dAw[1].n)
            gp.dma_start(out=DBG[:].rearrange("p (r q) -> p r q", r=NH // 2),
                         in_=AVS.rearrange("r p q -> p r q")).then_inc(dY.h, 16)

        @on("sync")
        def _(sy):
            sy.wait_ge(dY.h, FV)
        return nc, finish

    es_rec.close()
    es_ptr2 = ExitStackR()
    ptr2 = ps(es_ptr2, "ptr2", [128, 1024], BF16)
    wo_t, wo_v = load_wT(WO, woT, HEADS_LAST_T, sg_v[-1], ptr2, wstage2)
    es_ptr2.close()
    es_ws2.close()

    es_op = ExitStackR()
    pop = [ps(es_op, f"pop{i}", [128, 512]) for i in range(2)]
    prt = ps(es_op, "prt", [128, 1024])
    plg = ps(es_op, "plg", [128, 512])

    op_nx, op_t, op_add, op_var, op_x2, op_tp = [], [], [], [], [], []
    op_ev32, op_lg, op_lgv, op_sq, op_rstd = [], [], [], [], []
    gt_m, gt_s, gt_v, op_spill, op_l2sp = [], [], [], [], []
    op_av = []
    for i in range(NT):
        op_nx.append(dX2[i % 2].bump(16))
        op_av.append(dAv[i % 2].bump(16))
        op_t.append([eT.bump(), eT.bump()])
        op_add.append(eV.bump())
        op_var.append(eV.bump())
        op_x2.append(eV.bump())
        op_tp.append(eT.bump())
        op_ev32.append(eV.bump())
        op_lg.append(eT.bump())
        op_lgv.append(eV.bump())
        op_sq.append(eS.bump())
        op_rstd.append(eS.bump())
        op_spill.append(dSpX[i % 2].bump(16))
        op_l2sp.append(dSpL[i % 2].bump(16))
    for i in range(NT):
        gt_m.append(eV.bump())
        gt_v.append(eV.bump())
        gt_s.append(eS.bump())

    @on("sync")
    def _(sy):
        sy.wait_ge(dAw[0].h, dAw[0].n)
        sy.wait_ge(dAw[1].h, dAw[1].n)
        for i in range(NT):
            if i >= 2:
                sy.wait_ge(eV.h, op_add[i - 2])
                sy.wait_ge(eT.h, op_t[i - 2][1])
            sy.dma_start(out=avo[i % 2][:],
                         in_=AVS.rearrange("r p q -> p r q")[:, :, i * 128:(i + 1) * 128]
                         ).then_inc(dAv[i % 2].h, 16)
            sy.dma_start(out=xin2[i % 2][:], in_=Xt[i]).then_inc(dX2[i % 2].h, 16)
            sy.wait_ge(eV.h, op_x2[i])
            sy.dma_start(out=X1t[i], in_=x1b[i % 2][:]).then_inc(dSpX[i % 2].h, 16)
            sy.dma_start(out=XL2t[i], in_=xl2b[i % 2][:]).then_inc(dSpL[i % 2].h, 16)

    @on("tensor")
    def _(t_):
        t_.wait_ge(eV.h, wo_v)
        for i in range(NT):
            t_.wait_ge(dAv[i % 2].h, op_av[i])
            for dh in range(2):
                if i >= 1:
                    t_.wait_ge(eV.h, op_add[i - 1])
                for tc in range(DC):
                    ins = t_.matmul(pop[dh][:],
                                    avo[i % 2][:, tc * 128:(tc + 1) * 128],
                                    woT[:, tc * D + dh * 512: tc * D + dh * 512 + 512],
                                    start=(tc == 0), stop=(tc == DC - 1))
                ins.then_inc(eT.h, 1)
            t_.wait_ge(eV.h, op_x2[i])
            if i >= 1:
                t_.wait_ge(eV.h, op_ev32[i - 1])
            for dc in range(DC):
                ins = t_.transpose(prt[:, (dc % 8) * 128:(dc % 8) * 128 + 128],
                                   xl2f[:, dc * 128:dc * 128 + 128], idf_sb[:])
            ins.then_inc(eT.h, 1)
            t_.wait_ge(eV.h, op_ev32[i])
            if i >= 1:
                t_.wait_ge(eV.h, op_lgv[i - 1])
            for dc in range(DC):
                ins = t_.matmul(plg[:, 0:E], xfm32[:, dc * 128:dc * 128 + 128],
                                rw_sb[:, dc * E:(dc + 1) * E],
                                start=(dc == 0), stop=(dc == DC - 1))
            ins.then_inc(eT.h, 1)

    @on("vector")
    def _(v):
        for i in range(NT):
            b = (i % 2) * 8
            v.wait_ge(dX2[i % 2].h, op_nx[i])
            if i >= 2:
                v.wait_ge(eS.h, op_sq[i - 2])
                v.wait_ge(dSpX[i % 2].h, op_spill[i - 2])
            for dh in range(2):
                v.wait_ge(eT.h, op_t[i][dh])
                v.tensor_tensor(x1b[i % 2][:, dh * 512:(dh + 1) * 512],
                                xin2[i % 2][:, dh * 512:(dh + 1) * 512],
                                pop[dh][:], ALU.add)
            v.drain()
            v.reduce_sum(mcol2[:, b:b + 1], x1b[i % 2][:], axis=AX)
            v.drain()
            v.tensor_scalar(mcol2[:, b:b + 1], mcol2[:, b:b + 1], 1.0 / D, 0.0,
                            ALU.mult, ALU.add).then_inc(eV.h, 1)
            v.wait_ge(eS.h, op_sq[i])
            v.wait_ge(eV.h, op_add[i])
            v.tensor_scalar(mcol2[:, b + 2:b + 3], mcol2[:, b + 1:b + 2], 1.0 / D, 0.0,
                            ALU.mult, ALU.add)
            v.tensor_scalar(mcol2[:, b + 3:b + 4], mcol2[:, b:b + 1], mcol2[:, b:b + 1],
                            None, ALU.mult)
            v.drain()
            v.tensor_scalar(mcol2[:, b + 2:b + 3], mcol2[:, b + 2:b + 3],
                            mcol2[:, b + 3:b + 4], None, ALU.subtract)
            v.drain()
            v.tensor_scalar(mcol2[:, b + 2:b + 3], mcol2[:, b + 2:b + 3], 1e-5, 1.0,
                            ALU.add, ALU.mult).then_inc(eV.h, 1)
            v.wait_ge(eS.h, op_rstd[i])
            v.tensor_scalar(xl2f[:], x1b[i % 2][:], mcol2[:, b:b + 1], mcol2[:, b + 4:b + 5],
                            ALU.subtract, ALU.mult)
            if i >= 2:
                v.wait_ge(dSpL[i % 2].h, op_l2sp[i - 2])
            v.drain()
            v.tensor_copy(xl2b[i % 2][:], xl2f[:]).then_inc(eV.h, 1)
            v.wait_ge(eT.h, op_tp[i])
            v.tensor_copy(xfm32[:], prt[:]).then_inc(eV.h, 1)
            v.wait_ge(eT.h, op_lg[i])
            v.tensor_copy(logits[:, i * E:(i + 1) * E], plg[:, 0:E]).then_inc(eV.h, 1)


    @on("scalar")
    def _(s):
        for i in range(NT):
            b = (i % 2) * 8
            s.wait_ge(eV.h, op_add[i])
            s.activation(sqd2[:], x1b[i % 2][:], ACT.Square,
                         accum_out=mcol2[:, b + 1:b + 2]).then_inc(eS.h, 1)
            s.wait_ge(eV.h, op_var[i])
            s.activation(mcol2[:, b + 3:b + 4], mcol2[:, b + 2:b + 3], ACT.Ln)
            s.drain()
            s.activation(mcol2[:, b + 4:b + 5], mcol2[:, b + 3:b + 4], ACT.Exp,
                         scale=-0.5).then_inc(eS.h, 1)


    @on("vector")
    def _(v):
        for i in range(NT):
                L = logits[:, i * E:(i + 1) * E]
                if i >= 1:
                    v.wait_ge(eS.h, gt_s[i - 1])
                v.wait_ge(eV.h, op_lgv[i])
                v.drain()
                v.reduce_max(gsc[:, 0:1], L, axis=AX)
                v.drain()
                v.tensor_scalar(gsc[:, 1:2], gsc[:, 0:1], -1.0, 0.0,
                                ALU.mult, ALU.add).then_inc(eV.h, 1)
                v.wait_ge(eS.h, gt_s[i])
                v.reciprocal(gsc[:, 4:5], gsc[:, 3:4])
                v.tensor_scalar(gsc[:, 8:16], L, gsc[:, 0:1], None, ALU.is_equal)
                v.drain()
                v.tensor_tensor_scan(gsc[:, 16:24], gsc[:, 8:16], gsc[:, 8:16],
                                     0.0, ALU.add, ALU.bypass)
                v.drain()
                v.tensor_sub(gsc[:, 16:24], gsc[:, 16:24], gsc[:, 8:16])
                v.drain()
                v.tensor_scalar(gsc[:, 16:24], gsc[:, 16:24], 0.0, None, ALU.is_equal)
                v.drain()
                v.tensor_tensor(m1a[:, i * E:(i + 1) * E], gsc[:, 8:16], gsc[:, 16:24], ALU.mult)
                v.drain()
                v.tensor_tensor(gsc[:, 8:16], gsc[:, 24:32], m1a[:, i * E:(i + 1) * E], ALU.mult)
                v.drain()
                v.reduce_sum(gsc[:, 5:6], gsc[:, 8:16], axis=AX)
                v.drain()
                v.tensor_scalar(g1kc[:, i:i + 1], gsc[:, 5:6], gsc[:, 4:5], None, ALU.mult)
                v.scalar_tensor_tensor(gsc[:, 8:16], m1a[:, i * E:(i + 1) * E], -1e9, L,
                                       ALU.mult, ALU.add)
                v.drain()
                v.reduce_max(gsc[:, 6:7], gsc[:, 8:16], axis=AX)
                v.drain()
                v.tensor_scalar(gsc[:, 8:16], gsc[:, 8:16], gsc[:, 6:7], None, ALU.is_equal)
                v.drain()
                v.tensor_tensor_scan(gsc[:, 16:24], gsc[:, 8:16], gsc[:, 8:16],
                                     0.0, ALU.add, ALU.bypass)
                v.drain()
                v.tensor_sub(gsc[:, 16:24], gsc[:, 16:24], gsc[:, 8:16])
                v.drain()
                v.tensor_scalar(gsc[:, 16:24], gsc[:, 16:24], 0.0, None, ALU.is_equal)
                v.drain()
                v.tensor_tensor(m2a[:, i * E:(i + 1) * E], gsc[:, 8:16], gsc[:, 16:24], ALU.mult)
                v.drain()
                v.tensor_tensor(gsc[:, 8:16], gsc[:, 24:32], m2a[:, i * E:(i + 1) * E], ALU.mult)
                v.drain()
                v.reduce_sum(gsc[:, 5:6], gsc[:, 8:16], axis=AX)
                v.drain()
                v.tensor_scalar(g2kc[:, i:i + 1], gsc[:, 5:6], gsc[:, 4:5], None,
                                ALU.mult).then_inc(eV.h, 1)

    @on("scalar")
    def _(s):
        for i in range(NT):
            s.wait_ge(eV.h, gt_m[i])
            s.activation(gsc[:, 24:32], logits[:, i * E:(i + 1) * E], ACT.Exp,
                         bias=gsc[:, 1:2], accum_out=gsc[:, 3:4]).then_inc(eS.h, 1)

    es_op.close()
    es_ln2.close()
    es_wo.close()
    es_at.close()

    es_cs = ExitStackR()
    pcs = ps(es_cs, "pcs", [128, 512])
    pcnt = ps(es_cs, "pcnt", [1, 512])

    cs1_t = [eT.bump() for _ in range(NT)]
    cs1_v = [eV.bump() for _ in range(NT)]
    cnt_t = eT.bump()
    cnt_s = eS.bump()
    cs2_t = [eT.bump() for _ in range(NT)]
    cs2_v = [eV.bump() for _ in range(NT)]
    GATE_V = eV.bump()

    @on("tensor")
    def _(t_):
        t_.wait_ge(eV.h, gt_v[-1])
        for i in range(NT):
            if i >= 1:
                t_.wait_ge(eV.h, cs1_v[i - 1])
            for j in range(i + 1):
                ins = t_.matmul(pcs[:, 0:E],
                                (tri_sb if j == i else onesf_sb)[:],
                                m1a[:, j * E:(j + 1) * E],
                                start=(j == 0), stop=(j == i))
            ins.then_inc(eT.h, 1)
        t_.wait_ge(eV.h, cs1_v[-1])
        for i in range(NT):
            ins = t_.matmul(pcnt[:, 0:E], onecol_sb[:], m1k[:, i * E:(i + 1) * E],
                            start=(i == 0), stop=(i == NT - 1))
        ins.then_inc(eT.h, 1)
        t_.wait_ge(eS.h, cnt_s)
        for i in range(NT):
            if i >= 1:
                t_.wait_ge(eV.h, cs2_v[i - 1])
            for j in range(i + 1):
                t_.matmul(pcs[:, 0:E],
                          (tri_sb if j == i else onesf_sb)[:],
                          m2a[:, j * E:(j + 1) * E],
                          start=(j == 0), stop=False)
            ins = t_.matmul(pcs[:, 0:E], onerow_sb[:], county[:],
                            start=False, stop=True)
            ins.then_inc(eT.h, 1)

    @on("scalar")
    def _(s):
        s.wait_ge(eT.h, cnt_t)
        s.copy(county[:], pcnt[0:1, 0:E]).then_inc(eS.h, 1)

    @on("vector")
    def _(v):
        v.wait_ge(eV.h, gt_v[-1])
        for i in range(NT):
            v.wait_ge(eT.h, cs1_t[i])
            sl = slice(i * E, (i + 1) * E)
            v.tensor_copy(p1x[:, sl], pcs[:, 0:E])
            v.drain()
            v.tensor_scalar(gsc[:, 8:16], p1x[:, sl], float(CAP), None, ALU.is_lt)
            v.drain()
            v.tensor_tensor(m1k[:, sl], m1a[:, sl], gsc[:, 8:16], ALU.mult)
            v.drain()
            v.scalar_tensor_tensor(gsc[:, 8:16], p1x[:, sl], 1.0, m1k[:, sl],
                                   ALU.add, ALU.mult)
            v.drain()
            v.tensor_scalar(q1a[:, sl], gsc[:, 8:16], 1.0, 1.0,
                            ALU.subtract, ALU.mult).then_inc(eV.h, 1)
        for i in range(NT):
            v.wait_ge(eT.h, cs2_t[i])
            sl = slice(i * E, (i + 1) * E)
            v.tensor_copy(p2x[:, sl], pcs[:, 0:E])
            v.drain()
            v.tensor_scalar(gsc[:, 8:16], p2x[:, sl], float(CAP), None, ALU.is_lt)
            v.drain()
            v.tensor_tensor(m2k[:, sl], m2a[:, sl], gsc[:, 8:16], ALU.mult)
            v.drain()
            v.scalar_tensor_tensor(gsc[:, 8:16], p2x[:, sl], 1.0, m2k[:, sl],
                                   ALU.add, ALU.mult)
            v.drain()
            v.tensor_scalar(q2a[:, sl], gsc[:, 8:16], 1.0, 1.0,
                            ALU.subtract, ALU.mult).then_inc(eV.h, 1)
        for i in range(NT):
            sl = slice(i * E, (i + 1) * E)
            v.reduce_sum(gsc[:, 0:1], m1k[:, sl], axis=AX)
            v.drain()
            v.tensor_scalar(g1kc[:, i:i + 1], g1kc[:, i:i + 1], gsc[:, 0:1],
                            None, ALU.mult)
            v.reduce_sum(gsc[:, 1:2], m2k[:, sl], axis=AX)
            v.drain()
            ins = v.tensor_scalar(g2kc[:, i:i + 1], g2kc[:, i:i + 1], gsc[:, 1:2],
                                  None, ALU.mult)
        ins.then_inc(eV.h, 1)
    es_cs.close()

    QCA_V = eV.bump()

    @on("vector")
    def _(v):
        v.wait_ge(eV.h, GATE_V)
        v.tensor_tensor(p1x[:], q1a[:], q2a[:], ALU.max)
        v.tensor_scalar(m1k[:], q1a[:], -0.5, None, ALU.is_gt)
        v.tensor_scalar(m2k[:], q2a[:], -0.5, None, ALU.is_gt)
        v.drain()
        for i in range(NT):
            sl = slice(i * E, (i + 1) * E)
            v.tensor_scalar(m1k[:, sl], m1k[:, sl], g1kc[:, i:i + 1], None, ALU.mult)
            v.tensor_scalar(m2k[:, sl], m2k[:, sl], g2kc[:, i:i + 1], None, ALU.mult)
        v.drain()
        ins = v.tensor_tensor(p2x[:], m1k[:], m2k[:], ALU.add)
        ins.then_inc(eV.h, 1)

    if upto == "gate":
        FV = dY.bump(16 * 11)

        @on("gpsimd")
        def _(gp):
            gp.wait_ge(eV.h, GATE_V)
            for k, arr in enumerate([logits, m1a, m2a, m1k, m2k, q1a, q2a, p1x, p2x]):
                gp.dma_start(out=DBG[:, k * NT * E:(k + 1) * NT * E],
                             in_=arr[:]).then_inc(dY.h, 16)
            gp.dma_start(out=DBG[:, 9 * NT * E: 9 * NT * E + NT],
                         in_=g1kc[:]).then_inc(dY.h, 16)
            gp.dma_start(out=DBG[:, 9 * NT * E + NT: 9 * NT * E + 2 * NT],
                         in_=g2kc[:]).then_inc(dY.h, 16)

        @on("sync")
        def _(sy):
            sy.wait_ge(dY.h, FV)
        return nc, finish

    es_ga = ExitStackR()
    pg = [ps(es_ga, f"pg{mc}", [128, 512]) for mc in range(DC)]
    es_ffn = ExitStackR()
    xe = sb(es_ffn, "xe", [128, E * DC * 512], BF16)
    dtile = [sb(es_ga, f"dtile{i}", [128, 512], BF16) for i in range(2)]
    xl2g = [sb(es_ga, f"xl2g{i}", [128, D], BF16) for i in range(2)]

    ga_bv, ga_t, ga_ev, ga_rl = {}, {}, {}, {}
    for e in range(E):
        for i in range(NT):
            ga_bv[(e, i)] = eV.bump()
            ga_t[(e, i)] = eT.bump()
            ga_rl[(e, i)] = dGr[(e * NT + i) % 2].bump(16)
        for mc in range(DC):
            ga_ev[(e, mc)] = eV.bump()

    @on("sync")
    def _(sy):
        for q in range(2):
            sy.wait_ge(dSpX[q].h, dSpX[q].n)
            sy.wait_ge(dSpL[q].h, dSpL[q].n)
        for e in range(E):
            for i in range(NT):
                k = e * NT + i
                if k >= 2:
                    sy.wait_ge(eT.h, ga_t[((k - 2) // NT, (k - 2) % NT)])
                sy.dma_start(out=xl2g[k % 2][:], in_=XL2t[i]).then_inc(dGr[k % 2].h, 16)

    @on("vector")
    def _(v):
        v.wait_ge(eV.h, QCA_V)
        for e in range(E):
            for i in range(NT):
                k = e * NT + i
                if k >= 2:
                    v.wait_ge(eT.h, ga_t[((k - 2) // NT, (k - 2) % NT)])
                v.tensor_scalar(dtile[k % 2][:], iota_sb[:],
                                p1x[:, i * E + e:i * E + e + 1], None,
                                ALU.is_equal).then_inc(eV.h, 1)
            for mc in range(DC):
                v.wait_ge(eT.h, ga_t[(e, NT - 1)])
                v.tensor_copy(xe[:, e * DC * 512 + mc * 512:(e * DC + mc + 1) * 512],
                              pg[mc][:]).then_inc(eV.h, 1)

    @on("tensor")
    def _(t_):
        for e in range(E):
            for i in range(NT):
                t_.wait_ge(eV.h, ga_bv[(e, i)])
                if i == 0 and e >= 1:
                    for mc in range(DC):
                        t_.wait_ge(eV.h, ga_ev[(e - 1, mc)])
                t_.wait_ge(dGr[(e * NT + i) % 2].h, ga_rl[(e, i)])
                for mc in range(DC):
                    ins = t_.matmul(pg[mc][:],
                                    xl2g[(e * NT + i) % 2][:, mc * 128:(mc + 1) * 128],
                                    dtile[(e * NT + i) % 2][:],
                                    start=(i == 0), stop=(i == NT - 1))
                ins.then_inc(eT.h, 1)
    GATHER_V = eV.n
    es_ga.close()

    if upto == "gather":
        FV = dY.bump(16)

        @on("gpsimd")
        def _(gp):
            gp.wait_ge(eV.h, GATHER_V)
            gp.dma_start(out=DBG, in_=xe[:]).then_inc(dY.h, 16)

        @on("sync")
        def _(sy):
            sy.wait_ge(dY.h, FV)
        return nc, finish

    h_fm = sb(es_ffn, "h_fm", [128, NHC * 512], BF16)
    w0b = [sb(es_ffn, f"w0b{i}", [128, DC * 512], BF16) for i in range(3)]
    w1b = [sb(es_ffn, f"w1b{i}", [128, DC * 512], BF16) for i in range(3)]
    w0tl = sb(es_ffn, "w0tl", [128, DC * 384], BF16)
    w1tl = sb(es_ffn, "w1tl", [128, DC * 384], BF16)
    wob = [sb(es_ffn, f"wob{i}", [128, 2 * D], BF16) for i in range(2)]
    wotl = sb(es_ffn, "wotl", [128, D], BF16)
    gtmp = [sb(es_ffn, f"gtmp{i}", [128, 512], F32) for i in range(2)]
    eob = sb(es_ffn, "eob", [128, 4 * D], BF16)
    es_px = ExitStackR()
    pall = [ps(es_px, f"pall{i}", [128, 512]) for i in range(8)]

    ZTAILS = eV.bump()

    @on("vector")
    def _(v):
        v.memset(w0tl[:], 0.0)
        v.memset(w1tl[:], 0.0)
        v.memset(wotl[:], 0.0).then_inc(eV.h, 1)

    w01_dma, wo_dma = {}, {}
    hmm_t, gel_s, hfm_v, wo_t = {}, {}, {}, {}
    eo_s, eo_d = {}, {}
    for e in range(E):
        for lb in range(NW01 + 1):
            if lb < NW01:
                k = e * NW01 + lb
                w01_dma[(e, lb)] = (dW0[k % 3].bump(16), dW1[k % 3].bump(16))
            else:
                w01_dma[(e, lb)] = (dWt[0].bump(16), dWt[1].bump(16))
        for pb in range(WO_PAIRS + 1):
            if pb < WO_PAIRS:
                k = e * WO_PAIRS + pb
                wo_dma[(e, pb)] = dWo[k % 2].bump(16)
            else:
                wo_dma[(e, pb)] = dWot.bump(16)
        for hc in range(NHC):
            hmm_t[(e, hc)] = eT.bump(2)
            gel_s[(e, hc)] = eS.bump()
            hfm_v[(e, hc)] = eV.bump()
        for hc in range(NHC):
            wo_t[(e, hc)] = eT.bump()
        eo_s[e] = eS.bump(8)
        eo_d[e] = dEo.bump(16)

    W0r = W0.rearrange("e (mc p) h -> e p mc h", p=128)
    W1r = W1.rearrange("e (mc p) h -> e p mc h", p=128)
    WOr = WOE[:, 0:5376, :].rearrange("e (hc p) m -> e p hc m", p=128)

    @on("gpsimd")
    def _(gp):
        for q in range(2):
            gp.wait_ge(dSpX[q].h, dSpX[q].n)
            gp.wait_ge(dSpL[q].h, dSpL[q].n)
        gp.wait_ge(eV.h, ZTAILS)
        for e in range(E):
            for lb in range(NW01 + 1):
                if lb < NW01:
                    pty = (e * NW01 + lb) % 3
                    kk = e * NW01 + lb - 3
                    if kk >= 0:
                        pe, plb = kk // NW01, kk % NW01
                        gp.wait_ge(eT.h, hmm_t[(pe, plb * 4 + 3)])
                    gp.dma_start(out=w0b[pty][:],
                                 in_=W0r[e, :, :, lb * 512:(lb + 1) * 512]
                                 ).then_inc(dW0[pty].h, 16)
                    gp.dma_start(out=w1b[pty][:],
                                 in_=W1r[e, :, :, lb * 512:(lb + 1) * 512]
                                 ).then_inc(dW1[pty].h, 16)
                else:
                    if e >= 1:
                        gp.wait_ge(eT.h, hmm_t[(e - 1, NHC - 1)])
                    tl0 = w0tl[:].rearrange("p (mc h) -> p mc h", mc=DC)[:, :, 0:344]
                    tl1 = w1tl[:].rearrange("p (mc h) -> p mc h", mc=DC)[:, :, 0:344]
                    gp.dma_start(out=tl0, in_=W0r[e, :, :, 5120:HR]).then_inc(dWt[0].h, 16)
                    gp.dma_start(out=tl1, in_=W1r[e, :, :, 5120:HR]).then_inc(dWt[1].h, 16)
            for pb in range(WO_PAIRS + 1):
                if pb < WO_PAIRS:
                    kk = e * WO_PAIRS + pb - 2
                    pty = (e * WO_PAIRS + pb) % 2
                    if kk >= 0:
                        pe, ppb = kk // WO_PAIRS, kk % WO_PAIRS
                        gp.wait_ge(eT.h, wo_t[(pe, ppb * 2 + 1)])
                    gp.dma_start(out=wob[pty][:],
                                 in_=WOr[e, :, 2 * pb:2 * pb + 2, :]).then_inc(dWo[pty].h, 16)
                else:
                    if e >= 1:
                        gp.wait_ge(eT.h, wo_t[(e - 1, NHC - 1)])
                    gp.dma_start(out=wotl[0:88, :],
                                 in_=WOE[e, 5376:HR, :]).then_inc(dWot.h, 16)

    @on("tensor")
    def _(t_):
        for e in range(E):
            for hc in range(NHC):
                if hc < 40:
                    lb = hc // 4
                    pty = (e * NW01 + lb) % 3
                    s0 = w0b[pty][:].rearrange("p (mc x) -> p mc x", mc=DC)
                    s1 = w1b[pty][:].rearrange("p (mc x) -> p mc x", mc=DC)
                    col = (hc % 4) * 128
                    v0, v1 = w01_dma[(e, lb)]
                    t_.wait_ge(dW0[pty].h, v0)
                    t_.wait_ge(dW1[pty].h, v1)
                else:
                    s0 = w0tl[:].rearrange("p (mc x) -> p mc x", mc=DC)
                    s1 = w1tl[:].rearrange("p (mc x) -> p mc x", mc=DC)
                    col = (hc - 40) * 128
                    v0, v1 = w01_dma[(e, NW01)]
                    t_.wait_ge(dWt[0].h, v0)
                    t_.wait_ge(dWt[1].h, v1)
                kg = e * NHC + hc
                if kg >= 2:
                    pe, phc = (kg - 2) // NHC, (kg - 2) % NHC
                    t_.wait_ge(eS.h, gel_s[(pe, phc)])
                    t_.wait_ge(eV.h, hfm_v[(pe, phc)])
                if hc < 2 and e >= 1:
                    t_.wait_ge(eS.h, eo_s[e - 1])
                if hc == 0 and e == 0:
                    for mc in range(DC):
                        t_.wait_ge(eV.h, ga_ev[(E - 1, mc)])
                for mc in range(DC):
                    ins = t_.matmul(pall[hc % 2][:], s0[:, mc, col:col + 128],
                                    xe[:, e * DC * 512 + mc * 512:(e * DC + mc + 1) * 512],
                                    start=(mc == 0), stop=(mc == DC - 1))
                ins.then_inc(eT.h, 1)
                for mc in range(DC):
                    ins = t_.matmul(pall[2 + hc % 2][:], s1[:, mc, col:col + 128],
                                    xe[:, e * DC * 512 + mc * 512:(e * DC + mc + 1) * 512],
                                    start=(mc == 0), stop=(mc == DC - 1))
                ins.then_inc(eT.h, 1)
            for hc in range(NHC):
                t_.wait_ge(eV.h, hfm_v[(e, hc)])
                if hc == 0:
                    t_.wait_ge(eS.h, gel_s[(e, NHC - 1)])
                    t_.wait_ge(eV.h, hfm_v[(e, NHC - 1)])
                    if e >= 1:
                        t_.wait_ge(eS.h, eo_s[e - 1])
                if hc < 42:
                    pb = hc // 2
                    pty = (e * WO_PAIRS + pb) % 2
                    t_.wait_ge(dWo[pty].h, wo_dma[(e, pb)])
                    wsrc = wob[pty]
                    woff = (hc % 2) * D
                else:
                    t_.wait_ge(dWot.h, wo_dma[(e, WO_PAIRS)])
                    wsrc = wotl
                    woff = 0
                for st in range(4):
                    for mh in range(2):
                        ins = t_.matmul(pall[st * 2 + mh][:],
                                        h_fm[:, hc * 512 + st * 128: hc * 512 + st * 128 + 128],
                                        wsrc[:, woff + mh * 512: woff + (mh + 1) * 512],
                                        start=(hc == 0), stop=(hc == NHC - 1))
                ins.then_inc(eT.h, 1)

    @on("scalar")
    def _(s):
        for e in range(E):
            for hc in range(NHC):
                kg = e * NHC + hc
                s.wait_ge(eT.h, hmm_t[(e, hc)] - 1)
                if kg >= 2:
                    pe, phc = (kg - 2) // NHC, (kg - 2) % NHC
                    s.wait_ge(eV.h, hfm_v[(pe, phc)])
                s.activation(gtmp[hc % 2][:], pall[hc % 2][:],
                             ACT.Gelu_apprx_tanh).then_inc(eS.h, 1)
            s.wait_ge(eT.h, wo_t[(e, NHC - 1)])
            if e >= 1:
                s.wait_ge(dEo.h, eo_d[e - 1])
            for st in range(4):
                for mh in range(2):
                    s.copy(eob[:, st * D + mh * 512: st * D + mh * 512 + 512],
                           pall[st * 2 + mh][:]).then_inc(eS.h, 1)

    @on("vector")
    def _(v):
        for e in range(E):
            for hc in range(NHC):
                v.wait_ge(eS.h, gel_s[(e, hc)])
                v.wait_ge(eT.h, hmm_t[(e, hc)])
                if e >= 1:
                    v.wait_ge(eT.h, wo_t[(e - 1, hc)])
                v.tensor_tensor(h_fm[:, hc * 512:(hc + 1) * 512], gtmp[hc % 2][:],
                                pall[2 + hc % 2][:], ALU.mult).then_inc(eV.h, 1)

    @on("sync")
    def _(sy):
        for e in range(E):
            sy.wait_ge(eS.h, eo_s[e])
            sy.dma_start(out=EOSt[e],
                         in_=eob[:].rearrange("p (k m) -> p k m", k=4)).then_inc(dEo.h, 16)

    EXP_DONE_S = eS.n
    es_px.close()
    es_ffn.close()

    es_sc = ExitStackR()
    eors = sb(es_sc, "eors", [128, E * 4 * D], BF16)
    cTb = [sb(es_sc, f"cTb{i}", [128, E * 512], BF16) for i in range(2)]
    ctm = [sb(es_sc, f"ctm{i}", [128, 512], BF16) for i in range(2)]
    x1r = [sb(es_sc, f"x1r{i}", [128, D], F32) for i in range(2)]
    yout = [sb(es_sc, f"yout{i}", [128, D], F32) for i in range(2)]
    ptc = [ps(es_sc, f"ptc{i}", [128, 512], BF16) for i in range(2)]
    psc2 = [ps(es_sc, f"psc2{i}", [128, 512]) for i in range(4)]

    eo_rel = {e: dEr[0].bump(16) for e in range(E)}
    x1_rel, sc_bv, sc_tp, sc_te, sc_mm, sc_y, y_dma = {}, {}, {}, {}, {}, {}, {}
    for i in range(NT):
        x1_rel[i] = dXr[i % 2].bump(16)
        y_dma[i] = dY.bump(16)
    for ii in range(NT + 1):
        if ii < NT:
            for e in range(E):
                sc_bv[(ii, e)] = eV.bump()
                sc_te[(ii, e)] = eV.bump()
                sc_tp[(ii, e)] = eT.bump()
        if ii >= 1:
            sc_mm[ii - 1] = eT.bump()
            sc_y[ii - 1] = eV.bump()

    Yt = Y.rearrange("(i p) m -> i p m", p=128)

    @on("sync")
    def _(sy):
        sy.wait_ge(eS.h, EXP_DONE_S)
        for e in range(E):
            sy.wait_ge(dEo.h, eo_d[e])
            sy.dma_start(out=eors[:, e * 4 * D:(e + 1) * 4 * D]
                         .rearrange("p (k m) -> p k m", k=4),
                         in_=EOSt[e]).then_inc(dEr[0].h, 16)
        for i in range(NT):
            if i >= 2:
                sy.wait_ge(eV.h, sc_y[i - 2])
            sy.dma_start(out=x1r[i % 2][:], in_=X1t[i]).then_inc(dXr[i % 2].h, 16)
        sy.wait_ge(dY.h, dY.n)

    @on("gpsimd")
    def _(gp):
        for i in range(NT):
            gp.wait_ge(eV.h, sc_y[i])
            gp.dma_start(out=Yt[i], in_=yout[i % 2][:]).then_inc(dY.h, 16)

    def emit_build_chunk(i):
        @on("vector")
        def _(v, i=i):
            if i == 0:
                v.wait_ge(eS.h, EXP_DONE_S)
                v.wait_ge(eV.h, QCA_V)
            for e in range(E):
                k = i * E + e
                if k >= 2:
                    v.wait_ge(eT.h, sc_tp[((k - 2) // E, (k - 2) % E)])
                v.tensor_scalar(ctm[k % 2][:], iota_sb[:],
                                p1x[:, i * E + e:i * E + e + 1],
                                p2x[:, i * E + e:i * E + e + 1],
                                ALU.is_equal, ALU.mult).then_inc(eV.h, 1)
                v.wait_ge(eT.h, sc_tp[(i, e)])
                if i >= 2:
                    v.wait_ge(eT.h, sc_mm[i - 2])
                v.tensor_copy(cTb[i % 2][:, e * 512:(e + 1) * 512],
                              ptc[k % 2][:]).then_inc(eV.h, 1)

        @on("tensor")
        def _(t_, i=i):
            for e in range(E):
                k = i * E + e
                t_.wait_ge(eV.h, sc_bv[(i, e)])
                if k >= 2:
                    t_.wait_ge(eV.h, sc_te[((k - 2) // E, (k - 2) % E)])
                for kk4 in range(4):
                    ins = t_.transpose(ptc[k % 2][:, kk4 * 128:kk4 * 128 + 128],
                                       ctm[k % 2][:, kk4 * 128:kk4 * 128 + 128],
                                       idb_sb[:])
                ins.then_inc(eT.h, 1)

    def emit_mm_chunk(i):
        @on("tensor")
        def _(t_, i=i):
            if i >= 2:
                t_.wait_ge(eV.h, sc_y[i - 2])
            if i == 0:
                t_.wait_ge(dEr[0].h, eo_rel[E - 1])
            for e in range(E):
                t_.wait_ge(eV.h, sc_te[(i, e)])
                for kk4 in range(4):
                    for mh in range(2):
                        base = (e * 4 + kk4) * D + mh * 512
                        ins = t_.matmul(psc2[mh * 2 + i % 2][:],
                                        cTb[i % 2][:, e * 512 + kk4 * 128:
                                                   e * 512 + kk4 * 128 + 128],
                                        eors[:, base: base + 512],
                                        start=(e == 0 and kk4 == 0),
                                        stop=(e == E - 1 and kk4 == 3))
            ins.then_inc(eT.h, 1)

        @on("vector")
        def _(v, i=i):
            v.wait_ge(eT.h, sc_mm[i])
            v.wait_ge(dXr[i % 2].h, x1_rel[i])
            if i >= 2:
                v.wait_ge(dY.h, y_dma[i - 2])
            for mh in range(2):
                ins = v.tensor_tensor(yout[i % 2][:, mh * 512:(mh + 1) * 512],
                                      x1r[i % 2][:, mh * 512:(mh + 1) * 512],
                                      psc2[mh * 2 + i % 2][:], ALU.add)
            ins.then_inc(eV.h, 1)

    for i in range(NT + 1):
        if i < NT:
            emit_build_chunk(i)
        if i >= 1:
            emit_mm_chunk(i - 1)

    es_sc.close()
    return nc, finish


def _consts():
    c = {}
    c["iota"] = np.tile(np.arange(512, dtype=np.float32), (128, 1))
    c["idb"] = np.eye(128).astype(ml_dtypes.bfloat16)
    c["idf"] = np.eye(128, dtype=np.float32)
    c["tri"] = np.triu(np.ones((128, 128), dtype=np.float32), k=1)
    c["onesf"] = np.ones((128, 128), dtype=np.float32)
    q = np.arange(128)
    s_ = np.arange(128)
    c["maskt"] = np.where(q[None, :] >= s_[:, None], 0.0, -1e30).astype(np.float32)
    c["onecol"] = np.ones((128, 1), dtype=np.float32)
    c["onecolb"] = np.ones((128, 1)).astype(ml_dtypes.bfloat16)
    c["onerow"] = np.ones((1, 128), dtype=np.float32)
    e01 = np.zeros((1, 128), dtype=np.float32)
    e01[0, 0:64] = 1.0
    e10 = np.zeros((1, 128), dtype=np.float32)
    e10[0, 64:128] = 1.0
    c["e01"] = e01.astype(ml_dtypes.bfloat16)
    c["e10"] = e10.astype(ml_dtypes.bfloat16)
    return c


def kernel(L0_attn_X, L0_attn_q_proj, L0_attn_k_proj, L0_attn_v_proj,
           L0_attn_o_proj, L0_ffn_router_w, L0_ffn_w0_kernel,
           L0_ffn_w1_kernel, L0_ffn_wo_kernel, _upto="full", _nh=NH):
    X = np.ascontiguousarray(np.asarray(L0_attn_X, dtype=np.float32))
    shared = {
        "wq": np.ascontiguousarray(np.asarray(L0_attn_q_proj, dtype=np.float32)),
        "wk": np.ascontiguousarray(np.asarray(L0_attn_k_proj, dtype=np.float32)),
        "wv": np.ascontiguousarray(np.asarray(L0_attn_v_proj, dtype=np.float32)),
        "wo": np.ascontiguousarray(np.asarray(L0_attn_o_proj, dtype=np.float32)),
        "rw": np.ascontiguousarray(np.asarray(L0_ffn_router_w, dtype=np.float32)),
        "w0": np.ascontiguousarray(np.asarray(L0_ffn_w0_kernel, dtype=np.float32)),
        "w1": np.ascontiguousarray(np.asarray(L0_ffn_w1_kernel, dtype=np.float32)),
        "woe": np.ascontiguousarray(np.asarray(L0_ffn_wo_kernel, dtype=np.float32)),
    }
    shared.update(_consts())
    nc, fin = build_nc(upto=_upto, nhrun=_nh)
    fin()
    in_maps = [dict(shared, x=np.ascontiguousarray(X[c])) for c in range(8)]
    res = run_bass_kernel_spmd(nc, in_maps, core_ids=list(range(8)))
    out = np.stack([np.asarray(res.results[c]["y"]) for c in range(8)], axis=0)
    if _upto != "full":
        return out, [np.asarray(res.results[c]["dbg"]) for c in range(8)]
    return out


# version 06
# speedup vs baseline: 1.3091x; 1.0759x over previous
import numpy as np
import ml_dtypes
from contextlib import ExitStack
import concourse.bass as bass
import concourse.mybir as mybir
from concourse.bass_utils import run_bass_kernel_spmd

F32 = mybir.dt.float32
BF16 = mybir.dt.bfloat16
ALU = mybir.AluOpType
ACT = mybir.ActivationFunctionType
AX = mybir.AxisListType.X

S = 2048
D = 1024
NT = 16
DC = 8
NH = 16
E = 8
CAP = 512
HR = 5464
HP = 5504
NHC = 43
NW01 = 10
WO_PAIRS = 21


def build_nc(upto="full", nhrun=NH):
    nc = bass.Bass("TRN2", target_bir_lowering=False)

    def din(name, shape, dt=F32):
        return nc.dram_tensor(name, shape, dt, kind="ExternalInput").ap()

    X = din("x", [S, D])
    WQ = din("wq", [D, D])
    WK = din("wk", [D, D])
    WV = din("wv", [D, D])
    WO = din("wo", [D, D])
    RW = din("rw", [D, E])
    W0 = din("w0", [E, D, HR])
    W1 = din("w1", [E, D, HR])
    WOE = din("woe", [E, HR, D])
    IOTA = din("iota", [128, 512])
    IDB = din("idb", [128, 128], BF16)
    IDF = din("idf", [128, 128])
    TRI = din("tri", [128, 128])
    ONESF = din("onesf", [128, 128])
    MASKT = din("maskt", [128, 128])
    ONECOL = din("onecol", [128, 1])
    ONECOLB = din("onecolb", [128, 1], BF16)
    ONEROW = din("onerow", [1, 128])
    E01 = din("e01", [1, 128], BF16)
    E10 = din("e10", [1, 128], BF16)

    Y = nc.dram_tensor("y", [S, D], F32, kind="ExternalOutput").ap()
    NDBG = {"ln1": DC * S, "attn": (NH // 2) * S, "gate": 10 * NT * E,
            "gather": E * DC * 512, "vprime": NT * NH * 65}.get(upto, 1)
    DBG = nc.dram_tensor("dbg", [128, NDBG], F32, kind="ExternalOutput").ap()
    X1S = nc.dram_tensor("x1s", [S, D], F32, kind="Internal").ap()
    XL2S = nc.dram_tensor("xl2s", [S, D], BF16, kind="Internal").ap()
    AVS = nc.dram_tensor("avs", [NH // 2, 128, S], BF16, kind="Internal").ap()
    EOS = nc.dram_tensor("eos", [E, CAP, D], BF16, kind="Internal").ap()

    Xt = X.rearrange("(i p) m -> i p m", p=128)
    X1t = X1S.rearrange("(i p) m -> i p m", p=128)
    XL2t = XL2S.rearrange("(i p) m -> i p m", p=128)
    EOSt = EOS.rearrange("e (k p) m -> e p k m", p=128)

    stack = ExitStack()
    scopes = []
    _ES = ExitStack

    def ExitStackR():
        es = _ES()
        scopes.append(es)
        return es

    def finish():
        for es in reversed(scopes):
            es.close()
        stack.close()

    def sb(es, name, shape, dt):
        return es.enter_context(nc.sbuf_tensor(name, shape, dt))

    def ps(es, name, shape, dt=F32):
        return es.enter_context(nc.psum_tensor(name, shape, dt))

    class Ctr:
        def __init__(self, name):
            self.h = stack.enter_context(nc.semaphore(name))
            self.n = 0

        def bump(self, k=1):
            self.n += k
            return self.n

    dC = Ctr("dC")
    dW = Ctr("dW")
    dEo = Ctr("dEo")
    dY = Ctr("dY")
    eT = Ctr("eT")
    eV = Ctr("eV")
    eS = Ctr("eS")
    eG = Ctr("eG")
    dXi = [Ctr("dXi0"), Ctr("dXi1")]
    dX2 = [Ctr("dX20"), Ctr("dX21")]
    dAv = [Ctr("dAv0"), Ctr("dAv1")]
    dSpX = [Ctr("dSpX0"), Ctr("dSpX1")]
    dSpL = [Ctr("dSpL0"), Ctr("dSpL1")]
    dBr = Ctr("dBr")
    dRc = Ctr("dRc")
    dAt = Ctr("dAt")
    dR0 = Ctr("dR0")
    dR1 = Ctr("dR1")
    dAl = [Ctr("dAl0"), Ctr("dAl1")]
    dAw = [Ctr("dAw0"), Ctr("dAw1")]
    dW0 = [Ctr("dW00"), Ctr("dW01"), Ctr("dW02")]
    dW1 = [Ctr("dW10"), Ctr("dW11"), Ctr("dW12")]
    dWt = [Ctr("dWt0"), Ctr("dWt1")]
    dWo = [Ctr("dWo0"), Ctr("dWo1")]
    dWot = Ctr("dWot")
    dEr = [Ctr("dEr0"), Ctr("dEr1")]
    dGr = [Ctr("dGr0"), Ctr("dGr1")]
    dXr = [Ctr("dXr0"), Ctr("dXr1")]

    gl = stack
    iota_sb = sb(gl, "iota_sb", [128, 512], F32)
    idb_sb = sb(gl, "idb_sb", [128, 128], BF16)
    idf_sb = sb(gl, "idf_sb", [128, 128], F32)
    tri_sb = sb(gl, "tri_sb", [128, 128], F32)
    onesf_sb = sb(gl, "onesf_sb", [128, 128], F32)
    maskt_sb = sb(gl, "maskt_sb", [128, 128], F32)
    onecol_sb = sb(gl, "onecol_sb", [128, 1], F32)
    onecolb_sb = sb(gl, "onecolb_sb", [128, 1], BF16)
    onerow_sb = sb(gl, "onerow_sb", [1, 128], F32)
    e01_sb = sb(gl, "e01_sb", [1, 128], BF16)
    e10_sb = sb(gl, "e10_sb", [1, 128], BF16)
    rw_sb = sb(gl, "rw_sb", [128, DC * E], F32)
    logits = sb(gl, "logits", [128, NT * E], F32)
    m1a = sb(gl, "m1a", [128, NT * E], F32)
    m2a = sb(gl, "m2a", [128, NT * E], F32)
    m1k = sb(gl, "m1k", [128, NT * E], F32)
    m2k = sb(gl, "m2k", [128, NT * E], F32)
    q1a = sb(gl, "q1a", [128, NT * E], F32)
    q2a = sb(gl, "q2a", [128, NT * E], F32)
    p1x = sb(gl, "p1x", [128, NT * E], F32)
    p2x = sb(gl, "p2x", [128, NT * E], F32)
    g1kc = sb(gl, "g1kc", [128, NT], F32)
    g2kc = sb(gl, "g2kc", [128, NT], F32)
    county = sb(gl, "county", [1, E], F32)
    gsc = sb(gl, "gsc", [128, 48], F32)

    block = stack.enter_context(nc.Block())

    def on(eng):
        return getattr(block, eng)

    @on("sync")
    def _(sy):
        for dst, src in [(iota_sb, IOTA), (idb_sb, IDB), (idf_sb, IDF),
                         (tri_sb, TRI), (onesf_sb, ONESF), (maskt_sb, MASKT),
                         (onecol_sb, ONECOL), (onecolb_sb, ONECOLB),
                         (onerow_sb, ONEROW), (e01_sb, E01), (e10_sb, E10)]:
            sy.dma_start(out=dst[:], in_=src).then_inc(dC.h, 16)
        sy.dma_start(out=rw_sb[:],
                     in_=RW.rearrange("(dc p) e -> p dc e", p=128)).then_inc(dC.h, 16)
    CONSTS = dC.bump(16 * 12)

    es_at = ExitStackR()
    es_qk = ExitStackR()
    es_ws = ExitStackR()
    es_wv = ExitStackR()
    es_ln = ExitStackR()
    avtmp = sb(es_at, "avtmp", [64, S], BF16)
    recrow = sb(es_at, "recrow", [65, S], BF16)
    recall = sb(es_at, "recall", [16, S], BF16)
    wqT = sb(es_qk, "wqT", [128, DC * D], BF16)
    wkT = sb(es_qk, "wkT", [128, DC * D], BF16)
    xfm = sb(es_qk, "xfm", [128, DC * S], BF16)
    vall = sb(es_qk, "vall", [128, NT * NH * 65], BF16)
    qbuf = [sb(es_qk, f"qb{i}", [65, S], BF16) for i in range(2)]
    kbuf = [sb(es_qk, f"kb{i}", [65, S], BF16) for i in range(2)]
    ptb = [sb(es_qk, f"ptb{i}", [128, S], BF16) for i in range(2)]
    sqq = sb(es_qk, "sqq", [64, S], BF16)
    nrow = sb(es_qk, "nrow", [1, S + 16], F32)
    brow = sb(es_qk, "brow", [1, S], BF16)
    wstage = sb(es_ws, "wstage", [128, D], BF16)
    wvT = sb(es_wv, "wvT", [128, DC * D], BF16)
    xin = [sb(es_ln, f"xin{i}", [128, D], F32) for i in range(2)]
    xlt = sb(es_ln, "xlt", [128, D], BF16)
    sqd = sb(es_ln, "sqd", [128, D], BF16)
    mcol = sb(es_ln, "mcol", [128, 16], F32)

    es_ptr = ExitStackR()
    ptr = ps(es_ptr, "ptr", [128, 1024], BF16)

    def load_wT(Wsrc, dstT, stage_free_t, ptr_free_v, ptrps, stg):
        Wr = Wsrc.rearrange("(oc p) d -> oc p d", p=128)
        ndma = [dW.bump(16) for _ in range(DC)]
        t_vals = [eT.bump() for _ in range(DC)]
        v_vals = [eV.bump() for _ in range(DC)]

        @on("gpsimd")
        def _(gp):
            for oc in range(DC):
                if oc == 0 and stage_free_t:
                    gp.wait_ge(eT.h, stage_free_t)
                if oc >= 1:
                    gp.wait_ge(eT.h, t_vals[oc - 1])
                gp.dma_start(out=stg[:], in_=Wr[oc]).then_inc(dW.h, 16)

        @on("tensor")
        def _(t):
            t.wait_ge(dC.h, CONSTS)
            for oc in range(DC):
                t.wait_ge(dW.h, ndma[oc])
                if oc == 0 and ptr_free_v:
                    t.wait_ge(eV.h, ptr_free_v)
                if oc >= 1:
                    t.wait_ge(eV.h, v_vals[oc - 1])
                for dc in range(DC):
                    ins = t.transpose(ptrps[:, dc * 128:dc * 128 + 128],
                                      stg[:, dc * 128:dc * 128 + 128],
                                      idb_sb[:])
                ins.then_inc(eT.h, 1)

        @on("vector")
        def _(v):
            for oc in range(DC):
                v.wait_ge(eT.h, t_vals[oc])
                dst = dstT[:].rearrange("p (dc x) -> p dc x", dc=DC)[:, :, oc * 128:oc * 128 + 128]
                src = ptrps[:].rearrange("p (dc x) -> p dc x", dc=DC)
                v.tensor_copy(dst, src).then_inc(eV.h, 1)
        return t_vals[-1], v_vals[-1]

    wv_t, wv_v = load_wT(WV, wvT, 0, 0, ptr, wstage)

    ln_nx = [dXi[i % 2].bump(16) for i in range(NT)]
    ln_mean, ln_var, ln_norm, ln_ev = [], [], [], []
    ln_sq, ln_rstd, ln_tp = [], [], []
    for i in range(NT):
        ln_mean.append(eV.bump())
        ln_var.append(eV.bump())
        ln_norm.append(eV.bump())
        ln_ev.append(eV.bump())
        ln_sq.append(eS.bump())
        ln_rstd.append(eS.bump())
        ln_tp.append(eT.bump())

    @on("sync")
    def _(sy):
        for i in range(NT):
            if i >= 2:
                sy.wait_ge(eS.h, ln_sq[i - 2])
                sy.wait_ge(eV.h, ln_norm[i - 2])
            sy.dma_start(out=xin[i % 2][:], in_=Xt[i]).then_inc(dXi[i % 2].h, 16)

    @on("vector")
    def _(v):
        for i in range(NT):
            b = (i % 2) * 8
            v.wait_ge(dXi[i % 2].h, ln_nx[i])
            v.reduce_sum(mcol[:, b:b + 1], xin[i % 2][:], axis=AX)
            v.drain()
            v.tensor_scalar(mcol[:, b:b + 1], mcol[:, b:b + 1], 1.0 / D, 0.0,
                            ALU.mult, ALU.add).then_inc(eV.h, 1)
            v.wait_ge(eS.h, ln_sq[i])
            v.wait_ge(eV.h, ln_mean[i])
            v.tensor_scalar(mcol[:, b + 2:b + 3], mcol[:, b + 1:b + 2], 1.0 / D, 0.0,
                            ALU.mult, ALU.add)
            v.tensor_scalar(mcol[:, b + 3:b + 4], mcol[:, b:b + 1], mcol[:, b:b + 1],
                            None, ALU.mult)
            v.drain()
            v.tensor_scalar(mcol[:, b + 2:b + 3], mcol[:, b + 2:b + 3],
                            mcol[:, b + 3:b + 4], None, ALU.subtract)
            v.drain()
            v.tensor_scalar(mcol[:, b + 2:b + 3], mcol[:, b + 2:b + 3], 1e-5, 1.0,
                            ALU.add, ALU.mult).then_inc(eV.h, 1)
            v.wait_ge(eS.h, ln_rstd[i])
            if i >= 1:
                v.wait_ge(eT.h, ln_tp[i - 1])
            v.tensor_scalar(xlt[:], xin[i % 2][:], mcol[:, b:b + 1], mcol[:, b + 4:b + 5],
                            ALU.subtract, ALU.mult).then_inc(eV.h, 1)
            v.wait_ge(eT.h, ln_tp[i])
            dst = xfm[:].rearrange("p (dc x) -> p dc x", dc=DC)[:, :, i * 128:i * 128 + 128]
            src = ptr[:].rearrange("p (dc x) -> p dc x", dc=DC)
            v.tensor_copy(dst, src).then_inc(eV.h, 1)

    @on("scalar")
    def _(s):
        for i in range(NT):
            b = (i % 2) * 8
            s.wait_ge(dXi[i % 2].h, ln_nx[i])
            s.activation(sqd[:], xin[i % 2][:], ACT.Square,
                         accum_out=mcol[:, b + 1:b + 2]).then_inc(eS.h, 1)
            s.wait_ge(eV.h, ln_var[i])
            s.activation(mcol[:, b + 3:b + 4], mcol[:, b + 2:b + 3], ACT.Ln)
            s.drain()
            s.activation(mcol[:, b + 4:b + 5], mcol[:, b + 3:b + 4], ACT.Exp,
                         scale=-0.5).then_inc(eS.h, 1)

    @on("tensor")
    def _(t):
        for i in range(NT):
            t.wait_ge(eV.h, ln_norm[i])
            t.wait_ge(eV.h, wv_v if i == 0 else ln_ev[i - 1])
            for dc in range(DC):
                ins = t.transpose(ptr[:, dc * 128:dc * 128 + 128],
                                  xlt[:, dc * 128:dc * 128 + 128], idb_sb[:])
            ins.then_inc(eT.h, 1)

    XFM_V = ln_ev[-1]

    if upto == "ln1":
        FV = dY.bump(16)

        @on("gpsimd")
        def _(gp):
            gp.wait_ge(eV.h, XFM_V)
            gp.dma_start(out=DBG, in_=xfm[:]).then_inc(dY.h, 16)

        @on("sync")
        def _(sy):
            sy.wait_ge(dY.h, FV)
        return nc, finish

    es_vp = ExitStackR()
    pvp = [ps(es_vp, f"pvp{i}", [128, 512]) for i in range(2)]
    VONES = eV.bump()
    vp_t, vp_v = [], []
    for _ in range(NT * 2):
        vp_t.append(eT.bump())
        vp_v.append(eV.bump())

    @on("vector")
    def _(v):
        v.memset(vall[:], 1.0).then_inc(eV.h, 1)

    @on("tensor")
    def _(t):
        t.wait_ge(eV.h, max(wv_v, XFM_V))
        k = 0
        for i in range(NT):
            for h2 in range(2):
                if k >= 2:
                    t.wait_ge(eV.h, vp_v[k - 2])
                for dc in range(DC):
                    ins = t.matmul(pvp[k % 2][:],
                                   xfm[:, dc * S + i * 128: dc * S + i * 128 + 128],
                                   wvT[:, dc * D + h2 * 512: dc * D + h2 * 512 + 512],
                                   start=(dc == 0), stop=(dc == DC - 1))
                ins.then_inc(eT.h, 1)
                k += 1

    @on("vector")
    def _(v):
        v.wait_ge(eV.h, VONES)
        k = 0
        for i in range(NT):
            for h2 in range(2):
                v.wait_ge(eT.h, vp_t[k])
                dst = vall[:, i * NH * 65: (i + 1) * NH * 65]
                dst = dst.rearrange("p (h c) -> p h c", h=NH)[:, h2 * 8:(h2 + 1) * 8, 0:64]
                src = pvp[k % 2][:].rearrange("p (h c) -> p h c", c=64)
                v.tensor_copy(dst, src).then_inc(eV.h, 1)
                k += 1
    es_vp.close()
    es_wv_close_marker = vp_v[-1]

    if upto == "vprime":
        FV = dY.bump(16)

        @on("gpsimd")
        def _(gp):
            gp.wait_ge(eV.h, vp_v[-1])
            gp.dma_start(out=DBG, in_=vall[:]).then_inc(dY.h, 16)

        @on("sync")
        def _(sy):
            sy.wait_ge(dY.h, FV)
        return nc, finish

    wq_t, wq_v = load_wT(WQ, wqT, wv_t, ln_ev[-1], ptr, wstage)
    wk_t, wk_v = load_wT(WK, wkT, wq_t, wq_v, ptr, wstage)
    es_ln.close()
    es_wv.close()
    es_ws.close()
    es_ptr.close()

    es_hd = ExitStackR()
    pqk = [ps(es_hd, f"pqk{i}", [64, 512]) for i in range(2)]
    psc = [ps(es_hd, f"psc{i}", [128, 512]) for i in range(2)]
    pav = ps(es_hd, "pav", [65, S])

    KONES = eG.bump()

    @on("gpsimd")
    def _(gp):
        gp.memset(kbuf[0][64:65, :], -1.0)
        gp.memset(kbuf[1][64:65, :], -1.0).then_inc(eG.h, 1)

    qk_free = [0, 0]
    pav_free_s = 0
    pav_free_v2 = 0
    avtmp_free = 0
    psc_free = [0, 0]
    pqk_free = [("S", 0), ("S", 0)]
    ptb_free = [0, 0]
    rec_free = 0
    hst = {}

    def emit_qk_bound(h):
        nonlocal_state = None
        hp = h % 2
        st = {}
        st["qkt"] = qkt = [eT.bump() for _ in range(8)]
        st["qev"] = qev = [eV.bump() for _ in range(4)]
        st["kev"] = kev = [eV.bump() for _ in range(4)]

        @on("tensor")
        def _(t_, h=h, hp=hp, qw=qk_free[hp], pqw=list(pqk_free),
              qev=qev, kev=kev):
            if h == 0:
                t_.wait_ge(eV.h, wk_v)
            if qw:
                t_.wait_ge(eT.h, qw)
            k = 0
            for wT in (wqT, wkT):
                for c in range(4):
                    if k >= 2:
                        t_.wait_ge(eV.h, (qev + kev)[k - 2])
                    else:
                        kind, val = pqw[k % 2]
                        if val:
                            t_.wait_ge((eS if kind == "S" else eV).h, val)
                    for dc in range(DC):
                        ins = t_.matmul(pqk[k % 2][:],
                                        wT[:, dc * D + h * 64: dc * D + h * 64 + 64],
                                        xfm[:, dc * S + c * 512: dc * S + c * 512 + 512],
                                        start=(dc == 0), stop=(dc == DC - 1))
                    ins.then_inc(eT.h, 1)
                    k += 1

        @on("vector")
        def _(v, hp=hp, qkt=qkt):
            for c in range(4):
                v.wait_ge(eT.h, qkt[c])
                v.tensor_scalar(qbuf[hp][0:64, c * 512:(c + 1) * 512], pqk[c % 2][:],
                                0.125, 0.0, ALU.mult, ALU.add).then_inc(eV.h, 1)
            for c in range(4):
                v.wait_ge(eT.h, qkt[4 + c])
                v.tensor_copy(kbuf[hp][0:64, c * 512:(c + 1) * 512],
                              pqk[c % 2][:]).then_inc(eV.h, 1)

        st["sq_k"] = sq_k = eV.bump()
        st["nmk_v"] = nmk_v = [eV.bump() for _ in range(4)]
        st["mk_done"] = mk_done = eV.bump()
        st["sq_q"] = sq_q = eV.bump()
        st["nmq_v"] = nmq_v = [eV.bump() for _ in range(4)]
        st["qn_done"] = qn_done = eV.bump()
        st["nmk_t"] = nmk_t = [eT.bump() for _ in range(4)]
        st["nmq_t"] = nmq_t = [eT.bump() for _ in range(4)]
        st["bexp_done"] = bexp_done = eS.bump()
        st["brow_dma"] = dBr.bump(16)
        prev_nmq_t = hst[h - 1]["nmq_t"][-1] if h >= 1 else 0
        prev_brow = hst[h - 1]["brow_dma"] if h >= 1 else 0
        prev_bexp = hst[h - 1]["bexp_done"] if h >= 1 else 0

        @on("vector")
        def _(v, hp=hp, kev=kev, nmk_t=nmk_t, nmq_t=nmq_t,
              prev_nmq_t=prev_nmq_t, prev_bexp=prev_bexp):
            v.wait_ge(eV.h, kev[-1])
            if prev_nmq_t:
                v.wait_ge(eT.h, prev_nmq_t)
            v.tensor_tensor(sqq[:], kbuf[hp][0:64, :], kbuf[hp][0:64, :],
                            ALU.mult).then_inc(eV.h, 1)
            if prev_bexp:
                v.wait_ge(eS.h, prev_bexp)
            for c in range(4):
                v.wait_ge(eT.h, nmk_t[c])
                v.tensor_copy(nrow[0:1, c * 512:(c + 1) * 512],
                              pqk[c % 2][0:1, :]).then_inc(eV.h, 1)
            v.drain()
            v.reduce_max(nrow[0:1, S:S + 1], nrow[0:1, 0:S],
                         axis=AX).then_inc(eV.h, 1)
            v.tensor_tensor(sqq[:], qbuf[hp][0:64, :], qbuf[hp][0:64, :],
                            ALU.mult).then_inc(eV.h, 1)
            for c in range(4):
                v.wait_ge(eT.h, nmq_t[c])
                v.tensor_copy(nrow[0:1, c * 512:(c + 1) * 512],
                              pqk[c % 2][0:1, :]).then_inc(eV.h, 1)
            v.drain()
            v.tensor_scalar(nrow[0:1, 0:S], nrow[0:1, 0:S], nrow[0:1, S:S + 1],
                            None, ALU.mult).then_inc(eV.h, 1)

        @on("scalar")
        def _(s, qn_done=qn_done, prev_brow=prev_brow):
            s.wait_ge(eV.h, qn_done)
            if prev_brow:
                s.wait_ge(dBr.h, prev_brow)
            s.activation(brow[0:1, :], nrow[0:1, 0:S], ACT.Sqrt,
                         scale=1.1025).then_inc(eS.h, 1)

        @on("tensor")
        def _(t_, sq_k=sq_k, sq_q=sq_q, nmk_v=nmk_v, nmq_v=nmq_v):
            t_.wait_ge(eV.h, sq_k)
            for c in range(4):
                if c >= 2:
                    t_.wait_ge(eV.h, nmk_v[c - 2])
                t_.matmul(pqk[c % 2][0:1, :], onecolb_sb[0:64, :],
                          sqq[:, c * 512:(c + 1) * 512],
                          start=True, stop=True).then_inc(eT.h, 1)
            t_.wait_ge(eV.h, sq_q)
            for c in range(4):
                t_.wait_ge(eV.h, (nmk_v + nmq_v)[c + 2])
                t_.matmul(pqk[c % 2][0:1, :], onecolb_sb[0:64, :],
                          sqq[:, c * 512:(c + 1) * 512],
                          start=True, stop=True).then_inc(eT.h, 1)

        @on("sync")
        def _(sy, hp=hp, bexp_done=bexp_done):
            sy.wait_ge(eS.h, bexp_done)
            sy.dma_start(out=qbuf[hp][64:65, :], in_=brow[:]).then_inc(dBr.h, 16)

        hst[h] = st
        return [("V", nmq_v[2]), ("V", nmq_v[3])]

    def emit_pass2_sigma(h):
        hp = h % 2
        pr = h // 2
        st = hst[h]
        chunks = []
        for j in range(NT):
            for c in range(j // 4, 4):
                chunks.append((j, c))
        par = {jc: k % 2 for k, jc in enumerate(chunks)}
        ops = [("S", 0)]
        for k in range(1, len(chunks)):
            ops.append(("S", k))
            ops.append(("A", k - 1))
        ops.append(("A", len(chunks) - 1))
        sc_t, av_t, mask_v, exp_s = {}, {}, {}, {}
        for typ, k in ops:
            j, c = chunks[k]
            if typ == "S":
                sc_t[(j, c)] = eT.bump()
                if c == j // 4:
                    mask_v[(j, c)] = eV.bump()
            else:
                av_t[(j, c)] = eT.bump()
        for (j, c) in chunks:
            exp_s[(j, c)] = eS.bump()

        @on("tensor")
        def _(t_, h=h, hp=hp, brow_dma=st["brow_dma"], chunks=chunks,
              exp_s=exp_s, psc_w=list(psc_free), pav_s=pav_free_s,
              pav_v2=pav_free_v2, av_free=avtmp_free, rec_f=rec_free):
            t_.wait_ge(dBr.h, brow_dma)
            if h == 0:
                t_.wait_ge(eG.h, KONES)
                t_.wait_ge(eV.h, vp_v[-1])
            if pav_s:
                t_.wait_ge(eS.h, pav_s)
            if pav_v2:
                t_.wait_ge(eV.h, pav_v2)
            if av_free:
                t_.wait_ge(dAt.h, av_free)
            if rec_f:
                t_.wait_ge(dRc.h, rec_f)
            for typ, k in ops:
                j, c = chunks[k]
                off = j * 128 - c * 512 if c == j // 4 else 0
                w = 512 - off
                q0 = c * 512 + off
                if typ == "S":
                    if k >= 2:
                        t_.wait_ge(eS.h, exp_s[chunks[k - 2]])
                    elif psc_w[k % 2]:
                        t_.wait_ge(eS.h, psc_w[k % 2])
                    t_.matmul(psc[k % 2][:, 0:w], kbuf[hp][:, j * 128:(j + 1) * 128],
                              qbuf[hp][:, q0:q0 + w],
                              start=True, stop=True).then_inc(eT.h, 1)
                else:
                    t_.wait_ge(eS.h, exp_s[(j, c)])
                    t_.matmul(pav[0:65, q0:q0 + w],
                              vall[:, j * NH * 65 + h * 65: j * NH * 65 + (h + 1) * 65],
                              ptb[j % 2][:, q0:q0 + w],
                              start=(j == 0), stop=(j == min(15, 4 * c + 3))
                              ).then_inc(eT.h, 1)

        @on("vector")
        def _(v, sc_t=sc_t, par=par):
            for j in range(NT):
                c = j // 4
                v.wait_ge(eT.h, sc_t[(j, c)])
                p = psc[par[(j, c)]]
                v.tensor_tensor(p[:, 0:128], p[:, 0:128], maskt_sb[:],
                                ALU.add).then_inc(eV.h, 1)

        @on("scalar")
        def _(s, chunks=chunks, par=par, sc_t=sc_t, mask_v=mask_v,
              av_t=av_t, ptb_w=list(ptb_free)):
            for j in range(NT):
                if j >= 2:
                    s.wait_ge(eT.h, av_t[(j - 2, 3)])
                elif ptb_w[j % 2]:
                    s.wait_ge(eT.h, ptb_w[j % 2])
                for c in range(j // 4, 4):
                    off = j * 128 - c * 512 if c == j // 4 else 0
                    w = 512 - off
                    q0 = c * 512 + off
                    s.wait_ge(eT.h, sc_t[(j, c)])
                    if c == j // 4:
                        s.wait_ge(eV.h, mask_v[(j, c)])
                    s.activation(ptb[j % 2][:, q0:q0 + w], psc[par[(j, c)]][:, 0:w],
                                 ACT.Exp).then_inc(eS.h, 1)

        sig_evac = eS.bump()
        av_evac = eV.bump()

        @on("scalar")
        def _(s, av_t=av_t, rec_f=rec_free):
            s.wait_ge(eT.h, av_t[(15, 3)])
            if rec_f:
                s.wait_ge(dRc.h, rec_f)
            s.copy(recrow[64:65, :], pav[64:65, :]).then_inc(eS.h, 1)

        @on("vector")
        def _(v, av_t=av_t, av_free=avtmp_free):
            v.wait_ge(eT.h, av_t[(15, 3)])
            if av_free:
                v.wait_ge(dAt.h, av_free)
            v.tensor_copy(avtmp[:], pav[0:64, :]).then_inc(eV.h, 1)

        @on("sync")
        def _(sy, h=h, sig_evac=sig_evac):
            sy.wait_ge(eS.h, sig_evac)
            sy.dma_start(out=recall[h:h + 1, :], in_=recrow[64:65, :]).then_inc(dRc.h, 16)

        @on("sync")
        def _(sy, hp=hp, pr=pr, av_evac=av_evac):
            sy.wait_ge(eV.h, av_evac)
            sy.dma_start(out=AVS[pr, hp * 64:hp * 64 + 64, :],
                         in_=avtmp[:]).then_inc(dAt.h, 16)

        pf = [0, 0]
        for (j, c) in chunks:
            pf[par[(j, c)]] = exp_s[(j, c)]
        return (pf, [av_t[(14, 3)], av_t[(15, 3)]], av_t[(15, 3)],
                sig_evac, av_evac, dRc.bump(16), dAt.bump(16))

    for h in range(nhrun + 1):
        if h >= 1:
            (psc_free, ptb_free, qk_free[(h - 1) % 2], pav_free_s,
             pav_free_v2, rec_free, avtmp_free) = emit_pass2_sigma(h - 1)
        if h < nhrun:
            pqk_free = emit_qk_bound(h)

    SIG_S = pav_free_s
    HEADS_LAST_T = max(qk_free)
    es_hd.close()
    es_qk.close()

    es_wo = ExitStackR()
    woT = sb(es_wo, "woT", [128, DC * D], BF16)
    es_ln2 = ExitStackR()
    xin2 = [sb(es_ln2, f"xin2{i}", [128, D], F32) for i in range(2)]
    x1b = [sb(es_ln2, f"x1b{i}", [128, D], F32) for i in range(2)]
    xl2f = sb(es_ln2, "xl2f", [128, D], F32)
    xfm32 = sb(es_ln2, "xfm32", [128, D], F32)
    sqd2 = sb(es_ln2, "sqd2", [128, D], BF16)
    mcol2 = sb(es_ln2, "mcol2", [128, 16], F32)
    xl2b = [sb(es_ln2, f"xl2b{i}", [128, D], BF16) for i in range(2)]
    avo = [sb(es_ln2, f"avo{i}", [128, D], BF16) for i in range(2)]
    es_ws2 = ExitStackR()
    wstage2 = sb(es_ws2, "wstage2", [128, D], BF16)
    es_rec = ExitStackR()
    rec0 = sb(es_rec, "rec0", [1, S], BF16)
    rec1 = sb(es_rec, "rec1", [1, S], BF16)
    avsc = [sb(es_rec, f"avsc{i}", [128, S], BF16) for i in range(2)]
    rrecb = sb(es_rec, "rrecb", [16, S], F32)

    es_sg = ExitStackR()
    psb = [ps(es_sg, f"psb{i}", [128, 512]) for i in range(2)]

    RREC_V = eV.bump()
    sg_r0, sg_r1, sg_ld, sg_wb = [], [], [], []
    sg_t = []
    sg_v = []
    for pr in range(nhrun // 2):
        sg_r0.append(dR0.bump(16))
        sg_r1.append(dR1.bump(16))
        sg_ld.append(dAl[pr % 2].bump(16))
        for c in range(4):
            sg_t.append(eT.bump())
            sg_v.append(eV.bump())
        sg_wb.append(dAw[pr % 2].bump(16))

    @on("vector")
    def _(v):
        v.wait_ge(eS.h, SIG_S)
        v.wait_ge(dRc.h, dRc.n)
        v.reciprocal(rrecb[:], recall[:]).then_inc(eV.h, 1)

    @on("gpsimd")
    def _(gp):
        gp.wait_ge(eV.h, RREC_V)
        for pr in range(nhrun // 2):
            if pr >= 1:
                gp.wait_ge(eT.h, sg_t[pr * 4 - 1])
            gp.dma_start(out=rec0[:], in_=rrecb[2 * pr:2 * pr + 1, :]).then_inc(dR0.h, 16)
            gp.dma_start(out=rec1[:], in_=rrecb[2 * pr + 1:2 * pr + 2, :]).then_inc(dR1.h, 16)

    @on("sync")
    def _(sy):
        sy.wait_ge(dBr.h, dBr.n)
        sy.wait_ge(dAt.h, dAt.n)
        for pr in range(nhrun // 2):
            if pr >= 2:
                sy.wait_ge(dAw[pr % 2].h, sg_wb[pr - 2])
            sy.dma_start(out=avsc[pr % 2][:], in_=AVS[pr]).then_inc(dAl[pr % 2].h, 16)
            sy.wait_ge(eV.h, sg_v[pr * 4 + 3])
            sy.dma_start(out=AVS[pr], in_=avsc[pr % 2][:]).then_inc(dAw[pr % 2].h, 16)

    @on("tensor")
    def _(t_):
        k = 0
        for pr in range(nhrun // 2):
            t_.wait_ge(dR0.h, sg_r0[pr])
            t_.wait_ge(dR1.h, sg_r1[pr])
            for c in range(4):
                if k >= 2:
                    t_.wait_ge(eV.h, sg_v[k - 2])
                t_.matmul(psb[k % 2][:], e01_sb[:], rec0[0:1, c * 512:(c + 1) * 512],
                          start=True, stop=False)
                t_.matmul(psb[k % 2][:], e10_sb[:], rec1[0:1, c * 512:(c + 1) * 512],
                          start=False, stop=True).then_inc(eT.h, 1)
                k += 1

    @on("vector")
    def _(v):
        k = 0
        for pr in range(nhrun // 2):
            v.wait_ge(dAl[pr % 2].h, sg_ld[pr])
            for c in range(4):
                v.wait_ge(eT.h, sg_t[k])
                sl = avsc[pr % 2][:, c * 512:(c + 1) * 512]
                v.tensor_tensor(sl, sl, psb[k % 2][:], ALU.mult).then_inc(eV.h, 1)
                k += 1
    SGV = eV.n
    es_sg.close()

    if upto == "attn":
        FV = dY.bump(16)

        @on("gpsimd")
        def _(gp):
            gp.wait_ge(dAw[0].h, dAw[0].n)
            gp.wait_ge(dAw[1].h, dAw[1].n)
            gp.dma_start(out=DBG[:].rearrange("p (r q) -> p r q", r=NH // 2),
                         in_=AVS.rearrange("r p q -> p r q")).then_inc(dY.h, 16)

        @on("sync")
        def _(sy):
            sy.wait_ge(dY.h, FV)
        return nc, finish

    es_rec.close()
    es_ptr2 = ExitStackR()
    ptr2 = ps(es_ptr2, "ptr2", [128, 1024], BF16)
    wo_t, wo_v = load_wT(WO, woT, HEADS_LAST_T, sg_v[-1], ptr2, wstage2)
    es_ptr2.close()
    es_ws2.close()

    es_op = ExitStackR()
    pop = [ps(es_op, f"pop{i}", [128, 512]) for i in range(2)]
    prt = ps(es_op, "prt", [128, 1024])
    plg = ps(es_op, "plg", [128, 512])

    op_nx, op_t, op_add, op_var, op_x2, op_tp = [], [], [], [], [], []
    op_ev32, op_lg, op_lgv, op_sq, op_rstd = [], [], [], [], []
    gt_m, gt_s, gt_v, op_spill, op_l2sp = [], [], [], [], []
    op_av = []
    for i in range(NT):
        op_nx.append(dX2[i % 2].bump(16))
        op_av.append(dAv[i % 2].bump(16))
        op_t.append([eT.bump(), eT.bump()])
        op_add.append(eV.bump())
        op_var.append(eV.bump())
        op_x2.append(eV.bump())
        op_tp.append(eT.bump())
        op_ev32.append(eV.bump())
        op_lg.append(eT.bump())
        op_lgv.append(eV.bump())
        op_sq.append(eS.bump())
        op_rstd.append(eS.bump())
        op_spill.append(dSpX[i % 2].bump(16))
        op_l2sp.append(dSpL[i % 2].bump(16))
    for i in range(NT):
        gt_m.append(eV.bump())
        gt_v.append(eV.bump())
        gt_s.append(eS.bump())

    @on("sync")
    def _(sy):
        sy.wait_ge(dAw[0].h, dAw[0].n)
        sy.wait_ge(dAw[1].h, dAw[1].n)
        for i in range(NT):
            if i >= 2:
                sy.wait_ge(eV.h, op_add[i - 2])
                sy.wait_ge(eT.h, op_t[i - 2][1])
            sy.dma_start(out=avo[i % 2][:],
                         in_=AVS.rearrange("r p q -> p r q")[:, :, i * 128:(i + 1) * 128]
                         ).then_inc(dAv[i % 2].h, 16)
            sy.dma_start(out=xin2[i % 2][:], in_=Xt[i]).then_inc(dX2[i % 2].h, 16)
            sy.wait_ge(eV.h, op_x2[i])
            sy.dma_start(out=X1t[i], in_=x1b[i % 2][:]).then_inc(dSpX[i % 2].h, 16)
            sy.dma_start(out=XL2t[i], in_=xl2b[i % 2][:]).then_inc(dSpL[i % 2].h, 16)

    @on("tensor")
    def _(t_):
        t_.wait_ge(eV.h, wo_v)
        for i in range(NT):
            t_.wait_ge(dAv[i % 2].h, op_av[i])
            for dh in range(2):
                if i >= 1:
                    t_.wait_ge(eV.h, op_add[i - 1])
                for tc in range(DC):
                    ins = t_.matmul(pop[dh][:],
                                    avo[i % 2][:, tc * 128:(tc + 1) * 128],
                                    woT[:, tc * D + dh * 512: tc * D + dh * 512 + 512],
                                    start=(tc == 0), stop=(tc == DC - 1))
                ins.then_inc(eT.h, 1)
            t_.wait_ge(eV.h, op_x2[i])
            if i >= 1:
                t_.wait_ge(eV.h, op_ev32[i - 1])
            for dc in range(DC):
                ins = t_.transpose(prt[:, (dc % 8) * 128:(dc % 8) * 128 + 128],
                                   xl2f[:, dc * 128:dc * 128 + 128], idf_sb[:])
            ins.then_inc(eT.h, 1)
            t_.wait_ge(eV.h, op_ev32[i])
            if i >= 1:
                t_.wait_ge(eV.h, op_lgv[i - 1])
            for dc in range(DC):
                ins = t_.matmul(plg[:, 0:E], xfm32[:, dc * 128:dc * 128 + 128],
                                rw_sb[:, dc * E:(dc + 1) * E],
                                start=(dc == 0), stop=(dc == DC - 1))
            ins.then_inc(eT.h, 1)

    @on("vector")
    def _(v):
        for i in range(NT):
            b = (i % 2) * 8
            v.wait_ge(dX2[i % 2].h, op_nx[i])
            if i >= 2:
                v.wait_ge(eS.h, op_sq[i - 2])
                v.wait_ge(dSpX[i % 2].h, op_spill[i - 2])
            for dh in range(2):
                v.wait_ge(eT.h, op_t[i][dh])
                v.tensor_tensor(x1b[i % 2][:, dh * 512:(dh + 1) * 512],
                                xin2[i % 2][:, dh * 512:(dh + 1) * 512],
                                pop[dh][:], ALU.add)
            v.drain()
            v.reduce_sum(mcol2[:, b:b + 1], x1b[i % 2][:], axis=AX)
            v.drain()
            v.tensor_scalar(mcol2[:, b:b + 1], mcol2[:, b:b + 1], 1.0 / D, 0.0,
                            ALU.mult, ALU.add).then_inc(eV.h, 1)
            v.wait_ge(eS.h, op_sq[i])
            v.wait_ge(eV.h, op_add[i])
            v.tensor_scalar(mcol2[:, b + 2:b + 3], mcol2[:, b + 1:b + 2], 1.0 / D, 0.0,
                            ALU.mult, ALU.add)
            v.tensor_scalar(mcol2[:, b + 3:b + 4], mcol2[:, b:b + 1], mcol2[:, b:b + 1],
                            None, ALU.mult)
            v.drain()
            v.tensor_scalar(mcol2[:, b + 2:b + 3], mcol2[:, b + 2:b + 3],
                            mcol2[:, b + 3:b + 4], None, ALU.subtract)
            v.drain()
            v.tensor_scalar(mcol2[:, b + 2:b + 3], mcol2[:, b + 2:b + 3], 1e-5, 1.0,
                            ALU.add, ALU.mult).then_inc(eV.h, 1)
            v.wait_ge(eS.h, op_rstd[i])
            v.tensor_scalar(xl2f[:], x1b[i % 2][:], mcol2[:, b:b + 1], mcol2[:, b + 4:b + 5],
                            ALU.subtract, ALU.mult)
            if i >= 2:
                v.wait_ge(dSpL[i % 2].h, op_l2sp[i - 2])
            v.drain()
            v.tensor_copy(xl2b[i % 2][:], xl2f[:]).then_inc(eV.h, 1)
            v.wait_ge(eT.h, op_tp[i])
            v.tensor_copy(xfm32[:], prt[:]).then_inc(eV.h, 1)
            v.wait_ge(eT.h, op_lg[i])
            v.tensor_copy(logits[:, i * E:(i + 1) * E], plg[:, 0:E]).then_inc(eV.h, 1)


    @on("scalar")
    def _(s):
        for i in range(NT):
            b = (i % 2) * 8
            s.wait_ge(eV.h, op_add[i])
            s.activation(sqd2[:], x1b[i % 2][:], ACT.Square,
                         accum_out=mcol2[:, b + 1:b + 2]).then_inc(eS.h, 1)
            s.wait_ge(eV.h, op_var[i])
            s.activation(mcol2[:, b + 3:b + 4], mcol2[:, b + 2:b + 3], ACT.Ln)
            s.drain()
            s.activation(mcol2[:, b + 4:b + 5], mcol2[:, b + 3:b + 4], ACT.Exp,
                         scale=-0.5).then_inc(eS.h, 1)


    @on("vector")
    def _(v):
        for i in range(NT):
                L = logits[:, i * E:(i + 1) * E]
                if i >= 1:
                    v.wait_ge(eS.h, gt_s[i - 1])
                v.wait_ge(eV.h, op_lgv[i])
                v.drain()
                v.reduce_max(gsc[:, 0:1], L, axis=AX)
                v.drain()
                v.tensor_scalar(gsc[:, 1:2], gsc[:, 0:1], -1.0, 0.0,
                                ALU.mult, ALU.add).then_inc(eV.h, 1)
                v.wait_ge(eS.h, gt_s[i])
                v.reciprocal(gsc[:, 4:5], gsc[:, 3:4])
                v.tensor_scalar(gsc[:, 8:16], L, gsc[:, 0:1], None, ALU.is_equal)
                v.drain()
                v.tensor_tensor_scan(gsc[:, 16:24], gsc[:, 8:16], gsc[:, 8:16],
                                     0.0, ALU.add, ALU.bypass)
                v.drain()
                v.tensor_sub(gsc[:, 16:24], gsc[:, 16:24], gsc[:, 8:16])
                v.drain()
                v.tensor_scalar(gsc[:, 16:24], gsc[:, 16:24], 0.0, None, ALU.is_equal)
                v.drain()
                v.tensor_tensor(m1a[:, i * E:(i + 1) * E], gsc[:, 8:16], gsc[:, 16:24], ALU.mult)
                v.drain()
                v.tensor_tensor(gsc[:, 8:16], gsc[:, 24:32], m1a[:, i * E:(i + 1) * E], ALU.mult)
                v.drain()
                v.reduce_sum(gsc[:, 5:6], gsc[:, 8:16], axis=AX)
                v.drain()
                v.tensor_scalar(g1kc[:, i:i + 1], gsc[:, 5:6], gsc[:, 4:5], None, ALU.mult)
                v.scalar_tensor_tensor(gsc[:, 8:16], m1a[:, i * E:(i + 1) * E], -1e9, L,
                                       ALU.mult, ALU.add)
                v.drain()
                v.reduce_max(gsc[:, 6:7], gsc[:, 8:16], axis=AX)
                v.drain()
                v.tensor_scalar(gsc[:, 8:16], gsc[:, 8:16], gsc[:, 6:7], None, ALU.is_equal)
                v.drain()
                v.tensor_tensor_scan(gsc[:, 16:24], gsc[:, 8:16], gsc[:, 8:16],
                                     0.0, ALU.add, ALU.bypass)
                v.drain()
                v.tensor_sub(gsc[:, 16:24], gsc[:, 16:24], gsc[:, 8:16])
                v.drain()
                v.tensor_scalar(gsc[:, 16:24], gsc[:, 16:24], 0.0, None, ALU.is_equal)
                v.drain()
                v.tensor_tensor(m2a[:, i * E:(i + 1) * E], gsc[:, 8:16], gsc[:, 16:24], ALU.mult)
                v.drain()
                v.tensor_tensor(gsc[:, 8:16], gsc[:, 24:32], m2a[:, i * E:(i + 1) * E], ALU.mult)
                v.drain()
                v.reduce_sum(gsc[:, 5:6], gsc[:, 8:16], axis=AX)
                v.drain()
                v.tensor_scalar(g2kc[:, i:i + 1], gsc[:, 5:6], gsc[:, 4:5], None,
                                ALU.mult).then_inc(eV.h, 1)

    @on("scalar")
    def _(s):
        for i in range(NT):
            s.wait_ge(eV.h, gt_m[i])
            s.activation(gsc[:, 24:32], logits[:, i * E:(i + 1) * E], ACT.Exp,
                         bias=gsc[:, 1:2], accum_out=gsc[:, 3:4]).then_inc(eS.h, 1)

    es_op.close()
    es_ln2.close()
    es_wo.close()
    es_at.close()

    es_cs = ExitStackR()
    pcs = ps(es_cs, "pcs", [128, 512])
    pcnt = ps(es_cs, "pcnt", [1, 512])

    cs1_t = [eT.bump() for _ in range(NT)]
    cs1_v = [eV.bump() for _ in range(NT)]
    cnt_t = eT.bump()
    cnt_s = eS.bump()
    cs2_t = [eT.bump() for _ in range(NT)]
    cs2_v = [eV.bump() for _ in range(NT)]
    GATE_V = eV.bump()

    @on("tensor")
    def _(t_):
        t_.wait_ge(eV.h, gt_v[-1])
        for i in range(NT):
            if i >= 1:
                t_.wait_ge(eV.h, cs1_v[i - 1])
            for j in range(i + 1):
                ins = t_.matmul(pcs[:, 0:E],
                                (tri_sb if j == i else onesf_sb)[:],
                                m1a[:, j * E:(j + 1) * E],
                                start=(j == 0), stop=(j == i))
            ins.then_inc(eT.h, 1)
        t_.wait_ge(eV.h, cs1_v[-1])
        for i in range(NT):
            ins = t_.matmul(pcnt[:, 0:E], onecol_sb[:], m1k[:, i * E:(i + 1) * E],
                            start=(i == 0), stop=(i == NT - 1))
        ins.then_inc(eT.h, 1)
        t_.wait_ge(eS.h, cnt_s)
        for i in range(NT):
            if i >= 1:
                t_.wait_ge(eV.h, cs2_v[i - 1])
            for j in range(i + 1):
                t_.matmul(pcs[:, 0:E],
                          (tri_sb if j == i else onesf_sb)[:],
                          m2a[:, j * E:(j + 1) * E],
                          start=(j == 0), stop=False)
            ins = t_.matmul(pcs[:, 0:E], onerow_sb[:], county[:],
                            start=False, stop=True)
            ins.then_inc(eT.h, 1)

    @on("scalar")
    def _(s):
        s.wait_ge(eT.h, cnt_t)
        s.copy(county[:], pcnt[0:1, 0:E]).then_inc(eS.h, 1)

    @on("vector")
    def _(v):
        v.wait_ge(eV.h, gt_v[-1])
        for i in range(NT):
            v.wait_ge(eT.h, cs1_t[i])
            sl = slice(i * E, (i + 1) * E)
            v.tensor_copy(p1x[:, sl], pcs[:, 0:E])
            v.drain()
            v.tensor_scalar(gsc[:, 8:16], p1x[:, sl], float(CAP), None, ALU.is_lt)
            v.drain()
            v.tensor_tensor(m1k[:, sl], m1a[:, sl], gsc[:, 8:16], ALU.mult)
            v.drain()
            v.scalar_tensor_tensor(gsc[:, 8:16], p1x[:, sl], 1.0, m1k[:, sl],
                                   ALU.add, ALU.mult)
            v.drain()
            v.tensor_scalar(q1a[:, sl], gsc[:, 8:16], 1.0, 1.0,
                            ALU.subtract, ALU.mult).then_inc(eV.h, 1)
        for i in range(NT):
            v.wait_ge(eT.h, cs2_t[i])
            sl = slice(i * E, (i + 1) * E)
            v.tensor_copy(p2x[:, sl], pcs[:, 0:E])
            v.drain()
            v.tensor_scalar(gsc[:, 8:16], p2x[:, sl], float(CAP), None, ALU.is_lt)
            v.drain()
            v.tensor_tensor(m2k[:, sl], m2a[:, sl], gsc[:, 8:16], ALU.mult)
            v.drain()
            v.scalar_tensor_tensor(gsc[:, 8:16], p2x[:, sl], 1.0, m2k[:, sl],
                                   ALU.add, ALU.mult)
            v.drain()
            v.tensor_scalar(q2a[:, sl], gsc[:, 8:16], 1.0, 1.0,
                            ALU.subtract, ALU.mult).then_inc(eV.h, 1)
        for i in range(NT):
            sl = slice(i * E, (i + 1) * E)
            v.reduce_sum(gsc[:, 0:1], m1k[:, sl], axis=AX)
            v.drain()
            v.tensor_scalar(g1kc[:, i:i + 1], g1kc[:, i:i + 1], gsc[:, 0:1],
                            None, ALU.mult)
            v.reduce_sum(gsc[:, 1:2], m2k[:, sl], axis=AX)
            v.drain()
            ins = v.tensor_scalar(g2kc[:, i:i + 1], g2kc[:, i:i + 1], gsc[:, 1:2],
                                  None, ALU.mult)
        ins.then_inc(eV.h, 1)
    es_cs.close()

    QCA_V = eV.bump()

    @on("vector")
    def _(v):
        v.wait_ge(eV.h, GATE_V)
        v.tensor_tensor(p1x[:], q1a[:], q2a[:], ALU.max)
        v.tensor_scalar(m1k[:], q1a[:], -0.5, None, ALU.is_gt)
        v.tensor_scalar(m2k[:], q2a[:], -0.5, None, ALU.is_gt)
        v.drain()
        for i in range(NT):
            sl = slice(i * E, (i + 1) * E)
            v.tensor_scalar(m1k[:, sl], m1k[:, sl], g1kc[:, i:i + 1], None, ALU.mult)
            v.tensor_scalar(m2k[:, sl], m2k[:, sl], g2kc[:, i:i + 1], None, ALU.mult)
        v.drain()
        ins = v.tensor_tensor(p2x[:], m1k[:], m2k[:], ALU.add)
        ins.then_inc(eV.h, 1)

    if upto == "gate":
        FV = dY.bump(16 * 11)

        @on("gpsimd")
        def _(gp):
            gp.wait_ge(eV.h, GATE_V)
            for k, arr in enumerate([logits, m1a, m2a, m1k, m2k, q1a, q2a, p1x, p2x]):
                gp.dma_start(out=DBG[:, k * NT * E:(k + 1) * NT * E],
                             in_=arr[:]).then_inc(dY.h, 16)
            gp.dma_start(out=DBG[:, 9 * NT * E: 9 * NT * E + NT],
                         in_=g1kc[:]).then_inc(dY.h, 16)
            gp.dma_start(out=DBG[:, 9 * NT * E + NT: 9 * NT * E + 2 * NT],
                         in_=g2kc[:]).then_inc(dY.h, 16)

        @on("sync")
        def _(sy):
            sy.wait_ge(dY.h, FV)
        return nc, finish

    es_ga = ExitStackR()
    pg = [ps(es_ga, f"pg{mc}", [128, 512]) for mc in range(DC)]
    es_ffn = ExitStackR()
    xe = sb(es_ffn, "xe", [128, E * DC * 512], BF16)
    dtile = [sb(es_ga, f"dtile{i}", [128, 512], BF16) for i in range(2)]
    xl2g = [sb(es_ga, f"xl2g{i}", [128, D], BF16) for i in range(2)]

    ga_bv, ga_t, ga_ev, ga_rl = {}, {}, {}, {}
    for e in range(E):
        for i in range(NT):
            ga_bv[(e, i)] = eV.bump()
            ga_t[(e, i)] = eT.bump()
            ga_rl[(e, i)] = dGr[(e * NT + i) % 2].bump(16)
        for mc in range(DC):
            ga_ev[(e, mc)] = eV.bump()

    @on("sync")
    def _(sy):
        for q in range(2):
            sy.wait_ge(dSpX[q].h, dSpX[q].n)
            sy.wait_ge(dSpL[q].h, dSpL[q].n)
        for e in range(E):
            for i in range(NT):
                k = e * NT + i
                if k >= 2:
                    sy.wait_ge(eT.h, ga_t[((k - 2) // NT, (k - 2) % NT)])
                sy.dma_start(out=xl2g[k % 2][:], in_=XL2t[i]).then_inc(dGr[k % 2].h, 16)

    @on("vector")
    def _(v):
        v.wait_ge(eV.h, QCA_V)
        for e in range(E):
            for i in range(NT):
                k = e * NT + i
                if k >= 2:
                    v.wait_ge(eT.h, ga_t[((k - 2) // NT, (k - 2) % NT)])
                v.tensor_scalar(dtile[k % 2][:], iota_sb[:],
                                p1x[:, i * E + e:i * E + e + 1], None,
                                ALU.is_equal).then_inc(eV.h, 1)
            for mc in range(DC):
                v.wait_ge(eT.h, ga_t[(e, NT - 1)])
                v.tensor_copy(xe[:, e * DC * 512 + mc * 512:(e * DC + mc + 1) * 512],
                              pg[mc][:]).then_inc(eV.h, 1)

    @on("tensor")
    def _(t_):
        for e in range(E):
            for i in range(NT):
                t_.wait_ge(eV.h, ga_bv[(e, i)])
                if i == 0 and e >= 1:
                    for mc in range(DC):
                        t_.wait_ge(eV.h, ga_ev[(e - 1, mc)])
                t_.wait_ge(dGr[(e * NT + i) % 2].h, ga_rl[(e, i)])
                for mc in range(DC):
                    ins = t_.matmul(pg[mc][:],
                                    xl2g[(e * NT + i) % 2][:, mc * 128:(mc + 1) * 128],
                                    dtile[(e * NT + i) % 2][:],
                                    start=(i == 0), stop=(i == NT - 1))
                ins.then_inc(eT.h, 1)
    GATHER_V = eV.n
    es_ga.close()

    if upto == "gather":
        FV = dY.bump(16)

        @on("gpsimd")
        def _(gp):
            gp.wait_ge(eV.h, GATHER_V)
            gp.dma_start(out=DBG, in_=xe[:]).then_inc(dY.h, 16)

        @on("sync")
        def _(sy):
            sy.wait_ge(dY.h, FV)
        return nc, finish

    h_fm = sb(es_ffn, "h_fm", [128, NHC * 512], BF16)
    w0b = [sb(es_ffn, f"w0b{i}", [128, DC * 512], BF16) for i in range(3)]
    w1b = [sb(es_ffn, f"w1b{i}", [128, DC * 512], BF16) for i in range(3)]
    w0tl = sb(es_ffn, "w0tl", [128, DC * 384], BF16)
    w1tl = sb(es_ffn, "w1tl", [128, DC * 384], BF16)
    wob = [sb(es_ffn, f"wob{i}", [128, 2 * D], BF16) for i in range(2)]
    wotl = sb(es_ffn, "wotl", [128, D], BF16)
    gtmp = [sb(es_ffn, f"gtmp{i}", [128, 512], F32) for i in range(2)]
    eob = sb(es_ffn, "eob", [128, 4 * D], BF16)
    es_px = ExitStackR()
    pall = [ps(es_px, f"pall{i}", [128, 512]) for i in range(8)]

    ZTAILS = eV.bump()

    @on("vector")
    def _(v):
        v.memset(w0tl[:], 0.0)
        v.memset(w1tl[:], 0.0)
        v.memset(wotl[:], 0.0).then_inc(eV.h, 1)

    w01_dma, wo_dma = {}, {}
    hmm_t, gel_s, hfm_v, wo_t = {}, {}, {}, {}
    eo_s, eo_d = {}, {}
    for e in range(E):
        for lb in range(NW01 + 1):
            if lb < NW01:
                k = e * NW01 + lb
                w01_dma[(e, lb)] = (dW0[k % 3].bump(16), dW1[k % 3].bump(16))
            else:
                w01_dma[(e, lb)] = (dWt[0].bump(16), dWt[1].bump(16))
        for pb in range(WO_PAIRS + 1):
            if pb < WO_PAIRS:
                k = e * WO_PAIRS + pb
                wo_dma[(e, pb)] = dWo[k % 2].bump(16)
            else:
                wo_dma[(e, pb)] = dWot.bump(16)
        for hc in range(NHC):
            hmm_t[(e, hc)] = eT.bump(2)
            gel_s[(e, hc)] = eS.bump()
            hfm_v[(e, hc)] = eV.bump()
        for hc in range(NHC):
            wo_t[(e, hc)] = eT.bump()
        eo_s[e] = eS.bump(8)
        eo_d[e] = dEo.bump(16)

    W0r = W0.rearrange("e (mc p) h -> e p mc h", p=128)
    W1r = W1.rearrange("e (mc p) h -> e p mc h", p=128)
    WOr = WOE[:, 0:5376, :].rearrange("e (hc p) m -> e p hc m", p=128)

    @on("gpsimd")
    def _(gp):
        for q in range(2):
            gp.wait_ge(dSpX[q].h, dSpX[q].n)
            gp.wait_ge(dSpL[q].h, dSpL[q].n)
        gp.wait_ge(eV.h, ZTAILS)
        for e in range(E):
            for lb in range(NW01 + 1):
                if lb < NW01:
                    pty = (e * NW01 + lb) % 3
                    kk = e * NW01 + lb - 3
                    if kk >= 0:
                        pe, plb = kk // NW01, kk % NW01
                        gp.wait_ge(eT.h, hmm_t[(pe, plb * 4 + 3)])
                    gp.dma_start(out=w0b[pty][:],
                                 in_=W0r[e, :, :, lb * 512:(lb + 1) * 512]
                                 ).then_inc(dW0[pty].h, 16)
                    gp.dma_start(out=w1b[pty][:],
                                 in_=W1r[e, :, :, lb * 512:(lb + 1) * 512]
                                 ).then_inc(dW1[pty].h, 16)
                else:
                    if e >= 1:
                        gp.wait_ge(eT.h, hmm_t[(e - 1, NHC - 1)])
                    tl0 = w0tl[:].rearrange("p (mc h) -> p mc h", mc=DC)[:, :, 0:344]
                    tl1 = w1tl[:].rearrange("p (mc h) -> p mc h", mc=DC)[:, :, 0:344]
                    gp.dma_start(out=tl0, in_=W0r[e, :, :, 5120:HR]).then_inc(dWt[0].h, 16)
                    gp.dma_start(out=tl1, in_=W1r[e, :, :, 5120:HR]).then_inc(dWt[1].h, 16)
            for pb in range(WO_PAIRS + 1):
                if pb < WO_PAIRS:
                    kk = e * WO_PAIRS + pb - 2
                    pty = (e * WO_PAIRS + pb) % 2
                    if kk >= 0:
                        pe, ppb = kk // WO_PAIRS, kk % WO_PAIRS
                        gp.wait_ge(eT.h, wo_t[(pe, ppb * 2 + 1)])
                    gp.dma_start(out=wob[pty][:],
                                 in_=WOr[e, :, 2 * pb:2 * pb + 2, :]).then_inc(dWo[pty].h, 16)
                else:
                    if e >= 1:
                        gp.wait_ge(eT.h, wo_t[(e - 1, NHC - 1)])
                    gp.dma_start(out=wotl[0:88, :],
                                 in_=WOE[e, 5376:HR, :]).then_inc(dWot.h, 16)

    @on("tensor")
    def _(t_):
        for e in range(E):
            for hc in range(NHC):
                if hc < 40:
                    lb = hc // 4
                    pty = (e * NW01 + lb) % 3
                    s0 = w0b[pty][:].rearrange("p (mc x) -> p mc x", mc=DC)
                    s1 = w1b[pty][:].rearrange("p (mc x) -> p mc x", mc=DC)
                    col = (hc % 4) * 128
                    v0, v1 = w01_dma[(e, lb)]
                    t_.wait_ge(dW0[pty].h, v0)
                    t_.wait_ge(dW1[pty].h, v1)
                else:
                    s0 = w0tl[:].rearrange("p (mc x) -> p mc x", mc=DC)
                    s1 = w1tl[:].rearrange("p (mc x) -> p mc x", mc=DC)
                    col = (hc - 40) * 128
                    v0, v1 = w01_dma[(e, NW01)]
                    t_.wait_ge(dWt[0].h, v0)
                    t_.wait_ge(dWt[1].h, v1)
                kg = e * NHC + hc
                if kg >= 2:
                    pe, phc = (kg - 2) // NHC, (kg - 2) % NHC
                    t_.wait_ge(eS.h, gel_s[(pe, phc)])
                    t_.wait_ge(eV.h, hfm_v[(pe, phc)])
                if hc < 2 and e >= 1:
                    t_.wait_ge(eS.h, eo_s[e - 1])
                if hc == 0 and e == 0:
                    for mc in range(DC):
                        t_.wait_ge(eV.h, ga_ev[(E - 1, mc)])
                for mc in range(DC):
                    ins = t_.matmul(pall[hc % 2][:], s0[:, mc, col:col + 128],
                                    xe[:, e * DC * 512 + mc * 512:(e * DC + mc + 1) * 512],
                                    start=(mc == 0), stop=(mc == DC - 1))
                ins.then_inc(eT.h, 1)
                for mc in range(DC):
                    ins = t_.matmul(pall[2 + hc % 2][:], s1[:, mc, col:col + 128],
                                    xe[:, e * DC * 512 + mc * 512:(e * DC + mc + 1) * 512],
                                    start=(mc == 0), stop=(mc == DC - 1))
                ins.then_inc(eT.h, 1)
            for hc in range(NHC):
                t_.wait_ge(eV.h, hfm_v[(e, hc)])
                if hc == 0:
                    t_.wait_ge(eS.h, gel_s[(e, NHC - 1)])
                    t_.wait_ge(eV.h, hfm_v[(e, NHC - 1)])
                    if e >= 1:
                        t_.wait_ge(eS.h, eo_s[e - 1])
                if hc < 42:
                    pb = hc // 2
                    pty = (e * WO_PAIRS + pb) % 2
                    t_.wait_ge(dWo[pty].h, wo_dma[(e, pb)])
                    wsrc = wob[pty]
                    woff = (hc % 2) * D
                else:
                    t_.wait_ge(dWot.h, wo_dma[(e, WO_PAIRS)])
                    wsrc = wotl
                    woff = 0
                for st in range(4):
                    for mh in range(2):
                        ins = t_.matmul(pall[st * 2 + mh][:],
                                        h_fm[:, hc * 512 + st * 128: hc * 512 + st * 128 + 128],
                                        wsrc[:, woff + mh * 512: woff + (mh + 1) * 512],
                                        start=(hc == 0), stop=(hc == NHC - 1))
                ins.then_inc(eT.h, 1)

    @on("scalar")
    def _(s):
        for e in range(E):
            for hc in range(NHC):
                kg = e * NHC + hc
                s.wait_ge(eT.h, hmm_t[(e, hc)] - 1)
                if kg >= 2:
                    pe, phc = (kg - 2) // NHC, (kg - 2) % NHC
                    s.wait_ge(eV.h, hfm_v[(pe, phc)])
                s.activation(gtmp[hc % 2][:], pall[hc % 2][:],
                             ACT.Gelu_apprx_tanh).then_inc(eS.h, 1)
            s.wait_ge(eT.h, wo_t[(e, NHC - 1)])
            if e >= 1:
                s.wait_ge(dEo.h, eo_d[e - 1])
            for st in range(4):
                for mh in range(2):
                    s.copy(eob[:, st * D + mh * 512: st * D + mh * 512 + 512],
                           pall[st * 2 + mh][:]).then_inc(eS.h, 1)

    @on("vector")
    def _(v):
        for e in range(E):
            for hc in range(NHC):
                v.wait_ge(eS.h, gel_s[(e, hc)])
                v.wait_ge(eT.h, hmm_t[(e, hc)])
                if e >= 1:
                    v.wait_ge(eT.h, wo_t[(e - 1, hc)])
                v.tensor_tensor(h_fm[:, hc * 512:(hc + 1) * 512], gtmp[hc % 2][:],
                                pall[2 + hc % 2][:], ALU.mult).then_inc(eV.h, 1)

    @on("sync")
    def _(sy):
        for e in range(E):
            sy.wait_ge(eS.h, eo_s[e])
            sy.dma_start(out=EOSt[e],
                         in_=eob[:].rearrange("p (k m) -> p k m", k=4)).then_inc(dEo.h, 16)

    EXP_DONE_S = eS.n
    es_px.close()
    es_ffn.close()

    es_sc = ExitStackR()
    eors = sb(es_sc, "eors", [128, E * 4 * D], BF16)
    cTb = [sb(es_sc, f"cTb{i}", [128, E * 512], BF16) for i in range(2)]
    ctm8 = [sb(es_sc, f"ctm8{i}", [128, E * 512], BF16) for i in range(2)]
    x1r = [sb(es_sc, f"x1r{i}", [128, D], F32) for i in range(2)]
    yout = [sb(es_sc, f"yout{i}", [128, D], F32) for i in range(2)]
    ptc = [ps(es_sc, f"ptc{i}", [128, 512], BF16) for i in range(2)]
    psc2 = [ps(es_sc, f"psc2{i}", [128, 512]) for i in range(4)]

    eo_rel = {e: dEr[0].bump(16) for e in range(E)}
    x1_rel, sc_bv, sc_tp, sc_te, sc_mm, sc_y, y_dma = {}, {}, {}, {}, {}, {}, {}
    for i in range(NT):
        x1_rel[i] = dXr[i % 2].bump(16)
        y_dma[i] = dY.bump(16)
    for ii in range(NT + 1):
        if ii < NT:
            for e in range(E):
                sc_bv[(ii, e)] = eV.bump()
            for e in range(E):
                sc_te[(ii, e)] = eV.bump()
            if ii == 0:
                for e in range(E):
                    sc_tp[(0, e)] = eT.bump()
        if ii >= 1:
            for e in range(E):
                if e == E - 1:
                    sc_mm[ii - 1] = eT.bump()
                if ii < NT:
                    sc_tp[(ii, e)] = eT.bump()
            sc_y[ii - 1] = eV.bump()

    Yt = Y.rearrange("(i p) m -> i p m", p=128)

    @on("sync")
    def _(sy):
        sy.wait_ge(eS.h, EXP_DONE_S)
        for e in range(E):
            sy.wait_ge(dEo.h, eo_d[e])
            sy.dma_start(out=eors[:, e * 4 * D:(e + 1) * 4 * D]
                         .rearrange("p (k m) -> p k m", k=4),
                         in_=EOSt[e]).then_inc(dEr[0].h, 16)
        for i in range(NT):
            if i >= 2:
                sy.wait_ge(eV.h, sc_y[i - 2])
            sy.dma_start(out=x1r[i % 2][:], in_=X1t[i]).then_inc(dXr[i % 2].h, 16)
        sy.wait_ge(dY.h, dY.n)

    @on("gpsimd")
    def _(gp):
        for i in range(NT):
            gp.wait_ge(eV.h, sc_y[i])
            gp.dma_start(out=Yt[i], in_=yout[i % 2][:]).then_inc(dY.h, 16)

    def emit_build_chunk(i):
        @on("vector")
        def _(v, i=i):
            if i == 0:
                v.wait_ge(eS.h, EXP_DONE_S)
                v.wait_ge(eV.h, QCA_V)
            for e in range(E):
                if i >= 2:
                    v.wait_ge(eT.h, sc_tp[(i - 2, e)])
                v.tensor_scalar(ctm8[i % 2][:, e * 512:(e + 1) * 512], iota_sb[:],
                                p1x[:, i * E + e:i * E + e + 1],
                                p2x[:, i * E + e:i * E + e + 1],
                                ALU.is_equal, ALU.mult).then_inc(eV.h, 1)
            for e in range(E):
                v.wait_ge(eT.h, sc_tp[(i, e)])
                if i >= 2:
                    v.wait_ge(eT.h, sc_mm[i - 2])
                v.tensor_copy(cTb[i % 2][:, e * 512:(e + 1) * 512],
                              ptc[(i * E + e) % 2][:]).then_inc(eV.h, 1)

    def emit_tp_group(t_, i, e):
        k = i * E + e
        t_.wait_ge(eV.h, sc_bv[(i, e)])
        if k >= 2:
            t_.wait_ge(eV.h, sc_te[((k - 2) // E, (k - 2) % E)])
        for kk4 in range(4):
            ins = t_.transpose(ptc[k % 2][:, kk4 * 128:kk4 * 128 + 128],
                               ctm8[i % 2][:, e * 512 + kk4 * 128:
                                            e * 512 + kk4 * 128 + 128],
                               idb_sb[:])
        ins.then_inc(eT.h, 1)

    def emit_tp0():
        @on("tensor")
        def _(t_):
            for e in range(E):
                emit_tp_group(t_, 0, e)

    def emit_mm_chunk(i):
        @on("tensor")
        def _(t_, i=i):
            if i >= 2:
                t_.wait_ge(eV.h, sc_y[i - 2])
            if i == 0:
                t_.wait_ge(dEr[0].h, eo_rel[E - 1])
            for e in range(E):
                t_.wait_ge(eV.h, sc_te[(i, e)])
                for kk4 in range(4):
                    for mh in range(2):
                        base = (e * 4 + kk4) * D + mh * 512
                        ins = t_.matmul(psc2[mh * 2 + i % 2][:],
                                        cTb[i % 2][:, e * 512 + kk4 * 128:
                                                   e * 512 + kk4 * 128 + 128],
                                        eors[:, base: base + 512],
                                        start=(e == 0 and kk4 == 0),
                                        stop=(e == E - 1 and kk4 == 3))
                if e == E - 1:
                    ins.then_inc(eT.h, 1)
                if i + 1 < NT:
                    emit_tp_group(t_, i + 1, e)

        @on("vector")
        def _(v, i=i):
            v.wait_ge(eT.h, sc_mm[i])
            v.wait_ge(dXr[i % 2].h, x1_rel[i])
            if i >= 2:
                v.wait_ge(dY.h, y_dma[i - 2])
            for mh in range(2):
                ins = v.tensor_tensor(yout[i % 2][:, mh * 512:(mh + 1) * 512],
                                      x1r[i % 2][:, mh * 512:(mh + 1) * 512],
                                      psc2[mh * 2 + i % 2][:], ALU.add)
            ins.then_inc(eV.h, 1)

    for i in range(NT + 1):
        if i < NT:
            emit_build_chunk(i)
            if i == 0:
                emit_tp0()
        if i >= 1:
            emit_mm_chunk(i - 1)

    es_sc.close()
    return nc, finish


def _consts():
    c = {}
    c["iota"] = np.tile(np.arange(512, dtype=np.float32), (128, 1))
    c["idb"] = np.eye(128).astype(ml_dtypes.bfloat16)
    c["idf"] = np.eye(128, dtype=np.float32)
    c["tri"] = np.triu(np.ones((128, 128), dtype=np.float32), k=1)
    c["onesf"] = np.ones((128, 128), dtype=np.float32)
    q = np.arange(128)
    s_ = np.arange(128)
    c["maskt"] = np.where(q[None, :] >= s_[:, None], 0.0, -1e30).astype(np.float32)
    c["onecol"] = np.ones((128, 1), dtype=np.float32)
    c["onecolb"] = np.ones((128, 1)).astype(ml_dtypes.bfloat16)
    c["onerow"] = np.ones((1, 128), dtype=np.float32)
    e01 = np.zeros((1, 128), dtype=np.float32)
    e01[0, 0:64] = 1.0
    e10 = np.zeros((1, 128), dtype=np.float32)
    e10[0, 64:128] = 1.0
    c["e01"] = e01.astype(ml_dtypes.bfloat16)
    c["e10"] = e10.astype(ml_dtypes.bfloat16)
    return c


def kernel(L0_attn_X, L0_attn_q_proj, L0_attn_k_proj, L0_attn_v_proj,
           L0_attn_o_proj, L0_ffn_router_w, L0_ffn_w0_kernel,
           L0_ffn_w1_kernel, L0_ffn_wo_kernel, _upto="full", _nh=NH):
    X = np.ascontiguousarray(np.asarray(L0_attn_X, dtype=np.float32))
    shared = {
        "wq": np.ascontiguousarray(np.asarray(L0_attn_q_proj, dtype=np.float32)),
        "wk": np.ascontiguousarray(np.asarray(L0_attn_k_proj, dtype=np.float32)),
        "wv": np.ascontiguousarray(np.asarray(L0_attn_v_proj, dtype=np.float32)),
        "wo": np.ascontiguousarray(np.asarray(L0_attn_o_proj, dtype=np.float32)),
        "rw": np.ascontiguousarray(np.asarray(L0_ffn_router_w, dtype=np.float32)),
        "w0": np.ascontiguousarray(np.asarray(L0_ffn_w0_kernel, dtype=np.float32)),
        "w1": np.ascontiguousarray(np.asarray(L0_ffn_w1_kernel, dtype=np.float32)),
        "woe": np.ascontiguousarray(np.asarray(L0_ffn_wo_kernel, dtype=np.float32)),
    }
    shared.update(_consts())
    nc, fin = build_nc(upto=_upto, nhrun=_nh)
    fin()
    in_maps = [dict(shared, x=np.ascontiguousarray(X[c])) for c in range(8)]
    res = run_bass_kernel_spmd(nc, in_maps, core_ids=list(range(8)))
    out = np.stack([np.asarray(res.results[c]["y"]) for c in range(8)], axis=0)
    if _upto != "full":
        return out, [np.asarray(res.results[c]["dbg"]) for c in range(8)]
    return out


# version 07 (final)
# speedup vs baseline: 1.3750x; 1.0504x over previous
import numpy as np
import ml_dtypes
from contextlib import ExitStack
import concourse.bass as bass
import concourse.mybir as mybir
from concourse.bass_utils import run_bass_kernel_spmd

F32 = mybir.dt.float32
BF16 = mybir.dt.bfloat16
ALU = mybir.AluOpType
ACT = mybir.ActivationFunctionType
AX = mybir.AxisListType.X

S = 2048
D = 1024
NT = 16
DC = 8
NH = 16
E = 8
CAP = 512
HR = 5464
HP = 5504
NHC = 43
NW01 = 10
WO_PAIRS = 21


def build_nc(upto="full", nhrun=NH):
    nc = bass.Bass("TRN2", target_bir_lowering=False)

    def din(name, shape, dt=F32):
        return nc.dram_tensor(name, shape, dt, kind="ExternalInput").ap()

    X = din("x", [S, D])
    WQ = din("wq", [D, D])
    WK = din("wk", [D, D])
    WV = din("wv", [D, D])
    WO = din("wo", [D, D])
    RW = din("rw", [D, E])
    W0 = din("w0", [E, D, HR])
    W1 = din("w1", [E, D, HR])
    WOE = din("woe", [E, HR, D])
    IOTA = din("iota", [128, 512])
    IDB = din("idb", [128, 128], BF16)
    IDF = din("idf", [128, 128])
    TRI = din("tri", [128, 128])
    ONESF = din("onesf", [128, 128])
    MASKT = din("maskt", [128, 128])
    ONECOL = din("onecol", [128, 1])
    ONECOLB = din("onecolb", [128, 1], BF16)
    ONEROW = din("onerow", [1, 128])
    E01 = din("e01", [1, 128], BF16)
    E10 = din("e10", [1, 128], BF16)

    Y = nc.dram_tensor("y", [S, D], F32, kind="ExternalOutput").ap()
    NDBG = {"ln1": DC * S, "attn": (NH // 2) * S, "gate": 10 * NT * E,
            "gather": E * DC * 512, "vprime": NT * NH * 65}.get(upto, 1)
    DBG = nc.dram_tensor("dbg", [128, NDBG], F32, kind="ExternalOutput").ap()
    X1S = nc.dram_tensor("x1s", [S, D], F32, kind="Internal").ap()
    XL2S = nc.dram_tensor("xl2s", [S, D], BF16, kind="Internal").ap()
    AVS = nc.dram_tensor("avs", [NH // 2, 128, S], BF16, kind="Internal").ap()
    EOS = nc.dram_tensor("eos", [E, CAP, D], BF16, kind="Internal").ap()

    Xt = X.rearrange("(i p) m -> i p m", p=128)
    X1t = X1S.rearrange("(i p) m -> i p m", p=128)
    XL2t = XL2S.rearrange("(i p) m -> i p m", p=128)
    EOSt = EOS.rearrange("e (k p) m -> e p k m", p=128)

    stack = ExitStack()
    scopes = []
    _ES = ExitStack

    def ExitStackR():
        es = _ES()
        scopes.append(es)
        return es

    def finish():
        for es in reversed(scopes):
            es.close()
        stack.close()

    def sb(es, name, shape, dt):
        return es.enter_context(nc.sbuf_tensor(name, shape, dt))

    def ps(es, name, shape, dt=F32):
        return es.enter_context(nc.psum_tensor(name, shape, dt))

    class Ctr:
        def __init__(self, name):
            self.h = stack.enter_context(nc.semaphore(name))
            self.n = 0

        def bump(self, k=1):
            self.n += k
            return self.n

    dC = Ctr("dC")
    dW = Ctr("dW")
    dEo = Ctr("dEo")
    dY = Ctr("dY")
    eT = Ctr("eT")
    eV = Ctr("eV")
    eS = Ctr("eS")
    eG = Ctr("eG")
    dXi = [Ctr("dXi0"), Ctr("dXi1")]
    dX2 = [Ctr("dX20"), Ctr("dX21")]
    dAv = [Ctr("dAv0"), Ctr("dAv1")]
    dSpX = [Ctr("dSpX0"), Ctr("dSpX1")]
    dSpL = [Ctr("dSpL0"), Ctr("dSpL1")]
    dBr = Ctr("dBr")
    dRc = Ctr("dRc")
    dAt = Ctr("dAt")
    dR0 = Ctr("dR0")
    dR1 = Ctr("dR1")
    dAl = [Ctr("dAl0"), Ctr("dAl1")]
    dAw = [Ctr("dAw0"), Ctr("dAw1")]
    dW0 = [Ctr("dW00"), Ctr("dW01"), Ctr("dW02")]
    dW1 = [Ctr("dW10"), Ctr("dW11"), Ctr("dW12")]
    dWt = [Ctr("dWt0"), Ctr("dWt1")]
    dWo = [Ctr("dWo0"), Ctr("dWo1"), Ctr("dWo2")]
    dWot = Ctr("dWot")
    dEr = [Ctr("dEr0"), Ctr("dEr1")]
    dGr = [Ctr("dGr0"), Ctr("dGr1")]
    dXr = [Ctr("dXr0"), Ctr("dXr1")]

    gl = stack
    iota_sb = sb(gl, "iota_sb", [128, 512], F32)
    idb_sb = sb(gl, "idb_sb", [128, 128], BF16)
    idf_sb = sb(gl, "idf_sb", [128, 128], F32)
    tri_sb = sb(gl, "tri_sb", [128, 128], F32)
    onesf_sb = sb(gl, "onesf_sb", [128, 128], F32)
    maskt_sb = sb(gl, "maskt_sb", [128, 128], F32)
    onecol_sb = sb(gl, "onecol_sb", [128, 1], F32)
    onecolb_sb = sb(gl, "onecolb_sb", [128, 1], BF16)
    onerow_sb = sb(gl, "onerow_sb", [1, 128], F32)
    e01_sb = sb(gl, "e01_sb", [1, 128], BF16)
    e10_sb = sb(gl, "e10_sb", [1, 128], BF16)
    rw_sb = sb(gl, "rw_sb", [128, DC * E], F32)
    logits = sb(gl, "logits", [128, NT * E], F32)
    m1a = sb(gl, "m1a", [128, NT * E], F32)
    m2a = sb(gl, "m2a", [128, NT * E], F32)
    m1k = sb(gl, "m1k", [128, NT * E], F32)
    m2k = sb(gl, "m2k", [128, NT * E], F32)
    q1a = sb(gl, "q1a", [128, NT * E], F32)
    q2a = sb(gl, "q2a", [128, NT * E], F32)
    p1x = sb(gl, "p1x", [128, NT * E], F32)
    p2x = sb(gl, "p2x", [128, NT * E], F32)
    g1kc = sb(gl, "g1kc", [128, NT], F32)
    g2kc = sb(gl, "g2kc", [128, NT], F32)
    county = sb(gl, "county", [1, E], F32)
    gsc = sb(gl, "gsc", [128, 48], F32)

    block = stack.enter_context(nc.Block())

    def on(eng):
        return getattr(block, eng)

    @on("sync")
    def _(sy):
        for dst, src in [(iota_sb, IOTA), (idb_sb, IDB), (idf_sb, IDF),
                         (tri_sb, TRI), (onesf_sb, ONESF), (maskt_sb, MASKT),
                         (onecol_sb, ONECOL), (onecolb_sb, ONECOLB),
                         (onerow_sb, ONEROW), (e01_sb, E01), (e10_sb, E10)]:
            sy.dma_start(out=dst[:], in_=src).then_inc(dC.h, 16)
        sy.dma_start(out=rw_sb[:],
                     in_=RW.rearrange("(dc p) e -> p dc e", p=128)).then_inc(dC.h, 16)
    CONSTS = dC.bump(16 * 12)

    es_at = ExitStackR()
    es_qk = ExitStackR()
    es_ws = ExitStackR()
    es_wv = ExitStackR()
    es_ln = ExitStackR()
    avtmp = sb(es_at, "avtmp", [64, S], BF16)
    recrow = sb(es_at, "recrow", [65, S], BF16)
    recall = sb(es_at, "recall", [16, S], BF16)
    wqT = sb(es_qk, "wqT", [128, DC * D], BF16)
    wkT = sb(es_qk, "wkT", [128, DC * D], BF16)
    xfm = sb(es_qk, "xfm", [128, DC * S], BF16)
    vall = sb(es_qk, "vall", [128, NT * NH * 65], BF16)
    qbuf = [sb(es_qk, f"qb{i}", [65, S], BF16) for i in range(2)]
    kbuf = [sb(es_qk, f"kb{i}", [65, S], BF16) for i in range(2)]
    ptb = [sb(es_qk, f"ptb{i}", [128, S], BF16) for i in range(2)]
    sqq = sb(es_qk, "sqq", [64, S], BF16)
    nrow = sb(es_qk, "nrow", [1, S + 16], F32)
    brow = sb(es_qk, "brow", [1, S], BF16)
    wstage = [sb(es_ws, f"wstage{q}", [128, D], BF16) for q in range(2)]
    wvT = sb(es_wv, "wvT", [128, DC * D], BF16)
    xin = [sb(es_ln, f"xin{i}", [128, D], F32) for i in range(2)]
    xlt = sb(es_ln, "xlt", [128, D], BF16)
    sqd = sb(es_ln, "sqd", [128, D], BF16)
    mcol = sb(es_ln, "mcol", [128, 16], F32)

    es_ptr = ExitStackR()
    ptr = ps(es_ptr, "ptr", [128, 1024], BF16)

    def load_wT(Wsrc, dstT, stage_free_t, ptr_free_v, ptrps, stg):
        Wr = Wsrc.rearrange("(oc p) d -> oc p d", p=128)
        ndma = [dW.bump(16) for _ in range(DC)]
        t_vals = [eT.bump() for _ in range(DC)]
        v_vals = [eV.bump() for _ in range(DC)]

        @on("gpsimd")
        def _(gp):
            for oc in range(DC):
                if oc <= 1 and stage_free_t:
                    gp.wait_ge(eT.h, stage_free_t)
                if oc >= 2:
                    gp.wait_ge(eT.h, t_vals[oc - 2])
                gp.dma_start(out=stg[oc % 2][:], in_=Wr[oc]).then_inc(dW.h, 16)

        @on("tensor")
        def _(t):
            t.wait_ge(dC.h, CONSTS)
            for oc in range(DC):
                t.wait_ge(dW.h, ndma[oc])
                if oc == 0 and ptr_free_v:
                    t.wait_ge(eV.h, ptr_free_v)
                if oc >= 1:
                    t.wait_ge(eV.h, v_vals[oc - 1])
                for dc in range(DC):
                    ins = t.transpose(ptrps[:, dc * 128:dc * 128 + 128],
                                      stg[oc % 2][:, dc * 128:dc * 128 + 128],
                                      idb_sb[:])
                ins.then_inc(eT.h, 1)

        @on("vector")
        def _(v):
            for oc in range(DC):
                v.wait_ge(eT.h, t_vals[oc])
                dst = dstT[:].rearrange("p (dc x) -> p dc x", dc=DC)[:, :, oc * 128:oc * 128 + 128]
                src = ptrps[:].rearrange("p (dc x) -> p dc x", dc=DC)
                v.tensor_copy(dst, src).then_inc(eV.h, 1)
        return t_vals[-1], v_vals[-1]

    wv_t, wv_v = load_wT(WV, wvT, 0, 0, ptr, wstage)

    ln_nx = [dXi[i % 2].bump(16) for i in range(NT)]
    ln_mean, ln_var, ln_norm, ln_ev = [], [], [], []
    ln_sq, ln_rstd, ln_tp = [], [], []
    for i in range(NT):
        ln_mean.append(eV.bump())
        ln_var.append(eV.bump())
        ln_norm.append(eV.bump())
        ln_ev.append(eV.bump())
        ln_sq.append(eS.bump())
        ln_rstd.append(eS.bump())
        ln_tp.append(eT.bump())

    @on("sync")
    def _(sy):
        for i in range(NT):
            if i >= 2:
                sy.wait_ge(eS.h, ln_sq[i - 2])
                sy.wait_ge(eV.h, ln_norm[i - 2])
            sy.dma_start(out=xin[i % 2][:], in_=Xt[i]).then_inc(dXi[i % 2].h, 16)

    @on("vector")
    def _(v):
        for i in range(NT):
            b = (i % 2) * 8
            v.wait_ge(dXi[i % 2].h, ln_nx[i])
            v.reduce_sum(mcol[:, b:b + 1], xin[i % 2][:], axis=AX)
            v.drain()
            v.tensor_scalar(mcol[:, b:b + 1], mcol[:, b:b + 1], 1.0 / D, 0.0,
                            ALU.mult, ALU.add).then_inc(eV.h, 1)
            v.wait_ge(eS.h, ln_sq[i])
            v.wait_ge(eV.h, ln_mean[i])
            v.tensor_scalar(mcol[:, b + 2:b + 3], mcol[:, b + 1:b + 2], 1.0 / D, 0.0,
                            ALU.mult, ALU.add)
            v.tensor_scalar(mcol[:, b + 3:b + 4], mcol[:, b:b + 1], mcol[:, b:b + 1],
                            None, ALU.mult)
            v.drain()
            v.tensor_scalar(mcol[:, b + 2:b + 3], mcol[:, b + 2:b + 3],
                            mcol[:, b + 3:b + 4], None, ALU.subtract)
            v.drain()
            v.tensor_scalar(mcol[:, b + 2:b + 3], mcol[:, b + 2:b + 3], 1e-5, 1.0,
                            ALU.add, ALU.mult).then_inc(eV.h, 1)
            v.wait_ge(eS.h, ln_rstd[i])
            if i >= 1:
                v.wait_ge(eT.h, ln_tp[i - 1])
            v.tensor_scalar(xlt[:], xin[i % 2][:], mcol[:, b:b + 1], mcol[:, b + 4:b + 5],
                            ALU.subtract, ALU.mult).then_inc(eV.h, 1)
            v.wait_ge(eT.h, ln_tp[i])
            dst = xfm[:].rearrange("p (dc x) -> p dc x", dc=DC)[:, :, i * 128:i * 128 + 128]
            src = ptr[:].rearrange("p (dc x) -> p dc x", dc=DC)
            v.tensor_copy(dst, src).then_inc(eV.h, 1)

    @on("scalar")
    def _(s):
        for i in range(NT):
            b = (i % 2) * 8
            s.wait_ge(dXi[i % 2].h, ln_nx[i])
            s.activation(sqd[:], xin[i % 2][:], ACT.Square,
                         accum_out=mcol[:, b + 1:b + 2]).then_inc(eS.h, 1)
            s.wait_ge(eV.h, ln_var[i])
            s.activation(mcol[:, b + 3:b + 4], mcol[:, b + 2:b + 3], ACT.Ln)
            s.drain()
            s.activation(mcol[:, b + 4:b + 5], mcol[:, b + 3:b + 4], ACT.Exp,
                         scale=-0.5).then_inc(eS.h, 1)

    @on("tensor")
    def _(t):
        for i in range(NT):
            t.wait_ge(eV.h, ln_norm[i])
            t.wait_ge(eV.h, wv_v if i == 0 else ln_ev[i - 1])
            for dc in range(DC):
                ins = t.transpose(ptr[:, dc * 128:dc * 128 + 128],
                                  xlt[:, dc * 128:dc * 128 + 128], idb_sb[:])
            ins.then_inc(eT.h, 1)

    XFM_V = ln_ev[-1]

    if upto == "ln1":
        FV = dY.bump(16)

        @on("gpsimd")
        def _(gp):
            gp.wait_ge(eV.h, XFM_V)
            gp.dma_start(out=DBG, in_=xfm[:]).then_inc(dY.h, 16)

        @on("sync")
        def _(sy):
            sy.wait_ge(dY.h, FV)
        return nc, finish

    es_vp = ExitStackR()
    pvp = [ps(es_vp, f"pvp{i}", [128, 512]) for i in range(2)]
    VONES = eV.bump()
    vp_t, vp_v = [], []
    for _ in range(NT * 2):
        vp_t.append(eT.bump())
        vp_v.append(eV.bump())

    @on("vector")
    def _(v):
        v.memset(vall[:], 1.0).then_inc(eV.h, 1)

    @on("tensor")
    def _(t):
        t.wait_ge(eV.h, max(wv_v, XFM_V))
        k = 0
        for i in range(NT):
            for h2 in range(2):
                if k >= 2:
                    t.wait_ge(eV.h, vp_v[k - 2])
                for dc in range(DC):
                    ins = t.matmul(pvp[k % 2][:],
                                   xfm[:, dc * S + i * 128: dc * S + i * 128 + 128],
                                   wvT[:, dc * D + h2 * 512: dc * D + h2 * 512 + 512],
                                   start=(dc == 0), stop=(dc == DC - 1))
                ins.then_inc(eT.h, 1)
                k += 1

    @on("vector")
    def _(v):
        v.wait_ge(eV.h, VONES)
        k = 0
        for i in range(NT):
            for h2 in range(2):
                v.wait_ge(eT.h, vp_t[k])
                dst = vall[:, i * NH * 65: (i + 1) * NH * 65]
                dst = dst.rearrange("p (h c) -> p h c", h=NH)[:, h2 * 8:(h2 + 1) * 8, 0:64]
                src = pvp[k % 2][:].rearrange("p (h c) -> p h c", c=64)
                v.tensor_copy(dst, src).then_inc(eV.h, 1)
                k += 1
    es_vp.close()
    es_wv_close_marker = vp_v[-1]

    if upto == "vprime":
        FV = dY.bump(16)

        @on("gpsimd")
        def _(gp):
            gp.wait_ge(eV.h, vp_v[-1])
            gp.dma_start(out=DBG, in_=vall[:]).then_inc(dY.h, 16)

        @on("sync")
        def _(sy):
            sy.wait_ge(dY.h, FV)
        return nc, finish

    wq_t, wq_v = load_wT(WQ, wqT, wv_t, ln_ev[-1], ptr, wstage)
    wk_t, wk_v = load_wT(WK, wkT, wq_t, wq_v, ptr, wstage)
    es_ln.close()
    es_wv.close()
    es_ws.close()
    es_ptr.close()

    es_hd = ExitStackR()
    pqk = [ps(es_hd, f"pqk{i}", [64, 512]) for i in range(2)]
    psc = [ps(es_hd, f"psc{i}", [128, 512]) for i in range(2)]
    pav = ps(es_hd, "pav", [65, S])

    KONES = eG.bump()

    @on("gpsimd")
    def _(gp):
        gp.memset(kbuf[0][64:65, :], -1.0)
        gp.memset(kbuf[1][64:65, :], -1.0).then_inc(eG.h, 1)

    qk_free = [0, 0]
    pav_free_s = 0
    pav_free_v2 = 0
    avtmp_free = 0
    psc_free = [0, 0]
    pqk_free = [("S", 0), ("S", 0)]
    ptb_free = [0, 0]
    rec_free = 0
    hst = {}

    def emit_qk_bound(h):
        nonlocal_state = None
        hp = h % 2
        st = {}
        st["qkt"] = qkt = [eT.bump() for _ in range(8)]
        st["qev"] = qev = [eV.bump() for _ in range(4)]
        st["kev"] = kev = [eV.bump() for _ in range(4)]

        @on("tensor")
        def _(t_, h=h, hp=hp, qw=qk_free[hp], pqw=list(pqk_free),
              qev=qev, kev=kev):
            if h == 0:
                t_.wait_ge(eV.h, wk_v)
            if qw:
                t_.wait_ge(eT.h, qw)
            k = 0
            for wT in (wqT, wkT):
                for c in range(4):
                    if k >= 2:
                        t_.wait_ge(eV.h, (qev + kev)[k - 2])
                    else:
                        kind, val = pqw[k % 2]
                        if val:
                            t_.wait_ge((eS if kind == "S" else eV).h, val)
                    for dc in range(DC):
                        ins = t_.matmul(pqk[k % 2][:],
                                        wT[:, dc * D + h * 64: dc * D + h * 64 + 64],
                                        xfm[:, dc * S + c * 512: dc * S + c * 512 + 512],
                                        start=(dc == 0), stop=(dc == DC - 1))
                    ins.then_inc(eT.h, 1)
                    k += 1

        @on("vector")
        def _(v, hp=hp, qkt=qkt):
            for c in range(4):
                v.wait_ge(eT.h, qkt[c])
                v.tensor_scalar(qbuf[hp][0:64, c * 512:(c + 1) * 512], pqk[c % 2][:],
                                0.125, 0.0, ALU.mult, ALU.add).then_inc(eV.h, 1)
            for c in range(4):
                v.wait_ge(eT.h, qkt[4 + c])
                v.tensor_copy(kbuf[hp][0:64, c * 512:(c + 1) * 512],
                              pqk[c % 2][:]).then_inc(eV.h, 1)

        st["sq_k"] = sq_k = eV.bump()
        st["nmk_v"] = nmk_v = [eV.bump() for _ in range(4)]
        st["mk_done"] = mk_done = eV.bump()
        st["sq_q"] = sq_q = eV.bump()
        st["nmq_v"] = nmq_v = [eV.bump() for _ in range(4)]
        st["qn_done"] = qn_done = eV.bump()
        st["nmk_t"] = nmk_t = [eT.bump() for _ in range(4)]
        st["nmq_t"] = nmq_t = [eT.bump() for _ in range(4)]
        st["bexp_done"] = bexp_done = eS.bump()
        st["brow_dma"] = dBr.bump(16)
        prev_nmq_t = hst[h - 1]["nmq_t"][-1] if h >= 1 else 0
        prev_brow = hst[h - 1]["brow_dma"] if h >= 1 else 0
        prev_bexp = hst[h - 1]["bexp_done"] if h >= 1 else 0

        @on("vector")
        def _(v, hp=hp, kev=kev, nmk_t=nmk_t, nmq_t=nmq_t,
              prev_nmq_t=prev_nmq_t, prev_bexp=prev_bexp):
            v.wait_ge(eV.h, kev[-1])
            if prev_nmq_t:
                v.wait_ge(eT.h, prev_nmq_t)
            v.tensor_tensor(sqq[:], kbuf[hp][0:64, :], kbuf[hp][0:64, :],
                            ALU.mult).then_inc(eV.h, 1)
            if prev_bexp:
                v.wait_ge(eS.h, prev_bexp)
            for c in range(4):
                v.wait_ge(eT.h, nmk_t[c])
                v.tensor_copy(nrow[0:1, c * 512:(c + 1) * 512],
                              pqk[c % 2][0:1, :]).then_inc(eV.h, 1)
            v.drain()
            v.reduce_max(nrow[0:1, S:S + 1], nrow[0:1, 0:S],
                         axis=AX).then_inc(eV.h, 1)
            v.tensor_tensor(sqq[:], qbuf[hp][0:64, :], qbuf[hp][0:64, :],
                            ALU.mult).then_inc(eV.h, 1)
            for c in range(4):
                v.wait_ge(eT.h, nmq_t[c])
                v.tensor_copy(nrow[0:1, c * 512:(c + 1) * 512],
                              pqk[c % 2][0:1, :]).then_inc(eV.h, 1)
            v.drain()
            v.tensor_scalar(nrow[0:1, 0:S], nrow[0:1, 0:S], nrow[0:1, S:S + 1],
                            None, ALU.mult).then_inc(eV.h, 1)

        @on("scalar")
        def _(s, qn_done=qn_done, prev_brow=prev_brow):
            s.wait_ge(eV.h, qn_done)
            if prev_brow:
                s.wait_ge(dBr.h, prev_brow)
            s.activation(brow[0:1, :], nrow[0:1, 0:S], ACT.Sqrt,
                         scale=1.1025).then_inc(eS.h, 1)

        @on("tensor")
        def _(t_, sq_k=sq_k, sq_q=sq_q, nmk_v=nmk_v, nmq_v=nmq_v):
            t_.wait_ge(eV.h, sq_k)
            for c in range(4):
                if c >= 2:
                    t_.wait_ge(eV.h, nmk_v[c - 2])
                t_.matmul(pqk[c % 2][0:1, :], onecolb_sb[0:64, :],
                          sqq[:, c * 512:(c + 1) * 512],
                          start=True, stop=True).then_inc(eT.h, 1)
            t_.wait_ge(eV.h, sq_q)
            for c in range(4):
                t_.wait_ge(eV.h, (nmk_v + nmq_v)[c + 2])
                t_.matmul(pqk[c % 2][0:1, :], onecolb_sb[0:64, :],
                          sqq[:, c * 512:(c + 1) * 512],
                          start=True, stop=True).then_inc(eT.h, 1)

        @on("sync")
        def _(sy, hp=hp, bexp_done=bexp_done):
            sy.wait_ge(eS.h, bexp_done)
            sy.dma_start(out=qbuf[hp][64:65, :], in_=brow[:]).then_inc(dBr.h, 16)

        hst[h] = st
        return [("V", nmq_v[2]), ("V", nmq_v[3])]

    def emit_pass2_sigma(h):
        hp = h % 2
        pr = h // 2
        st = hst[h]
        chunks = []
        for j in range(NT):
            for c in range(j // 4, 4):
                chunks.append((j, c))
        par = {jc: k % 2 for k, jc in enumerate(chunks)}
        ops = [("S", 0)]
        for k in range(1, len(chunks)):
            ops.append(("S", k))
            ops.append(("A", k - 1))
        ops.append(("A", len(chunks) - 1))
        sc_t, av_t, mask_v, exp_s = {}, {}, {}, {}
        for typ, k in ops:
            j, c = chunks[k]
            if typ == "S":
                sc_t[(j, c)] = eT.bump()
                if c == j // 4:
                    mask_v[(j, c)] = eV.bump()
            else:
                av_t[(j, c)] = eT.bump()
        for (j, c) in chunks:
            exp_s[(j, c)] = eS.bump()

        @on("tensor")
        def _(t_, h=h, hp=hp, brow_dma=st["brow_dma"], chunks=chunks,
              exp_s=exp_s, psc_w=list(psc_free), pav_s=pav_free_s,
              pav_v2=pav_free_v2, av_free=avtmp_free, rec_f=rec_free):
            t_.wait_ge(dBr.h, brow_dma)
            if h == 0:
                t_.wait_ge(eG.h, KONES)
                t_.wait_ge(eV.h, vp_v[-1])
            if pav_s:
                t_.wait_ge(eS.h, pav_s)
            if pav_v2:
                t_.wait_ge(eV.h, pav_v2)
            if av_free:
                t_.wait_ge(dAt.h, av_free)
            if rec_f:
                t_.wait_ge(dRc.h, rec_f)
            for typ, k in ops:
                j, c = chunks[k]
                off = j * 128 - c * 512 if c == j // 4 else 0
                w = 512 - off
                q0 = c * 512 + off
                if typ == "S":
                    if k >= 2:
                        t_.wait_ge(eS.h, exp_s[chunks[k - 2]])
                    elif psc_w[k % 2]:
                        t_.wait_ge(eS.h, psc_w[k % 2])
                    t_.matmul(psc[k % 2][:, 0:w], kbuf[hp][:, j * 128:(j + 1) * 128],
                              qbuf[hp][:, q0:q0 + w],
                              start=True, stop=True).then_inc(eT.h, 1)
                else:
                    t_.wait_ge(eS.h, exp_s[(j, c)])
                    t_.matmul(pav[0:65, q0:q0 + w],
                              vall[:, j * NH * 65 + h * 65: j * NH * 65 + (h + 1) * 65],
                              ptb[j % 2][:, q0:q0 + w],
                              start=(j == 0), stop=(j == min(15, 4 * c + 3))
                              ).then_inc(eT.h, 1)

        @on("vector")
        def _(v, sc_t=sc_t, par=par):
            for j in range(NT):
                c = j // 4
                v.wait_ge(eT.h, sc_t[(j, c)])
                p = psc[par[(j, c)]]
                v.tensor_tensor(p[:, 0:128], p[:, 0:128], maskt_sb[:],
                                ALU.add).then_inc(eV.h, 1)

        @on("scalar")
        def _(s, chunks=chunks, par=par, sc_t=sc_t, mask_v=mask_v,
              av_t=av_t, ptb_w=list(ptb_free)):
            for j in range(NT):
                if j >= 2:
                    s.wait_ge(eT.h, av_t[(j - 2, 3)])
                elif ptb_w[j % 2]:
                    s.wait_ge(eT.h, ptb_w[j % 2])
                for c in range(j // 4, 4):
                    off = j * 128 - c * 512 if c == j // 4 else 0
                    w = 512 - off
                    q0 = c * 512 + off
                    s.wait_ge(eT.h, sc_t[(j, c)])
                    if c == j // 4:
                        s.wait_ge(eV.h, mask_v[(j, c)])
                    s.activation(ptb[j % 2][:, q0:q0 + w], psc[par[(j, c)]][:, 0:w],
                                 ACT.Exp).then_inc(eS.h, 1)

        sig_evac = eS.bump()
        av_evac = eV.bump()

        @on("scalar")
        def _(s, av_t=av_t, rec_f=rec_free):
            s.wait_ge(eT.h, av_t[(15, 3)])
            if rec_f:
                s.wait_ge(dRc.h, rec_f)
            s.copy(recrow[64:65, :], pav[64:65, :]).then_inc(eS.h, 1)

        @on("vector")
        def _(v, av_t=av_t, av_free=avtmp_free):
            v.wait_ge(eT.h, av_t[(15, 3)])
            if av_free:
                v.wait_ge(dAt.h, av_free)
            v.tensor_copy(avtmp[:], pav[0:64, :]).then_inc(eV.h, 1)

        @on("sync")
        def _(sy, h=h, sig_evac=sig_evac):
            sy.wait_ge(eS.h, sig_evac)
            sy.dma_start(out=recall[h:h + 1, :], in_=recrow[64:65, :]).then_inc(dRc.h, 16)

        @on("sync")
        def _(sy, hp=hp, pr=pr, av_evac=av_evac):
            sy.wait_ge(eV.h, av_evac)
            sy.dma_start(out=AVS[pr, hp * 64:hp * 64 + 64, :],
                         in_=avtmp[:]).then_inc(dAt.h, 16)

        pf = [0, 0]
        for (j, c) in chunks:
            pf[par[(j, c)]] = exp_s[(j, c)]
        return (pf, [av_t[(14, 3)], av_t[(15, 3)]], av_t[(15, 3)],
                sig_evac, av_evac, dRc.bump(16), dAt.bump(16))

    for h in range(nhrun + 1):
        if h >= 1:
            (psc_free, ptb_free, qk_free[(h - 1) % 2], pav_free_s,
             pav_free_v2, rec_free, avtmp_free) = emit_pass2_sigma(h - 1)
        if h < nhrun:
            pqk_free = emit_qk_bound(h)

    SIG_S = pav_free_s
    HEADS_LAST_T = max(qk_free)
    es_hd.close()
    es_qk.close()

    es_wo = ExitStackR()
    woT = sb(es_wo, "woT", [128, DC * D], BF16)
    es_ln2 = ExitStackR()
    xin2 = [sb(es_ln2, f"xin2{i}", [128, D], F32) for i in range(2)]
    x1b = [sb(es_ln2, f"x1b{i}", [128, D], F32) for i in range(2)]
    xl2f = sb(es_ln2, "xl2f", [128, D], F32)
    xfm32 = sb(es_ln2, "xfm32", [128, D], F32)
    sqd2 = sb(es_ln2, "sqd2", [128, D], BF16)
    mcol2 = sb(es_ln2, "mcol2", [128, 16], F32)
    xl2b = [sb(es_ln2, f"xl2b{i}", [128, D], BF16) for i in range(2)]
    avo = [sb(es_ln2, f"avo{i}", [128, D], BF16) for i in range(2)]
    es_ws2 = ExitStackR()
    wstage2 = [sb(es_ws2, f"wstage2{q}", [128, D], BF16) for q in range(2)]
    es_rec = ExitStackR()
    rec0 = sb(es_rec, "rec0", [1, S], BF16)
    rec1 = sb(es_rec, "rec1", [1, S], BF16)
    avsc = [sb(es_rec, f"avsc{i}", [128, S], BF16) for i in range(2)]
    rrecb = sb(es_rec, "rrecb", [16, S], F32)

    es_sg = ExitStackR()
    psb = [ps(es_sg, f"psb{i}", [128, 512]) for i in range(2)]

    RREC_V = eV.bump()
    sg_r0, sg_r1, sg_ld, sg_wb = [], [], [], []
    sg_t = []
    sg_v = []
    for pr in range(nhrun // 2):
        sg_r0.append(dR0.bump(16))
        sg_r1.append(dR1.bump(16))
        sg_ld.append(dAl[pr % 2].bump(16))
        for c in range(4):
            sg_t.append(eT.bump())
            sg_v.append(eV.bump())
        sg_wb.append(dAw[pr % 2].bump(16))

    @on("vector")
    def _(v):
        v.wait_ge(eS.h, SIG_S)
        v.wait_ge(dRc.h, dRc.n)
        v.reciprocal(rrecb[:], recall[:]).then_inc(eV.h, 1)

    @on("gpsimd")
    def _(gp):
        gp.wait_ge(eV.h, RREC_V)
        for pr in range(nhrun // 2):
            if pr >= 1:
                gp.wait_ge(eT.h, sg_t[pr * 4 - 1])
            gp.dma_start(out=rec0[:], in_=rrecb[2 * pr:2 * pr + 1, :]).then_inc(dR0.h, 16)
            gp.dma_start(out=rec1[:], in_=rrecb[2 * pr + 1:2 * pr + 2, :]).then_inc(dR1.h, 16)

    @on("sync")
    def _(sy):
        sy.wait_ge(dBr.h, dBr.n)
        sy.wait_ge(dAt.h, dAt.n)
        for pr in range(nhrun // 2):
            if pr >= 2:
                sy.wait_ge(dAw[pr % 2].h, sg_wb[pr - 2])
            sy.dma_start(out=avsc[pr % 2][:], in_=AVS[pr]).then_inc(dAl[pr % 2].h, 16)
            sy.wait_ge(eV.h, sg_v[pr * 4 + 3])
            sy.dma_start(out=AVS[pr], in_=avsc[pr % 2][:]).then_inc(dAw[pr % 2].h, 16)

    @on("tensor")
    def _(t_):
        k = 0
        for pr in range(nhrun // 2):
            t_.wait_ge(dR0.h, sg_r0[pr])
            t_.wait_ge(dR1.h, sg_r1[pr])
            for c in range(4):
                if k >= 2:
                    t_.wait_ge(eV.h, sg_v[k - 2])
                t_.matmul(psb[k % 2][:], e01_sb[:], rec0[0:1, c * 512:(c + 1) * 512],
                          start=True, stop=False)
                t_.matmul(psb[k % 2][:], e10_sb[:], rec1[0:1, c * 512:(c + 1) * 512],
                          start=False, stop=True).then_inc(eT.h, 1)
                k += 1

    @on("vector")
    def _(v):
        k = 0
        for pr in range(nhrun // 2):
            v.wait_ge(dAl[pr % 2].h, sg_ld[pr])
            for c in range(4):
                v.wait_ge(eT.h, sg_t[k])
                sl = avsc[pr % 2][:, c * 512:(c + 1) * 512]
                v.tensor_tensor(sl, sl, psb[k % 2][:], ALU.mult).then_inc(eV.h, 1)
                k += 1
    SGV = eV.n
    es_sg.close()

    if upto == "attn":
        FV = dY.bump(16)

        @on("gpsimd")
        def _(gp):
            gp.wait_ge(dAw[0].h, dAw[0].n)
            gp.wait_ge(dAw[1].h, dAw[1].n)
            gp.dma_start(out=DBG[:].rearrange("p (r q) -> p r q", r=NH // 2),
                         in_=AVS.rearrange("r p q -> p r q")).then_inc(dY.h, 16)

        @on("sync")
        def _(sy):
            sy.wait_ge(dY.h, FV)
        return nc, finish

    es_rec.close()
    es_ptr2 = ExitStackR()
    ptr2 = ps(es_ptr2, "ptr2", [128, 1024], BF16)
    wo_t, wo_v = load_wT(WO, woT, HEADS_LAST_T, sg_v[-1], ptr2, wstage2)
    es_ptr2.close()
    es_ws2.close()

    es_op = ExitStackR()
    pop = [ps(es_op, f"pop{i}", [128, 512]) for i in range(2)]
    prt = ps(es_op, "prt", [128, 1024])
    plg = ps(es_op, "plg", [128, 512])

    op_nx, op_t, op_add, op_var, op_x2, op_tp = [], [], [], [], [], []
    op_ev32, op_lg, op_lgv, op_sq, op_rstd = [], [], [], [], []
    gt_m, gt_s, gt_v, op_spill, op_l2sp = [], [], [], [], []
    op_av = []
    for i in range(NT):
        op_nx.append(dX2[i % 2].bump(16))
        op_av.append(dAv[i % 2].bump(16))
        op_t.append([eT.bump(), eT.bump()])
        op_add.append(eV.bump())
        op_var.append(eV.bump())
        op_x2.append(eV.bump())
        op_tp.append(eT.bump())
        op_ev32.append(eV.bump())
        op_lg.append(eT.bump())
        op_lgv.append(eV.bump())
        op_sq.append(eS.bump())
        op_rstd.append(eS.bump())
        op_spill.append(dSpX[i % 2].bump(16))
        op_l2sp.append(dSpL[i % 2].bump(16))
    for i in range(NT):
        gt_m.append(eV.bump())
        gt_v.append(eV.bump())
        gt_s.append(eS.bump())

    @on("sync")
    def _(sy):
        sy.wait_ge(dAw[0].h, dAw[0].n)
        sy.wait_ge(dAw[1].h, dAw[1].n)
        for i in range(NT):
            if i >= 2:
                sy.wait_ge(eV.h, op_add[i - 2])
                sy.wait_ge(eT.h, op_t[i - 2][1])
            sy.dma_start(out=avo[i % 2][:],
                         in_=AVS.rearrange("r p q -> p r q")[:, :, i * 128:(i + 1) * 128]
                         ).then_inc(dAv[i % 2].h, 16)
            sy.dma_start(out=xin2[i % 2][:], in_=Xt[i]).then_inc(dX2[i % 2].h, 16)
            sy.wait_ge(eV.h, op_x2[i])
            sy.dma_start(out=X1t[i], in_=x1b[i % 2][:]).then_inc(dSpX[i % 2].h, 16)
            sy.dma_start(out=XL2t[i], in_=xl2b[i % 2][:]).then_inc(dSpL[i % 2].h, 16)

    @on("tensor")
    def _(t_):
        t_.wait_ge(eV.h, wo_v)
        for i in range(NT):
            t_.wait_ge(dAv[i % 2].h, op_av[i])
            for dh in range(2):
                if i >= 1:
                    t_.wait_ge(eV.h, op_add[i - 1])
                for tc in range(DC):
                    ins = t_.matmul(pop[dh][:],
                                    avo[i % 2][:, tc * 128:(tc + 1) * 128],
                                    woT[:, tc * D + dh * 512: tc * D + dh * 512 + 512],
                                    start=(tc == 0), stop=(tc == DC - 1))
                ins.then_inc(eT.h, 1)
            t_.wait_ge(eV.h, op_x2[i])
            if i >= 1:
                t_.wait_ge(eV.h, op_ev32[i - 1])
            for dc in range(DC):
                ins = t_.transpose(prt[:, (dc % 8) * 128:(dc % 8) * 128 + 128],
                                   xl2f[:, dc * 128:dc * 128 + 128], idf_sb[:])
            ins.then_inc(eT.h, 1)
            t_.wait_ge(eV.h, op_ev32[i])
            if i >= 1:
                t_.wait_ge(eV.h, op_lgv[i - 1])
            for dc in range(DC):
                ins = t_.matmul(plg[:, 0:E], xfm32[:, dc * 128:dc * 128 + 128],
                                rw_sb[:, dc * E:(dc + 1) * E],
                                start=(dc == 0), stop=(dc == DC - 1))
            ins.then_inc(eT.h, 1)

    @on("vector")
    def _(v):
        for i in range(NT):
            b = (i % 2) * 8
            v.wait_ge(dX2[i % 2].h, op_nx[i])
            if i >= 2:
                v.wait_ge(eS.h, op_sq[i - 2])
                v.wait_ge(dSpX[i % 2].h, op_spill[i - 2])
            for dh in range(2):
                v.wait_ge(eT.h, op_t[i][dh])
                v.tensor_tensor(x1b[i % 2][:, dh * 512:(dh + 1) * 512],
                                xin2[i % 2][:, dh * 512:(dh + 1) * 512],
                                pop[dh][:], ALU.add)
            v.drain()
            v.reduce_sum(mcol2[:, b:b + 1], x1b[i % 2][:], axis=AX)
            v.drain()
            v.tensor_scalar(mcol2[:, b:b + 1], mcol2[:, b:b + 1], 1.0 / D, 0.0,
                            ALU.mult, ALU.add).then_inc(eV.h, 1)
            v.wait_ge(eS.h, op_sq[i])
            v.wait_ge(eV.h, op_add[i])
            v.tensor_scalar(mcol2[:, b + 2:b + 3], mcol2[:, b + 1:b + 2], 1.0 / D, 0.0,
                            ALU.mult, ALU.add)
            v.tensor_scalar(mcol2[:, b + 3:b + 4], mcol2[:, b:b + 1], mcol2[:, b:b + 1],
                            None, ALU.mult)
            v.drain()
            v.tensor_scalar(mcol2[:, b + 2:b + 3], mcol2[:, b + 2:b + 3],
                            mcol2[:, b + 3:b + 4], None, ALU.subtract)
            v.drain()
            v.tensor_scalar(mcol2[:, b + 2:b + 3], mcol2[:, b + 2:b + 3], 1e-5, 1.0,
                            ALU.add, ALU.mult).then_inc(eV.h, 1)
            v.wait_ge(eS.h, op_rstd[i])
            v.tensor_scalar(xl2f[:], x1b[i % 2][:], mcol2[:, b:b + 1], mcol2[:, b + 4:b + 5],
                            ALU.subtract, ALU.mult)
            if i >= 2:
                v.wait_ge(dSpL[i % 2].h, op_l2sp[i - 2])
            v.drain()
            v.tensor_copy(xl2b[i % 2][:], xl2f[:]).then_inc(eV.h, 1)
            v.wait_ge(eT.h, op_tp[i])
            v.tensor_copy(xfm32[:], prt[:]).then_inc(eV.h, 1)
            v.wait_ge(eT.h, op_lg[i])
            v.tensor_copy(logits[:, i * E:(i + 1) * E], plg[:, 0:E]).then_inc(eV.h, 1)


    @on("scalar")
    def _(s):
        for i in range(NT):
            b = (i % 2) * 8
            s.wait_ge(eV.h, op_add[i])
            s.activation(sqd2[:], x1b[i % 2][:], ACT.Square,
                         accum_out=mcol2[:, b + 1:b + 2]).then_inc(eS.h, 1)
            s.wait_ge(eV.h, op_var[i])
            s.activation(mcol2[:, b + 3:b + 4], mcol2[:, b + 2:b + 3], ACT.Ln)
            s.drain()
            s.activation(mcol2[:, b + 4:b + 5], mcol2[:, b + 3:b + 4], ACT.Exp,
                         scale=-0.5).then_inc(eS.h, 1)


    @on("vector")
    def _(v):
        for i in range(NT):
                L = logits[:, i * E:(i + 1) * E]
                if i >= 1:
                    v.wait_ge(eS.h, gt_s[i - 1])
                v.wait_ge(eV.h, op_lgv[i])
                v.drain()
                v.reduce_max(gsc[:, 0:1], L, axis=AX)
                v.drain()
                v.tensor_scalar(gsc[:, 1:2], gsc[:, 0:1], -1.0, 0.0,
                                ALU.mult, ALU.add).then_inc(eV.h, 1)
                v.wait_ge(eS.h, gt_s[i])
                v.reciprocal(gsc[:, 4:5], gsc[:, 3:4])
                v.tensor_scalar(gsc[:, 8:16], L, gsc[:, 0:1], None, ALU.is_equal)
                v.drain()
                v.tensor_tensor_scan(gsc[:, 16:24], gsc[:, 8:16], gsc[:, 8:16],
                                     0.0, ALU.add, ALU.bypass)
                v.drain()
                v.tensor_sub(gsc[:, 16:24], gsc[:, 16:24], gsc[:, 8:16])
                v.drain()
                v.tensor_scalar(gsc[:, 16:24], gsc[:, 16:24], 0.0, None, ALU.is_equal)
                v.drain()
                v.tensor_tensor(m1a[:, i * E:(i + 1) * E], gsc[:, 8:16], gsc[:, 16:24], ALU.mult)
                v.drain()
                v.tensor_tensor(gsc[:, 8:16], gsc[:, 24:32], m1a[:, i * E:(i + 1) * E], ALU.mult)
                v.drain()
                v.reduce_sum(gsc[:, 5:6], gsc[:, 8:16], axis=AX)
                v.drain()
                v.tensor_scalar(g1kc[:, i:i + 1], gsc[:, 5:6], gsc[:, 4:5], None, ALU.mult)
                v.scalar_tensor_tensor(gsc[:, 8:16], m1a[:, i * E:(i + 1) * E], -1e9, L,
                                       ALU.mult, ALU.add)
                v.drain()
                v.reduce_max(gsc[:, 6:7], gsc[:, 8:16], axis=AX)
                v.drain()
                v.tensor_scalar(gsc[:, 8:16], gsc[:, 8:16], gsc[:, 6:7], None, ALU.is_equal)
                v.drain()
                v.tensor_tensor_scan(gsc[:, 16:24], gsc[:, 8:16], gsc[:, 8:16],
                                     0.0, ALU.add, ALU.bypass)
                v.drain()
                v.tensor_sub(gsc[:, 16:24], gsc[:, 16:24], gsc[:, 8:16])
                v.drain()
                v.tensor_scalar(gsc[:, 16:24], gsc[:, 16:24], 0.0, None, ALU.is_equal)
                v.drain()
                v.tensor_tensor(m2a[:, i * E:(i + 1) * E], gsc[:, 8:16], gsc[:, 16:24], ALU.mult)
                v.drain()
                v.tensor_tensor(gsc[:, 8:16], gsc[:, 24:32], m2a[:, i * E:(i + 1) * E], ALU.mult)
                v.drain()
                v.reduce_sum(gsc[:, 5:6], gsc[:, 8:16], axis=AX)
                v.drain()
                v.tensor_scalar(g2kc[:, i:i + 1], gsc[:, 5:6], gsc[:, 4:5], None,
                                ALU.mult).then_inc(eV.h, 1)

    @on("scalar")
    def _(s):
        for i in range(NT):
            s.wait_ge(eV.h, gt_m[i])
            s.activation(gsc[:, 24:32], logits[:, i * E:(i + 1) * E], ACT.Exp,
                         bias=gsc[:, 1:2], accum_out=gsc[:, 3:4]).then_inc(eS.h, 1)

    es_op.close()
    es_ln2.close()
    es_wo.close()
    es_at.close()

    es_cs = ExitStackR()
    pcs = ps(es_cs, "pcs", [128, 512])
    pcnt = ps(es_cs, "pcnt", [1, 512])

    cs1_t = [eT.bump() for _ in range(NT)]
    cs1_v = [eV.bump() for _ in range(NT)]
    cnt_t = eT.bump()
    cnt_s = eS.bump()
    cs2_t = [eT.bump() for _ in range(NT)]
    cs2_v = [eV.bump() for _ in range(NT)]
    GATE_V = eV.bump()

    @on("tensor")
    def _(t_):
        t_.wait_ge(eV.h, gt_v[-1])
        for i in range(NT):
            if i >= 1:
                t_.wait_ge(eV.h, cs1_v[i - 1])
            for j in range(i + 1):
                ins = t_.matmul(pcs[:, 0:E],
                                (tri_sb if j == i else onesf_sb)[:],
                                m1a[:, j * E:(j + 1) * E],
                                start=(j == 0), stop=(j == i))
            ins.then_inc(eT.h, 1)
        t_.wait_ge(eV.h, cs1_v[-1])
        for i in range(NT):
            ins = t_.matmul(pcnt[:, 0:E], onecol_sb[:], m1k[:, i * E:(i + 1) * E],
                            start=(i == 0), stop=(i == NT - 1))
        ins.then_inc(eT.h, 1)
        t_.wait_ge(eS.h, cnt_s)
        for i in range(NT):
            if i >= 1:
                t_.wait_ge(eV.h, cs2_v[i - 1])
            for j in range(i + 1):
                t_.matmul(pcs[:, 0:E],
                          (tri_sb if j == i else onesf_sb)[:],
                          m2a[:, j * E:(j + 1) * E],
                          start=(j == 0), stop=False)
            ins = t_.matmul(pcs[:, 0:E], onerow_sb[:], county[:],
                            start=False, stop=True)
            ins.then_inc(eT.h, 1)

    @on("scalar")
    def _(s):
        s.wait_ge(eT.h, cnt_t)
        s.copy(county[:], pcnt[0:1, 0:E]).then_inc(eS.h, 1)

    @on("vector")
    def _(v):
        v.wait_ge(eV.h, gt_v[-1])
        for i in range(NT):
            v.wait_ge(eT.h, cs1_t[i])
            sl = slice(i * E, (i + 1) * E)
            v.tensor_copy(p1x[:, sl], pcs[:, 0:E])
            v.drain()
            v.tensor_scalar(gsc[:, 8:16], p1x[:, sl], float(CAP), None, ALU.is_lt)
            v.drain()
            v.tensor_tensor(m1k[:, sl], m1a[:, sl], gsc[:, 8:16], ALU.mult)
            v.drain()
            v.scalar_tensor_tensor(gsc[:, 8:16], p1x[:, sl], 1.0, m1k[:, sl],
                                   ALU.add, ALU.mult)
            v.drain()
            v.tensor_scalar(q1a[:, sl], gsc[:, 8:16], 1.0, 1.0,
                            ALU.subtract, ALU.mult).then_inc(eV.h, 1)
        for i in range(NT):
            v.wait_ge(eT.h, cs2_t[i])
            sl = slice(i * E, (i + 1) * E)
            v.tensor_copy(p2x[:, sl], pcs[:, 0:E])
            v.drain()
            v.tensor_scalar(gsc[:, 8:16], p2x[:, sl], float(CAP), None, ALU.is_lt)
            v.drain()
            v.tensor_tensor(m2k[:, sl], m2a[:, sl], gsc[:, 8:16], ALU.mult)
            v.drain()
            v.scalar_tensor_tensor(gsc[:, 8:16], p2x[:, sl], 1.0, m2k[:, sl],
                                   ALU.add, ALU.mult)
            v.drain()
            v.tensor_scalar(q2a[:, sl], gsc[:, 8:16], 1.0, 1.0,
                            ALU.subtract, ALU.mult).then_inc(eV.h, 1)
        for i in range(NT):
            sl = slice(i * E, (i + 1) * E)
            v.reduce_sum(gsc[:, 0:1], m1k[:, sl], axis=AX)
            v.drain()
            v.tensor_scalar(g1kc[:, i:i + 1], g1kc[:, i:i + 1], gsc[:, 0:1],
                            None, ALU.mult)
            v.reduce_sum(gsc[:, 1:2], m2k[:, sl], axis=AX)
            v.drain()
            ins = v.tensor_scalar(g2kc[:, i:i + 1], g2kc[:, i:i + 1], gsc[:, 1:2],
                                  None, ALU.mult)
        ins.then_inc(eV.h, 1)
    es_cs.close()

    QCA_V = eV.bump()

    @on("vector")
    def _(v):
        v.wait_ge(eV.h, GATE_V)
        v.tensor_tensor(p1x[:], q1a[:], q2a[:], ALU.max)
        v.tensor_scalar(m1k[:], q1a[:], -0.5, None, ALU.is_gt)
        v.tensor_scalar(m2k[:], q2a[:], -0.5, None, ALU.is_gt)
        v.drain()
        for i in range(NT):
            sl = slice(i * E, (i + 1) * E)
            v.tensor_scalar(m1k[:, sl], m1k[:, sl], g1kc[:, i:i + 1], None, ALU.mult)
            v.tensor_scalar(m2k[:, sl], m2k[:, sl], g2kc[:, i:i + 1], None, ALU.mult)
        v.drain()
        ins = v.tensor_tensor(p2x[:], m1k[:], m2k[:], ALU.add)
        ins.then_inc(eV.h, 1)

    if upto == "gate":
        FV = dY.bump(16 * 11)

        @on("gpsimd")
        def _(gp):
            gp.wait_ge(eV.h, GATE_V)
            for k, arr in enumerate([logits, m1a, m2a, m1k, m2k, q1a, q2a, p1x, p2x]):
                gp.dma_start(out=DBG[:, k * NT * E:(k + 1) * NT * E],
                             in_=arr[:]).then_inc(dY.h, 16)
            gp.dma_start(out=DBG[:, 9 * NT * E: 9 * NT * E + NT],
                         in_=g1kc[:]).then_inc(dY.h, 16)
            gp.dma_start(out=DBG[:, 9 * NT * E + NT: 9 * NT * E + 2 * NT],
                         in_=g2kc[:]).then_inc(dY.h, 16)

        @on("sync")
        def _(sy):
            sy.wait_ge(dY.h, FV)
        return nc, finish

    es_ga = ExitStackR()
    pg = [ps(es_ga, f"pg{mc}", [128, 512]) for mc in range(DC)]
    es_ffn = ExitStackR()
    xe = sb(es_ffn, "xe", [128, E * DC * 512], BF16)
    dtile = [sb(es_ga, f"dtile{i}", [128, 512], BF16) for i in range(2)]
    xl2g = [sb(es_ga, f"xl2g{i}", [128, D], BF16) for i in range(2)]

    ga_bv, ga_t, ga_ev, ga_rl = {}, {}, {}, {}
    for e in range(E):
        for i in range(NT):
            ga_bv[(e, i)] = eV.bump()
            ga_t[(e, i)] = eT.bump()
            ga_rl[(e, i)] = dGr[(e * NT + i) % 2].bump(16)
        for mc in range(DC):
            ga_ev[(e, mc)] = eV.bump()

    @on("sync")
    def _(sy):
        for q in range(2):
            sy.wait_ge(dSpX[q].h, dSpX[q].n)
            sy.wait_ge(dSpL[q].h, dSpL[q].n)
        for e in range(E):
            for i in range(NT):
                k = e * NT + i
                if k >= 2:
                    sy.wait_ge(eT.h, ga_t[((k - 2) // NT, (k - 2) % NT)])
                sy.dma_start(out=xl2g[k % 2][:], in_=XL2t[i]).then_inc(dGr[k % 2].h, 16)

    @on("vector")
    def _(v):
        v.wait_ge(eV.h, QCA_V)
        for e in range(E):
            for i in range(NT):
                k = e * NT + i
                if k >= 2:
                    v.wait_ge(eT.h, ga_t[((k - 2) // NT, (k - 2) % NT)])
                v.tensor_scalar(dtile[k % 2][:], iota_sb[:],
                                p1x[:, i * E + e:i * E + e + 1], None,
                                ALU.is_equal).then_inc(eV.h, 1)
            for mc in range(DC):
                v.wait_ge(eT.h, ga_t[(e, NT - 1)])
                v.tensor_copy(xe[:, e * DC * 512 + mc * 512:(e * DC + mc + 1) * 512],
                              pg[mc][:]).then_inc(eV.h, 1)

    @on("tensor")
    def _(t_):
        for e in range(E):
            for i in range(NT):
                t_.wait_ge(eV.h, ga_bv[(e, i)])
                if i == 0 and e >= 1:
                    for mc in range(DC):
                        t_.wait_ge(eV.h, ga_ev[(e - 1, mc)])
                t_.wait_ge(dGr[(e * NT + i) % 2].h, ga_rl[(e, i)])
                for mc in range(DC):
                    ins = t_.matmul(pg[mc][:],
                                    xl2g[(e * NT + i) % 2][:, mc * 128:(mc + 1) * 128],
                                    dtile[(e * NT + i) % 2][:],
                                    start=(i == 0), stop=(i == NT - 1))
                ins.then_inc(eT.h, 1)
    GATHER_V = eV.n
    es_ga.close()

    if upto == "gather":
        FV = dY.bump(16)

        @on("gpsimd")
        def _(gp):
            gp.wait_ge(eV.h, GATHER_V)
            gp.dma_start(out=DBG, in_=xe[:]).then_inc(dY.h, 16)

        @on("sync")
        def _(sy):
            sy.wait_ge(dY.h, FV)
        return nc, finish

    h_fm = sb(es_ffn, "h_fm", [128, NHC * 512], BF16)
    w0b = [sb(es_ffn, f"w0b{i}", [128, DC * 512], BF16) for i in range(3)]
    w1b = [sb(es_ffn, f"w1b{i}", [128, DC * 512], BF16) for i in range(3)]
    w0tl = sb(es_ffn, "w0tl", [128, DC * 384], BF16)
    w1tl = sb(es_ffn, "w1tl", [128, DC * 384], BF16)
    wob = [sb(es_ffn, f"wob{i}", [128, 2 * D], BF16) for i in range(3)]
    wotl = sb(es_ffn, "wotl", [128, D], BF16)
    gtmp = [sb(es_ffn, f"gtmp{i}", [128, 512], F32) for i in range(2)]
    eob = sb(es_ffn, "eob", [128, 4 * D], BF16)
    es_px = ExitStackR()
    pall = [ps(es_px, f"pall{i}", [128, 512]) for i in range(8)]

    ZTAILS = eV.bump()

    @on("vector")
    def _(v):
        v.memset(w0tl[:], 0.0)
        v.memset(w1tl[:], 0.0)
        v.memset(wotl[:], 0.0).then_inc(eV.h, 1)

    w01_dma, wo_dma = {}, {}
    hmm_t, gel_s, hfm_v, wo_t = {}, {}, {}, {}
    eo_s, eo_d = {}, {}
    for e in range(E):
        for lb in range(NW01 + 1):
            if lb < NW01:
                k = e * NW01 + lb
                w01_dma[(e, lb)] = (dW0[k % 3].bump(16), dW1[k % 3].bump(16))
            else:
                w01_dma[(e, lb)] = (dWt[0].bump(16), dWt[1].bump(16))
        for pb in range(WO_PAIRS + 1):
            if pb < WO_PAIRS:
                k = e * WO_PAIRS + pb
                wo_dma[(e, pb)] = dWo[k % 3].bump(16)
            else:
                wo_dma[(e, pb)] = dWot.bump(16)
        for hc in range(NHC):
            hmm_t[(e, hc)] = eT.bump(2)
            gel_s[(e, hc)] = eS.bump()
            hfm_v[(e, hc)] = eV.bump()
        for hc in range(NHC):
            wo_t[(e, hc)] = eT.bump()
        eo_s[e] = eS.bump(8)
        eo_d[e] = dEo.bump(16)

    W0r = W0.rearrange("e (mc p) h -> e p mc h", p=128)
    W1r = W1.rearrange("e (mc p) h -> e p mc h", p=128)
    WOr = WOE[:, 0:5376, :].rearrange("e (hc p) m -> e p hc m", p=128)

    @on("gpsimd")
    def _(gp):
        for q in range(2):
            gp.wait_ge(dSpX[q].h, dSpX[q].n)
            gp.wait_ge(dSpL[q].h, dSpL[q].n)
        gp.wait_ge(eV.h, ZTAILS)
        for e in range(E):
            for lb in range(NW01 + 1):
                if lb < NW01:
                    pty = (e * NW01 + lb) % 3
                    kk = e * NW01 + lb - 3
                    if kk >= 0:
                        pe, plb = kk // NW01, kk % NW01
                        gp.wait_ge(eT.h, hmm_t[(pe, plb * 4 + 3)])
                    gp.dma_start(out=w0b[pty][:],
                                 in_=W0r[e, :, :, lb * 512:(lb + 1) * 512]
                                 ).then_inc(dW0[pty].h, 16)
                    gp.dma_start(out=w1b[pty][:],
                                 in_=W1r[e, :, :, lb * 512:(lb + 1) * 512]
                                 ).then_inc(dW1[pty].h, 16)
                else:
                    if e >= 1:
                        gp.wait_ge(eT.h, hmm_t[(e - 1, NHC - 1)])
                    tl0 = w0tl[:].rearrange("p (mc h) -> p mc h", mc=DC)[:, :, 0:344]
                    tl1 = w1tl[:].rearrange("p (mc h) -> p mc h", mc=DC)[:, :, 0:344]
                    gp.dma_start(out=tl0, in_=W0r[e, :, :, 5120:HR]).then_inc(dWt[0].h, 16)
                    gp.dma_start(out=tl1, in_=W1r[e, :, :, 5120:HR]).then_inc(dWt[1].h, 16)
            for pb in range(WO_PAIRS + 1):
                if pb < WO_PAIRS:
                    kk = e * WO_PAIRS + pb - 3
                    pty = (e * WO_PAIRS + pb) % 3
                    if kk >= 0:
                        pe, ppb = kk // WO_PAIRS, kk % WO_PAIRS
                        gp.wait_ge(eT.h, wo_t[(pe, ppb * 2 + 1)])
                    gp.dma_start(out=wob[pty][:],
                                 in_=WOr[e, :, 2 * pb:2 * pb + 2, :]).then_inc(dWo[pty].h, 16)
                else:
                    if e >= 1:
                        gp.wait_ge(eT.h, wo_t[(e - 1, NHC - 1)])
                    gp.dma_start(out=wotl[0:88, :],
                                 in_=WOE[e, 5376:HR, :]).then_inc(dWot.h, 16)

    @on("tensor")
    def _(t_):
        for e in range(E):
            for hc in range(NHC):
                if hc < 40:
                    lb = hc // 4
                    pty = (e * NW01 + lb) % 3
                    s0 = w0b[pty][:].rearrange("p (mc x) -> p mc x", mc=DC)
                    s1 = w1b[pty][:].rearrange("p (mc x) -> p mc x", mc=DC)
                    col = (hc % 4) * 128
                    v0, v1 = w01_dma[(e, lb)]
                    t_.wait_ge(dW0[pty].h, v0)
                    t_.wait_ge(dW1[pty].h, v1)
                else:
                    s0 = w0tl[:].rearrange("p (mc x) -> p mc x", mc=DC)
                    s1 = w1tl[:].rearrange("p (mc x) -> p mc x", mc=DC)
                    col = (hc - 40) * 128
                    v0, v1 = w01_dma[(e, NW01)]
                    t_.wait_ge(dWt[0].h, v0)
                    t_.wait_ge(dWt[1].h, v1)
                kg = e * NHC + hc
                if kg >= 2:
                    pe, phc = (kg - 2) // NHC, (kg - 2) % NHC
                    t_.wait_ge(eS.h, gel_s[(pe, phc)])
                    t_.wait_ge(eV.h, hfm_v[(pe, phc)])
                if hc < 2 and e >= 1:
                    t_.wait_ge(eS.h, eo_s[e - 1])
                if hc == 0 and e == 0:
                    for mc in range(DC):
                        t_.wait_ge(eV.h, ga_ev[(E - 1, mc)])
                for mc in range(DC):
                    ins = t_.matmul(pall[hc % 2][:], s0[:, mc, col:col + 128],
                                    xe[:, e * DC * 512 + mc * 512:(e * DC + mc + 1) * 512],
                                    start=(mc == 0), stop=(mc == DC - 1))
                ins.then_inc(eT.h, 1)
                for mc in range(DC):
                    ins = t_.matmul(pall[2 + hc % 2][:], s1[:, mc, col:col + 128],
                                    xe[:, e * DC * 512 + mc * 512:(e * DC + mc + 1) * 512],
                                    start=(mc == 0), stop=(mc == DC - 1))
                ins.then_inc(eT.h, 1)
            for hc in range(NHC):
                t_.wait_ge(eV.h, hfm_v[(e, hc)])
                if hc == 0:
                    t_.wait_ge(eS.h, gel_s[(e, NHC - 1)])
                    t_.wait_ge(eV.h, hfm_v[(e, NHC - 1)])
                    if e >= 1:
                        t_.wait_ge(eS.h, eo_s[e - 1])
                if hc < 42:
                    pb = hc // 2
                    pty = (e * WO_PAIRS + pb) % 3
                    t_.wait_ge(dWo[pty].h, wo_dma[(e, pb)])
                    wsrc = wob[pty]
                    woff = (hc % 2) * D
                else:
                    t_.wait_ge(dWot.h, wo_dma[(e, WO_PAIRS)])
                    wsrc = wotl
                    woff = 0
                for st in range(4):
                    for mh in range(2):
                        ins = t_.matmul(pall[st * 2 + mh][:],
                                        h_fm[:, hc * 512 + st * 128: hc * 512 + st * 128 + 128],
                                        wsrc[:, woff + mh * 512: woff + (mh + 1) * 512],
                                        start=(hc == 0), stop=(hc == NHC - 1))
                ins.then_inc(eT.h, 1)

    @on("scalar")
    def _(s):
        for e in range(E):
            for hc in range(NHC):
                kg = e * NHC + hc
                s.wait_ge(eT.h, hmm_t[(e, hc)] - 1)
                if kg >= 2:
                    pe, phc = (kg - 2) // NHC, (kg - 2) % NHC
                    s.wait_ge(eV.h, hfm_v[(pe, phc)])
                s.activation(gtmp[hc % 2][:], pall[hc % 2][:],
                             ACT.Gelu_apprx_tanh).then_inc(eS.h, 1)
            s.wait_ge(eT.h, wo_t[(e, NHC - 1)])
            if e >= 1:
                s.wait_ge(dEo.h, eo_d[e - 1])
            for st in range(4):
                for mh in range(2):
                    s.copy(eob[:, st * D + mh * 512: st * D + mh * 512 + 512],
                           pall[st * 2 + mh][:]).then_inc(eS.h, 1)

    @on("vector")
    def _(v):
        for e in range(E):
            for hc in range(NHC):
                v.wait_ge(eS.h, gel_s[(e, hc)])
                v.wait_ge(eT.h, hmm_t[(e, hc)])
                if e >= 1:
                    v.wait_ge(eT.h, wo_t[(e - 1, hc)])
                v.tensor_tensor(h_fm[:, hc * 512:(hc + 1) * 512], gtmp[hc % 2][:],
                                pall[2 + hc % 2][:], ALU.mult).then_inc(eV.h, 1)

    @on("sync")
    def _(sy):
        for e in range(E):
            sy.wait_ge(eS.h, eo_s[e])
            sy.dma_start(out=EOSt[e],
                         in_=eob[:].rearrange("p (k m) -> p k m", k=4)).then_inc(dEo.h, 16)

    EXP_DONE_S = eS.n
    es_px.close()
    es_ffn.close()

    es_sc = ExitStackR()
    eors = sb(es_sc, "eors", [128, E * 4 * D], BF16)
    cTb = [sb(es_sc, f"cTb{i}", [128, E * 512], BF16) for i in range(2)]
    ctm8 = [sb(es_sc, f"ctm8{i}", [128, E * 512], BF16) for i in range(2)]
    x1r = [sb(es_sc, f"x1r{i}", [128, D], F32) for i in range(2)]
    yout = [sb(es_sc, f"yout{i}", [128, D], F32) for i in range(2)]
    ptc = [ps(es_sc, f"ptc{i}", [128, 512], BF16) for i in range(2)]
    psc2 = [ps(es_sc, f"psc2{i}", [128, 512]) for i in range(4)]

    eo_rel = {e: dEr[0].bump(16) for e in range(E)}
    x1_rel, sc_bv, sc_tp, sc_te, sc_mm, sc_y, y_dma = {}, {}, {}, {}, {}, {}, {}
    for i in range(NT):
        x1_rel[i] = dXr[i % 2].bump(16)
        y_dma[i] = dY.bump(16)
    for ii in range(NT + 1):
        if ii < NT:
            for e in range(E):
                sc_bv[(ii, e)] = eV.bump()
            for e in range(E):
                sc_te[(ii, e)] = eV.bump()
            if ii == 0:
                for e in range(E):
                    sc_tp[(0, e)] = eT.bump()
        if ii >= 1:
            for e in range(E):
                if e == E - 1:
                    sc_mm[ii - 1] = eT.bump()
                if ii < NT:
                    sc_tp[(ii, e)] = eT.bump()
            sc_y[ii - 1] = eV.bump()

    Yt = Y.rearrange("(i p) m -> i p m", p=128)

    @on("sync")
    def _(sy):
        sy.wait_ge(eS.h, EXP_DONE_S)
        for e in range(E):
            sy.wait_ge(dEo.h, eo_d[e])
            sy.dma_start(out=eors[:, e * 4 * D:(e + 1) * 4 * D]
                         .rearrange("p (k m) -> p k m", k=4),
                         in_=EOSt[e]).then_inc(dEr[0].h, 16)
        for i in range(NT):
            if i >= 2:
                sy.wait_ge(eV.h, sc_y[i - 2])
            sy.dma_start(out=x1r[i % 2][:], in_=X1t[i]).then_inc(dXr[i % 2].h, 16)
        sy.wait_ge(dY.h, dY.n)

    @on("gpsimd")
    def _(gp):
        for i in range(NT):
            gp.wait_ge(eV.h, sc_y[i])
            gp.dma_start(out=Yt[i], in_=yout[i % 2][:]).then_inc(dY.h, 16)

    def emit_build_chunk(i):
        @on("vector")
        def _(v, i=i):
            if i == 0:
                v.wait_ge(eS.h, EXP_DONE_S)
                v.wait_ge(eV.h, QCA_V)
            for e in range(E):
                if i >= 2:
                    v.wait_ge(eT.h, sc_tp[(i - 2, e)])
                v.tensor_scalar(ctm8[i % 2][:, e * 512:(e + 1) * 512], iota_sb[:],
                                p1x[:, i * E + e:i * E + e + 1],
                                p2x[:, i * E + e:i * E + e + 1],
                                ALU.is_equal, ALU.mult).then_inc(eV.h, 1)
            for e in range(E):
                v.wait_ge(eT.h, sc_tp[(i, e)])
                if i >= 2:
                    v.wait_ge(eT.h, sc_mm[i - 2])
                v.tensor_copy(cTb[i % 2][:, e * 512:(e + 1) * 512],
                              ptc[(i * E + e) % 2][:]).then_inc(eV.h, 1)

    def emit_tp_group(t_, i, e):
        k = i * E + e
        t_.wait_ge(eV.h, sc_bv[(i, e)])
        if k >= 2:
            t_.wait_ge(eV.h, sc_te[((k - 2) // E, (k - 2) % E)])
        for kk4 in range(4):
            ins = t_.transpose(ptc[k % 2][:, kk4 * 128:kk4 * 128 + 128],
                               ctm8[i % 2][:, e * 512 + kk4 * 128:
                                            e * 512 + kk4 * 128 + 128],
                               idb_sb[:])
        ins.then_inc(eT.h, 1)

    def emit_tp0():
        @on("tensor")
        def _(t_):
            for e in range(E):
                emit_tp_group(t_, 0, e)

    def emit_mm_chunk(i):
        @on("tensor")
        def _(t_, i=i):
            if i >= 2:
                t_.wait_ge(eV.h, sc_y[i - 2])
            if i == 0:
                t_.wait_ge(dEr[0].h, eo_rel[E - 1])
            for e in range(E):
                t_.wait_ge(eV.h, sc_te[(i, e)])
                for kk4 in range(4):
                    for mh in range(2):
                        base = (e * 4 + kk4) * D + mh * 512
                        ins = t_.matmul(psc2[mh * 2 + i % 2][:],
                                        cTb[i % 2][:, e * 512 + kk4 * 128:
                                                   e * 512 + kk4 * 128 + 128],
                                        eors[:, base: base + 512],
                                        start=(e == 0 and kk4 == 0),
                                        stop=(e == E - 1 and kk4 == 3))
                if e == E - 1:
                    ins.then_inc(eT.h, 1)
                if i + 1 < NT:
                    emit_tp_group(t_, i + 1, e)

        @on("vector")
        def _(v, i=i):
            v.wait_ge(eT.h, sc_mm[i])
            v.wait_ge(dXr[i % 2].h, x1_rel[i])
            if i >= 2:
                v.wait_ge(dY.h, y_dma[i - 2])
            for mh in range(2):
                ins = v.tensor_tensor(yout[i % 2][:, mh * 512:(mh + 1) * 512],
                                      x1r[i % 2][:, mh * 512:(mh + 1) * 512],
                                      psc2[mh * 2 + i % 2][:], ALU.add)
            ins.then_inc(eV.h, 1)

    for i in range(NT + 1):
        if i < NT:
            emit_build_chunk(i)
            if i == 0:
                emit_tp0()
        if i >= 1:
            emit_mm_chunk(i - 1)

    es_sc.close()
    return nc, finish


def _consts():
    c = {}
    c["iota"] = np.tile(np.arange(512, dtype=np.float32), (128, 1))
    c["idb"] = np.eye(128).astype(ml_dtypes.bfloat16)
    c["idf"] = np.eye(128, dtype=np.float32)
    c["tri"] = np.triu(np.ones((128, 128), dtype=np.float32), k=1)
    c["onesf"] = np.ones((128, 128), dtype=np.float32)
    q = np.arange(128)
    s_ = np.arange(128)
    c["maskt"] = np.where(q[None, :] >= s_[:, None], 0.0, -1e30).astype(np.float32)
    c["onecol"] = np.ones((128, 1), dtype=np.float32)
    c["onecolb"] = np.ones((128, 1)).astype(ml_dtypes.bfloat16)
    c["onerow"] = np.ones((1, 128), dtype=np.float32)
    e01 = np.zeros((1, 128), dtype=np.float32)
    e01[0, 0:64] = 1.0
    e10 = np.zeros((1, 128), dtype=np.float32)
    e10[0, 64:128] = 1.0
    c["e01"] = e01.astype(ml_dtypes.bfloat16)
    c["e10"] = e10.astype(ml_dtypes.bfloat16)
    return c


def kernel(L0_attn_X, L0_attn_q_proj, L0_attn_k_proj, L0_attn_v_proj,
           L0_attn_o_proj, L0_ffn_router_w, L0_ffn_w0_kernel,
           L0_ffn_w1_kernel, L0_ffn_wo_kernel, _upto="full", _nh=NH):
    X = np.ascontiguousarray(np.asarray(L0_attn_X, dtype=np.float32))
    shared = {
        "wq": np.ascontiguousarray(np.asarray(L0_attn_q_proj, dtype=np.float32)),
        "wk": np.ascontiguousarray(np.asarray(L0_attn_k_proj, dtype=np.float32)),
        "wv": np.ascontiguousarray(np.asarray(L0_attn_v_proj, dtype=np.float32)),
        "wo": np.ascontiguousarray(np.asarray(L0_attn_o_proj, dtype=np.float32)),
        "rw": np.ascontiguousarray(np.asarray(L0_ffn_router_w, dtype=np.float32)),
        "w0": np.ascontiguousarray(np.asarray(L0_ffn_w0_kernel, dtype=np.float32)),
        "w1": np.ascontiguousarray(np.asarray(L0_ffn_w1_kernel, dtype=np.float32)),
        "woe": np.ascontiguousarray(np.asarray(L0_ffn_wo_kernel, dtype=np.float32)),
    }
    shared.update(_consts())
    nc, fin = build_nc(upto=_upto, nhrun=_nh)
    fin()
    in_maps = [dict(shared, x=np.ascontiguousarray(X[c])) for c in range(8)]
    res = run_bass_kernel_spmd(nc, in_maps, core_ids=list(range(8)))
    out = np.stack([np.asarray(res.results[c]["y"]) for c in range(8)], axis=0)
    if _upto != "full":
        return out, [np.asarray(res.results[c]["dbg"]) for c in range(8)]
    return out
